# version 05
# baseline (speedup 1.0000x reference)
import contextlib
import numpy as np
import ml_dtypes

import concourse.bass as bass
import concourse.mybir as mybir
from concourse.bass_utils import run_bass_kernel_spmd

F32 = mybir.dt.float32
BF16 = mybir.dt.bfloat16
I32 = mybir.dt.int32
F16 = mybir.dt.float16
ALU = mybir.AluOpType
AF = mybir.ActivationFunctionType
BF = ml_dtypes.bfloat16

B, S, D = 4, 2048, 2048
H, DN, DR, DV = 16, 128, 64, 128
NI = 5632
TOK = 1024
NT = 16
NKT = 16
NP = 8
GU = 2 * NI
NIT = 44
LN_EPS = 1e-5
RMS_EPS = 1e-6
SC_NOPE = float(1.0 / np.sqrt(128.0))
SC_ROPE = 0.125

_r = np.arange(D)
PERM = (1024 * (_r // 1024) + 8 * (_r % 128) + (_r // 128) % 8).astype(np.int64)

ENGINES = ("sync", "tensor", "vector", "scalar", "gpsimd")
_DEBUG = False


class Sem:
    __slots__ = ("h", "n", "name")

    def __init__(self, h, name):
        self.h, self.n, self.name = h, 0, name


class Bld:
    def __init__(self):
        self.nc = bass.Bass("TRN2", target_bir_lowering=False)
        self.plan = {e: [] for e in ENGINES}
        self.es = contextlib.ExitStack()
        self.sems = {}

    def sem(self, name):
        if name not in self.sems:
            self.sems[name] = Sem(self.es.enter_context(self.nc.semaphore(name)), name)
        return self.sems[name]

    def op(self, eng, make, waits=(), inc=None, incn=1):
        wl = [(s.h, int(v)) for (s, v) in waits if v > 0]
        ih = None
        if inc is not None:
            inc.n += incn
            ih = inc.h

        def fn(e, make=make, wl=wl, ih=ih, iv=incn):
            for (h, v) in wl:
                e.wait_ge(h, v)
            ins = make(e)
            if ih is not None:
                ins.then_inc(ih, iv)
        self.plan[eng].append(fn)
        return inc.n if inc is not None else None

    def dma(self, eng, out, in_, sem, waits=()):
        return self.op(eng, (lambda e, o=out, i=in_: e.dma_start(out=o, in_=i)),
                       waits=waits, inc=sem, incn=16)

    def sb(self, name, shape, dtype, offset):
        base = (self.nc.sbuf_base + 31) & ~31
        return self.nc.alloc_sbuf_tensor_at(name, list(shape), dtype,
                                            offset=base + offset)

    def barrier(self):
        ph = self.sem("ph")
        ph.n += 5
        t = ph.n
        for e in ENGINES:
            def fn(eng, h=ph.h, t=t):
                eng.nop().then_inc(h, 1)
                eng.wait_ge(h, t)
            self.plan[e].append(fn)


def build_program(debug=False):
    b = Bld()
    nc = b.nc
    okind = "ExternalOutput" if debug is True else "Internal"

    XT = nc.dram_tensor("XT", [D, S], F32, kind="ExternalInput").ap()
    WQT = nc.dram_tensor("WQT", [D, 3072], F32, kind="ExternalInput").ap()
    WKT = nc.dram_tensor("WKT", [D, 3072], F32, kind="ExternalInput").ap()
    WVT = nc.dram_tensor("WVT", [D, 2048], F32, kind="ExternalInput").ap()
    WOT = nc.dram_tensor("WOT", [D, 2048], F32, kind="ExternalInput").ap()
    PKG = nc.dram_tensor("PKG", [256, GU], I32, kind="ExternalInput").ap()
    PKD = nc.dram_tensor("PKD", [704, 2048], I32, kind="ExternalInput").ap()
    SCG = nc.dram_tensor("SCG", [256, GU], F16, kind="ExternalInput").ap()
    SCD = nc.dram_tensor("SCD", [704, 2048], F16, kind="ExternalInput").ap()
    COSR = nc.dram_tensor("COSR", [128, S], F32, kind="ExternalInput").ap()
    SINA = nc.dram_tensor("SINA", [128, S], F32, kind="ExternalInput").ap()
    OUT = nc.dram_tensor("OUT", [NT, 128, TOK], F32, kind="ExternalOutput").ap()

    dbgO = nc.dram_tensor("dbgO", [128, NT, 1024], BF16, kind=okind).ap()
    dbgE = nc.dram_tensor("dbgE", [128, 64], F32, kind=okind).ap()
    dbgM = nc.dram_tensor("dbgM", [H, 1, 32], F32, kind=okind).ap()
    dbgL2 = nc.dram_tensor("dbgL2", [128, NT, 1024], BF16, kind=okind).ap()
    dbgH = nc.dram_tensor("dbgH", [128, NIT, 1024], BF16, kind=okind).ap()
    dbgY = nc.dram_tensor("dbgY", [2, 128, NT, 512], F32, kind=okind).ap()
    dbgR = nc.dram_tensor("dbgR", [2 * H, 128, 512], F32, kind=okind).ap()
    k_sc = nc.dram_tensor("k_sc", [NP, 3, 128, S], BF16, kind=okind).ap()
    q_sc = nc.dram_tensor("q_sc", [NP, 3, 128, TOK], BF16, kind=okind).ap()
    v_sc = nc.dram_tensor("v_sc", [NP, NKT, 128, 256], BF16, kind=okind).ap()
    w1_sc = nc.dram_tensor("w1_sc", [NT, 128, GU], BF16, kind=okind).ap()
    wd_sc = nc.dram_tensor("wd_sc", [NIT, 128, 2048], BF16, kind=okind).ap()
    x2_sc = nc.dram_tensor("x2_sc", [NT, 128, TOK], F32, kind=("ExternalOutput" if debug in (True, 2) else "Internal")).ap()

    cells = b.sb("cells", [128, 32], F32, 0)
    ones_bf = b.sb("ones_bf", [128, 1], BF16, 128)
    ones_f32 = b.sb("ones_f32", [128, 1], F32, 160)
    ones1_f32 = b.sb("ones1_f32", [1, 128], F32, 192)
    negones1_f32 = b.sb("negones1_f32", [1, 128], F32, 207872)
    dbgE_sb = b.sb("dbgE_sb", [128, 16], F32, 208384)
    ones64A = b.sb("ones64A", [128, 1], BF16, 704)
    ones64B = b.sb("ones64B", [128, 1], BF16, 736)
    expbias = b.sb("expbias", [128, 16], F32, 768)
    rowbank = b.sb("rowbank", [128, 2048], F32, 1024)
    fbufA = b.sb("fbufA", [128, 2, 1024], F32, 9216)
    dq_t05 = b.sb("dq_t05", [128, 2, 1024], BF16, 9216)
    dq_smul = b.sb("dq_smul", [128, 2, 1024], BF16, 13312)
    wbufs = b.sb("wbufs", [128, 2, 16, 128], BF16, 17408)
    wpiece = b.sb("wpiece", [128, 2, 1024], BF16, 17408)
    tmpF = b.sb("tmpF", [128, 2048], F32, 17408)

    RD = 25600
    pkb = b.sb("pkb", [128, 2, 512], I32, RD)
    scb = b.sb("scb", [128, 2, 512], BF16, RD + 4096)
    m32 = b.sb("m32", [128, 512], I32, RD + 6144)
    s32 = b.sb("s32", [128, 512], I32, RD + 8192)
    dq_r1 = b.sb("dq_r1", [128, 2, 1024], BF16, RD + 10240)
    dq_r2 = b.sb("dq_r2", [128, 2, 1024], BF16, RD + 14336)
    dq_t2 = b.sb("dq_t2", [128, 1024], BF16, RD + 18432)
    dq_u = b.sb("dq_u", [128, 1024], BF16, RD + 20480)
    wout = b.sb("wout", [128, 2, 2, 512], BF16, RD + 22528)

    R1 = 52224
    ln1 = b.sb("ln1", [128, NT, 2048], BF16, R1)
    sqtile = b.sb("sqtile", [128, 2048], BF16, R1)
    Pbuf = b.sb("Pbuf", [128, 2, NKT, 512], BF16, R1 + 8192)
    rBuf = b.sb("rBuf", [128, 512], F32, R1 + 40960)
    xb2 = b.sb("xb2", [128, 1024], BF16, R1)
    xsq2 = b.sb("xsq2", [128, 1024], BF16, R1 + 2048)
    ln2 = b.sb("ln2", [128, NT, 1024], BF16, R1)
    sgbuf = b.sb("sgbuf", [128, 8, 512], BF16, R1 + 32768)
    htmp = b.sb("htmp", [128, 512], BF16, R1 + 40960)
    ybuf = b.sb("ybuf", [128, NT, 512], F32, R1)
    ysqb = b.sb("ysqb", [128, 512], BF16, R1 + 32768)
    rrB = b.sb("rrB", [128, 512], F32, R1 + 33792)

    R2 = 117760
    xsq = b.sb("xsq", [128, 2, 2048], BF16, R2)
    xb = b.sb("xb", [128, 2, 2048], BF16, R2 + 8192)
    xsA = b.sb("xsA", [128, 2, 2048], F32, R2 + 16384)
    cosR = b.sb("cosR", [128, 2048], F32, R2 + 32768)
    sinA = b.sb("sinA", [128, 2048], F32, R2 + 40960)
    aB = b.sb("aB", [128, 2048], F32, R2 + 49152)
    bB = b.sb("bB", [128, 2048], F32, R2 + 57344)
    estg = b.sb("estg", [128, 512], BF16, R2 + 65536)
    s_rope = b.sb("s_rope", [128, 512], BF16, R2 + 66560)
    sh_rope = b.sb("sh_rope", [128, 512], BF16, R2 + 67584)
    ropetmp = b.sb("ropetmp", [128, 512], F32, R2 + 68608)
    ropetmp2 = b.sb("ropetmp2", [128, 512], F32, R2 + 70656)
    wvb = b.sb("wvb", [128, 16, 512], BF16, R2 + 72704)
    kb = b.sb("kb", [128, 2, 3, 2048], BF16, R2)
    qb = b.sb("qb", [128, 2, 3, 1024], BF16, R2 + 24576)
    vb = b.sb("vb", [128, 2, NKT, 256], BF16, R2 + 36864)
    Obuf = b.sb("Obuf", [128, NT, 1024], BF16, R2 + 53248)
    aB2 = b.sb("aB2", [128, 1024], F32, R2)
    bB2 = b.sb("bB2", [128, 1024], F32, R2 + 4096)
    Hbuf = b.sb("Hbuf", [128, NIT, 1024], BF16, R2)

    psAll = b.es.enter_context(nc.psum_tensor("psAll", [128, 4096], F32))

    def pb(bank):
        return psAll[:, 512 * bank:512 * (bank + 1)]

    def pbr(bank, r0, r1, c0=0, c1=512):
        return psAll[r0:r1, 512 * bank + c0:512 * bank + c1]

    dXp = [b.sem("dX0"), b.sem("dX1")]
    dWp = [b.sem("dW0"), b.sem("dW1")]
    dWV = b.sem("dWV"); dTb = b.sem("dTb")
    dPKp = [b.sem("dPK0"), b.sem("dPK1")]
    dSCp = [b.sem("dSC0"), b.sem("dSC1")]
    dKW = b.sem("dKW")
    dKRp = [b.sem("dKR0"), b.sem("dKR1")]
    dQWp = [b.sem("dQW0"), b.sem("dQW1")]
    dFRp = [b.sem("dFR0"), b.sem("dFR1")]
    dX2W = b.sem("dX2W"); dX2R = b.sem("dX2R")
    dOse = b.sem("dOse"); dSh = b.sem("dSh")
    tMM = b.sem("tMM"); vEv = b.sem("vEv"); sEv = b.sem("sEv")
    dqa = b.sem("dqa"); dqv = b.sem("dqv"); dqx = b.sem("dqx")

    bank_guard = {i: [] for i in range(8)}

    def mm(bank, lhsT, rhs, start, stop, out_rows=None, waits=()):
        w = list(waits)
        if start:
            w += bank_guard[bank]
            bank_guard[bank] = []
        if out_rows is None:
            out = psAll[:, 512 * bank:512 * bank + rhs.shape[-1]]
        else:
            out = psAll[out_rows[0]:out_rows[1], 512 * bank:512 * bank + rhs.shape[-1]]
        return b.op("tensor",
                    lambda e, o=out, l=lhsT, r=rhs, st=start, sp=stop:
                    e.matmul(o, l, r, start=st, stop=sp),
                    waits=w, inc=tMM)

    def ldw(wap, waits=()):
        return b.op("tensor", lambda e, w=wap: e.ldweights(w), waits=waits)

    for ci, cv in {0: -2.0, 1: 0.5, 2: -6.0, 3: 1.0, 4: LN_EPS, 5: RMS_EPS,
                   6: 1.0 / 2048.0, 7: 1.0, 8: 2.0}.items():
        b.op("vector", lambda e, ci=ci, cv=cv: e.memset(cells[:, ci:ci + 1], cv))
    b.op("vector", lambda e: e.memset(ones_bf[:], 1.0))
    b.op("vector", lambda e: e.memset(ones_f32[:], 1.0))
    b.op("vector", lambda e: e.memset(ones1_f32[:], 1.0))
    b.op("vector", lambda e: e.memset(negones1_f32[:], -1.0))
    b.op("vector", lambda e: e.memset(ones64A[0:64, :], 1.0))
    b.op("vector", lambda e: e.memset(ones64A[64:128, :], 0.0))
    b.op("vector", lambda e: e.memset(ones64B[0:64, :], 0.0))
    consts_done = b.op("vector", lambda e: e.memset(ones64B[64:128, :], 1.0), inc=vEv)

    b.dma("sync", cosR[:], COSR, dTb)
    tables_done = b.dma("sync", sinA[:], SINA, dTb)

    dq_chunks = []
    for cb in range(2):
        for oc in range(GU // 512):
            dq_chunks.append(("g", cb, 512 * oc, 128))
    for cb in range(6):
        rows = 64 if cb == 5 else 128
        for oc in range(4):
            dq_chunks.append(("d", cb, 512 * oc, rows))
    N_DQ_UNITS = 4 * len(dq_chunks)
    dqs = {"u": 0, "pk_t": None, "sc_t": None, "a_t": 0, "pending2": None,
           "qw_done": [None] * N_DQ_UNITS}

    def emit_dq_unit():
        u = dqs["u"]
        if u >= N_DQ_UNITS:
            return False
        ci, jp = u // 4, u % 4
        kind, cb, c0, rows = dq_chunks[ci]
        slot = ci % 2
        par = u % 2
        if jp == 0:
            src_pk = (PKG if kind == "g" else PKD)[128 * cb:128 * cb + rows, c0:c0 + 512]
            src_sc = (SCG if kind == "g" else SCD)[128 * cb:128 * cb + rows, c0:c0 + 512]
            dqs["pk_t"] = (dPKp[slot], b.dma("scalar", pkb[0:rows, slot, :], src_pk,
                           dPKp[slot], waits=([(dqx, 4 * (ci - 1))] if ci >= 2 else [])))
            dqs["sc_t"] = (dSCp[slot], b.dma("gpsimd", scb[0:rows, slot, :], src_sc,
                           dSCp[slot], waits=([(dqv, 4 * (ci - 1))] if ci >= 2 else [])))
        xw = [dqs["pk_t"]] if jp == 0 else []
        if u >= 2:
            xw.append((dqa, u - 1))
        for jh in range(2):
            j = 2 * jp + jh
            b.op("vector", lambda e, r=rows, sl=slot, j=j:
                 e.tensor_scalar(m32[0:r, :], pkb[0:r, sl, :], 4 * j, 7,
                                 ALU.logical_shift_right, ALU.bitwise_and),
                 waits=(xw if jh == 0 else []))
            b.op("vector", lambda e, r=rows, sl=slot, j=j:
                 e.tensor_scalar(s32[0:r, :], pkb[0:r, sl, :], 4 * j + 3, 1,
                                 ALU.logical_shift_right, ALU.bitwise_and))
            b.op("vector", lambda e, r=rows, p=par, jh=jh:
                 e.tensor_scalar(dq_t05[0:r, p, 512 * jh:512 * (jh + 1)],
                                 m32[0:r, :], 0.5, None, ALU.mult))
            b.op("vector", lambda e, r=rows, p=par, jh=jh:
                 e.tensor_scalar(dq_smul[0:r, p, 512 * jh:512 * (jh + 1)],
                                 s32[0:r, :], -2.0, 1.0, ALU.mult, ALU.add),
                 inc=(dqx if jh == 1 else None))
        x_t = dqx.n
        sw = [(dqx, x_t)]
        if u == 0:
            sw.append((vEv, consts_done))
        if u >= 2:
            sw.append((dqv, u - 1))
        b.op("scalar", lambda e, r=rows, p=par: e.activation(
            dq_r1[0:r, p, :], dq_t05[0:r, p, :], AF.Relu,
            bias=cells[0:r, 0:1], scale=cells[0:r, 3:4]),
            waits=sw)
        a_t = b.op("scalar", lambda e, r=rows, p=par: e.activation(
            dq_r2[0:r, p, :], dq_t05[0:r, p, :], AF.Relu,
            bias=cells[0:r, 2:3], scale=cells[0:r, 8:9]),
            inc=dqa)
        dqs["a_t"] = a_t
        def stage2(e_unused=None, u=u, rows=rows, slot=slot, par=par, kind=kind,
                   cb=cb, c0=c0, a_t=a_t, sc_t=dqs["sc_t"]):
            vw = [(dqa, a_t), sc_t]
            if u >= 2:
                vw.append(dqs["qw_done"][u - 2])
            b.op("vector", lambda e, r=rows, p=par: e.tensor_add(
                dq_u[0:r, :], dq_t05[0:r, p, :], dq_r1[0:r, p, :]), waits=vw)
            b.op("vector", lambda e, r=rows, p=par: e.tensor_add(
                dq_t2[0:r, :], dq_u[0:r, :], dq_r2[0:r, p, :]))
            b.op("vector", lambda e, r=rows, p=par: e.tensor_mul(
                dq_u[0:r, :], dq_t2[0:r, :], dq_smul[0:r, p, :]))
            if kind == "g":
                b.op("vector", lambda e, r=rows, sl=slot: e.tensor_mul(
                    wout[0:r, par, 0, :], dq_u[0:r, 0:512], scb[0:r, sl, :]))
                v_t = b.op("vector", lambda e, r=rows, sl=slot: e.tensor_mul(
                    wout[0:r, par, 1, :], dq_u[0:r, 512:1024], scb[0:r, sl, :]),
                    inc=dqv)
            else:
                v_t = None
                for pl in range(2):
                    shout = wout[0:rows, par, pl, :].rearrange(
                        "p (a b) -> p a b", a=8)
                    shin = dq_u[0:rows, 512 * pl:512 * (pl + 1)].rearrange(
                        "p (b a) -> p a b", a=8)
                    shsc = scb[0:rows, slot, :].rearrange("p (b a) -> p a b", a=8)
                    v_t = b.op("vector", lambda e, so=shout, si=shin, ss=shsc:
                               e.tensor_tensor(so, si, ss, ALU.mult),
                               inc=(dqv if pl == 1 else None))
            jp_ = u % 4
            for jh in range(2):
                j = 2 * jp_ + jh
                if kind == "g":
                    dst = w1_sc[8 * cb + j, :, c0:c0 + 512]
                    b.dma("sync", dst, wout[0:rows, par, jh, :], dQWp[par],
                          waits=([(dqv, v_t)] if jh == 0 else []))
                else:
                    it = (40 + j // 2) if cb == 5 else (8 * cb + j)
                    r0 = 64 * (j % 2) if cb == 5 else 0
                    hb_ = c0 // 1024
                    p0 = 64 * ((c0 % 1024) // 512)
                    dst = wd_sc[it, r0:r0 + rows,
                                1024 * hb_:1024 * (hb_ + 1)].rearrange(
                        "p (a b) -> p a b", a=8)[:, :, p0:p0 + 64]
                    b.dma("sync", dst,
                          wout[0:rows, par, jh, :].rearrange("p (a b) -> p a b", a=8),
                          dQWp[par], waits=([(dqv, v_t)] if jh == 0 else []))
            dqs["qw_done"][u] = (dQWp[par], dQWp[par].n)
        if dqs["pending2"] is not None:
            pend = dqs["pending2"]
            dqs["pending2"] = stage2
            pend()
        else:
            dqs["pending2"] = stage2
        dqs["u"] = u + 1
        return True

    def emit_dq_flush():
        if dqs["pending2"] is not None:
            dqs["pending2"]()
            dqs["pending2"] = None

    def emit_dq_units(n):
        for _ in range(n):
            if not emit_dq_unit():
                return

    XT_t = XT.rearrange("(t p) n -> t p n", p=128)
    v_cnt = [0] * NT
    s_cnt = [0] * NT
    t_cnt = [0] * NT
    x_cnt = [0] * NT
    for tau in range(NT):
        sl = tau % 2
        wv = []
        if tau >= 2:
            wv = [(vEv, v_cnt[tau - 2]), (sEv, s_cnt[tau - 2])]
        x_cnt[tau] = b.dma("sync", xsA[:, sl, :], XT_t[tau], dXp[sl], waits=wv)
        wv2 = [(dXp[sl], x_cnt[tau])]
        if tau >= 2:
            wv2.append((tMM, t_cnt[tau - 2]))
        v_cnt[tau] = b.op("vector", lambda e, sl=sl: e.tensor_copy(xb[:, sl, :], xsA[:, sl, :]),
                          waits=wv2, inc=vEv)
        s_cnt[tau] = b.op("scalar", lambda e, sl=sl: e.activation(
            xsq[:, sl, :], xsA[:, sl, :], AF.Square), waits=wv2, inc=sEv)
        mw = [(vEv, v_cnt[tau]), (sEv, s_cnt[tau])]
        for c in range(4):
            mm(c, ones_bf[:], xb[:, sl, 512 * c:512 * (c + 1)],
               start=(tau == 0), stop=(tau == NT - 1), out_rows=(0, 1),
               waits=(mw if c == 0 else []))
        for c in range(4):
            t_cnt[tau] = mm(4 + c, ones_bf[:], xsq[:, sl, 512 * c:512 * (c + 1)],
                            start=(tau == 0), stop=(tau == NT - 1), out_rows=(0, 1))
        emit_dq_units(1)
    stats_mm_done = t_cnt[NT - 1]

    sA, sB, sC, sD = (rowbank[0:1, 0:512], rowbank[0:1, 512:1024],
                      rowbank[0:1, 1024:1536], rowbank[0:1, 1536:2048])

    def ln_coeffs(nchunk, sum_banks, sq_banks, eps_cell, dst_a, dst_b, first_wait):
        abc = 0
        prev_bt = 0
        for c in range(nchunk):
            bs, bq = sum_banks[c], sq_banks[c]
            w0 = list(first_wait) if c == 0 else [(tMM, prev_bt)]
            cp1 = b.op("scalar", lambda e, bs=bs: e.copy(sA, pbr(bs, 0, 1)),
                       waits=w0, inc=sEv)
            cp2 = b.op("scalar", lambda e, bq=bq: e.copy(sB, pbr(bq, 0, 1)), inc=sEv)
            b.op("vector", lambda e: e.tensor_scalar(sC, sA, 1.0 / 2048.0, None,
                                                     ALU.mult), waits=[(sEv, cp2)])
            b.op("vector", lambda e: e.tensor_scalar(sD, sB, 1.0 / 2048.0, None,
                                                     ALU.mult))
            b.op("vector", lambda e: e.tensor_mul(sA, sC, sC))
            var_v = b.op("vector", lambda e: e.tensor_sub(sB, sD, sA), inc=vEv)
            sq_v = b.op("scalar", lambda e, ec=eps_cell: e.activation(
                sD, sB, AF.Sqrt, bias=cells[0:1, ec:ec + 1], scale=cells[0:1, 7:8]),
                waits=[(vEv, var_v)], inc=sEv)
            b.op("vector", lambda e: e.reciprocal(sA, sD), waits=[(sEv, sq_v)])
            ab_v = b.op("vector", lambda e: e.scalar_tensor_tensor(
                sB, sC, -1.0, sA, ALU.mult, ALU.mult), inc=vEv)
            mm(bs, ones1_f32[:], sA, True, True,
               waits=[(vEv, ab_v), (sEv, cp2)])
            bt = mm(bq, ones1_f32[:], sB, True, True)
            prev_bt = bt
            a1 = b.op("scalar", lambda e, c=c, bs=bs: e.copy(
                dst_a[:, 512 * c:512 * (c + 1)], pb(bs)), waits=[(tMM, bt)], inc=sEv)
            abc = b.op("scalar", lambda e, c=c, bq=bq: e.copy(
                dst_b[:, 512 * c:512 * (c + 1)], pb(bq)), inc=sEv)
            bank_guard[bs] = [(sEv, abc)]
            bank_guard[bq] = [(sEv, abc)]
        return abc

    abc2 = ln_coeffs(4, [0, 1, 2, 3], [4, 5, 6, 7], 4, aB, bB,
                     [(tMM, stats_mm_done)])

    ln_v = [0] * NT
    for tau in range(NT):
        wv = [(sEv, abc2)] if tau == 0 else [(vEv, ln_v[tau - 1])]
        x_t = b.dma("sync", xsA[:, 0, :], XT_t[tau], dXp[0], waits=wv)
        b.op("vector", lambda e: e.tensor_mul(tmpF[:], xsA[:, 0, :], aB[:]),
             waits=[(dXp[0], x_t)])
        ln_v[tau] = b.op("vector", lambda e, tau=tau: e.tensor_add(
            ln1[:, tau, :], tmpF[:], bB[:]), inc=vEv)
        emit_dq_units(1)
    ln_done = ln_v[NT - 1]

    kw_done = [0] * NP
    vw_done = [0] * NP

    rope_state = {"v": 0}

    def rope_evac(bank, scale_c, tab_sl, dst, stg_guard):
        sc_v = b.op("scalar", lambda e, bk=bank: e.copy(s_rope[:], pb(bk)),
                    waits=list(stg_guard) + [(vEv, rope_state["v"])], inc=sEv)
        sh_t = 0
        for (d0, s0) in ((0, 32), (32, 0), (64, 96), (96, 64)):
            sh_t = b.dma("sync", sh_rope[d0:d0 + 32, :], s_rope[s0:s0 + 32, :], dSh,
                         waits=([(sEv, sc_v), (vEv, rope_state["v"])] if d0 == 0 else []))
        b.op("vector", lambda e, ts=tab_sl, c=scale_c: e.scalar_tensor_tensor(
            ropetmp[:], s_rope[:], c, cosR[:, ts[0]:ts[1]], ALU.mult, ALU.mult),
            waits=[(dSh, sh_t), (dTb, tables_done)])
        b.op("vector", lambda e, ts=tab_sl, c=scale_c: e.scalar_tensor_tensor(
            ropetmp2[:], sh_rope[:], c, sinA[:, ts[0]:ts[1]], ALU.mult, ALU.mult))
        ev = b.op("vector", lambda e: e.tensor_add(estg[:], ropetmp[:], ropetmp2[:]),
                  inc=vEv)
        rope_state["v"] = ev
        w_t = b.dma("sync", dst, estg[:], dKW, waits=[(vEv, ev)])
        return (sEv, sc_v), (dKW, w_t)

    prev_w_used = 0
    wslot_free = [(), ()]
    estg_guard = []
    for which, WT, dsts, ntc, rope_scale in (
        ("q", WQT, q_sc, 2, SC_ROPE),
        ("k", WKT, k_sc, 4, 1.0),
    ):
        WT_r = WT.rearrange("(t p) d -> p t d", p=128)
        for dc in range(24):
            hp, ct = dc // 3, dc % 3
            sl = dc % 2
            wv0 = list(wslot_free[sl])
            if which == "q" and dc < 2:
                wv0.append((vEv, ln_done))
            w_t = b.dma("gpsimd", wbufs[:, sl], WT_r[:, :, 128 * dc:128 * (dc + 1)],
                        dWp[sl], waits=wv0)
            banks = [2 * (dc % 2), 2 * (dc % 2) + 1] if ntc == 2 else \
                    ([0, 1, 2, 3] if dc % 2 == 0 else [4, 5, 6, 7])
            last_ldw_t = None
            for it in range(NT):
                lw = [(dWp[sl], w_t)] if it == 0 else []
                if it == 0:
                    lw.append((vEv, ln_done))
                ldw(wbufs[:, sl, it, :], waits=lw)
                for tc in range(ntc):
                    mm(banks[tc], wbufs[:, sl, it, :],
                       ln1[:, it, 512 * tc:512 * (tc + 1)],
                       start=(it == 0), stop=(it == NT - 1))
            mm_done = tMM.n
            wslot_free[sl] = [(tMM, mm_done)]
            for tc in range(ntc):
                bk = banks[tc]
                if ct < 2:
                    if which == "q":
                        ev = b.op("vector", lambda e, bk=bk: e.tensor_scalar(
                            estg[:], pb(bk), SC_NOPE, None, ALU.mult),
                            waits=[(tMM, mm_done)] + estg_guard, inc=vEv)
                        gsem = (vEv, ev)
                    else:
                        ev = b.op("scalar", lambda e, bk=bk: e.copy(estg[:], pb(bk)),
                                  waits=[(tMM, mm_done)] + estg_guard, inc=sEv)
                        gsem = (sEv, ev)
                    w2 = b.dma("sync", dsts[hp, ct, :, 512 * tc:512 * (tc + 1)],
                               estg[:], dKW, waits=[gsem])
                    bank_guard[bk] = [gsem]
                    estg_guard = [(dKW, w2)]
                else:
                    pg, wg = rope_evac(bk, rope_scale, (512 * tc, 512 * (tc + 1)),
                                       dsts[hp, 2, :, 512 * tc:512 * (tc + 1)],
                                       [(tMM, mm_done)] + estg_guard)
                    bank_guard[bk] = [pg]
                    estg_guard = [wg]
            if which == "k" and ct == 2:
                kw_done[hp] = dKW.n
            emit_dq_units(2)

    WVT_r = WVT.rearrange("(t p) d -> p t d", p=128)
    wv_free = []
    for dvc in range(4):
        w_t = b.dma("gpsimd", wvb[:], WVT_r[:, :, 512 * dvc:512 * (dvc + 1)], dWV,
                    waits=wv_free)
        for kc in range(NKT):
            bk = 4 + kc % 4
            for it in range(NT):
                lw = [(dWV, w_t), (vEv, ln_done)] if it == 0 else []
                ldw(ln1[:, it, 128 * kc:128 * (kc + 1)], waits=lw)
                mm(bk, ln1[:, it, 128 * kc:128 * (kc + 1)], wvb[:, it, :],
                   start=(it == 0), stop=(it == NT - 1))
            mm_done = tMM.n
            ev = b.op("scalar", lambda e, bk=bk: e.copy(estg[:], pb(bk)),
                      waits=[(tMM, mm_done)] + estg_guard, inc=sEv)
            bank_guard[bk] = [(sEv, ev)]
            w2 = 0
            for s in range(2):
                w2 = b.dma("sync", v_sc[2 * dvc + s, kc, :, :],
                           estg[:, 256 * s:256 * (s + 1)], dKW,
                           waits=([(sEv, ev)] if s == 0 else []))
            estg_guard = [(dKW, w2)]
            if kc % 4 == 3:
                emit_dq_units(1)
        wv_free = [(tMM, tMM.n)]
    for hp in range(NP):
        vw_done[hp] = dKW.n

    b.barrier()

    dkw_final = dKW.n
    pair_cons = [(), ()]
    p_free2 = [[], []]
    o_evac_v = 0
    sq_guard_outer = [0]
    eb_guard = [0]
    rb_guard = [0]
    bound_g = [0]
    for hp in range(NP):
        sl = hp % 2
        kr1 = b.dma("sync", kb[:, sl], k_sc[hp].rearrange("c p n -> p c n"), dKRp[sl],
                    waits=list(pair_cons[sl]) + [(dKW, dkw_final)])
        kr2 = b.dma("sync", qb[:, sl], q_sc[hp].rearrange("c p n -> p c n"), dKRp[sl])
        kr3 = b.dma("sync", vb[:, sl], v_sc[hp].rearrange("t p n -> p t n"), dKRp[sl])
        dKR = dKRp[sl]
        kqv_t = dKRp[sl].n
        sq_guard = sq_guard_outer[0]
        for hh in range(2):
            h = 2 * hp + hh
            o64 = ones64A if hh == 0 else ones64B
            for (which_k, kc_hh, dstcell) in ((0, hh, 20), (2, 2, 21)):
                ck = kc_hh if which_k == 0 else 2
                sq_s = b.op("scalar", lambda e, sl=sl, ck=ck: e.activation(
                    sqtile[:], kb[:, sl, ck, :], AF.Square),
                    waits=[(dKR, kqv_t), (tMM, sq_guard), (vEv, bound_g[0])],
                    inc=sEv)
                oo = ones_bf if which_k == 0 else o64
                for c in range(4):
                    m_t = mm(5, oo[:], sqtile[:, 512 * c:512 * (c + 1)], True, True,
                             out_rows=(0, 1), waits=[(sEv, sq_s)] if c == 0 else [])
                    c_t = b.op("scalar", lambda e, c=c: e.copy(
                        rowbank[0:1, 512 * c:512 * (c + 1)], pbr(5, 0, 1)),
                        waits=[(tMM, m_t), (vEv, bound_g[0]), (tMM, rb_guard[0])],
                        inc=sEv)
                    bank_guard[5] = [(sEv, c_t)]
                sq_guard = m_t
                bound_g[0] = b.op("vector", lambda e, dc=dstcell: e.reduce_max(
                    cells[0:1, dc:dc + 1], rowbank[0:1, 0:2048],
                    mybir.AxisListType.X), waits=[(sEv, c_t)], inc=vEv)
            kx_s = b.op("scalar", lambda e: e.activation(
                cells[0:1, 22:24], cells[0:1, 20:22], AF.Sqrt),
                waits=[(vEv, bound_g[0])], inc=sEv)
            sqn_s = b.op("scalar", lambda e, sl=sl, hh=hh: e.activation(
                sqtile[:, 0:1024], qb[:, sl, hh, :], AF.Square),
                waits=[(tMM, sq_guard)], inc=sEv)
            sqr_s = b.op("scalar", lambda e, sl=sl: e.activation(
                sqtile[:, 1024:2048], qb[:, sl, 2, :], AF.Square), inc=sEv)
            for c in range(2):
                mseg = rowbank[0:1, 512 * c:512 * (c + 1)]
                m1 = mm(5, ones_bf[:], sqtile[:, 512 * c:512 * (c + 1)], True, True,
                        out_rows=(0, 1), waits=[(sEv, sqn_s)] if c == 0 else [])
                c1 = b.op("scalar", lambda e: e.copy(sC, pbr(5, 0, 1)),
                          waits=[(tMM, m1), (vEv, bound_g[0])], inc=sEv)
                bank_guard[5] = [(sEv, c1)]
                m2 = mm(5, o64[:], sqtile[:, 1024 + 512 * c:1536 + 512 * c], True, True,
                        out_rows=(0, 1), waits=[(sEv, sqr_s)])
                c2 = b.op("scalar", lambda e: e.copy(sD, pbr(5, 0, 1)),
                          waits=[(tMM, m2), (tMM, rb_guard[0])], inc=sEv)
                bank_guard[5] = [(sEv, c2)]
                sq_guard = m2
                b.op("scalar", lambda e, ms=mseg: e.activation(ms, sC, AF.Sqrt))
                qs_s = b.op("scalar", lambda e: e.activation(sC, sD, AF.Sqrt),
                            inc=sEv)
                b.op("vector", lambda e, ms=mseg: e.tensor_scalar(
                    sD, ms, cells[0:1, 22:23], None, ALU.mult),
                    waits=[(sEv, qs_s), (sEv, kx_s)])
                bound_g[0] = b.op("vector", lambda e, ms=mseg: e.scalar_tensor_tensor(
                    ms, sC, cells[0:1, 23:24], sD, ALU.mult, ALU.add), inc=vEv)
            mx_v = b.op("vector", lambda e: e.reduce_max(
                cells[0:1, 10:11], rowbank[0:1, 0:1024], mybir.AxisListType.X),
                waits=[(tMM, eb_guard[0])], inc=vEv)
            bound_g[0] = mx_v
            m_t = mm(7, negones1_f32[:], cells[0:1, 10:11], True, True,
                     waits=[(vEv, mx_v)])
            eb_guard[0] = m_t
            eb_s = b.op("scalar", lambda e, h=h: e.copy(
                expbias[:, h:h + 1], pbr(7, 0, 128, 0, 1)),
                waits=[(tMM, m_t)], inc=sEv)
            bank_guard[7] = [(sEv, eb_s)]
            if debug is True:
                b.dma("sync", dbgM[h], cells[0:1, :], dKW,
                      waits=[(sEv, eb_s), (vEv, bound_g[0])])
                if hp == 0 and hh == 1:
                    b.dma("sync", dbgE[:, 16:32], expbias[:], dKW,
                          waits=[(sEv, eb_s)])
                ps7 = b.op("scalar", lambda e, h=h: e.copy(
                    dbgE_sb[:, h:h + 1], pbr(7, 0, 128, 0, 1)), inc=sEv)
                if hp == 3 and hh == 1:
                    b.dma("sync", dbgE[:, 32:48], expbias[:], dKW,
                          waits=[(sEv, ps7)])
            sq_guard_outer[0] = sq_guard
            emit_dq_units(2)
        for hh in range(2):
            h = 2 * hp + hh
            r0, r1_ = (0, 64) if hh == 0 else (64, 128)
            for qc in range(2):
                ppar = (2 * hh + qc) % 2
                sb_rot = [0, 1]
                exp_v = [0] * NKT
                for kt in range(NKT):
                    bk = sb_rot[kt % 2]
                    mm(bk, kb[:, sl, hh, 128 * kt:128 * (kt + 1)],
                       qb[:, sl, hh, 512 * qc:512 * (qc + 1)],
                       start=True, stop=False)
                    sc_t = mm(bk, kb[r0:r1_, sl, 2, 128 * kt:128 * (kt + 1)],
                              qb[r0:r1_, sl, 2, 512 * qc:512 * (qc + 1)],
                              start=False, stop=True)
                    e_t = b.op("scalar", lambda e, bk=bk, kt=kt, h=h, pp=ppar:
                               e.activation(Pbuf[:, pp, kt, :], pb(bk), AF.Exp,
                                            bias=expbias[:, h:h + 1]),
                        waits=[(tMM, sc_t), (sEv, eb_s)] +
                              (p_free2[ppar] if kt == 0 else []), inc=sEv)
                    exp_v[kt] = e_t
                    bank_guard[bk] = [(sEv, e_t)]
                for kt in range(NKT):
                    s_t = mm(2, ones_bf[:], Pbuf[:, ppar, kt, :], start=(kt == 0),
                             stop=(kt == NKT - 1), out_rows=(0, 1),
                             waits=[(sEv, exp_v[kt])])
                rc_v = b.op("vector", lambda e: e.reciprocal(sD, pbr(2, 0, 1)),
                            waits=[(tMM, s_t), (sEv, exp_v[NKT - 1]),
                                   (tMM, rb_guard[0])], inc=vEv)
                bank_guard[2] = [(vEv, rc_v)]
                m_t = mm(7, ones1_f32[:], sD, True, True, waits=[(vEv, rc_v)])
                rb_guard[0] = m_t
                rb_s = b.op("scalar", lambda e: e.copy(rBuf[:], pb(7)),
                            waits=[(tMM, m_t), (vEv, o_evac_v)], inc=sEv)
                bank_guard[7] = [(sEv, rb_s)]
                if debug:
                    b.dma("sync", dbgR[2 * h + qc], rBuf[:], dKW,
                          waits=[(sEv, rb_s)])
                ob = 3 + (qc % 2)
                for kt in range(NKT):
                    av_t = mm(ob, vb[:, sl, kt, 128 * hh:128 * (hh + 1)],
                              Pbuf[:, ppar, kt, :], start=(kt == 0),
                              stop=(kt == NKT - 1),
                              waits=[(sEv, exp_v[NKT - 1]), (dKR, kqv_t)]
                              if kt == 0 else [])
                o_evac_v = b.op("vector", lambda e, ob=ob, h=h, qc=qc: e.tensor_mul(
                    Obuf[:, h, 512 * qc:512 * (qc + 1)], pb(ob), rBuf[:]),
                    waits=[(tMM, av_t), (sEv, rb_s)], inc=vEv)
                bank_guard[ob] = [(vEv, o_evac_v)]
                p_free2[ppar] = [(tMM, av_t)]
                emit_dq_units(3)
        pair_cons[sl] = [(tMM, tMM.n)]
    while emit_dq_unit():
        pass
    emit_dq_flush()
    dq_all_v = dqv.n
    dq_w_final = [(dQWp[0], dQWp[0].n), (dQWp[1], dQWp[1].n)]
    if debug is True:
        b.dma("sync", dbgO, Obuf[:], dKW, waits=[(vEv, o_evac_v)])
        b.dma("sync", dbgE[:, 0:16], expbias[:], dKW)
        b.dma("sync", dbgE[:, 48:64], dbgE_sb[:, 0:16], dKW)

    b.barrier()

    WOT_r = WOT.rearrange("(t p) d -> p t d", p=128)
    wslot_free = [(), ()]
    xslot_guard = [(dqv, dq_all_v)]
    st_t = 0
    x2w = [0] * NT
    for ht in range(NT):
        sl = ht % 2
        w_t = b.dma("gpsimd", wbufs[:, sl], WOT_r[:, :, 128 * ht:128 * (ht + 1)],
                    dWp[sl], waits=wslot_free[sl])
        x_t = b.dma("sync", fbufA[:, 0, :], XT_t[ht][:, 0:TOK], dXp[0],
                    waits=xslot_guard)
        banks = [2 * (ht % 2), 2 * (ht % 2) + 1]
        for it in range(NT):
            lw = [(dWp[sl], w_t)] if it == 0 else []
            ldw(wbufs[:, sl, it, :], waits=lw)
            for tc in range(2):
                mm(banks[tc], wbufs[:, sl, it, :], Obuf[:, it, 512 * tc:512 * (tc + 1)],
                   start=(it == 0), stop=(it == NT - 1))
        mm_done = tMM.n
        wslot_free[sl] = [(tMM, mm_done)]
        for tc in range(2):
            wv = [(tMM, mm_done), (dXp[0], x_t)]
            if tc == 0 and ht > 0:
                wv.append((dX2W, x2w[ht - 1]))
            b.op("vector", lambda e, bk=banks[tc], tc=tc: e.tensor_add(
                fbufA[:, 1, 512 * tc:512 * (tc + 1)], pb(bk),
                fbufA[:, 0, 512 * tc:512 * (tc + 1)]),
                waits=wv)
        x2_v = b.op("vector", lambda e: e.tensor_copy(xb2[:], fbufA[:, 1, :]), inc=vEv)
        bank_guard[banks[0]] = [(vEv, x2_v)]
        bank_guard[banks[1]] = [(vEv, x2_v)]
        x2_s = b.op("scalar", lambda e: e.activation(xsq2[:], fbufA[:, 1, :], AF.Square),
                    waits=[(vEv, x2_v)], inc=sEv)
        for c in range(2):
            mm(4 + c, ones_bf[:], xb2[:, 512 * c:512 * (c + 1)],
               start=(ht == 0), stop=(ht == NT - 1), out_rows=(0, 1),
               waits=[(vEv, x2_v)] if c == 0 else [])
        for c in range(2):
            st_t = mm(6 + c, ones_bf[:], xsq2[:, 512 * c:512 * (c + 1)],
                      start=(ht == 0), stop=(ht == NT - 1), out_rows=(0, 1),
                      waits=[(sEv, x2_s)] if c == 0 else [])
        x2w[ht] = b.dma("sync", x2_sc[ht], fbufA[:, 1, :], dX2W, waits=[(vEv, x2_v)])
        xslot_guard = [(vEv, x2_v), (sEv, x2_s), (tMM, st_t)]

    ab2_s = ln_coeffs(2, [4, 5], [6, 7], 4, aB2, bB2, [(tMM, st_t)])
    ln2_v = 0
    for ht in range(NT):
        x_t = b.dma("sync", fbufA[:, 0, :], x2_sc[ht], dX2R,
                    waits=([(sEv, ab2_s), (dX2W, dX2W.n)] if ht == 0
                           else [(vEv, ln2_v)]))
        b.op("vector", lambda e: e.tensor_mul(fbufA[:, 1, :], fbufA[:, 0, :], aB2[:]),
             waits=[(dX2R, x_t)])
        ln2_v = b.op("vector", lambda e, ht=ht: e.tensor_add(
            ln2[:, ht, :], fbufA[:, 1, :], bB2[:]), inc=vEv)

    if debug is True:
        b.dma("sync", dbgL2, ln2[:], dKW, waits=[(vEv, ln2_v)])
    b.barrier()

    pc_free = [(), ()]
    htmp_guard = [0]
    hevac_g = [0]
    for cb in range(6):
        W = 512 if cb == 5 else 1024
        ncol = W // 8
        for qd in range(2):
            for half, base in (("g", 1024 * cb), ("u", NI + 1024 * cb)):
                piece_t = [0] * NT

                def d_piece(it, cb=cb, qd=qd, half=half, base=base, W=W):
                    psl = it % 2
                    wv = list(pc_free[psl])
                    if cb == 0 and qd == 0 and half == "g" and it < 2:
                        wv += dq_w_final
                    piece_t[it] = b.dma("scalar", wpiece[:, psl, 0:W],
                                        w1_sc[it, :, base:base + W], dFRp[psl],
                                        waits=wv)
                d_piece(0)
                d_piece(1)
                for it in range(NT):
                    psl = it % 2
                    for jj in range(4):
                        jq = 4 * qd + jj
                        lw = [(dFRp[psl], piece_t[it])] if jj == 0 else []
                        if cb == 0 and qd == 0 and half == "g" and it == 0 and jj == 0:
                            lw.append((vEv, ln2_v))
                        ldw(wpiece[:, psl, jq:W:8], waits=lw)
                        for tc in range(2):
                            mm(2 * jj + tc, wpiece[:, psl, jq:W:8],
                               ln2[:, it, 512 * tc:512 * (tc + 1)],
                               start=(it == 0), stop=(it == NT - 1),
                               out_rows=(0, ncol))
                    pc_free[psl] = [(tMM, tMM.n)]
                    if it + 2 < NT:
                        d_piece(it + 2)
                mm_done = tMM.n
                if half == "g":
                    sg_s = 0
                    for jj in range(4):
                        for tc in range(2):
                            sg_s = b.op("scalar", lambda e, bk=2 * jj + tc, jj=jj,
                                        tc=tc, nc_=ncol: e.activation(
                                            sgbuf[0:nc_, 2 * jj + tc, :],
                                            pbr(2 * jj + tc, 0, nc_), AF.Silu),
                                        waits=([(tMM, mm_done), (vEv, hevac_g[0])]
                                               if (jj == 0 and tc == 0) else []),
                                        inc=sEv)
                            bank_guard[2 * jj + tc] = [(sEv, sg_s)]
                else:
                    for jj in range(4):
                        jq = 4 * qd + jj
                        for tc in range(2):
                            bk = 2 * jj + tc
                            if cb < 5:
                                h_v = b.op("vector", lambda e, bk=bk, cb=cb, jq=jq,
                                           tc=tc: e.tensor_mul(
                                               Hbuf[:, 8 * cb + jq, 512 * tc:512 * (tc + 1)],
                                               pbr(bk, 0, 128), sgbuf[:, bk, :]),
                                           waits=[(tMM, mm_done), (sEv, sg_s)]
                                           if (jj == 0 and tc == 0) else [], inc=vEv)
                                bank_guard[bk] = [(vEv, h_v)]
                                hevac_g[0] = h_v
                            else:
                                itH = 40 + jq // 2
                                if jq % 2 == 0:
                                    h_v = b.op("vector", lambda e, bk=bk, itH=itH, tc=tc:
                                               e.tensor_mul(
                                                   Hbuf[0:64, itH, 512 * tc:512 * (tc + 1)],
                                                   pbr(bk, 0, 64), sgbuf[0:64, bk, :]),
                                               waits=[(tMM, mm_done), (sEv, sg_s)]
                                               if (jj == 0 and tc == 0) else [], inc=vEv)
                                    bank_guard[bk] = [(vEv, h_v)]
                                    hevac_g[0] = h_v
                                else:
                                    hw = ([(tMM, mm_done), (sEv, sg_s)]
                                          if (jj == 0 and tc == 0) else [])
                                    hw.append((dSh, htmp_guard[0]))
                                    h_v = b.op("vector", lambda e, bk=bk: e.tensor_mul(
                                        htmp[0:64, :], pbr(bk, 0, 64), sgbuf[0:64, bk, :]),
                                        waits=hw, inc=vEv)
                                    sh_t = b.dma("sync",
                                                 Hbuf[64:128, itH, 512 * tc:512 * (tc + 1)],
                                                 htmp[0:64, :], dSh, waits=[(vEv, h_v)])
                                    htmp_guard[0] = sh_t
                                    bank_guard[bk] = [(vEv, h_v)]
                                    hevac_g[0] = h_v
    h_shift_done = dSh.n
    h_done_v = vEv.n
    if debug in (True, 3):
        b.dma("sync", dbgH, Hbuf[:], dKW, waits=[(vEv, h_done_v), (dSh, h_shift_done)])

    y_guard = []
    for tc in range(2):
        y_done_s = 0
        for hb in range(2):
            piece_t = [0] * NIT

            def e_piece(it, tc=tc, hb=hb):
                psl = it % 2
                wv = list(pc_free[psl])
                if tc == 0 and hb == 0 and it < 2:
                    wv += dq_w_final
                piece_t[it] = b.dma("scalar", wpiece[:, psl, :],
                                    wd_sc[it, :, 1024 * hb:1024 * (hb + 1)], dFRp[psl],
                                    waits=wv)
            e_piece(0)
            e_piece(1)
            for it in range(NIT):
                psl = it % 2
                for jh in range(8):
                    lw = [(dFRp[psl], piece_t[it])] if jh == 0 else []
                    if tc == 0 and hb == 0 and it == 0 and jh == 0:
                        lw += [(vEv, h_done_v), (dSh, h_shift_done)]
                    ldw(wpiece[:, psl, 128 * jh:128 * (jh + 1)], waits=lw)
                    mm(jh, wpiece[:, psl, 128 * jh:128 * (jh + 1)],
                       Hbuf[:, it, 512 * tc:512 * (tc + 1)],
                       start=(it == 0), stop=(it == NIT - 1))
                pc_free[psl] = [(tMM, tMM.n)]
                if it + 2 < NIT:
                    e_piece(it + 2)
            mm_done = tMM.n
            for jh in range(8):
                y_done_s = b.op("scalar", lambda e, jh=jh, hb=hb: e.copy(
                    ybuf[:, 8 * hb + jh, :], pb(jh)),
                    waits=([(tMM, mm_done)] + list(y_guard)) if jh == 0 else [],
                    inc=sEv)
                bank_guard[jh] = [(sEv, y_done_s)]
        if debug in (True, 3):
            b.dma("sync", dbgY[tc], ybuf[:], dKW, waits=[(sEv, y_done_s)])
        ys_t = 0
        for yt in range(NT):
            ys_s = b.op("scalar", lambda e, yt=yt: e.activation(
                ysqb[:], ybuf[:, yt, :], AF.Square),
                waits=[(sEv, y_done_s)] if yt == 0 else [(tMM, ys_t)], inc=sEv)
            ys_t = mm(0, ones_bf[:], ysqb[:], start=(yt == 0), stop=(yt == NT - 1),
                      out_rows=(0, 1), waits=[(sEv, ys_s)])
        rr_s = b.op("scalar", lambda e: e.activation(
            sA, pbr(0, 0, 1), AF.Sqrt,
            bias=cells[0:1, 5:6], scale=cells[0:1, 6:7]),
            waits=[(tMM, ys_t), (tMM, rb_guard[0])], inc=sEv)
        bank_guard[0] = [(sEv, rr_s)]
        rr_v = b.op("vector", lambda e: e.reciprocal(sB, sA),
                    waits=[(sEv, rr_s)], inc=vEv)
        m_t = mm(1, ones1_f32[:], sB, True, True, waits=[(vEv, rr_v)])
        rb_guard[0] = m_t
        rrb_s = b.op("scalar", lambda e: e.copy(rrB[:], pb(1)),
                     waits=[(tMM, m_t)], inc=sEv)
        bank_guard[1] = [(sEv, rrb_s)]
        for yt in range(NT):
            x_t = b.dma("sync", fbufA[:, 0, 0:512],
                        x2_sc[yt, :, 512 * tc:512 * (tc + 1)], dX2R,
                        waits=[(dOse, dOse.n)])
            b.op("vector", lambda e, yt=yt: e.tensor_mul(
                fbufA[:, 1, 0:512], ybuf[:, yt, :], rrB[:]),
                waits=[(sEv, rrb_s), (dX2R, x_t)])
            o_v = b.op("vector", lambda e: e.tensor_add(
                fbufA[:, 1, 512:1024], fbufA[:, 1, 0:512], fbufA[:, 0, 0:512]),
                inc=vEv)
            y_guard = [(vEv, o_v)]
            b.dma("sync", OUT[yt, :, 512 * tc:512 * (tc + 1)], fbufA[:, 1, 512:1024],
                  dOse, waits=[(vEv, o_v)])
    final_out = dOse.n
    b.plan["sync"].append(lambda e, h=dOse.h, v=final_out: e.wait_ge(h, v))

    with nc.Block() as block:
        for eng_name, deco in (("sync", block.sync), ("tensor", block.tensor),
                               ("vector", block.vector), ("scalar", block.scalar),
                               ("gpsimd", block.gpsimd)):
            def run(e, items=b.plan[eng_name]):
                for f in items:
                    f(e)
            deco(run)
    b.es.close()
    return nc


_NC_CACHE = {}


def _get_nc(debug=False):
    key = bool(debug)
    if key not in _NC_CACHE:
        _NC_CACHE[key] = build_program(debug=key)
    return _NC_CACHE[key]


def _stage(L0_attn_hidden_states, L0_attn_Wq, L0_attn_Wk, L0_attn_Wv, L0_attn_Wo,
           L0_ffn_gate_up_packed, L0_ffn_gate_up_scales, L0_ffn_down_packed,
           L0_ffn_down_scales):
    x = np.asarray(L0_attn_hidden_states)
    Wq = np.asarray(L0_attn_Wq)
    Wk = np.asarray(L0_attn_Wk)
    Wv = np.asarray(L0_attn_Wv)
    Wo = np.asarray(L0_attn_Wo)
    colord = []
    for hp in range(NP):
        colord += list(range(128 * 2 * hp, 128 * 2 * hp + 256))
        colord += list(range(2048 + 64 * 2 * hp, 2048 + 64 * 2 * hp + 128))
    colord = np.array(colord)
    WQT = np.ascontiguousarray(Wq[colord][:, PERM].T.astype(np.float32))
    WKT = np.ascontiguousarray(Wk[colord][:, PERM].T.astype(np.float32))
    WVT = np.ascontiguousarray(Wv[:, PERM].T.astype(np.float32))
    WOT = np.ascontiguousarray(Wo[PERM, :].T.astype(np.float32))
    PKGa = np.ascontiguousarray(np.asarray(L0_ffn_gate_up_packed).T)
    PKDa = np.ascontiguousarray(np.asarray(L0_ffn_down_packed).T)
    SCGa = np.ascontiguousarray(np.repeat(np.asarray(L0_ffn_gate_up_scales), 16, axis=0))
    SCDa = np.ascontiguousarray(np.repeat(np.asarray(L0_ffn_down_scales), 16, axis=0))

    inv = 1.0 / (10000.0 ** (np.arange(0, DR, 2, dtype=np.float64) / DR))
    t = np.arange(S, dtype=np.float64)
    fr = np.outer(t, inv)
    cosT = np.cos(fr).astype(np.float32)
    sinT = np.sin(fr).astype(np.float32)

    in_maps = []
    for k in range(8):
        bb_, hh_ = k // 2, k % 2
        t0 = TOK * hh_
        tokord = np.concatenate([np.arange(t0, t0 + TOK),
                                 np.arange((1 - hh_) * TOK, (1 - hh_) * TOK + TOK)])
        XTk = np.ascontiguousarray(x[bb_][tokord][:, PERM].T.astype(np.float32))
        ck = cosT[tokord].T
        sk = sinT[tokord].T
        COSRk = np.ascontiguousarray(np.tile(ck, (4, 1)).astype(np.float32))
        SINAk = np.ascontiguousarray(
            np.concatenate([-sk, sk, -sk, sk], axis=0).astype(np.float32))
        in_maps.append({
            "XT": XTk, "WQT": WQT, "WKT": WKT, "WVT": WVT, "WOT": WOT,
            "PKG": PKGa, "PKD": PKDa, "SCG": SCGa, "SCD": SCDa,
            "COSR": COSRk, "SINA": SINAk,
        })
    return in_maps


def kernel(L0_attn_hidden_states, L0_attn_Wq, L0_attn_Wk, L0_attn_Wv, L0_attn_Wo,
           L0_ffn_gate_up_packed, L0_ffn_gate_up_scales, L0_ffn_down_packed,
           L0_ffn_down_scales):
    in_maps = _stage(L0_attn_hidden_states, L0_attn_Wq, L0_attn_Wk, L0_attn_Wv,
                     L0_attn_Wo, L0_ffn_gate_up_packed, L0_ffn_gate_up_scales,
                     L0_ffn_down_packed, L0_ffn_down_scales)
    nc = _get_nc(debug=_DEBUG)
    res = run_bass_kernel_spmd(nc, in_maps, core_ids=list(range(8)))
    out = np.empty((B, S, D), dtype=np.float32)
    for k in range(8):
        bb_, hh_ = k // 2, k % 2
        o = np.asarray(res.results[k]["OUT"]).reshape(D, TOK)
        arr = np.empty((TOK, D), dtype=np.float32)
        arr[:, PERM] = o.T
        out[bb_, TOK * hh_:TOK * (hh_ + 1), :] = arr
    if _DEBUG:
        kernel._last = res
    return out


# version 06
# speedup vs baseline: 1.2007x; 1.2007x over previous
import contextlib
import numpy as np
import ml_dtypes

import concourse.bass as bass
import concourse.mybir as mybir
from concourse.bass_utils import run_bass_kernel_spmd

F32 = mybir.dt.float32
BF16 = mybir.dt.bfloat16
I32 = mybir.dt.int32
F16 = mybir.dt.float16
ALU = mybir.AluOpType
AF = mybir.ActivationFunctionType
BF = ml_dtypes.bfloat16

B, S, D = 4, 2048, 2048
H, DN, DR, DV = 16, 128, 64, 128
NI = 5632
TOK = 1024
NT = 16
NKT = 16
NP = 8
GU = 2 * NI
NIT = 44
LN_EPS = 1e-5
RMS_EPS = 1e-6
SC_NOPE = float(1.0 / np.sqrt(128.0))
SC_ROPE = 0.125

_r = np.arange(D)
PERM = (1024 * (_r // 1024) + 8 * (_r % 128) + (_r // 128) % 8).astype(np.int64)

ENGINES = ("sync", "tensor", "vector", "scalar", "gpsimd")
_DEBUG = False


class Sem:
    __slots__ = ("h", "n", "name")

    def __init__(self, h, name):
        self.h, self.n, self.name = h, 0, name


class Bld:
    def __init__(self):
        self.nc = bass.Bass("TRN2", target_bir_lowering=False)
        self.plan = {e: [] for e in ENGINES}
        self.es = contextlib.ExitStack()
        self.sems = {}

    def sem(self, name):
        if name not in self.sems:
            self.sems[name] = Sem(self.es.enter_context(self.nc.semaphore(name)), name)
        return self.sems[name]

    def op(self, eng, make, waits=(), inc=None, incn=1):
        wl = [(s.h, int(v)) for (s, v) in waits if v > 0]
        ih = None
        if inc is not None:
            inc.n += incn
            ih = inc.h

        def fn(e, make=make, wl=wl, ih=ih, iv=incn):
            for (h, v) in wl:
                e.wait_ge(h, v)
            ins = make(e)
            if ih is not None:
                ins.then_inc(ih, iv)
        self.plan[eng].append(fn)
        return inc.n if inc is not None else None

    def dma(self, eng, out, in_, sem, waits=()):
        return self.op(eng, (lambda e, o=out, i=in_: e.dma_start(out=o, in_=i)),
                       waits=waits, inc=sem, incn=16)

    def sb(self, name, shape, dtype, offset):
        base = (self.nc.sbuf_base + 31) & ~31
        return self.nc.alloc_sbuf_tensor_at(name, list(shape), dtype,
                                            offset=base + offset)

    def barrier(self):
        ph = self.sem("ph")
        ph.n += 5
        t = ph.n
        for e in ENGINES:
            def fn(eng, h=ph.h, t=t):
                eng.nop().then_inc(h, 1)
                eng.wait_ge(h, t)
            self.plan[e].append(fn)


def build_program(debug=False):
    b = Bld()
    nc = b.nc
    okind = "ExternalOutput" if debug is True else "Internal"

    XT = nc.dram_tensor("XT", [D, S], F32, kind="ExternalInput").ap()
    WQT = nc.dram_tensor("WQT", [D, 3072], F32, kind="ExternalInput").ap()
    WKT = nc.dram_tensor("WKT", [D, 3072], F32, kind="ExternalInput").ap()
    WVT = nc.dram_tensor("WVT", [D, 2048], F32, kind="ExternalInput").ap()
    WOT = nc.dram_tensor("WOT", [D, 2048], F32, kind="ExternalInput").ap()
    PKG = nc.dram_tensor("PKG", [256, GU], I32, kind="ExternalInput").ap()
    PKD = nc.dram_tensor("PKD", [704, 2048], I32, kind="ExternalInput").ap()
    SCG = nc.dram_tensor("SCG", [256, GU], F16, kind="ExternalInput").ap()
    SCD = nc.dram_tensor("SCD", [704, 2048], F16, kind="ExternalInput").ap()
    COSR = nc.dram_tensor("COSR", [128, S], F32, kind="ExternalInput").ap()
    SINA = nc.dram_tensor("SINA", [128, S], F32, kind="ExternalInput").ap()
    OUT = nc.dram_tensor("OUT", [NT, 128, TOK], F32, kind="ExternalOutput").ap()

    dbgO = nc.dram_tensor("dbgO", [128, NT, 1024], BF16, kind=okind).ap()
    dbgE = nc.dram_tensor("dbgE", [128, 64], F32, kind=okind).ap()
    dbgM = nc.dram_tensor("dbgM", [H, 1, 32], F32, kind=okind).ap()
    dbgL2 = nc.dram_tensor("dbgL2", [128, NT, 1024], BF16, kind=okind).ap()
    dbgH = nc.dram_tensor("dbgH", [128, NIT, 1024], BF16, kind=okind).ap()
    dbgY = nc.dram_tensor("dbgY", [2, 128, NT, 512], F32, kind=okind).ap()
    dbgR = nc.dram_tensor("dbgR", [2 * H, 128, 512], F32, kind=okind).ap()
    k_sc = nc.dram_tensor("k_sc", [NP, 3, 128, S], BF16, kind=okind).ap()
    q_sc = nc.dram_tensor("q_sc", [NP, 3, 128, TOK], BF16, kind=okind).ap()
    v_sc = nc.dram_tensor("v_sc", [NP, NKT, 128, 256], BF16, kind=okind).ap()
    w1_sc = nc.dram_tensor("w1_sc", [NT, 128, GU], BF16, kind=okind).ap()
    wd_sc = nc.dram_tensor("wd_sc", [NIT, 128, 2048], BF16, kind=okind).ap()
    x2_sc = nc.dram_tensor("x2_sc", [NT, 128, TOK], F32, kind=("ExternalOutput" if debug in (True, 2) else "Internal")).ap()

    cells = b.sb("cells", [128, 32], F32, 0)
    ones_bf = b.sb("ones_bf", [128, 1], BF16, 128)
    ones_f32 = b.sb("ones_f32", [128, 1], F32, 160)
    ones1_f32 = b.sb("ones1_f32", [1, 128], F32, 192)
    negones1_f32 = b.sb("negones1_f32", [1, 128], F32, 207872)
    dbgE_sb = b.sb("dbgE_sb", [128, 16], F32, 208384)
    ones64A = b.sb("ones64A", [128, 1], BF16, 704)
    ones64B = b.sb("ones64B", [128, 1], BF16, 736)
    expbias = b.sb("expbias", [128, 16], F32, 768)
    rowbank = b.sb("rowbank", [128, 2048], F32, 1024)
    fbufA = b.sb("fbufA", [128, 2, 1024], F32, 9216)
    dq_t05 = b.sb("dq_t05", [128, 2, 1024], BF16, 9216)
    dq_smul = b.sb("dq_smul", [128, 2, 1024], BF16, 13312)
    wbufs = b.sb("wbufs", [128, 2, 16, 128], BF16, 17408)
    wpiece = b.sb("wpiece", [128, 2, 1024], BF16, 17408)
    tmpF = b.sb("tmpF", [128, 2048], F32, 17408)

    RD = 25600
    pkb = b.sb("pkb", [128, 2, 512], I32, RD)
    scb = b.sb("scb", [128, 2, 512], BF16, RD + 4096)
    m32 = b.sb("m32", [128, 512], I32, RD + 6144)
    s32 = b.sb("s32", [128, 512], I32, RD + 8192)
    dq_r1 = b.sb("dq_r1", [128, 2, 1024], BF16, RD + 10240)
    dq_r2 = b.sb("dq_r2", [128, 2, 1024], BF16, RD + 14336)
    dq_t2 = b.sb("dq_t2", [128, 1024], BF16, RD + 18432)
    dq_u = b.sb("dq_u", [128, 1024], BF16, RD + 20480)
    wout = b.sb("wout", [128, 2, 2, 512], BF16, RD + 22528)

    R1 = 52224
    ln1 = b.sb("ln1", [128, NT, 2048], BF16, R1)
    sqtile = b.sb("sqtile", [128, 2048], BF16, R1)
    Pbuf = b.sb("Pbuf", [128, 2, NKT, 512], BF16, R1 + 8192)
    rBuf = b.sb("rBuf", [128, 512], F32, R1 + 40960)
    xb2 = b.sb("xb2", [128, 1024], BF16, R1)
    xsq2 = b.sb("xsq2", [128, 1024], BF16, R1 + 2048)
    ln2 = b.sb("ln2", [128, NT, 1024], BF16, R1)
    sgbuf = b.sb("sgbuf", [128, 8, 512], BF16, R1 + 32768)
    htmp = b.sb("htmp", [128, 512], BF16, R1 + 40960)
    ybuf = b.sb("ybuf", [128, NT, 512], F32, R1)
    ysqb = b.sb("ysqb", [128, 512], BF16, R1 + 32768)
    rrB = b.sb("rrB", [128, 512], F32, R1 + 33792)

    R2 = 117760
    xsq = b.sb("xsq", [128, 2, 2048], BF16, R2)
    xb = b.sb("xb", [128, 2, 2048], BF16, R2 + 8192)
    xsA = b.sb("xsA", [128, 2, 2048], F32, R2 + 16384)
    cosR = b.sb("cosR", [128, 2048], F32, R2 + 32768)
    sinA = b.sb("sinA", [128, 2048], F32, R2 + 40960)
    aB = b.sb("aB", [128, 2048], F32, R2 + 49152)
    bB = b.sb("bB", [128, 2048], F32, R2 + 57344)
    estg = b.sb("estg", [128, 512], BF16, R2 + 65536)
    s_rope = b.sb("s_rope", [128, 512], BF16, R2 + 66560)
    sh_rope = b.sb("sh_rope", [128, 512], BF16, R2 + 67584)
    ropetmp = b.sb("ropetmp", [128, 512], F32, R2 + 68608)
    ropetmp2 = b.sb("ropetmp2", [128, 512], F32, R2 + 70656)
    wvb = b.sb("wvb", [128, 16, 512], BF16, R2 + 72704)
    kb = b.sb("kb", [128, 2, 3, 2048], BF16, R2)
    qb = b.sb("qb", [128, 2, 3, 1024], BF16, R2 + 24576)
    vb = b.sb("vb", [128, 2, NKT, 256], BF16, R2 + 36864)
    Obuf = b.sb("Obuf", [128, NT, 1024], BF16, R2 + 53248)
    aB2 = b.sb("aB2", [128, 1024], F32, R2)
    bB2 = b.sb("bB2", [128, 1024], F32, R2 + 4096)
    Hbuf = b.sb("Hbuf", [128, NIT, 1024], BF16, R2)

    psAll = b.es.enter_context(nc.psum_tensor("psAll", [128, 4096], F32))

    def pb(bank):
        return psAll[:, 512 * bank:512 * (bank + 1)]

    def pbr(bank, r0, r1, c0=0, c1=512):
        return psAll[r0:r1, 512 * bank + c0:512 * bank + c1]

    dXp = [b.sem("dX0"), b.sem("dX1")]
    dWp = [b.sem("dW0"), b.sem("dW1")]
    dWV = b.sem("dWV"); dTb = b.sem("dTb")
    dPKp = [b.sem("dPK0"), b.sem("dPK1")]
    dSCp = [b.sem("dSC0"), b.sem("dSC1")]
    dKW = b.sem("dKW")
    dKRp = [b.sem("dKR0"), b.sem("dKR1")]
    dQWp = [b.sem("dQW0"), b.sem("dQW1")]
    dFRp = [b.sem("dFR0"), b.sem("dFR1")]
    dX2W = b.sem("dX2W"); dX2R = b.sem("dX2R")
    dOse = b.sem("dOse"); dSh = b.sem("dSh")
    tMM = b.sem("tMM"); vEv = b.sem("vEv"); sEv = b.sem("sEv")
    dqa = b.sem("dqa"); dqv = b.sem("dqv"); dqx = b.sem("dqx")

    bank_guard = {i: [] for i in range(8)}

    def mm(bank, lhsT, rhs, start, stop, out_rows=None, waits=()):
        w = list(waits)
        if start:
            w += bank_guard[bank]
            bank_guard[bank] = []
        if out_rows is None:
            out = psAll[:, 512 * bank:512 * bank + rhs.shape[-1]]
        else:
            out = psAll[out_rows[0]:out_rows[1], 512 * bank:512 * bank + rhs.shape[-1]]
        return b.op("tensor",
                    lambda e, o=out, l=lhsT, r=rhs, st=start, sp=stop:
                    e.matmul(o, l, r, start=st, stop=sp),
                    waits=w, inc=tMM)

    def ldw(wap, waits=()):
        return b.op("tensor", lambda e, w=wap: e.ldweights(w), waits=waits)

    for ci, cv in {0: -2.0, 1: 0.5, 2: -6.0, 3: 1.0, 4: LN_EPS, 5: RMS_EPS,
                   6: 1.0 / 2048.0, 7: 1.0, 8: 2.0, 9: SC_NOPE}.items():
        b.op("vector", lambda e, ci=ci, cv=cv: e.memset(cells[:, ci:ci + 1], cv))
    b.op("vector", lambda e: e.memset(ones_bf[:], 1.0))
    b.op("vector", lambda e: e.memset(ones_f32[:], 1.0))
    b.op("vector", lambda e: e.memset(ones1_f32[:], 1.0))
    b.op("vector", lambda e: e.memset(negones1_f32[:], -1.0))
    b.op("vector", lambda e: e.memset(ones64A[0:64, :], 1.0))
    b.op("vector", lambda e: e.memset(ones64A[64:128, :], 0.0))
    b.op("vector", lambda e: e.memset(ones64B[0:64, :], 0.0))
    consts_done = b.op("vector", lambda e: e.memset(ones64B[64:128, :], 1.0), inc=vEv)

    b.dma("sync", cosR[:], COSR, dTb)
    tables_done = b.dma("sync", sinA[:], SINA, dTb)

    dq_chunks = []
    for cb in range(2):
        for oc in range(GU // 512):
            dq_chunks.append(("g", cb, 512 * oc, 128))
    for cb in range(6):
        rows = 64 if cb == 5 else 128
        for oc in range(4):
            dq_chunks.append(("d", cb, 512 * oc, rows))
    N_DQ_UNITS = 4 * len(dq_chunks)
    dqs = {"u": 0, "pk_t": None, "sc_t": None, "a_t": 0, "pending2": None,
           "qw_done": [None] * N_DQ_UNITS}

    def emit_dq_unit():
        u = dqs["u"]
        if u >= N_DQ_UNITS:
            return False
        ci, jp = u // 4, u % 4
        kind, cb, c0, rows = dq_chunks[ci]
        slot = ci % 2
        par = u % 2
        if jp == 0:
            src_pk = (PKG if kind == "g" else PKD)[128 * cb:128 * cb + rows, c0:c0 + 512]
            src_sc = (SCG if kind == "g" else SCD)[128 * cb:128 * cb + rows, c0:c0 + 512]
            dqs["pk_t"] = (dPKp[slot], b.dma("scalar", pkb[0:rows, slot, :], src_pk,
                           dPKp[slot], waits=([(dqx, 4 * (ci - 1))] if ci >= 2 else [])))
            dqs["sc_t"] = (dSCp[slot], b.dma("gpsimd", scb[0:rows, slot, :], src_sc,
                           dSCp[slot], waits=([(dqv, 4 * (ci - 1))] if ci >= 2 else [])))
        xw = [dqs["pk_t"]] if jp == 0 else []
        if u >= 2:
            xw.append((dqa, u - 1))
        for jh in range(2):
            j = 2 * jp + jh
            b.op("vector", lambda e, r=rows, sl=slot, j=j:
                 e.tensor_scalar(m32[0:r, :], pkb[0:r, sl, :], 4 * j, 7,
                                 ALU.logical_shift_right, ALU.bitwise_and),
                 waits=(xw if jh == 0 else []))
            b.op("vector", lambda e, r=rows, sl=slot, j=j:
                 e.tensor_scalar(s32[0:r, :], pkb[0:r, sl, :], 4 * j + 3, 1,
                                 ALU.logical_shift_right, ALU.bitwise_and))
            b.op("vector", lambda e, r=rows, p=par, jh=jh:
                 e.tensor_scalar(dq_t05[0:r, p, 512 * jh:512 * (jh + 1)],
                                 m32[0:r, :], 0.5, None, ALU.mult))
            b.op("vector", lambda e, r=rows, p=par, jh=jh:
                 e.tensor_scalar(dq_smul[0:r, p, 512 * jh:512 * (jh + 1)],
                                 s32[0:r, :], -2.0, 1.0, ALU.mult, ALU.add),
                 inc=(dqx if jh == 1 else None))
        x_t = dqx.n
        sw = [(dqx, x_t)]
        if u == 0:
            sw.append((vEv, consts_done))
        if u >= 2:
            sw.append((dqv, u - 1))
        b.op("scalar", lambda e, r=rows, p=par: e.activation(
            dq_r1[0:r, p, :], dq_t05[0:r, p, :], AF.Relu,
            bias=cells[0:r, 0:1], scale=cells[0:r, 3:4]),
            waits=sw)
        a_t = b.op("scalar", lambda e, r=rows, p=par: e.activation(
            dq_r2[0:r, p, :], dq_t05[0:r, p, :], AF.Relu,
            bias=cells[0:r, 2:3], scale=cells[0:r, 8:9]),
            inc=dqa)
        dqs["a_t"] = a_t
        def stage2(e_unused=None, u=u, rows=rows, slot=slot, par=par, kind=kind,
                   cb=cb, c0=c0, a_t=a_t, sc_t=dqs["sc_t"]):
            vw = [(dqa, a_t), sc_t]
            if u >= 2:
                vw.append(dqs["qw_done"][u - 2])
            b.op("vector", lambda e, r=rows, p=par: e.tensor_add(
                dq_u[0:r, :], dq_t05[0:r, p, :], dq_r1[0:r, p, :]), waits=vw)
            b.op("vector", lambda e, r=rows, p=par: e.tensor_add(
                dq_t2[0:r, :], dq_u[0:r, :], dq_r2[0:r, p, :]))
            b.op("vector", lambda e, r=rows, p=par: e.tensor_mul(
                dq_u[0:r, :], dq_t2[0:r, :], dq_smul[0:r, p, :]))
            if kind == "g":
                b.op("vector", lambda e, r=rows, sl=slot: e.tensor_mul(
                    wout[0:r, par, 0, :], dq_u[0:r, 0:512], scb[0:r, sl, :]))
                v_t = b.op("vector", lambda e, r=rows, sl=slot: e.tensor_mul(
                    wout[0:r, par, 1, :], dq_u[0:r, 512:1024], scb[0:r, sl, :]),
                    inc=dqv)
            else:
                v_t = None
                for pl in range(2):
                    shout = wout[0:rows, par, pl, :].rearrange(
                        "p (a b) -> p a b", a=8)
                    shin = dq_u[0:rows, 512 * pl:512 * (pl + 1)].rearrange(
                        "p (b a) -> p a b", a=8)
                    shsc = scb[0:rows, slot, :].rearrange("p (b a) -> p a b", a=8)
                    v_t = b.op("vector", lambda e, so=shout, si=shin, ss=shsc:
                               e.tensor_tensor(so, si, ss, ALU.mult),
                               inc=(dqv if pl == 1 else None))
            jp_ = u % 4
            for jh in range(2):
                j = 2 * jp_ + jh
                if kind == "g":
                    dst = w1_sc[8 * cb + j, :, c0:c0 + 512]
                    b.dma("sync", dst, wout[0:rows, par, jh, :], dQWp[par],
                          waits=([(dqv, v_t)] if jh == 0 else []))
                else:
                    it = (40 + j // 2) if cb == 5 else (8 * cb + j)
                    r0 = 64 * (j % 2) if cb == 5 else 0
                    hb_ = c0 // 1024
                    p0 = 64 * ((c0 % 1024) // 512)
                    dst = wd_sc[it, r0:r0 + rows,
                                1024 * hb_:1024 * (hb_ + 1)].rearrange(
                        "p (a b) -> p a b", a=8)[:, :, p0:p0 + 64]
                    b.dma("sync", dst,
                          wout[0:rows, par, jh, :].rearrange("p (a b) -> p a b", a=8),
                          dQWp[par], waits=([(dqv, v_t)] if jh == 0 else []))
            dqs["qw_done"][u] = (dQWp[par], dQWp[par].n)
        if dqs["pending2"] is not None:
            pend = dqs["pending2"]
            dqs["pending2"] = stage2
            pend()
        else:
            dqs["pending2"] = stage2
        dqs["u"] = u + 1
        return True

    def emit_dq_flush():
        if dqs["pending2"] is not None:
            dqs["pending2"]()
            dqs["pending2"] = None

    def emit_dq_units(n, limit=N_DQ_UNITS):
        for _ in range(n):
            if dqs["u"] >= limit:
                return
            if not emit_dq_unit():
                return

    XT_t = XT.rearrange("(t p) n -> t p n", p=128)
    v_cnt = [0] * NT
    s_cnt = [0] * NT
    t_cnt = [0] * NT
    x_cnt = [0] * NT
    for tau in range(NT):
        sl = tau % 2
        wv = []
        if tau >= 2:
            wv = [(vEv, v_cnt[tau - 2]), (sEv, s_cnt[tau - 2])]
        x_cnt[tau] = b.dma("sync", xsA[:, sl, :], XT_t[tau], dXp[sl], waits=wv)
        wv2 = [(dXp[sl], x_cnt[tau])]
        if tau >= 2:
            wv2.append((tMM, t_cnt[tau - 2]))
        v_cnt[tau] = b.op("vector", lambda e, sl=sl: e.tensor_copy(xb[:, sl, :], xsA[:, sl, :]),
                          waits=wv2, inc=vEv)
        s_cnt[tau] = b.op("scalar", lambda e, sl=sl: e.activation(
            xsq[:, sl, :], xsA[:, sl, :], AF.Square), waits=wv2, inc=sEv)
        mw = [(vEv, v_cnt[tau]), (sEv, s_cnt[tau])]
        for c in range(4):
            mm(c, ones_bf[:], xb[:, sl, 512 * c:512 * (c + 1)],
               start=(tau == 0), stop=(tau == NT - 1), out_rows=(0, 1),
               waits=(mw if c == 0 else []))
        for c in range(4):
            t_cnt[tau] = mm(4 + c, ones_bf[:], xsq[:, sl, 512 * c:512 * (c + 1)],
                            start=(tau == 0), stop=(tau == NT - 1), out_rows=(0, 1))
        emit_dq_units(1, limit=176)
    stats_mm_done = t_cnt[NT - 1]

    sA, sB, sC, sD = (rowbank[0:1, 0:512], rowbank[0:1, 512:1024],
                      rowbank[0:1, 1024:1536], rowbank[0:1, 1536:2048])

    def ln_coeffs(nchunk, sum_banks, sq_banks, eps_cell, dst_a, dst_b, first_wait):
        abc = 0
        prev_bt = 0
        for c in range(nchunk):
            bs, bq = sum_banks[c], sq_banks[c]
            w0 = list(first_wait) if c == 0 else [(tMM, prev_bt)]
            cp1 = b.op("scalar", lambda e, bs=bs: e.copy(sA, pbr(bs, 0, 1)),
                       waits=w0, inc=sEv)
            cp2 = b.op("scalar", lambda e, bq=bq: e.copy(sB, pbr(bq, 0, 1)), inc=sEv)
            b.op("vector", lambda e: e.tensor_scalar(sC, sA, 1.0 / 2048.0, None,
                                                     ALU.mult), waits=[(sEv, cp2)])
            b.op("vector", lambda e: e.tensor_scalar(sD, sB, 1.0 / 2048.0, None,
                                                     ALU.mult))
            b.op("vector", lambda e: e.tensor_mul(sA, sC, sC))
            var_v = b.op("vector", lambda e: e.tensor_sub(sB, sD, sA), inc=vEv)
            sq_v = b.op("scalar", lambda e, ec=eps_cell: e.activation(
                sD, sB, AF.Sqrt, bias=cells[0:1, ec:ec + 1], scale=cells[0:1, 7:8]),
                waits=[(vEv, var_v)], inc=sEv)
            b.op("vector", lambda e: e.reciprocal(sA, sD), waits=[(sEv, sq_v)])
            ab_v = b.op("vector", lambda e: e.scalar_tensor_tensor(
                sB, sC, -1.0, sA, ALU.mult, ALU.mult), inc=vEv)
            mm(bs, ones1_f32[:], sA, True, True,
               waits=[(vEv, ab_v), (sEv, cp2)])
            bt = mm(bq, ones1_f32[:], sB, True, True)
            prev_bt = bt
            a1 = b.op("scalar", lambda e, c=c, bs=bs: e.copy(
                dst_a[:, 512 * c:512 * (c + 1)], pb(bs)), waits=[(tMM, bt)], inc=sEv)
            abc = b.op("scalar", lambda e, c=c, bq=bq: e.copy(
                dst_b[:, 512 * c:512 * (c + 1)], pb(bq)), inc=sEv)
            bank_guard[bs] = [(sEv, abc)]
            bank_guard[bq] = [(sEv, abc)]
        return abc

    abc2 = ln_coeffs(4, [0, 1, 2, 3], [4, 5, 6, 7], 4, aB, bB,
                     [(tMM, stats_mm_done)])

    ln_v = [0] * NT
    for tau in range(NT):
        wv = [(sEv, abc2)] if tau == 0 else [(vEv, ln_v[tau - 1])]
        x_t = b.dma("sync", xsA[:, 0, :], XT_t[tau], dXp[0], waits=wv)
        b.op("vector", lambda e: e.tensor_mul(tmpF[:], xsA[:, 0, :], aB[:]),
             waits=[(dXp[0], x_t)])
        ln_v[tau] = b.op("vector", lambda e, tau=tau: e.tensor_add(
            ln1[:, tau, :], tmpF[:], bB[:]), inc=vEv)
        emit_dq_units(1, limit=176)
    ln_done = ln_v[NT - 1]

    kw_done = [0] * NP
    vw_done = [0] * NP

    rope_state = {"v": 0}

    def rope_evac(bank, scale_c, tab_sl, dst, stg_guard):
        sc_v = b.op("scalar", lambda e, bk=bank: e.copy(s_rope[:], pb(bk)),
                    waits=list(stg_guard) + [(vEv, rope_state["v"])], inc=sEv)
        sh_t = 0
        for (d0, s0) in ((0, 32), (32, 0), (64, 96), (96, 64)):
            sh_t = b.dma("sync", sh_rope[d0:d0 + 32, :], s_rope[s0:s0 + 32, :], dSh,
                         waits=([(sEv, sc_v), (vEv, rope_state["v"])] if d0 == 0 else []))
        b.op("vector", lambda e, ts=tab_sl, c=scale_c: e.scalar_tensor_tensor(
            ropetmp[:], s_rope[:], c, cosR[:, ts[0]:ts[1]], ALU.mult, ALU.mult),
            waits=[(dSh, sh_t), (dTb, tables_done)])
        b.op("vector", lambda e, ts=tab_sl, c=scale_c: e.scalar_tensor_tensor(
            ropetmp2[:], sh_rope[:], c, sinA[:, ts[0]:ts[1]], ALU.mult, ALU.mult))
        ev = b.op("vector", lambda e: e.tensor_add(estg[:], ropetmp[:], ropetmp2[:]),
                  inc=vEv)
        rope_state["v"] = ev
        w_t = b.dma("sync", dst, estg[:], dKW, waits=[(vEv, ev)])
        return (sEv, sc_v), (dKW, w_t)

    prev_w_used = 0
    wslot_free = [(), ()]
    estg_guard = []
    for which, WT, dsts, ntc, rope_scale in (
        ("q", WQT, q_sc, 2, SC_ROPE),
        ("k", WKT, k_sc, 4, 1.0),
    ):
        WT_r = WT.rearrange("(t p) d -> p t d", p=128)
        for dc in range(24):
            hp, ct = dc // 3, dc % 3
            sl = dc % 2
            wv0 = list(wslot_free[sl])
            if which == "q" and dc < 2:
                wv0.append((vEv, ln_done))
            w_t = b.dma("gpsimd", wbufs[:, sl], WT_r[:, :, 128 * dc:128 * (dc + 1)],
                        dWp[sl], waits=wv0)
            banks = [2 * (dc % 4), 2 * (dc % 4) + 1] if ntc == 2 else \
                    ([0, 1, 2, 3] if dc % 2 == 0 else [4, 5, 6, 7])
            last_ldw_t = None
            for it in range(NT):
                lw = [(dWp[sl], w_t)] if it == 0 else []
                if it == 0:
                    lw.append((vEv, ln_done))
                ldw(wbufs[:, sl, it, :], waits=lw)
                for tc in range(ntc):
                    mm(banks[tc], wbufs[:, sl, it, :],
                       ln1[:, it, 512 * tc:512 * (tc + 1)],
                       start=(it == 0), stop=(it == NT - 1))
            mm_done = tMM.n
            wslot_free[sl] = [(tMM, mm_done)]
            for tc in range(ntc):
                bk = banks[tc]
                if ct < 2:
                    if which == "q":
                        ev = b.op("scalar", lambda e, bk=bk: e.mul(
                            estg[:], pb(bk), cells[:, 9:10]),
                            waits=[(tMM, mm_done)] + estg_guard, inc=sEv)
                        gsem = (sEv, ev)
                    else:
                        ev = b.op("scalar", lambda e, bk=bk: e.copy(estg[:], pb(bk)),
                                  waits=[(tMM, mm_done)] + estg_guard, inc=sEv)
                        gsem = (sEv, ev)
                    w2 = b.dma("sync", dsts[hp, ct, :, 512 * tc:512 * (tc + 1)],
                               estg[:], dKW, waits=[gsem])
                    bank_guard[bk] = [gsem]
                    estg_guard = [(dKW, w2)]
                else:
                    pg, wg = rope_evac(bk, rope_scale, (512 * tc, 512 * (tc + 1)),
                                       dsts[hp, 2, :, 512 * tc:512 * (tc + 1)],
                                       [(tMM, mm_done)] + estg_guard)
                    bank_guard[bk] = [pg]
                    estg_guard = [wg]
            if which == "k" and ct == 2:
                kw_done[hp] = dKW.n
            emit_dq_units(2, limit=176)

    WVT_r = WVT.rearrange("(t p) d -> p t d", p=128)
    wv_free = []
    for dvc in range(4):
        w_t = b.dma("gpsimd", wvb[:], WVT_r[:, :, 512 * dvc:512 * (dvc + 1)], dWV,
                    waits=wv_free)
        for kc in range(NKT):
            bk = 4 + kc % 4
            for it in range(NT):
                lw = [(dWV, w_t), (vEv, ln_done)] if it == 0 else []
                ldw(ln1[:, it, 128 * kc:128 * (kc + 1)], waits=lw)
                mm(bk, ln1[:, it, 128 * kc:128 * (kc + 1)], wvb[:, it, :],
                   start=(it == 0), stop=(it == NT - 1))
            mm_done = tMM.n
            ev = b.op("scalar", lambda e, bk=bk: e.copy(estg[:], pb(bk)),
                      waits=[(tMM, mm_done)] + estg_guard, inc=sEv)
            bank_guard[bk] = [(sEv, ev)]
            w2 = 0
            for s in range(2):
                w2 = b.dma("sync", v_sc[2 * dvc + s, kc, :, :],
                           estg[:, 256 * s:256 * (s + 1)], dKW,
                           waits=([(sEv, ev)] if s == 0 else []))
            estg_guard = [(dKW, w2)]
            if kc % 4 == 3:
                emit_dq_units(1, limit=176)
        wv_free = [(tMM, tMM.n)]
    for hp in range(NP):
        vw_done[hp] = dKW.n

    b.barrier()

    dkw_final = dKW.n
    pair_cons = [(), ()]
    p_free2 = [[], []]
    o_evac_v = 0
    sq_guard_outer = [0]
    eb_guard = [0]
    rb_guard = [0]
    bound_g = [0]
    for hp in range(NP):
        sl = hp % 2
        kr1 = b.dma("sync", kb[:, sl], k_sc[hp].rearrange("c p n -> p c n"), dKRp[sl],
                    waits=list(pair_cons[sl]) + [(dKW, dkw_final)])
        kr2 = b.dma("sync", qb[:, sl], q_sc[hp].rearrange("c p n -> p c n"), dKRp[sl])
        kr3 = b.dma("sync", vb[:, sl], v_sc[hp].rearrange("t p n -> p t n"), dKRp[sl])
        dKR = dKRp[sl]
        kqv_t = dKRp[sl].n
        sq_guard = sq_guard_outer[0]
        for hh in range(2):
            h = 2 * hp + hh
            o64 = ones64A if hh == 0 else ones64B
            for (which_k, kc_hh, dstcell) in ((0, hh, 20), (2, 2, 21)):
                ck = kc_hh if which_k == 0 else 2
                sq_s = b.op("scalar", lambda e, sl=sl, ck=ck: e.activation(
                    sqtile[:], kb[:, sl, ck, :], AF.Square),
                    waits=[(dKR, kqv_t), (tMM, sq_guard), (vEv, bound_g[0])],
                    inc=sEv)
                oo = ones_bf if which_k == 0 else o64
                for c in range(4):
                    m_t = mm(5, oo[:], sqtile[:, 512 * c:512 * (c + 1)], True, True,
                             out_rows=(0, 1), waits=[(sEv, sq_s)] if c == 0 else [])
                    c_t = b.op("scalar", lambda e, c=c: e.copy(
                        rowbank[0:1, 512 * c:512 * (c + 1)], pbr(5, 0, 1)),
                        waits=[(tMM, m_t), (vEv, bound_g[0]), (tMM, rb_guard[0])],
                        inc=sEv)
                    bank_guard[5] = [(sEv, c_t)]
                sq_guard = m_t
                bound_g[0] = b.op("vector", lambda e, dc=dstcell: e.reduce_max(
                    cells[0:1, dc:dc + 1], rowbank[0:1, 0:2048],
                    mybir.AxisListType.X), waits=[(sEv, c_t)], inc=vEv)
            kx_s = b.op("scalar", lambda e: e.activation(
                cells[0:1, 22:24], cells[0:1, 20:22], AF.Sqrt),
                waits=[(vEv, bound_g[0])], inc=sEv)
            sqn_s = b.op("scalar", lambda e, sl=sl, hh=hh: e.activation(
                sqtile[:, 0:1024], qb[:, sl, hh, :], AF.Square),
                waits=[(tMM, sq_guard)], inc=sEv)
            sqr_s = b.op("scalar", lambda e, sl=sl: e.activation(
                sqtile[:, 1024:2048], qb[:, sl, 2, :], AF.Square), inc=sEv)
            for c in range(2):
                mseg = rowbank[0:1, 512 * c:512 * (c + 1)]
                m1 = mm(5, ones_bf[:], sqtile[:, 512 * c:512 * (c + 1)], True, True,
                        out_rows=(0, 1), waits=[(sEv, sqn_s)] if c == 0 else [])
                c1 = b.op("scalar", lambda e: e.copy(sC, pbr(5, 0, 1)),
                          waits=[(tMM, m1), (vEv, bound_g[0])], inc=sEv)
                bank_guard[5] = [(sEv, c1)]
                m2 = mm(5, o64[:], sqtile[:, 1024 + 512 * c:1536 + 512 * c], True, True,
                        out_rows=(0, 1), waits=[(sEv, sqr_s)])
                c2 = b.op("scalar", lambda e: e.copy(sD, pbr(5, 0, 1)),
                          waits=[(tMM, m2), (tMM, rb_guard[0])], inc=sEv)
                bank_guard[5] = [(sEv, c2)]
                sq_guard = m2
                b.op("scalar", lambda e, ms=mseg: e.activation(ms, sC, AF.Sqrt))
                qs_s = b.op("scalar", lambda e: e.activation(sC, sD, AF.Sqrt),
                            inc=sEv)
                b.op("vector", lambda e, ms=mseg: e.tensor_scalar(
                    sD, ms, cells[0:1, 22:23], None, ALU.mult),
                    waits=[(sEv, qs_s), (sEv, kx_s)])
                bound_g[0] = b.op("vector", lambda e, ms=mseg: e.scalar_tensor_tensor(
                    ms, sC, cells[0:1, 23:24], sD, ALU.mult, ALU.add), inc=vEv)
            mx_v = b.op("vector", lambda e: e.reduce_max(
                cells[0:1, 10:11], rowbank[0:1, 0:1024], mybir.AxisListType.X),
                waits=[(tMM, eb_guard[0])], inc=vEv)
            bound_g[0] = mx_v
            m_t = mm(7, negones1_f32[:], cells[0:1, 10:11], True, True,
                     waits=[(vEv, mx_v)])
            eb_guard[0] = m_t
            eb_s = b.op("scalar", lambda e, h=h: e.copy(
                expbias[:, h:h + 1], pbr(7, 0, 128, 0, 1)),
                waits=[(tMM, m_t)], inc=sEv)
            bank_guard[7] = [(sEv, eb_s)]
            if debug is True:
                b.dma("sync", dbgM[h], cells[0:1, :], dKW,
                      waits=[(sEv, eb_s), (vEv, bound_g[0])])
                if hp == 0 and hh == 1:
                    b.dma("sync", dbgE[:, 16:32], expbias[:], dKW,
                          waits=[(sEv, eb_s)])
                ps7 = b.op("scalar", lambda e, h=h: e.copy(
                    dbgE_sb[:, h:h + 1], pbr(7, 0, 128, 0, 1)), inc=sEv)
                if hp == 3 and hh == 1:
                    b.dma("sync", dbgE[:, 32:48], expbias[:], dKW,
                          waits=[(sEv, ps7)])
            sq_guard_outer[0] = sq_guard
            emit_dq_units(2, limit=176)
        for hh in range(2):
            h = 2 * hp + hh
            r0, r1_ = (0, 64) if hh == 0 else (64, 128)
            for qc in range(2):
                ppar = (2 * hh + qc) % 2
                sb_rot = [0, 1]
                exp_v = [0] * NKT
                for kt in range(NKT):
                    bk = sb_rot[kt % 2]
                    mm(bk, kb[:, sl, hh, 128 * kt:128 * (kt + 1)],
                       qb[:, sl, hh, 512 * qc:512 * (qc + 1)],
                       start=True, stop=False)
                    sc_t = mm(bk, kb[r0:r1_, sl, 2, 128 * kt:128 * (kt + 1)],
                              qb[r0:r1_, sl, 2, 512 * qc:512 * (qc + 1)],
                              start=False, stop=True)
                    e_t = b.op("scalar", lambda e, bk=bk, kt=kt, h=h, pp=ppar:
                               e.activation(Pbuf[:, pp, kt, :], pb(bk), AF.Exp,
                                            bias=expbias[:, h:h + 1]),
                        waits=[(tMM, sc_t), (sEv, eb_s)] +
                              (p_free2[ppar] if kt == 0 else []), inc=sEv)
                    exp_v[kt] = e_t
                    bank_guard[bk] = [(sEv, e_t)]
                for kt in range(NKT):
                    s_t = mm(2, ones_bf[:], Pbuf[:, ppar, kt, :], start=(kt == 0),
                             stop=(kt == NKT - 1), out_rows=(0, 1),
                             waits=[(sEv, exp_v[kt])])
                rc_v = b.op("vector", lambda e: e.reciprocal(sD, pbr(2, 0, 1)),
                            waits=[(tMM, s_t), (sEv, exp_v[NKT - 1]),
                                   (tMM, rb_guard[0])], inc=vEv)
                bank_guard[2] = [(vEv, rc_v)]
                m_t = mm(7, ones1_f32[:], sD, True, True, waits=[(vEv, rc_v)])
                rb_guard[0] = m_t
                rb_s = b.op("scalar", lambda e: e.copy(rBuf[:], pb(7)),
                            waits=[(tMM, m_t), (vEv, o_evac_v)], inc=sEv)
                bank_guard[7] = [(sEv, rb_s)]
                if debug:
                    b.dma("sync", dbgR[2 * h + qc], rBuf[:], dKW,
                          waits=[(sEv, rb_s)])
                ob = 3 + (qc % 2)
                for kt in range(NKT):
                    av_t = mm(ob, vb[:, sl, kt, 128 * hh:128 * (hh + 1)],
                              Pbuf[:, ppar, kt, :], start=(kt == 0),
                              stop=(kt == NKT - 1),
                              waits=[(sEv, exp_v[NKT - 1]), (dKR, kqv_t)]
                              if kt == 0 else [])
                o_evac_v = b.op("vector", lambda e, ob=ob, h=h, qc=qc: e.tensor_mul(
                    Obuf[:, h, 512 * qc:512 * (qc + 1)], pb(ob), rBuf[:]),
                    waits=[(tMM, av_t), (sEv, rb_s)], inc=vEv)
                bank_guard[ob] = [(vEv, o_evac_v)]
                p_free2[ppar] = [(tMM, av_t)]
                emit_dq_units(3, limit=176)
        pair_cons[sl] = [(tMM, tMM.n)]
    while dqs["u"] < 176 and emit_dq_unit():
        pass
    emit_dq_flush()
    gu_w_final = [(dQWp[0], dQWp[0].n), (dQWp[1], dQWp[1].n)]
    if debug is True:
        b.dma("sync", dbgO, Obuf[:], dKW, waits=[(vEv, o_evac_v)])
        b.dma("sync", dbgE[:, 0:16], expbias[:], dKW)
        b.dma("sync", dbgE[:, 48:64], dbgE_sb[:, 0:16], dKW)

    b.barrier()

    WOT_r = WOT.rearrange("(t p) d -> p t d", p=128)
    wslot_free = [(), ()]
    xslot_guard = [(dqv, dqv.n)]
    st_t = 0
    x2w = [0] * NT
    for ht in range(NT):
        sl = ht % 2
        w_t = b.dma("gpsimd", wbufs[:, sl], WOT_r[:, :, 128 * ht:128 * (ht + 1)],
                    dWp[sl], waits=wslot_free[sl])
        x_t = b.dma("sync", fbufA[:, 0, :], XT_t[ht][:, 0:TOK], dXp[0],
                    waits=xslot_guard)
        banks = [2 * (ht % 2), 2 * (ht % 2) + 1]
        for it in range(NT):
            lw = [(dWp[sl], w_t)] if it == 0 else []
            ldw(wbufs[:, sl, it, :], waits=lw)
            for tc in range(2):
                mm(banks[tc], wbufs[:, sl, it, :], Obuf[:, it, 512 * tc:512 * (tc + 1)],
                   start=(it == 0), stop=(it == NT - 1))
        mm_done = tMM.n
        wslot_free[sl] = [(tMM, mm_done)]
        for tc in range(2):
            wv = [(tMM, mm_done), (dXp[0], x_t)]
            if tc == 0 and ht > 0:
                wv.append((dX2W, x2w[ht - 1]))
            b.op("vector", lambda e, bk=banks[tc], tc=tc: e.tensor_add(
                fbufA[:, 1, 512 * tc:512 * (tc + 1)], pb(bk),
                fbufA[:, 0, 512 * tc:512 * (tc + 1)]),
                waits=wv)
        x2_v = b.op("vector", lambda e: e.tensor_copy(xb2[:], fbufA[:, 1, :]), inc=vEv)
        bank_guard[banks[0]] = [(vEv, x2_v)]
        bank_guard[banks[1]] = [(vEv, x2_v)]
        x2_s = b.op("scalar", lambda e: e.activation(xsq2[:], fbufA[:, 1, :], AF.Square),
                    waits=[(vEv, x2_v)], inc=sEv)
        for c in range(2):
            mm(4 + c, ones_bf[:], xb2[:, 512 * c:512 * (c + 1)],
               start=(ht == 0), stop=(ht == NT - 1), out_rows=(0, 1),
               waits=[(vEv, x2_v)] if c == 0 else [])
        for c in range(2):
            st_t = mm(6 + c, ones_bf[:], xsq2[:, 512 * c:512 * (c + 1)],
                      start=(ht == 0), stop=(ht == NT - 1), out_rows=(0, 1),
                      waits=[(sEv, x2_s)] if c == 0 else [])
        x2w[ht] = b.dma("sync", x2_sc[ht], fbufA[:, 1, :], dX2W, waits=[(vEv, x2_v)])
        xslot_guard = [(vEv, x2_v), (sEv, x2_s), (tMM, st_t)]

    ab2_s = ln_coeffs(2, [4, 5], [6, 7], 4, aB2, bB2, [(tMM, st_t)])
    ln2_v = 0
    for ht in range(NT):
        x_t = b.dma("sync", fbufA[:, 0, :], x2_sc[ht], dX2R,
                    waits=([(sEv, ab2_s), (dX2W, dX2W.n)] if ht == 0
                           else [(vEv, ln2_v)]))
        b.op("vector", lambda e: e.tensor_mul(fbufA[:, 1, :], fbufA[:, 0, :], aB2[:]),
             waits=[(dX2R, x_t)])
        ln2_v = b.op("vector", lambda e, ht=ht: e.tensor_add(
            ln2[:, ht, :], fbufA[:, 1, :], bB2[:]), inc=vEv)

    if debug is True:
        b.dma("sync", dbgL2, ln2[:], dKW, waits=[(vEv, ln2_v)])
    b.barrier()

    pc_free = [(), ()]
    htmp_guard = [0]
    hevac_g = [0]
    for cb in range(6):
        W = 512 if cb == 5 else 1024
        ncol = W // 8
        for qd in range(2):
            for half, base in (("g", 1024 * cb), ("u", NI + 1024 * cb)):
                piece_t = [0] * NT

                def d_piece(it, cb=cb, qd=qd, half=half, base=base, W=W):
                    psl = it % 2
                    wv = list(pc_free[psl])
                    if cb == 0 and qd == 0 and half == "g" and it < 2:
                        wv += gu_w_final
                    piece_t[it] = b.dma("scalar", wpiece[:, psl, 0:W],
                                        w1_sc[it, :, base:base + W], dFRp[psl],
                                        waits=wv)
                d_piece(0)
                d_piece(1)
                for it in range(NT):
                    psl = it % 2
                    for jj in range(4):
                        jq = 4 * qd + jj
                        lw = [(dFRp[psl], piece_t[it])] if jj == 0 else []
                        if cb == 0 and qd == 0 and half == "g" and it == 0 and jj == 0:
                            lw.append((vEv, ln2_v))
                        ldw(wpiece[:, psl, jq:W:8], waits=lw)
                        for tc in range(2):
                            mm(2 * jj + tc, wpiece[:, psl, jq:W:8],
                               ln2[:, it, 512 * tc:512 * (tc + 1)],
                               start=(it == 0), stop=(it == NT - 1),
                               out_rows=(0, ncol))
                    pc_free[psl] = [(tMM, tMM.n)]
                    if it + 2 < NT:
                        d_piece(it + 2)
                mm_done = tMM.n
                emit_dq_units(4)
                if half == "g":
                    sg_s = 0
                    for jj in range(4):
                        for tc in range(2):
                            sg_s = b.op("scalar", lambda e, bk=2 * jj + tc, jj=jj,
                                        tc=tc, nc_=ncol: e.activation(
                                            sgbuf[0:nc_, 2 * jj + tc, :],
                                            pbr(2 * jj + tc, 0, nc_), AF.Silu),
                                        waits=([(tMM, mm_done), (vEv, hevac_g[0])]
                                               if (jj == 0 and tc == 0) else []),
                                        inc=sEv)
                            bank_guard[2 * jj + tc] = [(sEv, sg_s)]
                else:
                    for jj in range(4):
                        jq = 4 * qd + jj
                        for tc in range(2):
                            bk = 2 * jj + tc
                            if cb < 5:
                                h_v = b.op("vector", lambda e, bk=bk, cb=cb, jq=jq,
                                           tc=tc: e.tensor_mul(
                                               Hbuf[:, 8 * cb + jq, 512 * tc:512 * (tc + 1)],
                                               pbr(bk, 0, 128), sgbuf[:, bk, :]),
                                           waits=[(tMM, mm_done), (sEv, sg_s)]
                                           if (jj == 0 and tc == 0) else [], inc=vEv)
                                bank_guard[bk] = [(vEv, h_v)]
                                hevac_g[0] = h_v
                            else:
                                itH = 40 + jq // 2
                                if jq % 2 == 0:
                                    h_v = b.op("vector", lambda e, bk=bk, itH=itH, tc=tc:
                                               e.tensor_mul(
                                                   Hbuf[0:64, itH, 512 * tc:512 * (tc + 1)],
                                                   pbr(bk, 0, 64), sgbuf[0:64, bk, :]),
                                               waits=[(tMM, mm_done), (sEv, sg_s)]
                                               if (jj == 0 and tc == 0) else [], inc=vEv)
                                    bank_guard[bk] = [(vEv, h_v)]
                                    hevac_g[0] = h_v
                                else:
                                    hw = ([(tMM, mm_done), (sEv, sg_s)]
                                          if (jj == 0 and tc == 0) else [])
                                    hw.append((dSh, htmp_guard[0]))
                                    h_v = b.op("vector", lambda e, bk=bk: e.tensor_mul(
                                        htmp[0:64, :], pbr(bk, 0, 64), sgbuf[0:64, bk, :]),
                                        waits=hw, inc=vEv)
                                    sh_t = b.dma("sync",
                                                 Hbuf[64:128, itH, 512 * tc:512 * (tc + 1)],
                                                 htmp[0:64, :], dSh, waits=[(vEv, h_v)])
                                    htmp_guard[0] = sh_t
                                    bank_guard[bk] = [(vEv, h_v)]
                                    hevac_g[0] = h_v
    while emit_dq_unit():
        pass
    emit_dq_flush()
    dq_w_final = [(dQWp[0], dQWp[0].n), (dQWp[1], dQWp[1].n)]
    dq_v_final = dqv.n
    h_shift_done = dSh.n
    h_done_v = vEv.n
    if debug in (True, 3):
        b.dma("sync", dbgH, Hbuf[:], dKW, waits=[(vEv, h_done_v), (dSh, h_shift_done)])

    y_guard = []
    for tc in range(2):
        y_done_s = 0
        for hb in range(2):
            piece_t = [0] * NIT

            def e_piece(it, tc=tc, hb=hb):
                psl = it % 2
                wv = list(pc_free[psl])
                if tc == 0 and hb == 0 and it < 2:
                    wv += dq_w_final
                piece_t[it] = b.dma("scalar", wpiece[:, psl, :],
                                    wd_sc[it, :, 1024 * hb:1024 * (hb + 1)], dFRp[psl],
                                    waits=wv)
            e_piece(0)
            e_piece(1)
            for it in range(NIT):
                psl = it % 2
                for jh in range(8):
                    lw = [(dFRp[psl], piece_t[it])] if jh == 0 else []
                    if tc == 0 and hb == 0 and it == 0 and jh == 0:
                        lw += [(vEv, h_done_v), (dSh, h_shift_done)]
                    ldw(wpiece[:, psl, 128 * jh:128 * (jh + 1)], waits=lw)
                    mm(jh, wpiece[:, psl, 128 * jh:128 * (jh + 1)],
                       Hbuf[:, it, 512 * tc:512 * (tc + 1)],
                       start=(it == 0), stop=(it == NIT - 1))
                pc_free[psl] = [(tMM, tMM.n)]
                if it + 2 < NIT:
                    e_piece(it + 2)
            mm_done = tMM.n
            for jh in range(8):
                y_done_s = b.op("scalar", lambda e, jh=jh, hb=hb: e.copy(
                    ybuf[:, 8 * hb + jh, :], pb(jh)),
                    waits=([(tMM, mm_done)] + list(y_guard)) if jh == 0 else [],
                    inc=sEv)
                bank_guard[jh] = [(sEv, y_done_s)]
        if debug in (True, 3):
            b.dma("sync", dbgY[tc], ybuf[:], dKW, waits=[(sEv, y_done_s)])
        ys_t = 0
        for yt in range(NT):
            ys_s = b.op("scalar", lambda e, yt=yt: e.activation(
                ysqb[:], ybuf[:, yt, :], AF.Square),
                waits=[(sEv, y_done_s)] if yt == 0 else [(tMM, ys_t)], inc=sEv)
            ys_t = mm(0, ones_bf[:], ysqb[:], start=(yt == 0), stop=(yt == NT - 1),
                      out_rows=(0, 1), waits=[(sEv, ys_s)])
        rr_s = b.op("scalar", lambda e: e.activation(
            sA, pbr(0, 0, 1), AF.Sqrt,
            bias=cells[0:1, 5:6], scale=cells[0:1, 6:7]),
            waits=[(tMM, ys_t), (tMM, rb_guard[0])], inc=sEv)
        bank_guard[0] = [(sEv, rr_s)]
        rr_v = b.op("vector", lambda e: e.reciprocal(sB, sA),
                    waits=[(sEv, rr_s)], inc=vEv)
        m_t = mm(1, ones1_f32[:], sB, True, True, waits=[(vEv, rr_v)])
        rb_guard[0] = m_t
        rrb_s = b.op("scalar", lambda e: e.copy(rrB[:], pb(1)),
                     waits=[(tMM, m_t)], inc=sEv)
        bank_guard[1] = [(sEv, rrb_s)]
        for yt in range(NT):
            wfb = [(dOse, dOse.n)]
            if tc == 0 and yt == 0:
                wfb.append((dqv, dq_v_final))
            x_t = b.dma("sync", fbufA[:, 0, 0:512],
                        x2_sc[yt, :, 512 * tc:512 * (tc + 1)], dX2R,
                        waits=wfb)
            b.op("vector", lambda e, yt=yt: e.tensor_mul(
                fbufA[:, 1, 0:512], ybuf[:, yt, :], rrB[:]),
                waits=[(sEv, rrb_s), (dX2R, x_t)])
            o_v = b.op("vector", lambda e: e.tensor_add(
                fbufA[:, 1, 512:1024], fbufA[:, 1, 0:512], fbufA[:, 0, 0:512]),
                inc=vEv)
            y_guard = [(vEv, o_v)]
            b.dma("sync", OUT[yt, :, 512 * tc:512 * (tc + 1)], fbufA[:, 1, 512:1024],
                  dOse, waits=[(vEv, o_v)])
    final_out = dOse.n
    b.plan["sync"].append(lambda e, h=dOse.h, v=final_out: e.wait_ge(h, v))

    with nc.Block() as block:
        for eng_name, deco in (("sync", block.sync), ("tensor", block.tensor),
                               ("vector", block.vector), ("scalar", block.scalar),
                               ("gpsimd", block.gpsimd)):
            def run(e, items=b.plan[eng_name]):
                for f in items:
                    f(e)
            deco(run)
    b.es.close()
    return nc


_NC_CACHE = {}


def _get_nc(debug=False):
    key = bool(debug)
    if key not in _NC_CACHE:
        _NC_CACHE[key] = build_program(debug=key)
    return _NC_CACHE[key]


def _stage(L0_attn_hidden_states, L0_attn_Wq, L0_attn_Wk, L0_attn_Wv, L0_attn_Wo,
           L0_ffn_gate_up_packed, L0_ffn_gate_up_scales, L0_ffn_down_packed,
           L0_ffn_down_scales):
    x = np.asarray(L0_attn_hidden_states)
    Wq = np.asarray(L0_attn_Wq)
    Wk = np.asarray(L0_attn_Wk)
    Wv = np.asarray(L0_attn_Wv)
    Wo = np.asarray(L0_attn_Wo)
    colord = []
    for hp in range(NP):
        colord += list(range(128 * 2 * hp, 128 * 2 * hp + 256))
        colord += list(range(2048 + 64 * 2 * hp, 2048 + 64 * 2 * hp + 128))
    colord = np.array(colord)
    WQT = np.ascontiguousarray(Wq[colord][:, PERM].T.astype(np.float32))
    WKT = np.ascontiguousarray(Wk[colord][:, PERM].T.astype(np.float32))
    WVT = np.ascontiguousarray(Wv[:, PERM].T.astype(np.float32))
    WOT = np.ascontiguousarray(Wo[PERM, :].T.astype(np.float32))
    PKGa = np.ascontiguousarray(np.asarray(L0_ffn_gate_up_packed).T)
    PKDa = np.ascontiguousarray(np.asarray(L0_ffn_down_packed).T)
    SCGa = np.ascontiguousarray(np.repeat(np.asarray(L0_ffn_gate_up_scales), 16, axis=0))
    SCDa = np.ascontiguousarray(np.repeat(np.asarray(L0_ffn_down_scales), 16, axis=0))

    inv = 1.0 / (10000.0 ** (np.arange(0, DR, 2, dtype=np.float64) / DR))
    t = np.arange(S, dtype=np.float64)
    fr = np.outer(t, inv)
    cosT = np.cos(fr).astype(np.float32)
    sinT = np.sin(fr).astype(np.float32)

    in_maps = []
    for k in range(8):
        bb_, hh_ = k // 2, k % 2
        t0 = TOK * hh_
        tokord = np.concatenate([np.arange(t0, t0 + TOK),
                                 np.arange((1 - hh_) * TOK, (1 - hh_) * TOK + TOK)])
        XTk = np.ascontiguousarray(x[bb_][tokord][:, PERM].T.astype(np.float32))
        ck = cosT[tokord].T
        sk = sinT[tokord].T
        COSRk = np.ascontiguousarray(np.tile(ck, (4, 1)).astype(np.float32))
        SINAk = np.ascontiguousarray(
            np.concatenate([-sk, sk, -sk, sk], axis=0).astype(np.float32))
        in_maps.append({
            "XT": XTk, "WQT": WQT, "WKT": WKT, "WVT": WVT, "WOT": WOT,
            "PKG": PKGa, "PKD": PKDa, "SCG": SCGa, "SCD": SCDa,
            "COSR": COSRk, "SINA": SINAk,
        })
    return in_maps


def kernel(L0_attn_hidden_states, L0_attn_Wq, L0_attn_Wk, L0_attn_Wv, L0_attn_Wo,
           L0_ffn_gate_up_packed, L0_ffn_gate_up_scales, L0_ffn_down_packed,
           L0_ffn_down_scales):
    in_maps = _stage(L0_attn_hidden_states, L0_attn_Wq, L0_attn_Wk, L0_attn_Wv,
                     L0_attn_Wo, L0_ffn_gate_up_packed, L0_ffn_gate_up_scales,
                     L0_ffn_down_packed, L0_ffn_down_scales)
    nc = _get_nc(debug=_DEBUG)
    res = run_bass_kernel_spmd(nc, in_maps, core_ids=list(range(8)))
    out = np.empty((B, S, D), dtype=np.float32)
    for k in range(8):
        bb_, hh_ = k // 2, k % 2
        o = np.asarray(res.results[k]["OUT"]).reshape(D, TOK)
        arr = np.empty((TOK, D), dtype=np.float32)
        arr[:, PERM] = o.T
        out[bb_, TOK * hh_:TOK * (hh_ + 1), :] = arr
    if _DEBUG:
        kernel._last = res
    return out


# version 08
# speedup vs baseline: 1.3286x; 1.1065x over previous
import contextlib
import numpy as np
import ml_dtypes

import concourse.bass as bass
import concourse.mybir as mybir
from concourse.bass_utils import run_bass_kernel_spmd

F32 = mybir.dt.float32
BF16 = mybir.dt.bfloat16
I32 = mybir.dt.int32
F16 = mybir.dt.float16
ALU = mybir.AluOpType
AF = mybir.ActivationFunctionType
BF = ml_dtypes.bfloat16

B, S, D = 4, 2048, 2048
H, DN, DR, DV = 16, 128, 64, 128
NI = 5632
TOK = 1024
NT = 16
NKT = 16
NP = 8
GU = 2 * NI
NIT = 44
LN_EPS = 1e-5
RMS_EPS = 1e-6
SC_NOPE = float(1.0 / np.sqrt(128.0))
SC_ROPE = 0.125

_r = np.arange(D)
PERM = (1024 * (_r // 1024) + 8 * (_r % 128) + (_r // 128) % 8).astype(np.int64)

ENGINES = ("sync", "tensor", "vector", "scalar", "gpsimd")
_DEBUG = False


class Sem:
    __slots__ = ("h", "n", "name")

    def __init__(self, h, name):
        self.h, self.n, self.name = h, 0, name


class Bld:
    def __init__(self):
        self.nc = bass.Bass("TRN2", target_bir_lowering=False)
        self.plan = {e: [] for e in ENGINES}
        self.es = contextlib.ExitStack()
        self.sems = {}

    def sem(self, name):
        if name not in self.sems:
            self.sems[name] = Sem(self.es.enter_context(self.nc.semaphore(name)), name)
        return self.sems[name]

    def op(self, eng, make, waits=(), inc=None, incn=1):
        wl = [(s.h, int(v)) for (s, v) in waits if v > 0]
        ih = None
        if inc is not None:
            inc.n += incn
            ih = inc.h

        def fn(e, make=make, wl=wl, ih=ih, iv=incn):
            for (h, v) in wl:
                e.wait_ge(h, v)
            ins = make(e)
            if ih is not None:
                ins.then_inc(ih, iv)
        self.plan[eng].append(fn)
        return inc.n if inc is not None else None

    def dma(self, eng, out, in_, sem, waits=()):
        return self.op(eng, (lambda e, o=out, i=in_: e.dma_start(out=o, in_=i)),
                       waits=waits, inc=sem, incn=16)

    def sb(self, name, shape, dtype, offset):
        base = (self.nc.sbuf_base + 31) & ~31
        return self.nc.alloc_sbuf_tensor_at(name, list(shape), dtype,
                                            offset=base + offset)

    def barrier(self):
        ph = self.sem("ph")
        ph.n += 5
        t = ph.n
        for e in ENGINES:
            def fn(eng, h=ph.h, t=t):
                eng.nop().then_inc(h, 1)
                eng.wait_ge(h, t)
            self.plan[e].append(fn)


def build_program(debug=False):
    b = Bld()
    nc = b.nc
    okind = "ExternalOutput" if debug is True else "Internal"

    XT = nc.dram_tensor("XT", [D, S], F32, kind="ExternalInput").ap()
    WQT = nc.dram_tensor("WQT", [D, 3072], F32, kind="ExternalInput").ap()
    WKT = nc.dram_tensor("WKT", [D, 3072], F32, kind="ExternalInput").ap()
    WVT = nc.dram_tensor("WVT", [D, 2048], F32, kind="ExternalInput").ap()
    WOT = nc.dram_tensor("WOT", [D, 2048], F32, kind="ExternalInput").ap()
    PKS = nc.dram_tensor("PKS", [9, 128, 512], I32, kind="ExternalInput").ap()
    SCS = nc.dram_tensor("SCS", [9, 128, 512], F16, kind="ExternalInput").ap()
    COSR = nc.dram_tensor("COSR", [128, S], F32, kind="ExternalInput").ap()
    SINA = nc.dram_tensor("SINA", [128, S], F32, kind="ExternalInput").ap()
    OUT = nc.dram_tensor("OUT", [NT, 128, TOK], F32, kind="ExternalOutput").ap()

    dbgO = nc.dram_tensor("dbgO", [128, NT, 1024], BF16, kind=okind).ap()
    dbgE = nc.dram_tensor("dbgE", [128, 64], F32, kind=okind).ap()
    dbgM = nc.dram_tensor("dbgM", [H, 1, 32], F32, kind=okind).ap()
    dbgS = nc.dram_tensor("dbgS", [72, 8, 128, 512], BF16, kind=okind).ap()
    dbgL2 = nc.dram_tensor("dbgL2", [128, NT, 1024], BF16, kind=okind).ap()
    dbgH = nc.dram_tensor("dbgH", [128, NIT, 1024], BF16, kind=okind).ap()
    dbgY = nc.dram_tensor("dbgY", [2, 128, NT, 512], F32, kind=okind).ap()
    dbgR = nc.dram_tensor("dbgR", [2 * H, 128, 512], F32, kind=okind).ap()
    k_sc = nc.dram_tensor("k_sc", [NP, 3, 128, S], BF16, kind=okind).ap()
    q_sc = nc.dram_tensor("q_sc", [NP, 3, 128, TOK], BF16, kind=okind).ap()
    v_sc = nc.dram_tensor("v_sc", [NP, NKT, 128, 256], BF16, kind=okind).ap()
    scr_l = nc.dram_tensor("scr_l", [9, 8, 128, 512], BF16, kind="Internal").ap()
    scr_f = nc.dram_tensor("scr_f", [72, 8, 128, 512], BF16, kind="Internal").ap()
    x2_sc = nc.dram_tensor("x2_sc", [NT, 128, TOK], F32, kind=("ExternalOutput" if debug in (True, 2) else "Internal")).ap()

    cells = b.sb("cells", [128, 32], F32, 0)
    ones_bf = b.sb("ones_bf", [128, 1], BF16, 128)
    ones_f32 = b.sb("ones_f32", [128, 1], F32, 160)
    ones1_f32 = b.sb("ones1_f32", [1, 128], F32, 192)
    negones1_f32 = b.sb("negones1_f32", [1, 128], F32, 207872)
    dbgE_sb = b.sb("dbgE_sb", [128, 16], F32, 208384)
    ones64A = b.sb("ones64A", [128, 1], BF16, 704)
    ones64B = b.sb("ones64B", [128, 1], BF16, 736)
    expbias = b.sb("expbias", [128, 16], F32, 768)
    rowbank = b.sb("rowbank", [128, 2048], F32, 1024)
    fbufA = b.sb("fbufA", [128, 2, 1024], F32, 9216)
    dq_t05 = b.sb("dq_t05", [128, 2, 1024], BF16, 9216)
    dq_smul = b.sb("dq_smul", [128, 2, 1024], BF16, 13312)
    wbufs = b.sb("wbufs", [128, 2, 16, 128], BF16, 17408)
    wpiece = b.sb("wpiece", [128, 2, 1024], BF16, 17408)
    tmpF = b.sb("tmpF", [128, 2048], F32, 17408)

    RD = 25600
    pkb = b.sb("pkb", [128, 2, 512], I32, RD)
    scb = b.sb("scb", [128, 2, 512], BF16, RD + 4096)
    m32 = b.sb("m32", [128, 512], I32, RD + 6144)
    s32 = b.sb("s32", [128, 512], I32, RD + 8192)
    dq_r1 = b.sb("dq_r1", [128, 2, 1024], BF16, RD + 10240)
    dq_r2 = b.sb("dq_r2", [128, 2, 1024], BF16, RD + 14336)
    dq_t2 = b.sb("dq_t2", [128, 1024], BF16, RD + 18432)
    dq_u = b.sb("dq_u", [128, 1024], BF16, RD + 20480)
    wout = b.sb("wout", [128, 2, 2, 512], BF16, RD + 22528)

    R1 = 52224
    ln1 = b.sb("ln1", [128, NT, 2048], BF16, R1)
    sqtile = b.sb("sqtile", [128, 2048], BF16, R1)
    Pbuf = b.sb("Pbuf", [128, 2, NKT, 512], BF16, R1 + 8192)
    rBuf = b.sb("rBuf", [128, 512], F32, R1 + 40960)
    xb2 = b.sb("xb2", [128, 1024], BF16, R1)
    xsq2 = b.sb("xsq2", [128, 1024], BF16, R1 + 2048)
    ln2 = b.sb("ln2", [128, NT, 1024], BF16, R1)
    sgbuf = b.sb("sgbuf", [128, 8, 512], BF16, R1 + 32768)
    htmp = b.sb("htmp", [128, 512], BF16, R1 + 40960)
    ybuf = b.sb("ybuf", [128, NT, 512], F32, R1)
    ysqb = b.sb("ysqb", [128, 512], BF16, R1 + 32768)
    rrB = b.sb("rrB", [128, 512], F32, R1 + 33792)

    R2 = 117760
    xsq = b.sb("xsq", [128, 2, 2048], BF16, R2)
    xb = b.sb("xb", [128, 2, 2048], BF16, R2 + 8192)
    xsA = b.sb("xsA", [128, 2, 2048], F32, R2 + 16384)
    cosR = b.sb("cosR", [128, 2048], F32, R2 + 32768)
    sinA = b.sb("sinA", [128, 2048], F32, R2 + 40960)
    aB = b.sb("aB", [128, 2048], F32, R2 + 49152)
    bB = b.sb("bB", [128, 2048], F32, R2 + 57344)
    estg = b.sb("estg", [128, 512], BF16, R2 + 65536)
    s_rope = b.sb("s_rope", [128, 512], BF16, R2 + 66560)
    sh_rope = b.sb("sh_rope", [128, 512], BF16, R2 + 67584)
    ropetmp = b.sb("ropetmp", [128, 512], F32, R2 + 68608)
    ropetmp2 = b.sb("ropetmp2", [128, 512], F32, R2 + 70656)
    wvb = b.sb("wvb", [128, 16, 512], BF16, R2 + 72704)
    kb = b.sb("kb", [128, 2, 3, 2048], BF16, R2)
    qb = b.sb("qb", [128, 2, 3, 1024], BF16, R2 + 24576)
    vb = b.sb("vb", [128, 2, NKT, 256], BF16, R2 + 36864)
    Obuf = b.sb("Obuf", [128, NT, 1024], BF16, R2 + 53248)
    aB2 = b.sb("aB2", [128, 1024], F32, R2)
    bB2 = b.sb("bB2", [128, 1024], F32, R2 + 4096)
    Hbuf = b.sb("Hbuf", [128, NIT, 1024], BF16, R2)

    psAll = b.es.enter_context(nc.psum_tensor("psAll", [128, 4096], F32))

    def pb(bank):
        return psAll[:, 512 * bank:512 * (bank + 1)]

    def pbr(bank, r0, r1, c0=0, c1=512):
        return psAll[r0:r1, 512 * bank + c0:512 * bank + c1]

    dXp = [b.sem("dX0"), b.sem("dX1")]
    dWp = [b.sem("dW0"), b.sem("dW1")]
    dWV = b.sem("dWV"); dTb = b.sem("dTb")
    dPKp = [b.sem("dPK0"), b.sem("dPK1")]
    dSCp = [b.sem("dSC0"), b.sem("dSC1")]
    dKW = b.sem("dKW")
    dKRp = [b.sem("dKR0"), b.sem("dKR1")]
    dQWp = [b.sem("dQW0"), b.sem("dQW1")]
    dFRp = [b.sem("dFR0"), b.sem("dFR1")]
    dX2W = b.sem("dX2W"); dX2R = b.sem("dX2R")
    dOse = b.sem("dOse"); dSh = b.sem("dSh")
    tMM = b.sem("tMM"); vEv = b.sem("vEv"); sEv = b.sem("sEv")
    dqa = b.sem("dqa"); dqv = b.sem("dqv"); dqx = b.sem("dqx")

    bank_guard = {i: [] for i in range(8)}

    def mm(bank, lhsT, rhs, start, stop, out_rows=None, waits=()):
        w = list(waits)
        if start:
            w += bank_guard[bank]
            bank_guard[bank] = []
        if out_rows is None:
            out = psAll[:, 512 * bank:512 * bank + rhs.shape[-1]]
        else:
            out = psAll[out_rows[0]:out_rows[1], 512 * bank:512 * bank + rhs.shape[-1]]
        return b.op("tensor",
                    lambda e, o=out, l=lhsT, r=rhs, st=start, sp=stop:
                    e.matmul(o, l, r, start=st, stop=sp),
                    waits=w, inc=tMM)

    def ldw(wap, waits=()):
        return b.op("tensor", lambda e, w=wap: e.ldweights(w), waits=waits)

    for ci, cv in {0: -2.0, 1: 0.5, 2: -6.0, 3: 1.0, 4: LN_EPS, 5: RMS_EPS,
                   6: 1.0 / 2048.0, 7: 1.0, 8: 2.0, 9: SC_NOPE}.items():
        b.op("vector", lambda e, ci=ci, cv=cv: e.memset(cells[:, ci:ci + 1], cv))
    b.op("vector", lambda e: e.memset(ones_bf[:], 1.0))
    b.op("vector", lambda e: e.memset(ones_f32[:], 1.0))
    b.op("vector", lambda e: e.memset(ones1_f32[:], 1.0))
    b.op("vector", lambda e: e.memset(negones1_f32[:], -1.0))
    b.op("vector", lambda e: e.memset(ones64A[0:64, :], 1.0))
    b.op("vector", lambda e: e.memset(ones64A[64:128, :], 0.0))
    b.op("vector", lambda e: e.memset(ones64B[0:64, :], 0.0))
    consts_done = b.op("vector", lambda e: e.memset(ones64B[64:128, :], 1.0), inc=vEv)

    b.dma("sync", cosR[:], COSR, dTb)
    tables_done = b.dma("sync", sinA[:], SINA, dTb)

    N_DQ_CH = 9
    N_DQ_UNITS = 4 * N_DQ_CH
    dqs = {"u": 0, "pk_t": None, "sc_t": None, "a_t": 0, "pending2": None,
           "qw_done": [None] * N_DQ_UNITS}

    def emit_dq_unit():
        u = dqs["u"]
        if u >= N_DQ_UNITS:
            return False
        ci, jp = u // 4, u % 4
        slot = ci % 2
        par = u % 2
        rows = 128
        if jp == 0:
            dqs["pk_t"] = (dPKp[slot], b.dma("scalar", pkb[:, slot, :], PKS[ci],
                           dPKp[slot], waits=([(dqx, 4 * (ci - 1))] if ci >= 2 else [])))
            dqs["sc_t"] = (dSCp[slot], b.dma("gpsimd", scb[:, slot, :], SCS[ci],
                           dSCp[slot], waits=([(dqv, 4 * (ci - 1))] if ci >= 2 else [])))
        xw = [dqs["pk_t"]] if jp == 0 else []
        if u >= 2:
            xw.append((dqa, u - 1))
        for jh in range(2):
            j = 2 * jp + jh
            b.op("vector", lambda e, sl=slot, j=j:
                 e.tensor_scalar(m32[:, :], pkb[:, sl, :], 4 * j, 7,
                                 ALU.logical_shift_right, ALU.bitwise_and),
                 waits=(xw if jh == 0 else []))
            b.op("vector", lambda e, sl=slot, j=j:
                 e.tensor_scalar(s32[:, :], pkb[:, sl, :], 4 * j + 3, 1,
                                 ALU.logical_shift_right, ALU.bitwise_and))
            b.op("vector", lambda e, p=par, jh=jh:
                 e.tensor_scalar(dq_t05[:, p, 512 * jh:512 * (jh + 1)],
                                 m32[:, :], 0.5, None, ALU.mult))
            b.op("vector", lambda e, p=par, jh=jh:
                 e.tensor_scalar(dq_smul[:, p, 512 * jh:512 * (jh + 1)],
                                 s32[:, :], -2.0, 1.0, ALU.mult, ALU.add),
                 inc=(dqx if jh == 1 else None))
        x_t = dqx.n
        sw = [(dqx, x_t)]
        if u == 0:
            sw.append((vEv, consts_done))
        if u >= 2:
            sw.append((dqv, u - 1))
        b.op("scalar", lambda e, p=par: e.activation(
            dq_r1[:, p, :], dq_t05[:, p, :], AF.Relu,
            bias=cells[:, 0:1], scale=cells[:, 3:4]),
            waits=sw)
        a_t = b.op("scalar", lambda e, p=par: e.activation(
            dq_r2[:, p, :], dq_t05[:, p, :], AF.Relu,
            bias=cells[:, 2:3], scale=cells[:, 8:9]),
            inc=dqa)
        dqs["a_t"] = a_t
        def stage2(u=u, slot=slot, par=par, a_t=a_t, sc_t=dqs["sc_t"]):
            ci_ = u // 4
            vw = [(dqa, a_t), sc_t]
            if u >= 2:
                vw.append(dqs["qw_done"][u - 2])
            b.op("vector", lambda e, p=par: e.tensor_add(
                dq_u[:, :], dq_t05[:, p, :], dq_r1[:, p, :]), waits=vw)
            b.op("vector", lambda e, p=par: e.tensor_add(
                dq_t2[:, :], dq_u[:, :], dq_r2[:, p, :]))
            b.op("vector", lambda e, p=par: e.tensor_mul(
                dq_u[:, :], dq_t2[:, :], dq_smul[:, p, :]))
            b.op("vector", lambda e, sl=slot, p=par: e.tensor_mul(
                wout[:, p, 0, :], dq_u[:, 0:512], scb[:, sl, :]))
            v_t = b.op("vector", lambda e, sl=slot, p=par: e.tensor_mul(
                wout[:, p, 1, :], dq_u[:, 512:1024], scb[:, sl, :]),
                inc=dqv)
            jp_ = u % 4
            for jh in range(2):
                j = 2 * jp_ + jh
                b.dma("sync", scr_l[ci_, j], wout[:, par, jh, :], dQWp[par],
                      waits=([(dqv, v_t)] if jh == 0 else []))
            dqs["qw_done"][u] = (dQWp[par], dQWp[par].n)
        if dqs["pending2"] is not None:
            pend = dqs["pending2"]
            dqs["pending2"] = stage2
            pend()
        else:
            dqs["pending2"] = stage2
        dqs["u"] = u + 1
        return True

    def emit_dq_flush():
        if dqs["pending2"] is not None:
            dqs["pending2"]()
            dqs["pending2"] = None

    def emit_dq_units(n):
        for _ in range(n):
            if not emit_dq_unit():
                return

    XT_t = XT.rearrange("(t p) n -> t p n", p=128)
    v_cnt = [0] * NT
    s_cnt = [0] * NT
    t_cnt = [0] * NT
    x_cnt = [0] * NT
    for tau in range(NT):
        sl = tau % 2
        wv = []
        if tau >= 2:
            wv = [(vEv, v_cnt[tau - 2]), (sEv, s_cnt[tau - 2])]
        x_cnt[tau] = b.dma("sync", xsA[:, sl, :], XT_t[tau], dXp[sl], waits=wv)
        wv2 = [(dXp[sl], x_cnt[tau])]
        if tau >= 2:
            wv2.append((tMM, t_cnt[tau - 2]))
        v_cnt[tau] = b.op("vector", lambda e, sl=sl: e.tensor_copy(xb[:, sl, :], xsA[:, sl, :]),
                          waits=wv2, inc=vEv)
        s_cnt[tau] = b.op("scalar", lambda e, sl=sl: e.activation(
            xsq[:, sl, :], xsA[:, sl, :], AF.Square), waits=wv2, inc=sEv)
        mw = [(vEv, v_cnt[tau]), (sEv, s_cnt[tau])]
        for c in range(4):
            mm(c, ones_bf[:], xb[:, sl, 512 * c:512 * (c + 1)],
               start=(tau == 0), stop=(tau == NT - 1), out_rows=(0, 1),
               waits=(mw if c == 0 else []))
        for c in range(4):
            t_cnt[tau] = mm(4 + c, ones_bf[:], xsq[:, sl, 512 * c:512 * (c + 1)],
                            start=(tau == 0), stop=(tau == NT - 1), out_rows=(0, 1))
        emit_dq_units(1)
    stats_mm_done = t_cnt[NT - 1]

    sA, sB, sC, sD = (rowbank[0:1, 0:512], rowbank[0:1, 512:1024],
                      rowbank[0:1, 1024:1536], rowbank[0:1, 1536:2048])

    def ln_coeffs(nchunk, sum_banks, sq_banks, eps_cell, dst_a, dst_b, first_wait):
        abc = 0
        prev_bt = 0
        for c in range(nchunk):
            bs, bq = sum_banks[c], sq_banks[c]
            w0 = list(first_wait) if c == 0 else [(tMM, prev_bt)]
            cp1 = b.op("scalar", lambda e, bs=bs: e.copy(sA, pbr(bs, 0, 1)),
                       waits=w0, inc=sEv)
            cp2 = b.op("scalar", lambda e, bq=bq: e.copy(sB, pbr(bq, 0, 1)), inc=sEv)
            b.op("vector", lambda e: e.tensor_scalar(sC, sA, 1.0 / 2048.0, None,
                                                     ALU.mult), waits=[(sEv, cp2)])
            b.op("vector", lambda e: e.tensor_scalar(sD, sB, 1.0 / 2048.0, None,
                                                     ALU.mult))
            b.op("vector", lambda e: e.tensor_mul(sA, sC, sC))
            var_v = b.op("vector", lambda e: e.tensor_sub(sB, sD, sA), inc=vEv)
            sq_v = b.op("scalar", lambda e, ec=eps_cell: e.activation(
                sD, sB, AF.Sqrt, bias=cells[0:1, ec:ec + 1], scale=cells[0:1, 7:8]),
                waits=[(vEv, var_v)], inc=sEv)
            b.op("vector", lambda e: e.reciprocal(sA, sD), waits=[(sEv, sq_v)])
            ab_v = b.op("vector", lambda e: e.scalar_tensor_tensor(
                sB, sC, -1.0, sA, ALU.mult, ALU.mult), inc=vEv)
            mm(bs, ones1_f32[:], sA, True, True,
               waits=[(vEv, ab_v), (sEv, cp2)])
            bt = mm(bq, ones1_f32[:], sB, True, True)
            prev_bt = bt
            a1 = b.op("scalar", lambda e, c=c, bs=bs: e.copy(
                dst_a[:, 512 * c:512 * (c + 1)], pb(bs)), waits=[(tMM, bt)], inc=sEv)
            abc = b.op("scalar", lambda e, c=c, bq=bq: e.copy(
                dst_b[:, 512 * c:512 * (c + 1)], pb(bq)), inc=sEv)
            bank_guard[bs] = [(sEv, abc)]
            bank_guard[bq] = [(sEv, abc)]
        return abc

    abc2 = ln_coeffs(4, [0, 1, 2, 3], [4, 5, 6, 7], 4, aB, bB,
                     [(tMM, stats_mm_done)])

    ln_v = [0] * NT
    for tau in range(NT):
        wv = [(sEv, abc2)] if tau == 0 else [(vEv, ln_v[tau - 1])]
        x_t = b.dma("sync", xsA[:, 0, :], XT_t[tau], dXp[0], waits=wv)
        b.op("vector", lambda e: e.tensor_mul(tmpF[:], xsA[:, 0, :], aB[:]),
             waits=[(dXp[0], x_t)])
        ln_v[tau] = b.op("vector", lambda e, tau=tau: e.tensor_add(
            ln1[:, tau, :], tmpF[:], bB[:]), inc=vEv)
        emit_dq_units(1)
    ln_done = ln_v[NT - 1]

    kw_done = [0] * NP
    vw_done = [0] * NP

    rope_state = {"v": 0}

    def rope_evac(bank, scale_c, tab_sl, dst, stg_guard):
        sc_v = b.op("scalar", lambda e, bk=bank: e.copy(s_rope[:], pb(bk)),
                    waits=list(stg_guard) + [(vEv, rope_state["v"])], inc=sEv)
        sh_t = 0
        for (d0, s0) in ((0, 32), (32, 0), (64, 96), (96, 64)):
            sh_t = b.dma("sync", sh_rope[d0:d0 + 32, :], s_rope[s0:s0 + 32, :], dSh,
                         waits=([(sEv, sc_v), (vEv, rope_state["v"])] if d0 == 0 else []))
        b.op("vector", lambda e, ts=tab_sl, c=scale_c: e.scalar_tensor_tensor(
            ropetmp[:], s_rope[:], c, cosR[:, ts[0]:ts[1]], ALU.mult, ALU.mult),
            waits=[(dSh, sh_t), (dTb, tables_done)])
        b.op("vector", lambda e, ts=tab_sl, c=scale_c: e.scalar_tensor_tensor(
            ropetmp2[:], sh_rope[:], c, sinA[:, ts[0]:ts[1]], ALU.mult, ALU.mult))
        ev = b.op("vector", lambda e: e.tensor_add(estg[:], ropetmp[:], ropetmp2[:]),
                  inc=vEv)
        rope_state["v"] = ev
        w_t = b.dma("sync", dst, estg[:], dKW, waits=[(vEv, ev)])
        return (sEv, sc_v), (dKW, w_t)

    prev_w_used = 0
    wslot_free = [(), ()]
    estg_guard = []
    for which, WT, dsts, ntc, rope_scale in (
        ("q", WQT, q_sc, 2, SC_ROPE),
        ("k", WKT, k_sc, 4, 1.0),
    ):
        WT_r = WT.rearrange("(t p) d -> p t d", p=128)
        for dc in range(24):
            hp, ct = dc // 3, dc % 3
            sl = dc % 2
            wv0 = list(wslot_free[sl])
            if which == "q" and dc < 2:
                wv0.append((vEv, ln_done))
            w_t = b.dma("gpsimd", wbufs[:, sl], WT_r[:, :, 128 * dc:128 * (dc + 1)],
                        dWp[sl], waits=wv0)
            banks = [2 * (dc % 4), 2 * (dc % 4) + 1] if ntc == 2 else \
                    ([0, 1, 2, 3] if dc % 2 == 0 else [4, 5, 6, 7])
            last_ldw_t = None
            for it in range(NT):
                lw = [(dWp[sl], w_t)] if it == 0 else []
                if it == 0:
                    lw.append((vEv, ln_done))
                ldw(wbufs[:, sl, it, :], waits=lw)
                for tc in range(ntc):
                    mm(banks[tc], wbufs[:, sl, it, :],
                       ln1[:, it, 512 * tc:512 * (tc + 1)],
                       start=(it == 0), stop=(it == NT - 1))
            mm_done = tMM.n
            wslot_free[sl] = [(tMM, mm_done)]
            for tc in range(ntc):
                bk = banks[tc]
                if ct < 2:
                    if which == "q":
                        ev = b.op("scalar", lambda e, bk=bk: e.mul(
                            estg[:], pb(bk), cells[:, 9:10]),
                            waits=[(tMM, mm_done)] + estg_guard, inc=sEv)
                        gsem = (sEv, ev)
                    else:
                        ev = b.op("scalar", lambda e, bk=bk: e.copy(estg[:], pb(bk)),
                                  waits=[(tMM, mm_done)] + estg_guard, inc=sEv)
                        gsem = (sEv, ev)
                    w2 = b.dma("sync", dsts[hp, ct, :, 512 * tc:512 * (tc + 1)],
                               estg[:], dKW, waits=[gsem])
                    bank_guard[bk] = [gsem]
                    estg_guard = [(dKW, w2)]
                else:
                    pg, wg = rope_evac(bk, rope_scale, (512 * tc, 512 * (tc + 1)),
                                       dsts[hp, 2, :, 512 * tc:512 * (tc + 1)],
                                       [(tMM, mm_done)] + estg_guard)
                    bank_guard[bk] = [pg]
                    estg_guard = [wg]
            if which == "k" and ct == 2:
                kw_done[hp] = dKW.n
            emit_dq_units(1)

    WVT_r = WVT.rearrange("(t p) d -> p t d", p=128)
    wv_free = []
    for dvc in range(4):
        w_t = b.dma("gpsimd", wvb[:], WVT_r[:, :, 512 * dvc:512 * (dvc + 1)], dWV,
                    waits=wv_free)
        for kc in range(NKT):
            bk = 4 + kc % 4
            for it in range(NT):
                lw = [(dWV, w_t), (vEv, ln_done)] if it == 0 else []
                ldw(ln1[:, it, 128 * kc:128 * (kc + 1)], waits=lw)
                mm(bk, ln1[:, it, 128 * kc:128 * (kc + 1)], wvb[:, it, :],
                   start=(it == 0), stop=(it == NT - 1))
            mm_done = tMM.n
            ev = b.op("scalar", lambda e, bk=bk: e.copy(estg[:], pb(bk)),
                      waits=[(tMM, mm_done)] + estg_guard, inc=sEv)
            bank_guard[bk] = [(sEv, ev)]
            w2 = 0
            for s in range(2):
                w2 = b.dma("sync", v_sc[2 * dvc + s, kc, :, :],
                           estg[:, 256 * s:256 * (s + 1)], dKW,
                           waits=([(sEv, ev)] if s == 0 else []))
            estg_guard = [(dKW, w2)]
            if kc % 4 == 3:
                emit_dq_units(1)
        wv_free = [(tMM, tMM.n)]
    for hp in range(NP):
        vw_done[hp] = dKW.n

    b.barrier()

    dkw_final = dKW.n
    pair_cons = [(), ()]
    p_free2 = [[], []]
    o_evac_v = 0
    sq_guard_outer = [0]
    eb_guard = [0]
    rb_guard = [0]
    bound_g = [0]
    for hp in range(NP):
        sl = hp % 2
        kr1 = b.dma("sync", kb[:, sl], k_sc[hp].rearrange("c p n -> p c n"), dKRp[sl],
                    waits=list(pair_cons[sl]) + [(dKW, dkw_final)])
        kr2 = b.dma("sync", qb[:, sl], q_sc[hp].rearrange("c p n -> p c n"), dKRp[sl])
        kr3 = b.dma("sync", vb[:, sl], v_sc[hp].rearrange("t p n -> p t n"), dKRp[sl])
        dKR = dKRp[sl]
        kqv_t = dKRp[sl].n
        sq_guard = sq_guard_outer[0]
        for hh in range(2):
            h = 2 * hp + hh
            o64 = ones64A if hh == 0 else ones64B
            for (which_k, kc_hh, dstcell) in ((0, hh, 20), (2, 2, 21)):
                ck = kc_hh if which_k == 0 else 2
                sq_s = b.op("scalar", lambda e, sl=sl, ck=ck: e.activation(
                    sqtile[:], kb[:, sl, ck, :], AF.Square),
                    waits=[(dKR, kqv_t), (tMM, sq_guard), (vEv, bound_g[0])],
                    inc=sEv)
                oo = ones_bf if which_k == 0 else o64
                for c in range(4):
                    m_t = mm(5, oo[:], sqtile[:, 512 * c:512 * (c + 1)], True, True,
                             out_rows=(0, 1), waits=[(sEv, sq_s)] if c == 0 else [])
                    c_t = b.op("scalar", lambda e, c=c: e.copy(
                        rowbank[0:1, 512 * c:512 * (c + 1)], pbr(5, 0, 1)),
                        waits=[(tMM, m_t), (vEv, bound_g[0]), (tMM, rb_guard[0])],
                        inc=sEv)
                    bank_guard[5] = [(sEv, c_t)]
                sq_guard = m_t
                bound_g[0] = b.op("vector", lambda e, dc=dstcell: e.reduce_max(
                    cells[0:1, dc:dc + 1], rowbank[0:1, 0:2048],
                    mybir.AxisListType.X), waits=[(sEv, c_t)], inc=vEv)
            kx_s = b.op("scalar", lambda e: e.activation(
                cells[0:1, 22:24], cells[0:1, 20:22], AF.Sqrt),
                waits=[(vEv, bound_g[0])], inc=sEv)
            sqn_s = b.op("scalar", lambda e, sl=sl, hh=hh: e.activation(
                sqtile[:, 0:1024], qb[:, sl, hh, :], AF.Square),
                waits=[(tMM, sq_guard)], inc=sEv)
            sqr_s = b.op("scalar", lambda e, sl=sl: e.activation(
                sqtile[:, 1024:2048], qb[:, sl, 2, :], AF.Square), inc=sEv)
            for c in range(2):
                mseg = rowbank[0:1, 512 * c:512 * (c + 1)]
                m1 = mm(5, ones_bf[:], sqtile[:, 512 * c:512 * (c + 1)], True, True,
                        out_rows=(0, 1), waits=[(sEv, sqn_s)] if c == 0 else [])
                c1 = b.op("scalar", lambda e: e.copy(sC, pbr(5, 0, 1)),
                          waits=[(tMM, m1), (vEv, bound_g[0])], inc=sEv)
                bank_guard[5] = [(sEv, c1)]
                m2 = mm(5, o64[:], sqtile[:, 1024 + 512 * c:1536 + 512 * c], True, True,
                        out_rows=(0, 1), waits=[(sEv, sqr_s)])
                c2 = b.op("scalar", lambda e: e.copy(sD, pbr(5, 0, 1)),
                          waits=[(tMM, m2), (tMM, rb_guard[0])], inc=sEv)
                bank_guard[5] = [(sEv, c2)]
                sq_guard = m2
                b.op("scalar", lambda e, ms=mseg: e.activation(ms, sC, AF.Sqrt))
                qs_s = b.op("scalar", lambda e: e.activation(sC, sD, AF.Sqrt),
                            inc=sEv)
                b.op("vector", lambda e, ms=mseg: e.tensor_scalar(
                    sD, ms, cells[0:1, 22:23], None, ALU.mult),
                    waits=[(sEv, qs_s), (sEv, kx_s)])
                bound_g[0] = b.op("vector", lambda e, ms=mseg: e.scalar_tensor_tensor(
                    ms, sC, cells[0:1, 23:24], sD, ALU.mult, ALU.add), inc=vEv)
            mx_v = b.op("vector", lambda e: e.reduce_max(
                cells[0:1, 10:11], rowbank[0:1, 0:1024], mybir.AxisListType.X),
                waits=[(tMM, eb_guard[0])], inc=vEv)
            bound_g[0] = mx_v
            m_t = mm(7, negones1_f32[:], cells[0:1, 10:11], True, True,
                     waits=[(vEv, mx_v)])
            eb_guard[0] = m_t
            eb_s = b.op("scalar", lambda e, h=h: e.copy(
                expbias[:, h:h + 1], pbr(7, 0, 128, 0, 1)),
                waits=[(tMM, m_t)], inc=sEv)
            bank_guard[7] = [(sEv, eb_s)]
            if debug is True:
                b.dma("sync", dbgM[h], cells[0:1, :], dKW,
                      waits=[(sEv, eb_s), (vEv, bound_g[0])])
                if hp == 0 and hh == 1:
                    b.dma("sync", dbgE[:, 16:32], expbias[:], dKW,
                          waits=[(sEv, eb_s)])
                ps7 = b.op("scalar", lambda e, h=h: e.copy(
                    dbgE_sb[:, h:h + 1], pbr(7, 0, 128, 0, 1)), inc=sEv)
                if hp == 3 and hh == 1:
                    b.dma("sync", dbgE[:, 32:48], expbias[:], dKW,
                          waits=[(sEv, ps7)])
            sq_guard_outer[0] = sq_guard
            emit_dq_units(1)
        for hh in range(2):
            h = 2 * hp + hh
            r0, r1_ = (0, 64) if hh == 0 else (64, 128)
            for qc in range(2):
                ppar = (2 * hh + qc) % 2
                sb_rot = [0, 1]
                exp_v = [0] * NKT
                for kt in range(NKT):
                    bk = sb_rot[kt % 2]
                    mm(bk, kb[:, sl, hh, 128 * kt:128 * (kt + 1)],
                       qb[:, sl, hh, 512 * qc:512 * (qc + 1)],
                       start=True, stop=False)
                    sc_t = mm(bk, kb[r0:r1_, sl, 2, 128 * kt:128 * (kt + 1)],
                              qb[r0:r1_, sl, 2, 512 * qc:512 * (qc + 1)],
                              start=False, stop=True)
                    e_t = b.op("scalar", lambda e, bk=bk, kt=kt, h=h, pp=ppar:
                               e.activation(Pbuf[:, pp, kt, :], pb(bk), AF.Exp,
                                            bias=expbias[:, h:h + 1]),
                        waits=[(tMM, sc_t), (sEv, eb_s)] +
                              (p_free2[ppar] if kt == 0 else []), inc=sEv)
                    exp_v[kt] = e_t
                    bank_guard[bk] = [(sEv, e_t)]
                for kt in range(NKT):
                    s_t = mm(2, ones_bf[:], Pbuf[:, ppar, kt, :], start=(kt == 0),
                             stop=(kt == NKT - 1), out_rows=(0, 1),
                             waits=[(sEv, exp_v[kt])])
                rc_v = b.op("vector", lambda e: e.reciprocal(sD, pbr(2, 0, 1)),
                            waits=[(tMM, s_t), (sEv, exp_v[NKT - 1]),
                                   (tMM, rb_guard[0])], inc=vEv)
                bank_guard[2] = [(vEv, rc_v)]
                m_t = mm(7, ones1_f32[:], sD, True, True, waits=[(vEv, rc_v)])
                rb_guard[0] = m_t
                rb_s = b.op("scalar", lambda e: e.copy(rBuf[:], pb(7)),
                            waits=[(tMM, m_t), (vEv, o_evac_v)], inc=sEv)
                bank_guard[7] = [(sEv, rb_s)]
                if debug:
                    b.dma("sync", dbgR[2 * h + qc], rBuf[:], dKW,
                          waits=[(sEv, rb_s)])
                ob = 3 + (qc % 2)
                for kt in range(NKT):
                    av_t = mm(ob, vb[:, sl, kt, 128 * hh:128 * (hh + 1)],
                              Pbuf[:, ppar, kt, :], start=(kt == 0),
                              stop=(kt == NKT - 1),
                              waits=[(sEv, exp_v[NKT - 1]), (dKR, kqv_t)]
                              if kt == 0 else [])
                o_evac_v = b.op("vector", lambda e, ob=ob, h=h, qc=qc: e.tensor_mul(
                    Obuf[:, h, 512 * qc:512 * (qc + 1)], pb(ob), rBuf[:]),
                    waits=[(tMM, av_t), (sEv, rb_s)], inc=vEv)
                bank_guard[ob] = [(vEv, o_evac_v)]
                p_free2[ppar] = [(tMM, av_t)]
                emit_dq_units(1)
        pair_cons[sl] = [(tMM, tMM.n)]
    while emit_dq_unit():
        pass
    emit_dq_flush()
    ccS = b.sem("ccS")
    for t_ in range(3):
        b.op("gpsimd", lambda e, t_=t_: e.collective_compute(
            "AllGather", ALU.bypass, replica_groups=[list(range(8))],
            ins=[scr_l[3 * t_:3 * t_ + 3]], outs=[scr_f[24 * t_:24 * t_ + 24]]),
            waits=[(dQWp[0], dQWp[0].n), (dQWp[1], dQWp[1].n)],
            inc=ccS)
    cc_done = ccS.n
    gu_w_final = [(ccS, cc_done)]
    if debug is True:
        for blk8 in range(4):
            b.dma("sync", dbgS[18 * blk8:18 * (blk8 + 1)],
                  scr_f[18 * blk8:18 * (blk8 + 1)], dKW,
                  waits=([(ccS, cc_done)] if blk8 == 0 else []))
    if debug is True:
        b.dma("sync", dbgO, Obuf[:], dKW, waits=[(vEv, o_evac_v)])
        b.dma("sync", dbgE[:, 0:16], expbias[:], dKW)
        b.dma("sync", dbgE[:, 48:64], dbgE_sb[:, 0:16], dKW)

    b.barrier()

    WOT_r = WOT.rearrange("(t p) d -> p t d", p=128)
    wslot_free = [(), ()]
    xslot_guard = [(dqv, dqv.n)]
    st_t = 0
    x2w = [0] * NT
    for ht in range(NT):
        sl = ht % 2
        w_t = b.dma("gpsimd", wbufs[:, sl], WOT_r[:, :, 128 * ht:128 * (ht + 1)],
                    dWp[sl], waits=wslot_free[sl])
        x_t = b.dma("sync", fbufA[:, 0, :], XT_t[ht][:, 0:TOK], dXp[0],
                    waits=xslot_guard)
        banks = [2 * (ht % 2), 2 * (ht % 2) + 1]
        for it in range(NT):
            lw = [(dWp[sl], w_t)] if it == 0 else []
            ldw(wbufs[:, sl, it, :], waits=lw)
            for tc in range(2):
                mm(banks[tc], wbufs[:, sl, it, :], Obuf[:, it, 512 * tc:512 * (tc + 1)],
                   start=(it == 0), stop=(it == NT - 1))
        mm_done = tMM.n
        wslot_free[sl] = [(tMM, mm_done)]
        for tc in range(2):
            wv = [(tMM, mm_done), (dXp[0], x_t)]
            if tc == 0 and ht > 0:
                wv.append((dX2W, x2w[ht - 1]))
            b.op("vector", lambda e, bk=banks[tc], tc=tc: e.tensor_add(
                fbufA[:, 1, 512 * tc:512 * (tc + 1)], pb(bk),
                fbufA[:, 0, 512 * tc:512 * (tc + 1)]),
                waits=wv)
        x2_v = b.op("vector", lambda e: e.tensor_copy(xb2[:], fbufA[:, 1, :]), inc=vEv)
        bank_guard[banks[0]] = [(vEv, x2_v)]
        bank_guard[banks[1]] = [(vEv, x2_v)]
        x2_s = b.op("scalar", lambda e: e.activation(xsq2[:], fbufA[:, 1, :], AF.Square),
                    waits=[(vEv, x2_v)], inc=sEv)
        for c in range(2):
            mm(4 + c, ones_bf[:], xb2[:, 512 * c:512 * (c + 1)],
               start=(ht == 0), stop=(ht == NT - 1), out_rows=(0, 1),
               waits=[(vEv, x2_v)] if c == 0 else [])
        for c in range(2):
            st_t = mm(6 + c, ones_bf[:], xsq2[:, 512 * c:512 * (c + 1)],
                      start=(ht == 0), stop=(ht == NT - 1), out_rows=(0, 1),
                      waits=[(sEv, x2_s)] if c == 0 else [])
        x2w[ht] = b.dma("sync", x2_sc[ht], fbufA[:, 1, :], dX2W, waits=[(vEv, x2_v)])
        xslot_guard = [(vEv, x2_v), (sEv, x2_s), (tMM, st_t)]

    ab2_s = ln_coeffs(2, [4, 5], [6, 7], 4, aB2, bB2, [(tMM, st_t)])
    ln2_v = 0
    for ht in range(NT):
        x_t = b.dma("sync", fbufA[:, 0, :], x2_sc[ht], dX2R,
                    waits=([(sEv, ab2_s), (dX2W, dX2W.n)] if ht == 0
                           else [(vEv, ln2_v)]))
        b.op("vector", lambda e: e.tensor_mul(fbufA[:, 1, :], fbufA[:, 0, :], aB2[:]),
             waits=[(dX2R, x_t)])
        ln2_v = b.op("vector", lambda e, ht=ht: e.tensor_add(
            ln2[:, ht, :], fbufA[:, 1, :], bB2[:]), inc=vEv)

    if debug is True:
        b.dma("sync", dbgL2, ln2[:], dKW, waits=[(vEv, ln2_v)])
    b.barrier()

    pc_free = [(), ()]
    htmp_guard = [0]
    hevac_g = [0]
    for cb in range(6):
        W = 512 if cb == 5 else 1024
        ncol = W // 8
        for qd in range(2):
            for half, base in (("g", 1024 * cb), ("u", NI + 1024 * cb)):
                piece_t = [0] * NT

                def d_piece(it, cb=cb, qd=qd, half=half, base=base, W=W):
                    psl = it % 2
                    wv = list(pc_free[psl])
                    if cb == 0 and qd == 0 and half == "g" and it < 2:
                        wv += gu_w_final
                    nch = W // 512
                    ci0 = 22 * (it // 8) + base // 512
                    srcap = scr_f[ci0:ci0 + nch, it % 8].rearrange("a p c -> p a c")
                    dstap = wpiece[:, psl, 0:W].rearrange("p (a c) -> p a c", c=512)
                    piece_t[it] = b.dma("scalar", dstap, srcap, dFRp[psl], waits=wv)
                d_piece(0)
                d_piece(1)
                for it in range(NT):
                    psl = it % 2
                    for jj in range(4):
                        jq = 4 * qd + jj
                        lw = [(dFRp[psl], piece_t[it])] if jj == 0 else []
                        if cb == 0 and qd == 0 and half == "g" and it == 0 and jj == 0:
                            lw.append((vEv, ln2_v))
                        ldw(wpiece[:, psl, jq:W:8], waits=lw)
                        for tc in range(2):
                            mm(2 * jj + tc, wpiece[:, psl, jq:W:8],
                               ln2[:, it, 512 * tc:512 * (tc + 1)],
                               start=(it == 0), stop=(it == NT - 1),
                               out_rows=(0, ncol))
                    pc_free[psl] = [(tMM, tMM.n)]
                    if it + 2 < NT:
                        d_piece(it + 2)
                mm_done = tMM.n
                if half == "g":
                    sg_s = 0
                    for jj in range(4):
                        for tc in range(2):
                            sg_s = b.op("scalar", lambda e, bk=2 * jj + tc, jj=jj,
                                        tc=tc, nc_=ncol: e.activation(
                                            sgbuf[0:nc_, 2 * jj + tc, :],
                                            pbr(2 * jj + tc, 0, nc_), AF.Silu),
                                        waits=([(tMM, mm_done), (vEv, hevac_g[0])]
                                               if (jj == 0 and tc == 0) else []),
                                        inc=sEv)
                            bank_guard[2 * jj + tc] = [(sEv, sg_s)]
                else:
                    for jj in range(4):
                        jq = 4 * qd + jj
                        for tc in range(2):
                            bk = 2 * jj + tc
                            if cb < 5:
                                h_v = b.op("vector", lambda e, bk=bk, cb=cb, jq=jq,
                                           tc=tc: e.tensor_mul(
                                               Hbuf[:, 8 * cb + jq, 512 * tc:512 * (tc + 1)],
                                               pbr(bk, 0, 128), sgbuf[:, bk, :]),
                                           waits=[(tMM, mm_done), (sEv, sg_s)]
                                           if (jj == 0 and tc == 0) else [], inc=vEv)
                                bank_guard[bk] = [(vEv, h_v)]
                                hevac_g[0] = h_v
                            else:
                                itH = 40 + jq // 2
                                if jq % 2 == 0:
                                    h_v = b.op("vector", lambda e, bk=bk, itH=itH, tc=tc:
                                               e.tensor_mul(
                                                   Hbuf[0:64, itH, 512 * tc:512 * (tc + 1)],
                                                   pbr(bk, 0, 64), sgbuf[0:64, bk, :]),
                                               waits=[(tMM, mm_done), (sEv, sg_s)]
                                               if (jj == 0 and tc == 0) else [], inc=vEv)
                                    bank_guard[bk] = [(vEv, h_v)]
                                    hevac_g[0] = h_v
                                else:
                                    hw = ([(tMM, mm_done), (sEv, sg_s)]
                                          if (jj == 0 and tc == 0) else [])
                                    hw.append((dSh, htmp_guard[0]))
                                    h_v = b.op("vector", lambda e, bk=bk: e.tensor_mul(
                                        htmp[0:64, :], pbr(bk, 0, 64), sgbuf[0:64, bk, :]),
                                        waits=hw, inc=vEv)
                                    sh_t = b.dma("sync",
                                                 Hbuf[64:128, itH, 512 * tc:512 * (tc + 1)],
                                                 htmp[0:64, :], dSh, waits=[(vEv, h_v)])
                                    htmp_guard[0] = sh_t
                                    bank_guard[bk] = [(vEv, h_v)]
                                    hevac_g[0] = h_v
    dq_w_final = [(ccS, cc_done)]
    h_shift_done = dSh.n
    h_done_v = vEv.n
    if debug in (True, 3):
        b.dma("sync", dbgH, Hbuf[:], dKW, waits=[(vEv, h_done_v), (dSh, h_shift_done)])

    y_guard = []
    for tc in range(2):
        y_done_s = 0
        for hb in range(2):
            piece_t = [0] * NIT

            def e_piece(it, tc=tc, hb=hb):
                psl = it % 2
                wv = list(pc_free[psl])
                if tc == 0 and hb == 0 and it < 2:
                    wv += dq_w_final
                dstap = wpiece[:, psl, :].rearrange("p (a c) -> p a c", c=512)
                if it < 40:
                    ci0 = 44 + 4 * (it // 8) + 2 * hb
                    srcap = scr_f[ci0:ci0 + 2, it % 8].rearrange("a p c -> p a c")
                    piece_t[it] = b.dma("scalar", dstap, srcap, dFRp[psl], waits=wv)
                else:
                    m = it - 40
                    ci0 = 64 + 2 * hb
                    b.dma("scalar", wpiece[0:64, psl, :].rearrange(
                        "p (a c) -> p a c", c=512),
                        scr_f[ci0:ci0 + 2, 2 * m, 0:64, :].rearrange("a p c -> p a c"),
                        dFRp[psl], waits=wv)
                    piece_t[it] = b.dma("scalar", wpiece[64:128, psl, :].rearrange(
                        "p (a c) -> p a c", c=512),
                        scr_f[ci0:ci0 + 2, 2 * m + 1, 0:64, :].rearrange("a p c -> p a c"),
                        dFRp[psl])
            e_piece(0)
            e_piece(1)
            for it in range(NIT):
                psl = it % 2
                for jh in range(8):
                    lw = [(dFRp[psl], piece_t[it])] if jh == 0 else []
                    if tc == 0 and hb == 0 and it == 0 and jh == 0:
                        lw += [(vEv, h_done_v), (dSh, h_shift_done)]
                    ldw(wpiece[:, psl, jh:1024:8], waits=lw)
                    mm(jh, wpiece[:, psl, jh:1024:8],
                       Hbuf[:, it, 512 * tc:512 * (tc + 1)],
                       start=(it == 0), stop=(it == NIT - 1))
                pc_free[psl] = [(tMM, tMM.n)]
                if it + 2 < NIT:
                    e_piece(it + 2)
            mm_done = tMM.n
            for jh in range(8):
                y_done_s = b.op("scalar", lambda e, jh=jh, hb=hb: e.copy(
                    ybuf[:, 8 * hb + jh, :], pb(jh)),
                    waits=([(tMM, mm_done)] + list(y_guard)) if jh == 0 else [],
                    inc=sEv)
                bank_guard[jh] = [(sEv, y_done_s)]
        if debug in (True, 3):
            b.dma("sync", dbgY[tc], ybuf[:], dKW, waits=[(sEv, y_done_s)])
        ys_t = 0
        for yt in range(NT):
            ys_s = b.op("scalar", lambda e, yt=yt: e.activation(
                ysqb[:], ybuf[:, yt, :], AF.Square),
                waits=[(sEv, y_done_s)] if yt == 0 else [(tMM, ys_t)], inc=sEv)
            ys_t = mm(0, ones_bf[:], ysqb[:], start=(yt == 0), stop=(yt == NT - 1),
                      out_rows=(0, 1), waits=[(sEv, ys_s)])
        rr_s = b.op("scalar", lambda e: e.activation(
            sA, pbr(0, 0, 1), AF.Sqrt,
            bias=cells[0:1, 5:6], scale=cells[0:1, 6:7]),
            waits=[(tMM, ys_t), (tMM, rb_guard[0])], inc=sEv)
        bank_guard[0] = [(sEv, rr_s)]
        rr_v = b.op("vector", lambda e: e.reciprocal(sB, sA),
                    waits=[(sEv, rr_s)], inc=vEv)
        m_t = mm(1, ones1_f32[:], sB, True, True, waits=[(vEv, rr_v)])
        rb_guard[0] = m_t
        rrb_s = b.op("scalar", lambda e: e.copy(rrB[:], pb(1)),
                     waits=[(tMM, m_t)], inc=sEv)
        bank_guard[1] = [(sEv, rrb_s)]
        for yt in range(NT):
            wfb = [(dOse, dOse.n)]
            x_t = b.dma("sync", fbufA[:, 0, 0:512],
                        x2_sc[yt, :, 512 * tc:512 * (tc + 1)], dX2R,
                        waits=wfb)
            b.op("vector", lambda e, yt=yt: e.tensor_mul(
                fbufA[:, 1, 0:512], ybuf[:, yt, :], rrB[:]),
                waits=[(sEv, rrb_s), (dX2R, x_t)])
            o_v = b.op("vector", lambda e: e.tensor_add(
                fbufA[:, 1, 512:1024], fbufA[:, 1, 0:512], fbufA[:, 0, 0:512]),
                inc=vEv)
            y_guard = [(vEv, o_v)]
            b.dma("sync", OUT[yt, :, 512 * tc:512 * (tc + 1)], fbufA[:, 1, 512:1024],
                  dOse, waits=[(vEv, o_v)])
    final_out = dOse.n
    b.plan["sync"].append(lambda e, h=dOse.h, v=final_out: e.wait_ge(h, v))

    with nc.Block() as block:
        for eng_name, deco in (("sync", block.sync), ("tensor", block.tensor),
                               ("vector", block.vector), ("scalar", block.scalar),
                               ("gpsimd", block.gpsimd)):
            def run(e, items=b.plan[eng_name]):
                for f in items:
                    f(e)
            deco(run)
    b.es.close()
    return nc


_NC_CACHE = {}


def _get_nc(debug=False):
    key = debug
    if key not in _NC_CACHE:
        _NC_CACHE[key] = build_program(debug=key)
    return _NC_CACHE[key]


def _stage(L0_attn_hidden_states, L0_attn_Wq, L0_attn_Wk, L0_attn_Wv, L0_attn_Wo,
           L0_ffn_gate_up_packed, L0_ffn_gate_up_scales, L0_ffn_down_packed,
           L0_ffn_down_scales):
    x = np.asarray(L0_attn_hidden_states)
    Wq = np.asarray(L0_attn_Wq)
    Wk = np.asarray(L0_attn_Wk)
    Wv = np.asarray(L0_attn_Wv)
    Wo = np.asarray(L0_attn_Wo)
    colord = []
    for hp in range(NP):
        colord += list(range(128 * 2 * hp, 128 * 2 * hp + 256))
        colord += list(range(2048 + 64 * 2 * hp, 2048 + 64 * 2 * hp + 128))
    colord = np.array(colord)
    WQT = np.ascontiguousarray(Wq[colord][:, PERM].T.astype(np.float32))
    WKT = np.ascontiguousarray(Wk[colord][:, PERM].T.astype(np.float32))
    WVT = np.ascontiguousarray(Wv[:, PERM].T.astype(np.float32))
    WOT = np.ascontiguousarray(Wo[PERM, :].T.astype(np.float32))
    PKGa = np.asarray(L0_ffn_gate_up_packed).T
    PKDa = np.asarray(L0_ffn_down_packed).T
    SCGa = np.repeat(np.asarray(L0_ffn_gate_up_scales), 16, axis=0)
    SCDa = np.repeat(np.asarray(L0_ffn_down_scales), 16, axis=0)
    chunks = [("g", cb, oc) for cb in range(2) for oc in range(22)] + \
             [("d", cb, oc) for cb in range(6) for oc in range(4)]
    PKS_all = np.zeros((72, 128, 512), dtype=np.int32)
    SCS_all = np.zeros((72, 128, 512), dtype=np.float16)
    for g, (kind, cb, oc) in enumerate(chunks):
        if kind == "g":
            PKS_all[g] = PKGa[128 * cb:128 * cb + 128, 512 * oc:512 * (oc + 1)]
            SCS_all[g] = SCGa[128 * cb:128 * cb + 128, 512 * oc:512 * (oc + 1)]
        else:
            rows = 64 if cb == 5 else 128
            PKS_all[g, 0:rows] = PKDa[128 * cb:128 * cb + rows, 512 * oc:512 * (oc + 1)]
            SCS_all[g, 0:rows] = SCDa[128 * cb:128 * cb + rows, 512 * oc:512 * (oc + 1)]

    inv = 1.0 / (10000.0 ** (np.arange(0, DR, 2, dtype=np.float64) / DR))
    t = np.arange(S, dtype=np.float64)
    fr = np.outer(t, inv)
    cosT = np.cos(fr).astype(np.float32)
    sinT = np.sin(fr).astype(np.float32)

    in_maps = []
    for k in range(8):
        bb_, hh_ = k // 2, k % 2
        t0 = TOK * hh_
        tokord = np.concatenate([np.arange(t0, t0 + TOK),
                                 np.arange((1 - hh_) * TOK, (1 - hh_) * TOK + TOK)])
        XTk = np.ascontiguousarray(x[bb_][tokord][:, PERM].T.astype(np.float32))
        ck = cosT[tokord].T
        sk = sinT[tokord].T
        COSRk = np.ascontiguousarray(np.tile(ck, (4, 1)).astype(np.float32))
        SINAk = np.ascontiguousarray(
            np.concatenate([-sk, sk, -sk, sk], axis=0).astype(np.float32))
        gidx = [24 * (l // 3) + 3 * k + (l % 3) for l in range(9)]
        in_maps.append({
            "XT": XTk, "WQT": WQT, "WKT": WKT, "WVT": WVT, "WOT": WOT,
            "PKS": np.ascontiguousarray(PKS_all[gidx]),
            "SCS": np.ascontiguousarray(SCS_all[gidx]),
            "COSR": COSRk, "SINA": SINAk,
        })
    return in_maps


def kernel(L0_attn_hidden_states, L0_attn_Wq, L0_attn_Wk, L0_attn_Wv, L0_attn_Wo,
           L0_ffn_gate_up_packed, L0_ffn_gate_up_scales, L0_ffn_down_packed,
           L0_ffn_down_scales):
    in_maps = _stage(L0_attn_hidden_states, L0_attn_Wq, L0_attn_Wk, L0_attn_Wv,
                     L0_attn_Wo, L0_ffn_gate_up_packed, L0_ffn_gate_up_scales,
                     L0_ffn_down_packed, L0_ffn_down_scales)
    nc = _get_nc(debug=_DEBUG)
    res = run_bass_kernel_spmd(nc, in_maps, core_ids=list(range(8)))
    out = np.empty((B, S, D), dtype=np.float32)
    for k in range(8):
        bb_, hh_ = k // 2, k % 2
        o = np.asarray(res.results[k]["OUT"]).reshape(D, TOK)
        arr = np.empty((TOK, D), dtype=np.float32)
        arr[:, PERM] = o.T
        out[bb_, TOK * hh_:TOK * (hh_ + 1), :] = arr
    if _DEBUG:
        kernel._last = res
    return out


# version 09
# speedup vs baseline: 1.4383x; 1.0826x over previous
import contextlib
import numpy as np
import ml_dtypes

import concourse.bass as bass
import concourse.mybir as mybir
from concourse.bass_utils import run_bass_kernel_spmd

F32 = mybir.dt.float32
BF16 = mybir.dt.bfloat16
I32 = mybir.dt.int32
F16 = mybir.dt.float16
ALU = mybir.AluOpType
AF = mybir.ActivationFunctionType
BF = ml_dtypes.bfloat16

B, S, D = 4, 2048, 2048
H, DN, DR, DV = 16, 128, 64, 128
NI = 5632
TOK = 1024
NT = 16
NKT = 16
NP = 8
GU = 2 * NI
NIT = 44
LN_EPS = 1e-5
RMS_EPS = 1e-6
SC_NOPE = float(1.0 / np.sqrt(128.0))
SC_ROPE = 0.125

_r = np.arange(D)
PERM = (1024 * (_r // 1024) + 8 * (_r % 128) + (_r // 128) % 8).astype(np.int64)

ENGINES = ("sync", "tensor", "vector", "scalar", "gpsimd")
_DEBUG = False


class Sem:
    __slots__ = ("h", "n", "name")

    def __init__(self, h, name):
        self.h, self.n, self.name = h, 0, name


class Bld:
    def __init__(self):
        self.nc = bass.Bass("TRN2", target_bir_lowering=False)
        self.plan = {e: [] for e in ENGINES}
        self.es = contextlib.ExitStack()
        self.sems = {}

    def sem(self, name):
        if name not in self.sems:
            self.sems[name] = Sem(self.es.enter_context(self.nc.semaphore(name)), name)
        return self.sems[name]

    def op(self, eng, make, waits=(), inc=None, incn=1):
        wl = [(s.h, int(v)) for (s, v) in waits if v > 0]
        ih = None
        if inc is not None:
            inc.n += incn
            ih = inc.h

        def fn(e, make=make, wl=wl, ih=ih, iv=incn):
            for (h, v) in wl:
                e.wait_ge(h, v)
            ins = make(e)
            if ih is not None:
                ins.then_inc(ih, iv)
        self.plan[eng].append(fn)
        return inc.n if inc is not None else None

    def dma(self, eng, out, in_, sem, waits=()):
        return self.op(eng, (lambda e, o=out, i=in_: e.dma_start(out=o, in_=i)),
                       waits=waits, inc=sem, incn=16)

    def sb(self, name, shape, dtype, offset):
        base = (self.nc.sbuf_base + 31) & ~31
        return self.nc.alloc_sbuf_tensor_at(name, list(shape), dtype,
                                            offset=base + offset)

    def barrier(self):
        ph = self.sem("ph")
        ph.n += 5
        t = ph.n
        for e in ENGINES:
            def fn(eng, h=ph.h, t=t):
                eng.nop().then_inc(h, 1)
                eng.wait_ge(h, t)
            self.plan[e].append(fn)


def build_program(debug=False):
    b = Bld()
    nc = b.nc
    okind = "ExternalOutput" if debug is True else "Internal"

    XT = nc.dram_tensor("XT", [D, S], F32, kind="ExternalInput").ap()
    WQT = nc.dram_tensor("WQT", [D, 3072], F32, kind="ExternalInput").ap()
    WKT = nc.dram_tensor("WKT", [D, 3072], F32, kind="ExternalInput").ap()
    WVT = nc.dram_tensor("WVT", [D, 2048], F32, kind="ExternalInput").ap()
    WOT = nc.dram_tensor("WOT", [D, 2048], F32, kind="ExternalInput").ap()
    PKS = nc.dram_tensor("PKS", [9, 128, 512], I32, kind="ExternalInput").ap()
    SCS = nc.dram_tensor("SCS", [9, 128, 512], F16, kind="ExternalInput").ap()
    COSR = nc.dram_tensor("COSR", [128, S], F32, kind="ExternalInput").ap()
    SINA = nc.dram_tensor("SINA", [128, S], F32, kind="ExternalInput").ap()
    OUT = nc.dram_tensor("OUT", [NT, 128, TOK], F32, kind="ExternalOutput").ap()

    dbgO = nc.dram_tensor("dbgO", [128, NT, 1024], BF16, kind=okind).ap()
    dbgE = nc.dram_tensor("dbgE", [128, 64], F32, kind=okind).ap()
    dbgM = nc.dram_tensor("dbgM", [H, 1, 32], F32, kind=okind).ap()
    dbgS = nc.dram_tensor("dbgS", [72, 8, 128, 512], BF16, kind=okind).ap()
    dbgL2 = nc.dram_tensor("dbgL2", [128, NT, 1024], BF16, kind=okind).ap()
    dbgH = nc.dram_tensor("dbgH", [128, NIT, 1024], BF16, kind=okind).ap()
    dbgY = nc.dram_tensor("dbgY", [2, 128, NT, 512], F32, kind=okind).ap()
    dbgR = nc.dram_tensor("dbgR", [2 * H, 128, 512], F32, kind=okind).ap()
    k_sc = nc.dram_tensor("k_sc", [NP, 3, 128, S], BF16, kind=okind).ap()
    q_sc = nc.dram_tensor("q_sc", [NP, 3, 128, TOK], BF16, kind=okind).ap()
    v_sc = nc.dram_tensor("v_sc", [NP, NKT, 128, 256], BF16, kind=okind).ap()
    scr_l = nc.dram_tensor("scr_l", [9, 8, 128, 512], BF16, kind="Internal").ap()
    scr_f = nc.dram_tensor("scr_f", [72, 8, 128, 512], BF16, kind="Internal").ap()
    x2_sc = nc.dram_tensor("x2_sc", [NT, 128, TOK], F32, kind=("ExternalOutput" if debug in (True, 2) else "Internal")).ap()

    cells = b.sb("cells", [128, 32], F32, 0)
    ones_bf = b.sb("ones_bf", [128, 1], BF16, 128)
    ones_f32 = b.sb("ones_f32", [128, 1], F32, 160)
    ones1_f32 = b.sb("ones1_f32", [1, 128], F32, 192)
    negones1_f32 = b.sb("negones1_f32", [1, 128], F32, 207872)
    dbgE_sb = b.sb("dbgE_sb", [128, 16], F32, 208384)
    ones64A = b.sb("ones64A", [128, 1], BF16, 704)
    ones64B = b.sb("ones64B", [128, 1], BF16, 736)
    expbias = b.sb("expbias", [128, 16], F32, 768)
    rowbank = b.sb("rowbank", [128, 2048], F32, 1024)
    fbufA = b.sb("fbufA", [128, 2, 1024], F32, 9216)
    dq_t05 = b.sb("dq_t05", [128, 2, 1024], BF16, 9216)
    dq_smul = b.sb("dq_smul", [128, 2, 1024], BF16, 13312)
    wbufs = b.sb("wbufs", [128, 2, 16, 128], BF16, 17408)
    wpiece = b.sb("wpiece", [128, 3, 1024], BF16, 17408)
    tmpF = b.sb("tmpF", [128, 2048], F32, 17408)

    RD = 25600
    pkb = b.sb("pkb", [128, 2, 512], I32, RD)
    scb = b.sb("scb", [128, 2, 512], BF16, RD + 4096)
    m32 = b.sb("m32", [128, 512], I32, RD + 6144)
    s32 = b.sb("s32", [128, 512], I32, RD + 8192)
    dq_r1 = b.sb("dq_r1", [128, 2, 1024], BF16, RD + 10240)
    dq_r2 = b.sb("dq_r2", [128, 2, 1024], BF16, RD + 14336)
    dq_t2 = b.sb("dq_t2", [128, 1024], BF16, RD + 18432)
    dq_u = b.sb("dq_u", [128, 1024], BF16, RD + 20480)
    wout = b.sb("wout", [128, 2, 2, 512], BF16, RD + 22528)

    R1 = 52224
    ln1 = b.sb("ln1", [128, NT, 2048], BF16, R1)
    sqtile = b.sb("sqtile", [128, 2048], BF16, R1)
    Pbuf = b.sb("Pbuf", [128, 2, NKT, 512], BF16, R1 + 8192)
    rBuf = b.sb("rBuf", [128, 512], F32, R1 + 40960)
    xb2 = b.sb("xb2", [128, 1024], BF16, R1)
    xsq2 = b.sb("xsq2", [128, 1024], BF16, R1 + 2048)
    ln2 = b.sb("ln2", [128, NT, 1024], BF16, R1)
    sgbuf = b.sb("sgbuf", [128, 8, 512], BF16, R1 + 32768)
    htmp = b.sb("htmp", [128, 512], BF16, R1 + 40960)
    ybuf = b.sb("ybuf", [128, NT, 512], F32, R1)
    ysqb = b.sb("ysqb", [128, 512], BF16, R1 + 32768)
    rrB = b.sb("rrB", [128, 512], F32, R1 + 33792)

    R2 = 117760
    xsq = b.sb("xsq", [128, 2, 2048], BF16, R2)
    xb = b.sb("xb", [128, 2, 2048], BF16, R2 + 8192)
    xsA = b.sb("xsA", [128, 2, 2048], F32, R2 + 16384)
    cosR = b.sb("cosR", [128, 2048], F32, R2 + 32768)
    sinA = b.sb("sinA", [128, 2048], F32, R2 + 40960)
    aB = b.sb("aB", [128, 2048], F32, R2 + 49152)
    bB = b.sb("bB", [128, 2048], F32, R2 + 57344)
    estg = b.sb("estg", [128, 512], BF16, R2 + 65536)
    s_rope = b.sb("s_rope", [128, 512], BF16, R2 + 66560)
    sh_rope = b.sb("sh_rope", [128, 512], BF16, R2 + 67584)
    ropetmp = b.sb("ropetmp", [128, 512], F32, R2 + 68608)
    ropetmp2 = b.sb("ropetmp2", [128, 512], F32, R2 + 70656)
    wvb = b.sb("wvb", [128, 16, 512], BF16, R2 + 72704)
    kb = b.sb("kb", [128, 2, 3, 2048], BF16, R2)
    qb = b.sb("qb", [128, 2, 3, 1024], BF16, R2 + 24576)
    vb = b.sb("vb", [128, 2, NKT, 256], BF16, R2 + 36864)
    Obuf = b.sb("Obuf", [128, NT, 1024], BF16, R2 + 53248)
    aB2 = b.sb("aB2", [128, 1024], F32, R2)
    bB2 = b.sb("bB2", [128, 1024], F32, R2 + 4096)
    Hbuf = b.sb("Hbuf", [128, NIT, 1024], BF16, R2)

    psAll = b.es.enter_context(nc.psum_tensor("psAll", [128, 4096], F32))

    def pb(bank):
        return psAll[:, 512 * bank:512 * (bank + 1)]

    def pbr(bank, r0, r1, c0=0, c1=512):
        return psAll[r0:r1, 512 * bank + c0:512 * bank + c1]

    dXp = [b.sem("dX0"), b.sem("dX1")]
    dWp = [b.sem("dW0"), b.sem("dW1")]
    dWV = b.sem("dWV"); dTb = b.sem("dTb")
    dPKp = [b.sem("dPK0"), b.sem("dPK1")]
    dSCp = [b.sem("dSC0"), b.sem("dSC1")]
    dKW = b.sem("dKW")
    dKRp = [b.sem("dKR0"), b.sem("dKR1")]
    dQWp = [b.sem("dQW0"), b.sem("dQW1")]
    dFRp = [b.sem("dFR0"), b.sem("dFR1"), b.sem("dFR2")]
    dX2W = b.sem("dX2W"); dX2R = b.sem("dX2R")
    dOse = b.sem("dOse"); dSh = b.sem("dSh")
    tMM = b.sem("tMM"); vEv = b.sem("vEv"); sEv = b.sem("sEv")
    dqa = b.sem("dqa"); dqv = b.sem("dqv"); dqx = b.sem("dqx")

    bank_guard = {i: [] for i in range(8)}

    def mm(bank, lhsT, rhs, start, stop, out_rows=None, waits=()):
        w = list(waits)
        if start:
            w += bank_guard[bank]
            bank_guard[bank] = []
        if out_rows is None:
            out = psAll[:, 512 * bank:512 * bank + rhs.shape[-1]]
        else:
            out = psAll[out_rows[0]:out_rows[1], 512 * bank:512 * bank + rhs.shape[-1]]
        return b.op("tensor",
                    lambda e, o=out, l=lhsT, r=rhs, st=start, sp=stop:
                    e.matmul(o, l, r, start=st, stop=sp),
                    waits=w, inc=tMM)

    def ldw(wap, waits=()):
        return b.op("tensor", lambda e, w=wap: e.ldweights(w), waits=waits)

    for ci, cv in {0: -2.0, 1: 0.5, 2: -6.0, 3: 1.0, 4: LN_EPS, 5: RMS_EPS,
                   6: 1.0 / 2048.0, 7: 1.0, 8: 2.0, 9: SC_NOPE}.items():
        b.op("vector", lambda e, ci=ci, cv=cv: e.memset(cells[:, ci:ci + 1], cv))
    b.op("vector", lambda e: e.memset(ones_bf[:], 1.0))
    b.op("vector", lambda e: e.memset(ones_f32[:], 1.0))
    b.op("vector", lambda e: e.memset(ones1_f32[:], 1.0))
    b.op("vector", lambda e: e.memset(negones1_f32[:], -1.0))
    b.op("vector", lambda e: e.memset(ones64A[0:64, :], 1.0))
    b.op("vector", lambda e: e.memset(ones64A[64:128, :], 0.0))
    b.op("vector", lambda e: e.memset(ones64B[0:64, :], 0.0))
    consts_done = b.op("vector", lambda e: e.memset(ones64B[64:128, :], 1.0), inc=vEv)

    b.dma("sync", cosR[:], COSR, dTb)
    tables_done = b.dma("sync", sinA[:], SINA, dTb)

    N_DQ_CH = 9
    N_DQ_UNITS = 4 * N_DQ_CH
    dqs = {"u": 0, "pk_t": None, "sc_t": None, "a_t": 0, "pending2": None,
           "qw_done": [None] * N_DQ_UNITS}

    def emit_dq_unit():
        u = dqs["u"]
        if u >= N_DQ_UNITS:
            return False
        ci, jp = u // 4, u % 4
        slot = ci % 2
        par = u % 2
        rows = 128
        if jp == 0:
            dqs["pk_t"] = (dPKp[slot], b.dma("scalar", pkb[:, slot, :], PKS[ci],
                           dPKp[slot], waits=([(dqx, 4 * (ci - 1))] if ci >= 2 else [])))
            dqs["sc_t"] = (dSCp[slot], b.dma("gpsimd", scb[:, slot, :], SCS[ci],
                           dSCp[slot], waits=([(dqv, 4 * (ci - 1))] if ci >= 2 else [])))
        xw = [dqs["pk_t"]] if jp == 0 else []
        if u >= 2:
            xw.append((dqa, u - 1))
        for jh in range(2):
            j = 2 * jp + jh
            b.op("vector", lambda e, sl=slot, j=j:
                 e.tensor_scalar(m32[:, :], pkb[:, sl, :], 4 * j, 7,
                                 ALU.logical_shift_right, ALU.bitwise_and),
                 waits=(xw if jh == 0 else []))
            b.op("vector", lambda e, sl=slot, j=j:
                 e.tensor_scalar(s32[:, :], pkb[:, sl, :], 4 * j + 3, 1,
                                 ALU.logical_shift_right, ALU.bitwise_and))
            b.op("vector", lambda e, p=par, jh=jh:
                 e.tensor_scalar(dq_t05[:, p, 512 * jh:512 * (jh + 1)],
                                 m32[:, :], 0.5, None, ALU.mult))
            b.op("vector", lambda e, p=par, jh=jh:
                 e.tensor_scalar(dq_smul[:, p, 512 * jh:512 * (jh + 1)],
                                 s32[:, :], -2.0, 1.0, ALU.mult, ALU.add),
                 inc=(dqx if jh == 1 else None))
        x_t = dqx.n
        sw = [(dqx, x_t)]
        if u == 0:
            sw.append((vEv, consts_done))
        if u >= 2:
            sw.append((dqv, u - 1))
        b.op("scalar", lambda e, p=par: e.activation(
            dq_r1[:, p, :], dq_t05[:, p, :], AF.Relu,
            bias=cells[:, 0:1], scale=cells[:, 3:4]),
            waits=sw)
        a_t = b.op("scalar", lambda e, p=par: e.activation(
            dq_r2[:, p, :], dq_t05[:, p, :], AF.Relu,
            bias=cells[:, 2:3], scale=cells[:, 8:9]),
            inc=dqa)
        dqs["a_t"] = a_t
        def stage2(u=u, slot=slot, par=par, a_t=a_t, sc_t=dqs["sc_t"]):
            ci_ = u // 4
            vw = [(dqa, a_t), sc_t]
            if u >= 2:
                vw.append(dqs["qw_done"][u - 2])
            b.op("vector", lambda e, p=par: e.tensor_add(
                dq_u[:, :], dq_t05[:, p, :], dq_r1[:, p, :]), waits=vw)
            b.op("vector", lambda e, p=par: e.tensor_add(
                dq_t2[:, :], dq_u[:, :], dq_r2[:, p, :]))
            b.op("vector", lambda e, p=par: e.tensor_mul(
                dq_u[:, :], dq_t2[:, :], dq_smul[:, p, :]))
            b.op("vector", lambda e, sl=slot, p=par: e.tensor_mul(
                wout[:, p, 0, :], dq_u[:, 0:512], scb[:, sl, :]))
            v_t = b.op("vector", lambda e, sl=slot, p=par: e.tensor_mul(
                wout[:, p, 1, :], dq_u[:, 512:1024], scb[:, sl, :]),
                inc=dqv)
            jp_ = u % 4
            for jh in range(2):
                j = 2 * jp_ + jh
                b.dma("sync", scr_l[ci_, j], wout[:, par, jh, :], dQWp[par],
                      waits=([(dqv, v_t)] if jh == 0 else []))
            dqs["qw_done"][u] = (dQWp[par], dQWp[par].n)
        if dqs["pending2"] is not None:
            pend = dqs["pending2"]
            dqs["pending2"] = stage2
            pend()
        else:
            dqs["pending2"] = stage2
        dqs["u"] = u + 1
        return True

    def emit_dq_flush():
        if dqs["pending2"] is not None:
            dqs["pending2"]()
            dqs["pending2"] = None

    def emit_dq_units(n):
        for _ in range(n):
            if not emit_dq_unit():
                return

    XT_t = XT.rearrange("(t p) n -> t p n", p=128)
    v_cnt = [0] * NT
    s_cnt = [0] * NT
    t_cnt = [0] * NT
    x_cnt = [0] * NT
    for tau in range(NT):
        sl = tau % 2
        wv = []
        if tau >= 2:
            wv = [(vEv, v_cnt[tau - 2]), (sEv, s_cnt[tau - 2])]
        x_cnt[tau] = b.dma("sync", xsA[:, sl, :], XT_t[tau], dXp[sl], waits=wv)
        wv2 = [(dXp[sl], x_cnt[tau])]
        if tau >= 2:
            wv2.append((tMM, t_cnt[tau - 2]))
        v_cnt[tau] = b.op("vector", lambda e, sl=sl: e.tensor_copy(xb[:, sl, :], xsA[:, sl, :]),
                          waits=wv2, inc=vEv)
        s_cnt[tau] = b.op("scalar", lambda e, sl=sl: e.activation(
            xsq[:, sl, :], xsA[:, sl, :], AF.Square), waits=wv2, inc=sEv)
        mw = [(vEv, v_cnt[tau]), (sEv, s_cnt[tau])]
        for c in range(4):
            mm(c, ones_bf[:], xb[:, sl, 512 * c:512 * (c + 1)],
               start=(tau == 0), stop=(tau == NT - 1), out_rows=(0, 1),
               waits=(mw if c == 0 else []))
        for c in range(4):
            t_cnt[tau] = mm(4 + c, ones_bf[:], xsq[:, sl, 512 * c:512 * (c + 1)],
                            start=(tau == 0), stop=(tau == NT - 1), out_rows=(0, 1))
    stats_mm_done = t_cnt[NT - 1]

    sA, sB, sC, sD = (rowbank[0:1, 0:512], rowbank[0:1, 512:1024],
                      rowbank[0:1, 1024:1536], rowbank[0:1, 1536:2048])

    def ln_coeffs(nchunk, sum_banks, sq_banks, eps_cell, dst_a, dst_b, first_wait):
        abc = 0
        prev_bt = 0
        for c in range(nchunk):
            bs, bq = sum_banks[c], sq_banks[c]
            w0 = list(first_wait) if c == 0 else [(tMM, prev_bt)]
            cp1 = b.op("scalar", lambda e, bs=bs: e.copy(sA, pbr(bs, 0, 1)),
                       waits=w0, inc=sEv)
            cp2 = b.op("scalar", lambda e, bq=bq: e.copy(sB, pbr(bq, 0, 1)), inc=sEv)
            b.op("vector", lambda e: e.tensor_scalar(sC, sA, 1.0 / 2048.0, None,
                                                     ALU.mult), waits=[(sEv, cp2)])
            b.op("vector", lambda e: e.tensor_scalar(sD, sB, 1.0 / 2048.0, None,
                                                     ALU.mult))
            b.op("vector", lambda e: e.tensor_mul(sA, sC, sC))
            var_v = b.op("vector", lambda e: e.tensor_sub(sB, sD, sA), inc=vEv)
            sq_v = b.op("scalar", lambda e, ec=eps_cell: e.activation(
                sD, sB, AF.Sqrt, bias=cells[0:1, ec:ec + 1], scale=cells[0:1, 7:8]),
                waits=[(vEv, var_v)], inc=sEv)
            b.op("vector", lambda e: e.reciprocal(sA, sD), waits=[(sEv, sq_v)])
            ab_v = b.op("vector", lambda e: e.scalar_tensor_tensor(
                sB, sC, -1.0, sA, ALU.mult, ALU.mult), inc=vEv)
            mm(bs, ones1_f32[:], sA, True, True,
               waits=[(vEv, ab_v), (sEv, cp2)])
            bt = mm(bq, ones1_f32[:], sB, True, True)
            prev_bt = bt
            a1 = b.op("scalar", lambda e, c=c, bs=bs: e.copy(
                dst_a[:, 512 * c:512 * (c + 1)], pb(bs)), waits=[(tMM, bt)], inc=sEv)
            abc = b.op("scalar", lambda e, c=c, bq=bq: e.copy(
                dst_b[:, 512 * c:512 * (c + 1)], pb(bq)), inc=sEv)
            bank_guard[bs] = [(sEv, abc)]
            bank_guard[bq] = [(sEv, abc)]
        return abc

    abc2 = ln_coeffs(4, [0, 1, 2, 3], [4, 5, 6, 7], 4, aB, bB,
                     [(tMM, stats_mm_done)])

    ln_v = [0] * NT
    for tau in range(NT):
        wv = [(sEv, abc2)] if tau == 0 else [(vEv, ln_v[tau - 1])]
        x_t = b.dma("sync", xsA[:, 0, :], XT_t[tau], dXp[0], waits=wv)
        b.op("vector", lambda e: e.tensor_mul(tmpF[:], xsA[:, 0, :], aB[:]),
             waits=[(dXp[0], x_t)])
        ln_v[tau] = b.op("vector", lambda e, tau=tau: e.tensor_add(
            ln1[:, tau, :], tmpF[:], bB[:]), inc=vEv)
    ln_done = ln_v[NT - 1]

    kw_done = [0] * NP
    vw_done = [0] * NP

    rope_state = {"v": 0}

    def rope_evac(bank, scale_c, tab_sl, dst, stg_guard):
        sc_v = b.op("scalar", lambda e, bk=bank: e.copy(s_rope[:], pb(bk)),
                    waits=list(stg_guard) + [(vEv, rope_state["v"])], inc=sEv)
        sh_t = 0
        for (d0, s0) in ((0, 32), (32, 0), (64, 96), (96, 64)):
            sh_t = b.dma("sync", sh_rope[d0:d0 + 32, :], s_rope[s0:s0 + 32, :], dSh,
                         waits=([(sEv, sc_v), (vEv, rope_state["v"])] if d0 == 0 else []))
        b.op("vector", lambda e, ts=tab_sl, c=scale_c: e.scalar_tensor_tensor(
            ropetmp[:], s_rope[:], c, cosR[:, ts[0]:ts[1]], ALU.mult, ALU.mult),
            waits=[(dSh, sh_t), (dTb, tables_done)])
        b.op("vector", lambda e, ts=tab_sl, c=scale_c: e.scalar_tensor_tensor(
            ropetmp2[:], sh_rope[:], c, sinA[:, ts[0]:ts[1]], ALU.mult, ALU.mult))
        ev = b.op("vector", lambda e: e.tensor_add(estg[:], ropetmp[:], ropetmp2[:]),
                  inc=vEv)
        rope_state["v"] = ev
        w_t = b.dma("sync", dst, estg[:], dKW, waits=[(vEv, ev)])
        return (sEv, sc_v), (dKW, w_t)

    prev_w_used = 0
    wslot_free = [(), ()]
    estg_guard = []
    for which, WT, dsts, ntc, rope_scale in (
        ("q", WQT, q_sc, 2, SC_ROPE),
        ("k", WKT, k_sc, 4, 1.0),
    ):
        WT_r = WT.rearrange("(t p) d -> p t d", p=128)
        for dc in range(24):
            hp, ct = dc // 3, dc % 3
            sl = dc % 2
            wv0 = list(wslot_free[sl])
            if which == "q" and dc < 2:
                wv0.append((vEv, ln_done))
            w_t = b.dma("gpsimd", wbufs[:, sl], WT_r[:, :, 128 * dc:128 * (dc + 1)],
                        dWp[sl], waits=wv0)
            banks = [2 * (dc % 4), 2 * (dc % 4) + 1] if ntc == 2 else \
                    ([0, 1, 2, 3] if dc % 2 == 0 else [4, 5, 6, 7])
            last_ldw_t = None
            for it in range(NT):
                lw = [(dWp[sl], w_t)] if it == 0 else []
                if it == 0:
                    lw.append((vEv, ln_done))
                ldw(wbufs[:, sl, it, :], waits=lw)
                for tc in range(ntc):
                    mm(banks[tc], wbufs[:, sl, it, :],
                       ln1[:, it, 512 * tc:512 * (tc + 1)],
                       start=(it == 0), stop=(it == NT - 1))
            mm_done = tMM.n
            wslot_free[sl] = [(tMM, mm_done)]
            for tc in range(ntc):
                bk = banks[tc]
                if ct < 2:
                    if which == "q":
                        ev = b.op("scalar", lambda e, bk=bk: e.mul(
                            estg[:], pb(bk), cells[:, 9:10]),
                            waits=[(tMM, mm_done)] + estg_guard, inc=sEv)
                        gsem = (sEv, ev)
                    else:
                        ev = b.op("scalar", lambda e, bk=bk: e.copy(estg[:], pb(bk)),
                                  waits=[(tMM, mm_done)] + estg_guard, inc=sEv)
                        gsem = (sEv, ev)
                    w2 = b.dma("sync", dsts[hp, ct, :, 512 * tc:512 * (tc + 1)],
                               estg[:], dKW, waits=[gsem])
                    bank_guard[bk] = [gsem]
                    estg_guard = [(dKW, w2)]
                else:
                    pg, wg = rope_evac(bk, rope_scale, (512 * tc, 512 * (tc + 1)),
                                       dsts[hp, 2, :, 512 * tc:512 * (tc + 1)],
                                       [(tMM, mm_done)] + estg_guard)
                    bank_guard[bk] = [pg]
                    estg_guard = [wg]
            if which == "k" and ct == 2:
                kw_done[hp] = dKW.n
            emit_dq_units(1)

    WVT_r = WVT.rearrange("(t p) d -> p t d", p=128)
    wv_free = []
    for dvc in range(4):
        w_t = b.dma("gpsimd", wvb[:], WVT_r[:, :, 512 * dvc:512 * (dvc + 1)], dWV,
                    waits=wv_free)
        for kc in range(NKT):
            bk = 4 + kc % 4
            for it in range(NT):
                lw = [(dWV, w_t), (vEv, ln_done)] if it == 0 else []
                ldw(ln1[:, it, 128 * kc:128 * (kc + 1)], waits=lw)
                mm(bk, ln1[:, it, 128 * kc:128 * (kc + 1)], wvb[:, it, :],
                   start=(it == 0), stop=(it == NT - 1))
            mm_done = tMM.n
            ev = b.op("scalar", lambda e, bk=bk: e.copy(estg[:], pb(bk)),
                      waits=[(tMM, mm_done)] + estg_guard, inc=sEv)
            bank_guard[bk] = [(sEv, ev)]
            w2 = 0
            for s in range(2):
                w2 = b.dma("sync", v_sc[2 * dvc + s, kc, :, :],
                           estg[:, 256 * s:256 * (s + 1)], dKW,
                           waits=([(sEv, ev)] if s == 0 else []))
            estg_guard = [(dKW, w2)]
            if kc % 4 == 3:
                emit_dq_units(1)
        wv_free = [(tMM, tMM.n)]
    for hp in range(NP):
        vw_done[hp] = dKW.n

    b.barrier()

    dkw_final = dKW.n
    pair_cons = [(), ()]
    p_free2 = [[], []]
    o_evac_v = 0
    sq_guard_outer = [0]
    eb_guard = [0]
    rb_guard = [0]
    bound_g = [0]
    for hp in range(NP):
        sl = hp % 2
        kr1 = b.dma("sync", kb[:, sl], k_sc[hp].rearrange("c p n -> p c n"), dKRp[sl],
                    waits=list(pair_cons[sl]) + [(dKW, dkw_final)])
        kr2 = b.dma("sync", qb[:, sl], q_sc[hp].rearrange("c p n -> p c n"), dKRp[sl])
        kr3 = b.dma("sync", vb[:, sl], v_sc[hp].rearrange("t p n -> p t n"), dKRp[sl])
        dKR = dKRp[sl]
        kqv_t = dKRp[sl].n
        sq_guard = sq_guard_outer[0]
        for hh in range(2):
            h = 2 * hp + hh
            o64 = ones64A if hh == 0 else ones64B
            for (which_k, kc_hh, dstcell) in ((0, hh, 20), (2, 2, 21)):
                ck = kc_hh if which_k == 0 else 2
                sq_s = b.op("scalar", lambda e, sl=sl, ck=ck: e.activation(
                    sqtile[:], kb[:, sl, ck, :], AF.Square),
                    waits=[(dKR, kqv_t), (tMM, sq_guard), (vEv, bound_g[0])],
                    inc=sEv)
                oo = ones_bf if which_k == 0 else o64
                for c in range(4):
                    m_t = mm(5, oo[:], sqtile[:, 512 * c:512 * (c + 1)], True, True,
                             out_rows=(0, 1), waits=[(sEv, sq_s)] if c == 0 else [])
                    c_t = b.op("scalar", lambda e, c=c: e.copy(
                        rowbank[0:1, 512 * c:512 * (c + 1)], pbr(5, 0, 1)),
                        waits=[(tMM, m_t), (vEv, bound_g[0]), (tMM, rb_guard[0])],
                        inc=sEv)
                    bank_guard[5] = [(sEv, c_t)]
                sq_guard = m_t
                bound_g[0] = b.op("vector", lambda e, dc=dstcell: e.reduce_max(
                    cells[0:1, dc:dc + 1], rowbank[0:1, 0:2048],
                    mybir.AxisListType.X), waits=[(sEv, c_t)], inc=vEv)
            kx_s = b.op("scalar", lambda e: e.activation(
                cells[0:1, 22:24], cells[0:1, 20:22], AF.Sqrt),
                waits=[(vEv, bound_g[0])], inc=sEv)
            sqn_s = b.op("scalar", lambda e, sl=sl, hh=hh: e.activation(
                sqtile[:, 0:1024], qb[:, sl, hh, :], AF.Square),
                waits=[(tMM, sq_guard)], inc=sEv)
            sqr_s = b.op("scalar", lambda e, sl=sl: e.activation(
                sqtile[:, 1024:2048], qb[:, sl, 2, :], AF.Square), inc=sEv)
            for c in range(2):
                mseg = rowbank[0:1, 512 * c:512 * (c + 1)]
                m1 = mm(5, ones_bf[:], sqtile[:, 512 * c:512 * (c + 1)], True, True,
                        out_rows=(0, 1), waits=[(sEv, sqn_s)] if c == 0 else [])
                c1 = b.op("scalar", lambda e: e.copy(sC, pbr(5, 0, 1)),
                          waits=[(tMM, m1), (vEv, bound_g[0])], inc=sEv)
                bank_guard[5] = [(sEv, c1)]
                m2 = mm(5, o64[:], sqtile[:, 1024 + 512 * c:1536 + 512 * c], True, True,
                        out_rows=(0, 1), waits=[(sEv, sqr_s)])
                c2 = b.op("scalar", lambda e: e.copy(sD, pbr(5, 0, 1)),
                          waits=[(tMM, m2), (tMM, rb_guard[0])], inc=sEv)
                bank_guard[5] = [(sEv, c2)]
                sq_guard = m2
                b.op("scalar", lambda e, ms=mseg: e.activation(ms, sC, AF.Sqrt))
                qs_s = b.op("scalar", lambda e: e.activation(sC, sD, AF.Sqrt),
                            inc=sEv)
                b.op("vector", lambda e, ms=mseg: e.tensor_scalar(
                    sD, ms, cells[0:1, 22:23], None, ALU.mult),
                    waits=[(sEv, qs_s), (sEv, kx_s)])
                bound_g[0] = b.op("vector", lambda e, ms=mseg: e.scalar_tensor_tensor(
                    ms, sC, cells[0:1, 23:24], sD, ALU.mult, ALU.add), inc=vEv)
            mx_v = b.op("vector", lambda e: e.reduce_max(
                cells[0:1, 10:11], rowbank[0:1, 0:1024], mybir.AxisListType.X),
                waits=[(tMM, eb_guard[0])], inc=vEv)
            bound_g[0] = mx_v
            m_t = mm(7, negones1_f32[:], cells[0:1, 10:11], True, True,
                     waits=[(vEv, mx_v)])
            eb_guard[0] = m_t
            eb_s = b.op("scalar", lambda e, h=h: e.copy(
                expbias[:, h:h + 1], pbr(7, 0, 128, 0, 1)),
                waits=[(tMM, m_t)], inc=sEv)
            bank_guard[7] = [(sEv, eb_s)]
            if debug is True:
                b.dma("sync", dbgM[h], cells[0:1, :], dKW,
                      waits=[(sEv, eb_s), (vEv, bound_g[0])])
                if hp == 0 and hh == 1:
                    b.dma("sync", dbgE[:, 16:32], expbias[:], dKW,
                          waits=[(sEv, eb_s)])
                ps7 = b.op("scalar", lambda e, h=h: e.copy(
                    dbgE_sb[:, h:h + 1], pbr(7, 0, 128, 0, 1)), inc=sEv)
                if hp == 3 and hh == 1:
                    b.dma("sync", dbgE[:, 32:48], expbias[:], dKW,
                          waits=[(sEv, ps7)])
            sq_guard_outer[0] = sq_guard
            emit_dq_units(1)
        for hh in range(2):
            h = 2 * hp + hh
            r0, r1_ = (0, 64) if hh == 0 else (64, 128)
            for qc in range(2):
                ppar = (2 * hh + qc) % 2
                sb_rot = [0, 1]
                exp_v = [0] * NKT
                for kt in range(NKT):
                    bk = sb_rot[kt % 2]
                    mm(bk, kb[:, sl, hh, 128 * kt:128 * (kt + 1)],
                       qb[:, sl, hh, 512 * qc:512 * (qc + 1)],
                       start=True, stop=False)
                    sc_t = mm(bk, kb[r0:r1_, sl, 2, 128 * kt:128 * (kt + 1)],
                              qb[r0:r1_, sl, 2, 512 * qc:512 * (qc + 1)],
                              start=False, stop=True)
                    e_t = b.op("scalar", lambda e, bk=bk, kt=kt, h=h, pp=ppar:
                               e.activation(Pbuf[:, pp, kt, :], pb(bk), AF.Exp,
                                            bias=expbias[:, h:h + 1]),
                        waits=[(tMM, sc_t), (sEv, eb_s)] +
                              (p_free2[ppar] if kt == 0 else []), inc=sEv)
                    exp_v[kt] = e_t
                    bank_guard[bk] = [(sEv, e_t)]
                for kt in range(NKT):
                    s_t = mm(2, ones_bf[:], Pbuf[:, ppar, kt, :], start=(kt == 0),
                             stop=(kt == NKT - 1), out_rows=(0, 1),
                             waits=[(sEv, exp_v[kt])])
                rc_v = b.op("vector", lambda e: e.reciprocal(sD, pbr(2, 0, 1)),
                            waits=[(tMM, s_t), (sEv, exp_v[NKT - 1]),
                                   (tMM, rb_guard[0])], inc=vEv)
                bank_guard[2] = [(vEv, rc_v)]
                m_t = mm(7, ones1_f32[:], sD, True, True, waits=[(vEv, rc_v)])
                rb_guard[0] = m_t
                rb_s = b.op("scalar", lambda e: e.copy(rBuf[:], pb(7)),
                            waits=[(tMM, m_t), (vEv, o_evac_v)], inc=sEv)
                bank_guard[7] = [(sEv, rb_s)]
                if debug:
                    b.dma("sync", dbgR[2 * h + qc], rBuf[:], dKW,
                          waits=[(sEv, rb_s)])
                ob = 3 + (qc % 2)
                for kt in range(NKT):
                    av_t = mm(ob, vb[:, sl, kt, 128 * hh:128 * (hh + 1)],
                              Pbuf[:, ppar, kt, :], start=(kt == 0),
                              stop=(kt == NKT - 1),
                              waits=[(sEv, exp_v[NKT - 1]), (dKR, kqv_t)]
                              if kt == 0 else [])
                o_evac_v = b.op("vector", lambda e, ob=ob, h=h, qc=qc: e.tensor_mul(
                    Obuf[:, h, 512 * qc:512 * (qc + 1)], pb(ob), rBuf[:]),
                    waits=[(tMM, av_t), (sEv, rb_s)], inc=vEv)
                bank_guard[ob] = [(vEv, o_evac_v)]
                p_free2[ppar] = [(tMM, av_t)]
                emit_dq_units(1)
        pair_cons[sl] = [(tMM, tMM.n)]
    while emit_dq_unit():
        pass
    emit_dq_flush()
    ccS = b.sem("ccS")
    for t_ in range(3):
        b.op("gpsimd", lambda e, t_=t_: e.collective_compute(
            "AllGather", ALU.bypass, replica_groups=[list(range(8))],
            ins=[scr_l[3 * t_:3 * t_ + 3]], outs=[scr_f[24 * t_:24 * t_ + 24]]),
            waits=[(dQWp[0], dQWp[0].n), (dQWp[1], dQWp[1].n)],
            inc=ccS)
    cc_done = ccS.n
    gu_w_final = [(ccS, cc_done)]
    if debug is True:
        for blk8 in range(4):
            b.dma("sync", dbgS[18 * blk8:18 * (blk8 + 1)],
                  scr_f[18 * blk8:18 * (blk8 + 1)], dKW,
                  waits=([(ccS, cc_done)] if blk8 == 0 else []))
    if debug is True:
        b.dma("sync", dbgO, Obuf[:], dKW, waits=[(vEv, o_evac_v)])
        b.dma("sync", dbgE[:, 0:16], expbias[:], dKW)
        b.dma("sync", dbgE[:, 48:64], dbgE_sb[:, 0:16], dKW)

    b.barrier()

    WOT_r = WOT.rearrange("(t p) d -> p t d", p=128)
    wslot_free = [(), ()]
    xslot_guard = [(dqv, dqv.n)]
    st_t = 0
    x2w = [0] * NT
    for ht in range(NT):
        sl = ht % 2
        w_t = b.dma("gpsimd", wbufs[:, sl], WOT_r[:, :, 128 * ht:128 * (ht + 1)],
                    dWp[sl], waits=wslot_free[sl])
        x_t = b.dma("sync", fbufA[:, 0, :], XT_t[ht][:, 0:TOK], dXp[0],
                    waits=xslot_guard)
        banks = [2 * (ht % 2), 2 * (ht % 2) + 1]
        for it in range(NT):
            lw = [(dWp[sl], w_t)] if it == 0 else []
            ldw(wbufs[:, sl, it, :], waits=lw)
            for tc in range(2):
                mm(banks[tc], wbufs[:, sl, it, :], Obuf[:, it, 512 * tc:512 * (tc + 1)],
                   start=(it == 0), stop=(it == NT - 1))
        mm_done = tMM.n
        wslot_free[sl] = [(tMM, mm_done)]
        for tc in range(2):
            wv = [(tMM, mm_done), (dXp[0], x_t)]
            if tc == 0 and ht > 0:
                wv.append((dX2W, x2w[ht - 1]))
            b.op("vector", lambda e, bk=banks[tc], tc=tc: e.tensor_add(
                fbufA[:, 1, 512 * tc:512 * (tc + 1)], pb(bk),
                fbufA[:, 0, 512 * tc:512 * (tc + 1)]),
                waits=wv)
        x2_v = b.op("vector", lambda e: e.tensor_copy(xb2[:], fbufA[:, 1, :]), inc=vEv)
        bank_guard[banks[0]] = [(vEv, x2_v)]
        bank_guard[banks[1]] = [(vEv, x2_v)]
        x2_s = b.op("scalar", lambda e: e.activation(xsq2[:], fbufA[:, 1, :], AF.Square),
                    waits=[(vEv, x2_v)], inc=sEv)
        for c in range(2):
            mm(4 + c, ones_bf[:], xb2[:, 512 * c:512 * (c + 1)],
               start=(ht == 0), stop=(ht == NT - 1), out_rows=(0, 1),
               waits=[(vEv, x2_v)] if c == 0 else [])
        for c in range(2):
            st_t = mm(6 + c, ones_bf[:], xsq2[:, 512 * c:512 * (c + 1)],
                      start=(ht == 0), stop=(ht == NT - 1), out_rows=(0, 1),
                      waits=[(sEv, x2_s)] if c == 0 else [])
        x2w[ht] = b.dma("sync", x2_sc[ht], fbufA[:, 1, :], dX2W, waits=[(vEv, x2_v)])
        xslot_guard = [(vEv, x2_v), (sEv, x2_s), (tMM, st_t)]

    ab2_s = ln_coeffs(2, [4, 5], [6, 7], 4, aB2, bB2, [(tMM, st_t)])
    ln2_v = 0
    for ht in range(NT):
        x_t = b.dma("sync", fbufA[:, 0, :], x2_sc[ht], dX2R,
                    waits=([(sEv, ab2_s), (dX2W, dX2W.n)] if ht == 0
                           else [(vEv, ln2_v)]))
        b.op("vector", lambda e: e.tensor_mul(fbufA[:, 1, :], fbufA[:, 0, :], aB2[:]),
             waits=[(dX2R, x_t)])
        ln2_v = b.op("vector", lambda e, ht=ht: e.tensor_add(
            ln2[:, ht, :], fbufA[:, 1, :], bB2[:]), inc=vEv)

    if debug is True:
        b.dma("sync", dbgL2, ln2[:], dKW, waits=[(vEv, ln2_v)])
    b.barrier()

    pc_free = [(), (), ()]
    htmp_guard = [0]
    hevac_g = [0]
    for cb in range(6):
        W = 512 if cb == 5 else 1024
        ncol = W // 8
        for qd in range(2):
            for half, base in (("g", 1024 * cb), ("u", NI + 1024 * cb)):
                piece_t = [0] * NT

                def d_piece(it, cb=cb, qd=qd, half=half, base=base, W=W):
                    psl = it % 3
                    wv = list(pc_free[psl])
                    if cb == 0 and qd == 0 and half == "g" and it < 2:
                        wv += gu_w_final
                    nch = W // 512
                    ci0 = 22 * (it // 8) + base // 512
                    srcap = scr_f[ci0:ci0 + nch, it % 8].rearrange("a p c -> p a c")
                    dstap = wpiece[:, psl, 0:W].rearrange("p (a c) -> p a c", c=512)
                    piece_t[it] = b.dma("scalar", dstap, srcap, dFRp[psl], waits=wv)
                d_piece(0)
                d_piece(1)
                d_piece(2)
                for it in range(NT):
                    psl = it % 3
                    for jj in range(4):
                        jq = 4 * qd + jj
                        lw = [(dFRp[psl], piece_t[it])] if jj == 0 else []
                        if cb == 0 and qd == 0 and half == "g" and it == 0 and jj == 0:
                            lw.append((vEv, ln2_v))
                        ldw(wpiece[:, psl, jq:W:8], waits=lw)
                        for tc in range(2):
                            mm(2 * jj + tc, wpiece[:, psl, jq:W:8],
                               ln2[:, it, 512 * tc:512 * (tc + 1)],
                               start=(it == 0), stop=(it == NT - 1),
                               out_rows=(0, ncol))
                    pc_free[psl] = [(tMM, tMM.n)]
                    if it + 3 < NT:
                        d_piece(it + 3)
                mm_done = tMM.n
                if half == "g":
                    sg_s = 0
                    for jj in range(4):
                        for tc in range(2):
                            sg_s = b.op("scalar", lambda e, bk=2 * jj + tc, jj=jj,
                                        tc=tc, nc_=ncol: e.activation(
                                            sgbuf[0:nc_, 2 * jj + tc, :],
                                            pbr(2 * jj + tc, 0, nc_), AF.Silu),
                                        waits=([(tMM, mm_done), (vEv, hevac_g[0])]
                                               if (jj == 0 and tc == 0) else []),
                                        inc=sEv)
                            bank_guard[2 * jj + tc] = [(sEv, sg_s)]
                else:
                    for jj in range(4):
                        jq = 4 * qd + jj
                        for tc in range(2):
                            bk = 2 * jj + tc
                            if cb < 5:
                                h_v = b.op("vector", lambda e, bk=bk, cb=cb, jq=jq,
                                           tc=tc: e.tensor_mul(
                                               Hbuf[:, 8 * cb + jq, 512 * tc:512 * (tc + 1)],
                                               pbr(bk, 0, 128), sgbuf[:, bk, :]),
                                           waits=[(tMM, mm_done), (sEv, sg_s)]
                                           if (jj == 0 and tc == 0) else [], inc=vEv)
                                bank_guard[bk] = [(vEv, h_v)]
                                hevac_g[0] = h_v
                            else:
                                itH = 40 + jq // 2
                                if jq % 2 == 0:
                                    h_v = b.op("vector", lambda e, bk=bk, itH=itH, tc=tc:
                                               e.tensor_mul(
                                                   Hbuf[0:64, itH, 512 * tc:512 * (tc + 1)],
                                                   pbr(bk, 0, 64), sgbuf[0:64, bk, :]),
                                               waits=[(tMM, mm_done), (sEv, sg_s)]
                                               if (jj == 0 and tc == 0) else [], inc=vEv)
                                    bank_guard[bk] = [(vEv, h_v)]
                                    hevac_g[0] = h_v
                                else:
                                    hw = ([(tMM, mm_done), (sEv, sg_s)]
                                          if (jj == 0 and tc == 0) else [])
                                    hw.append((dSh, htmp_guard[0]))
                                    h_v = b.op("vector", lambda e, bk=bk: e.tensor_mul(
                                        htmp[0:64, :], pbr(bk, 0, 64), sgbuf[0:64, bk, :]),
                                        waits=hw, inc=vEv)
                                    sh_t = b.dma("sync",
                                                 Hbuf[64:128, itH, 512 * tc:512 * (tc + 1)],
                                                 htmp[0:64, :], dSh, waits=[(vEv, h_v)])
                                    htmp_guard[0] = sh_t
                                    bank_guard[bk] = [(vEv, h_v)]
                                    hevac_g[0] = h_v
    dq_w_final = [(ccS, cc_done)]
    h_shift_done = dSh.n
    h_done_v = vEv.n
    if debug in (True, 3):
        b.dma("sync", dbgH, Hbuf[:], dKW, waits=[(vEv, h_done_v), (dSh, h_shift_done)])

    y_guard = []
    for tc in range(2):
        y_done_s = 0
        for hb in range(2):
            piece_t = [0] * NIT

            def e_piece(it, tc=tc, hb=hb):
                psl = it % 3
                wv = list(pc_free[psl])
                if tc == 0 and hb == 0 and it < 2:
                    wv += dq_w_final
                dstap = wpiece[:, psl, :].rearrange("p (a c) -> p a c", c=512)
                if it < 40:
                    ci0 = 44 + 4 * (it // 8) + 2 * hb
                    srcap = scr_f[ci0:ci0 + 2, it % 8].rearrange("a p c -> p a c")
                    piece_t[it] = b.dma("scalar", dstap, srcap, dFRp[psl], waits=wv)
                else:
                    m = it - 40
                    ci0 = 64 + 2 * hb
                    b.dma("scalar", wpiece[0:64, psl, :].rearrange(
                        "p (a c) -> p a c", c=512),
                        scr_f[ci0:ci0 + 2, 2 * m, 0:64, :].rearrange("a p c -> p a c"),
                        dFRp[psl], waits=wv)
                    piece_t[it] = b.dma("scalar", wpiece[64:128, psl, :].rearrange(
                        "p (a c) -> p a c", c=512),
                        scr_f[ci0:ci0 + 2, 2 * m + 1, 0:64, :].rearrange("a p c -> p a c"),
                        dFRp[psl])
            e_piece(0)
            e_piece(1)
            e_piece(2)
            for it in range(NIT):
                psl = it % 3
                for jh in range(8):
                    lw = [(dFRp[psl], piece_t[it])] if jh == 0 else []
                    if tc == 0 and hb == 0 and it == 0 and jh == 0:
                        lw += [(vEv, h_done_v), (dSh, h_shift_done)]
                    ldw(wpiece[:, psl, jh:1024:8], waits=lw)
                    mm(jh, wpiece[:, psl, jh:1024:8],
                       Hbuf[:, it, 512 * tc:512 * (tc + 1)],
                       start=(it == 0), stop=(it == NIT - 1))
                pc_free[psl] = [(tMM, tMM.n)]
                if it + 3 < NIT:
                    e_piece(it + 3)
            mm_done = tMM.n
            for jh in range(8):
                y_done_s = b.op("scalar", lambda e, jh=jh, hb=hb: e.copy(
                    ybuf[:, 8 * hb + jh, :], pb(jh)),
                    waits=([(tMM, mm_done)] + list(y_guard)) if jh == 0 else [],
                    inc=sEv)
                bank_guard[jh] = [(sEv, y_done_s)]
        if debug in (True, 3):
            b.dma("sync", dbgY[tc], ybuf[:], dKW, waits=[(sEv, y_done_s)])
        ys_t = 0
        for yt in range(NT):
            ys_s = b.op("scalar", lambda e, yt=yt: e.activation(
                ysqb[:], ybuf[:, yt, :], AF.Square),
                waits=[(sEv, y_done_s)] if yt == 0 else [(tMM, ys_t)], inc=sEv)
            ys_t = mm(0, ones_bf[:], ysqb[:], start=(yt == 0), stop=(yt == NT - 1),
                      out_rows=(0, 1), waits=[(sEv, ys_s)])
        rr_s = b.op("scalar", lambda e: e.activation(
            sA, pbr(0, 0, 1), AF.Sqrt,
            bias=cells[0:1, 5:6], scale=cells[0:1, 6:7]),
            waits=[(tMM, ys_t), (tMM, rb_guard[0])], inc=sEv)
        bank_guard[0] = [(sEv, rr_s)]
        rr_v = b.op("vector", lambda e: e.reciprocal(sB, sA),
                    waits=[(sEv, rr_s)], inc=vEv)
        m_t = mm(1, ones1_f32[:], sB, True, True, waits=[(vEv, rr_v)])
        rb_guard[0] = m_t
        rrb_s = b.op("scalar", lambda e: e.copy(rrB[:], pb(1)),
                     waits=[(tMM, m_t)], inc=sEv)
        bank_guard[1] = [(sEv, rrb_s)]
        for yt in range(NT):
            wfb = [(dOse, dOse.n)]
            x_t = b.dma("sync", fbufA[:, 0, 0:512],
                        x2_sc[yt, :, 512 * tc:512 * (tc + 1)], dX2R,
                        waits=wfb)
            b.op("vector", lambda e, yt=yt: e.tensor_mul(
                fbufA[:, 1, 0:512], ybuf[:, yt, :], rrB[:]),
                waits=[(sEv, rrb_s), (dX2R, x_t)])
            o_v = b.op("vector", lambda e: e.tensor_add(
                fbufA[:, 1, 512:1024], fbufA[:, 1, 0:512], fbufA[:, 0, 0:512]),
                inc=vEv)
            y_guard = [(vEv, o_v)]
            b.dma("sync", OUT[yt, :, 512 * tc:512 * (tc + 1)], fbufA[:, 1, 512:1024],
                  dOse, waits=[(vEv, o_v)])
    final_out = dOse.n
    b.plan["sync"].append(lambda e, h=dOse.h, v=final_out: e.wait_ge(h, v))

    with nc.Block() as block:
        for eng_name, deco in (("sync", block.sync), ("tensor", block.tensor),
                               ("vector", block.vector), ("scalar", block.scalar),
                               ("gpsimd", block.gpsimd)):
            def run(e, items=b.plan[eng_name]):
                for f in items:
                    f(e)
            deco(run)
    b.es.close()
    return nc


_NC_CACHE = {}


def _get_nc(debug=False):
    key = debug
    if key not in _NC_CACHE:
        _NC_CACHE[key] = build_program(debug=key)
    return _NC_CACHE[key]


def _stage(L0_attn_hidden_states, L0_attn_Wq, L0_attn_Wk, L0_attn_Wv, L0_attn_Wo,
           L0_ffn_gate_up_packed, L0_ffn_gate_up_scales, L0_ffn_down_packed,
           L0_ffn_down_scales):
    x = np.asarray(L0_attn_hidden_states)
    Wq = np.asarray(L0_attn_Wq)
    Wk = np.asarray(L0_attn_Wk)
    Wv = np.asarray(L0_attn_Wv)
    Wo = np.asarray(L0_attn_Wo)
    colord = []
    for hp in range(NP):
        colord += list(range(128 * 2 * hp, 128 * 2 * hp + 256))
        colord += list(range(2048 + 64 * 2 * hp, 2048 + 64 * 2 * hp + 128))
    colord = np.array(colord)
    WQT = np.ascontiguousarray(Wq[colord][:, PERM].T.astype(np.float32))
    WKT = np.ascontiguousarray(Wk[colord][:, PERM].T.astype(np.float32))
    WVT = np.ascontiguousarray(Wv[:, PERM].T.astype(np.float32))
    WOT = np.ascontiguousarray(Wo[PERM, :].T.astype(np.float32))
    PKGa = np.asarray(L0_ffn_gate_up_packed).T
    PKDa = np.asarray(L0_ffn_down_packed).T
    SCGa = np.repeat(np.asarray(L0_ffn_gate_up_scales), 16, axis=0)
    SCDa = np.repeat(np.asarray(L0_ffn_down_scales), 16, axis=0)
    chunks = [("g", cb, oc) for cb in range(2) for oc in range(22)] + \
             [("d", cb, oc) for cb in range(6) for oc in range(4)]
    PKS_all = np.zeros((72, 128, 512), dtype=np.int32)
    SCS_all = np.zeros((72, 128, 512), dtype=np.float16)
    for g, (kind, cb, oc) in enumerate(chunks):
        if kind == "g":
            PKS_all[g] = PKGa[128 * cb:128 * cb + 128, 512 * oc:512 * (oc + 1)]
            SCS_all[g] = SCGa[128 * cb:128 * cb + 128, 512 * oc:512 * (oc + 1)]
        else:
            rows = 64 if cb == 5 else 128
            PKS_all[g, 0:rows] = PKDa[128 * cb:128 * cb + rows, 512 * oc:512 * (oc + 1)]
            SCS_all[g, 0:rows] = SCDa[128 * cb:128 * cb + rows, 512 * oc:512 * (oc + 1)]

    inv = 1.0 / (10000.0 ** (np.arange(0, DR, 2, dtype=np.float64) / DR))
    t = np.arange(S, dtype=np.float64)
    fr = np.outer(t, inv)
    cosT = np.cos(fr).astype(np.float32)
    sinT = np.sin(fr).astype(np.float32)

    in_maps = []
    for k in range(8):
        bb_, hh_ = k // 2, k % 2
        t0 = TOK * hh_
        tokord = np.concatenate([np.arange(t0, t0 + TOK),
                                 np.arange((1 - hh_) * TOK, (1 - hh_) * TOK + TOK)])
        XTk = np.ascontiguousarray(x[bb_][tokord][:, PERM].T.astype(np.float32))
        ck = cosT[tokord].T
        sk = sinT[tokord].T
        COSRk = np.ascontiguousarray(np.tile(ck, (4, 1)).astype(np.float32))
        SINAk = np.ascontiguousarray(
            np.concatenate([-sk, sk, -sk, sk], axis=0).astype(np.float32))
        gidx = [24 * (l // 3) + 3 * k + (l % 3) for l in range(9)]
        in_maps.append({
            "XT": XTk, "WQT": WQT, "WKT": WKT, "WVT": WVT, "WOT": WOT,
            "PKS": np.ascontiguousarray(PKS_all[gidx]),
            "SCS": np.ascontiguousarray(SCS_all[gidx]),
            "COSR": COSRk, "SINA": SINAk,
        })
    return in_maps


def kernel(L0_attn_hidden_states, L0_attn_Wq, L0_attn_Wk, L0_attn_Wv, L0_attn_Wo,
           L0_ffn_gate_up_packed, L0_ffn_gate_up_scales, L0_ffn_down_packed,
           L0_ffn_down_scales):
    in_maps = _stage(L0_attn_hidden_states, L0_attn_Wq, L0_attn_Wk, L0_attn_Wv,
                     L0_attn_Wo, L0_ffn_gate_up_packed, L0_ffn_gate_up_scales,
                     L0_ffn_down_packed, L0_ffn_down_scales)
    nc = _get_nc(debug=_DEBUG)
    res = run_bass_kernel_spmd(nc, in_maps, core_ids=list(range(8)))
    out = np.empty((B, S, D), dtype=np.float32)
    for k in range(8):
        bb_, hh_ = k // 2, k % 2
        o = np.asarray(res.results[k]["OUT"]).reshape(D, TOK)
        arr = np.empty((TOK, D), dtype=np.float32)
        arr[:, PERM] = o.T
        out[bb_, TOK * hh_:TOK * (hh_ + 1), :] = arr
    if _DEBUG:
        kernel._last = res
    return out


# version 10
# speedup vs baseline: 1.4486x; 1.0072x over previous
import contextlib
import numpy as np
import ml_dtypes

import concourse.bass as bass
import concourse.mybir as mybir
from concourse.bass_utils import run_bass_kernel_spmd

F32 = mybir.dt.float32
BF16 = mybir.dt.bfloat16
I32 = mybir.dt.int32
F16 = mybir.dt.float16
ALU = mybir.AluOpType
AF = mybir.ActivationFunctionType
BF = ml_dtypes.bfloat16

B, S, D = 4, 2048, 2048
H, DN, DR, DV = 16, 128, 64, 128
NI = 5632
TOK = 1024
NT = 16
NKT = 16
NP = 8
GU = 2 * NI
NIT = 44
LN_EPS = 1e-5
RMS_EPS = 1e-6
SC_NOPE = float(1.0 / np.sqrt(128.0))
SC_ROPE = 0.125

_r = np.arange(D)
PERM = (1024 * (_r // 1024) + 8 * (_r % 128) + (_r // 128) % 8).astype(np.int64)

ENGINES = ("sync", "tensor", "vector", "scalar", "gpsimd")
_DEBUG = False


class Sem:
    __slots__ = ("h", "n", "name")

    def __init__(self, h, name):
        self.h, self.n, self.name = h, 0, name


class Bld:
    def __init__(self):
        self.nc = bass.Bass("TRN2", target_bir_lowering=False)
        self.plan = {e: [] for e in ENGINES}
        self.es = contextlib.ExitStack()
        self.sems = {}

    def sem(self, name):
        if name not in self.sems:
            self.sems[name] = Sem(self.es.enter_context(self.nc.semaphore(name)), name)
        return self.sems[name]

    def op(self, eng, make, waits=(), inc=None, incn=1):
        wl = [(s.h, int(v)) for (s, v) in waits if v > 0]
        ih = None
        if inc is not None:
            inc.n += incn
            ih = inc.h

        def fn(e, make=make, wl=wl, ih=ih, iv=incn):
            for (h, v) in wl:
                e.wait_ge(h, v)
            ins = make(e)
            if ih is not None:
                ins.then_inc(ih, iv)
        self.plan[eng].append(fn)
        return inc.n if inc is not None else None

    def dma(self, eng, out, in_, sem, waits=()):
        return self.op(eng, (lambda e, o=out, i=in_: e.dma_start(out=o, in_=i)),
                       waits=waits, inc=sem, incn=16)

    def sb(self, name, shape, dtype, offset):
        base = (self.nc.sbuf_base + 31) & ~31
        return self.nc.alloc_sbuf_tensor_at(name, list(shape), dtype,
                                            offset=base + offset)

    def barrier(self):
        ph = self.sem("ph")
        ph.n += 5
        t = ph.n
        for e in ENGINES:
            def fn(eng, h=ph.h, t=t):
                eng.nop().then_inc(h, 1)
                eng.wait_ge(h, t)
            self.plan[e].append(fn)


def build_program(debug=False):
    b = Bld()
    nc = b.nc
    okind = "ExternalOutput" if debug is True else "Internal"

    XT = nc.dram_tensor("XT", [D, S], F32, kind="ExternalInput").ap()
    WQT = nc.dram_tensor("WQT", [D, 3072], F32, kind="ExternalInput").ap()
    WKT = nc.dram_tensor("WKT", [D, 3072], F32, kind="ExternalInput").ap()
    WVT = nc.dram_tensor("WVT", [D, 2048], F32, kind="ExternalInput").ap()
    WOT = nc.dram_tensor("WOT", [D, 2048], F32, kind="ExternalInput").ap()
    PKS = nc.dram_tensor("PKS", [9, 128, 512], I32, kind="ExternalInput").ap()
    SCS = nc.dram_tensor("SCS", [9, 128, 512], F16, kind="ExternalInput").ap()
    COSR = nc.dram_tensor("COSR", [128, S], F32, kind="ExternalInput").ap()
    SINA = nc.dram_tensor("SINA", [128, S], F32, kind="ExternalInput").ap()
    OUT = nc.dram_tensor("OUT", [NT, 128, TOK], F32, kind="ExternalOutput").ap()

    dbgO = nc.dram_tensor("dbgO", [128, NT, 1024], BF16, kind=okind).ap()
    dbgE = nc.dram_tensor("dbgE", [128, 64], F32, kind=okind).ap()
    dbgM = nc.dram_tensor("dbgM", [H, 1, 32], F32, kind=okind).ap()
    dbgS = nc.dram_tensor("dbgS", [72, 8, 128, 512], BF16, kind=okind).ap()
    dbgL2 = nc.dram_tensor("dbgL2", [128, NT, 1024], BF16, kind=okind).ap()
    dbgH = nc.dram_tensor("dbgH", [128, NIT, 1024], BF16, kind=okind).ap()
    dbgY = nc.dram_tensor("dbgY", [2, 128, NT, 512], F32, kind=okind).ap()
    dbgR = nc.dram_tensor("dbgR", [2 * H, 128, 512], F32, kind=okind).ap()
    k_sc = nc.dram_tensor("k_sc", [NP, 3, 128, S], BF16, kind=okind).ap()
    q_sc = nc.dram_tensor("q_sc", [NP, 3, 128, TOK], BF16, kind=okind).ap()
    v_sc = nc.dram_tensor("v_sc", [NP, NKT, 128, 256], BF16, kind=okind).ap()
    scr_l = nc.dram_tensor("scr_l", [9, 8, 128, 512], BF16, kind="Internal").ap()
    scr_f = nc.dram_tensor("scr_f", [72, 8, 128, 512], BF16, kind="Internal").ap()
    x2_sc = nc.dram_tensor("x2_sc", [NT, 128, TOK], F32, kind=("ExternalOutput" if debug in (True, 2) else "Internal")).ap()

    cells = b.sb("cells", [128, 32], F32, 0)
    ones_bf = b.sb("ones_bf", [128, 1], BF16, 128)
    ones_f32 = b.sb("ones_f32", [128, 1], F32, 160)
    ones1_f32 = b.sb("ones1_f32", [1, 128], F32, 192)
    negones1_f32 = b.sb("negones1_f32", [1, 128], F32, 207872)
    dbgE_sb = b.sb("dbgE_sb", [128, 16], F32, 208384)
    ones64A = b.sb("ones64A", [128, 1], BF16, 704)
    ones64B = b.sb("ones64B", [128, 1], BF16, 736)
    expbias = b.sb("expbias", [128, 16], F32, 768)
    rowbank = b.sb("rowbank", [128, 2048], F32, 1024)
    fbufA = b.sb("fbufA", [128, 2, 1024], F32, 9216)
    dq_t05 = b.sb("dq_t05", [128, 2, 1024], BF16, 9216)
    dq_smul = b.sb("dq_smul", [128, 2, 1024], BF16, 13312)
    wbufs = b.sb("wbufs", [128, 2, 16, 128], BF16, 17408)
    wpiece = b.sb("wpiece", [128, 3, 1024], BF16, 17408)
    tmpF = b.sb("tmpF", [128, 2048], F32, 17408)

    RD = 25600
    fbufB = b.sb("fbufB", [128, 2, 1024], F32, RD)
    pkb = b.sb("pkb", [128, 2, 512], I32, RD)
    scb = b.sb("scb", [128, 2, 512], BF16, RD + 4096)
    m32 = b.sb("m32", [128, 512], I32, RD + 6144)
    s32 = b.sb("s32", [128, 512], I32, RD + 8192)
    dq_r1 = b.sb("dq_r1", [128, 2, 1024], BF16, RD + 10240)
    dq_r2 = b.sb("dq_r2", [128, 2, 1024], BF16, RD + 14336)
    dq_t2 = b.sb("dq_t2", [128, 1024], BF16, RD + 18432)
    dq_u = b.sb("dq_u", [128, 1024], BF16, RD + 20480)
    wout = b.sb("wout", [128, 2, 2, 512], BF16, RD + 22528)

    R1 = 52224
    ln1 = b.sb("ln1", [128, NT, 2048], BF16, R1)
    sqtile = b.sb("sqtile", [128, 2048], BF16, R1)
    Pbuf = b.sb("Pbuf", [128, 2, NKT, 512], BF16, R1 + 8192)
    rBuf = b.sb("rBuf", [128, 512], F32, R1 + 40960)
    xb2 = b.sb("xb2", [128, 1024], BF16, R1)
    xsq2 = b.sb("xsq2", [128, 1024], BF16, R1 + 2048)
    ln2 = b.sb("ln2", [128, NT, 1024], BF16, R1)
    sgbuf = b.sb("sgbuf", [128, 8, 512], BF16, R1 + 32768)
    htmp = b.sb("htmp", [128, 512], BF16, R1 + 40960)
    ybuf = b.sb("ybuf", [128, NT, 512], F32, R1)
    ysqb = b.sb("ysqb", [128, 512], BF16, R1 + 32768)
    rrB = b.sb("rrB", [128, 512], F32, R1 + 33792)

    R2 = 117760
    xsq = b.sb("xsq", [128, 2, 2048], BF16, R2)
    xb = b.sb("xb", [128, 2, 2048], BF16, R2 + 8192)
    xsA = b.sb("xsA", [128, 2, 2048], F32, R2 + 16384)
    cosR = b.sb("cosR", [128, 2048], F32, R2 + 32768)
    sinA = b.sb("sinA", [128, 2048], F32, R2 + 40960)
    aB = b.sb("aB", [128, 2048], F32, R2 + 49152)
    bB = b.sb("bB", [128, 2048], F32, R2 + 57344)
    estg = b.sb("estg", [128, 512], BF16, R2 + 65536)
    s_rope = b.sb("s_rope", [128, 512], BF16, R2 + 66560)
    sh_rope = b.sb("sh_rope", [128, 512], BF16, R2 + 67584)
    ropetmp = b.sb("ropetmp", [128, 512], F32, R2 + 68608)
    ropetmp2 = b.sb("ropetmp2", [128, 512], F32, R2 + 70656)
    wvb = b.sb("wvb", [128, 16, 512], BF16, R2 + 72704)
    kb = b.sb("kb", [128, 2, 3, 2048], BF16, R2)
    qb = b.sb("qb", [128, 2, 3, 1024], BF16, R2 + 24576)
    vb = b.sb("vb", [128, 2, NKT, 256], BF16, R2 + 36864)
    Obuf = b.sb("Obuf", [128, NT, 1024], BF16, R2 + 53248)
    aB2 = b.sb("aB2", [128, 1024], F32, R2)
    bB2 = b.sb("bB2", [128, 1024], F32, R2 + 4096)
    Hbuf = b.sb("Hbuf", [128, NIT, 1024], BF16, R2)

    psAll = b.es.enter_context(nc.psum_tensor("psAll", [128, 4096], F32))

    def pb(bank):
        return psAll[:, 512 * bank:512 * (bank + 1)]

    def pbr(bank, r0, r1, c0=0, c1=512):
        return psAll[r0:r1, 512 * bank + c0:512 * bank + c1]

    dXp = [b.sem("dX0"), b.sem("dX1")]
    dWp = [b.sem("dW0"), b.sem("dW1")]
    dWV = b.sem("dWV"); dTb = b.sem("dTb")
    dPKp = [b.sem("dPK0"), b.sem("dPK1")]
    dSCp = [b.sem("dSC0"), b.sem("dSC1")]
    dKW = b.sem("dKW")
    dKRp = [b.sem("dKR0"), b.sem("dKR1")]
    dQWp = [b.sem("dQW0"), b.sem("dQW1")]
    dFRp = [b.sem("dFR0"), b.sem("dFR1"), b.sem("dFR2")]
    dX2W = b.sem("dX2W"); dX2R = b.sem("dX2R")
    dOse = b.sem("dOse"); dSh = b.sem("dSh")
    tMM = b.sem("tMM"); vEv = b.sem("vEv"); sEv = b.sem("sEv")
    dqa = b.sem("dqa"); dqv = b.sem("dqv"); dqx = b.sem("dqx")

    bank_guard = {i: [] for i in range(8)}

    def mm(bank, lhsT, rhs, start, stop, out_rows=None, waits=()):
        w = list(waits)
        if start:
            w += bank_guard[bank]
            bank_guard[bank] = []
        if out_rows is None:
            out = psAll[:, 512 * bank:512 * bank + rhs.shape[-1]]
        else:
            out = psAll[out_rows[0]:out_rows[1], 512 * bank:512 * bank + rhs.shape[-1]]
        return b.op("tensor",
                    lambda e, o=out, l=lhsT, r=rhs, st=start, sp=stop:
                    e.matmul(o, l, r, start=st, stop=sp),
                    waits=w, inc=tMM)

    def ldw(wap, waits=()):
        return b.op("tensor", lambda e, w=wap: e.ldweights(w), waits=waits)

    for ci, cv in {0: -2.0, 1: 0.5, 2: -6.0, 3: 1.0, 4: LN_EPS, 5: RMS_EPS,
                   6: 1.0 / 2048.0, 7: 1.0, 8: 2.0, 9: SC_NOPE}.items():
        b.op("vector", lambda e, ci=ci, cv=cv: e.memset(cells[:, ci:ci + 1], cv))
    b.op("vector", lambda e: e.memset(ones_bf[:], 1.0))
    b.op("vector", lambda e: e.memset(ones_f32[:], 1.0))
    b.op("vector", lambda e: e.memset(ones1_f32[:], 1.0))
    b.op("vector", lambda e: e.memset(negones1_f32[:], -1.0))
    b.op("vector", lambda e: e.memset(ones64A[0:64, :], 1.0))
    b.op("vector", lambda e: e.memset(ones64A[64:128, :], 0.0))
    b.op("vector", lambda e: e.memset(ones64B[0:64, :], 0.0))
    consts_done = b.op("vector", lambda e: e.memset(ones64B[64:128, :], 1.0), inc=vEv)

    b.dma("sync", cosR[:], COSR, dTb)
    tables_done = b.dma("sync", sinA[:], SINA, dTb)

    N_DQ_CH = 9
    N_DQ_UNITS = 4 * N_DQ_CH
    dqs = {"u": 0, "pk_t": None, "sc_t": None, "a_t": 0, "pending2": None,
           "qw_done": [None] * N_DQ_UNITS}

    def emit_dq_unit():
        u = dqs["u"]
        if u >= N_DQ_UNITS:
            return False
        ci, jp = u // 4, u % 4
        slot = ci % 2
        par = u % 2
        rows = 128
        if jp == 0:
            dqs["pk_t"] = (dPKp[slot], b.dma("scalar", pkb[:, slot, :], PKS[ci],
                           dPKp[slot], waits=([(dqx, 4 * (ci - 1))] if ci >= 2 else [])))
            dqs["sc_t"] = (dSCp[slot], b.dma("gpsimd", scb[:, slot, :], SCS[ci],
                           dSCp[slot], waits=([(dqv, 4 * (ci - 1))] if ci >= 2 else [])))
        xw = [dqs["pk_t"]] if jp == 0 else []
        if u >= 2:
            xw.append((dqa, u - 1))
        for jh in range(2):
            j = 2 * jp + jh
            b.op("vector", lambda e, sl=slot, j=j:
                 e.tensor_scalar(m32[:, :], pkb[:, sl, :], 4 * j, 7,
                                 ALU.logical_shift_right, ALU.bitwise_and),
                 waits=(xw if jh == 0 else []))
            b.op("vector", lambda e, sl=slot, j=j:
                 e.tensor_scalar(s32[:, :], pkb[:, sl, :], 4 * j + 3, 1,
                                 ALU.logical_shift_right, ALU.bitwise_and))
            b.op("vector", lambda e, p=par, jh=jh:
                 e.tensor_scalar(dq_t05[:, p, 512 * jh:512 * (jh + 1)],
                                 m32[:, :], 0.5, None, ALU.mult))
            b.op("vector", lambda e, p=par, jh=jh:
                 e.tensor_scalar(dq_smul[:, p, 512 * jh:512 * (jh + 1)],
                                 s32[:, :], -2.0, 1.0, ALU.mult, ALU.add),
                 inc=(dqx if jh == 1 else None))
        x_t = dqx.n
        sw = [(dqx, x_t)]
        if u == 0:
            sw.append((vEv, consts_done))
        if u >= 2:
            sw.append((dqv, u - 1))
        b.op("scalar", lambda e, p=par: e.activation(
            dq_r1[:, p, :], dq_t05[:, p, :], AF.Relu,
            bias=cells[:, 0:1], scale=cells[:, 3:4]),
            waits=sw)
        a_t = b.op("scalar", lambda e, p=par: e.activation(
            dq_r2[:, p, :], dq_t05[:, p, :], AF.Relu,
            bias=cells[:, 2:3], scale=cells[:, 8:9]),
            inc=dqa)
        dqs["a_t"] = a_t
        def stage2(u=u, slot=slot, par=par, a_t=a_t, sc_t=dqs["sc_t"]):
            ci_ = u // 4
            vw = [(dqa, a_t), sc_t]
            if u >= 2:
                vw.append(dqs["qw_done"][u - 2])
            b.op("vector", lambda e, p=par: e.tensor_add(
                dq_u[:, :], dq_t05[:, p, :], dq_r1[:, p, :]), waits=vw)
            b.op("vector", lambda e, p=par: e.tensor_add(
                dq_t2[:, :], dq_u[:, :], dq_r2[:, p, :]))
            b.op("vector", lambda e, p=par: e.tensor_mul(
                dq_u[:, :], dq_t2[:, :], dq_smul[:, p, :]))
            b.op("vector", lambda e, sl=slot, p=par: e.tensor_mul(
                wout[:, p, 0, :], dq_u[:, 0:512], scb[:, sl, :]))
            v_t = b.op("vector", lambda e, sl=slot, p=par: e.tensor_mul(
                wout[:, p, 1, :], dq_u[:, 512:1024], scb[:, sl, :]),
                inc=dqv)
            jp_ = u % 4
            for jh in range(2):
                j = 2 * jp_ + jh
                b.dma("sync", scr_l[ci_, j], wout[:, par, jh, :], dQWp[par],
                      waits=([(dqv, v_t)] if jh == 0 else []))
            dqs["qw_done"][u] = (dQWp[par], dQWp[par].n)
        if dqs["pending2"] is not None:
            pend = dqs["pending2"]
            dqs["pending2"] = stage2
            pend()
        else:
            dqs["pending2"] = stage2
        dqs["u"] = u + 1
        return True

    def emit_dq_flush():
        if dqs["pending2"] is not None:
            dqs["pending2"]()
            dqs["pending2"] = None

    def emit_dq_units(n):
        for _ in range(n):
            if not emit_dq_unit():
                return

    XT_t = XT.rearrange("(t p) n -> t p n", p=128)
    v_cnt = [0] * NT
    s_cnt = [0] * NT
    t_cnt = [0] * NT
    x_cnt = [0] * NT
    for tau in range(NT):
        sl = tau % 2
        wv = []
        if tau >= 2:
            wv = [(vEv, v_cnt[tau - 2]), (sEv, s_cnt[tau - 2])]
        x_cnt[tau] = b.dma("sync", xsA[:, sl, :], XT_t[tau], dXp[sl], waits=wv)
        wv2 = [(dXp[sl], x_cnt[tau])]
        if tau >= 2:
            wv2.append((tMM, t_cnt[tau - 2]))
        v_cnt[tau] = b.op("vector", lambda e, sl=sl: e.tensor_copy(xb[:, sl, :], xsA[:, sl, :]),
                          waits=wv2, inc=vEv)
        s_cnt[tau] = b.op("scalar", lambda e, sl=sl: e.activation(
            xsq[:, sl, :], xsA[:, sl, :], AF.Square), waits=wv2, inc=sEv)
        mw = [(vEv, v_cnt[tau]), (sEv, s_cnt[tau])]
        for c in range(4):
            mm(c, ones_bf[:], xb[:, sl, 512 * c:512 * (c + 1)],
               start=(tau == 0), stop=(tau == NT - 1), out_rows=(0, 1),
               waits=(mw if c == 0 else []))
        for c in range(4):
            t_cnt[tau] = mm(4 + c, ones_bf[:], xsq[:, sl, 512 * c:512 * (c + 1)],
                            start=(tau == 0), stop=(tau == NT - 1), out_rows=(0, 1))
    stats_mm_done = t_cnt[NT - 1]

    sA, sB, sC, sD = (rowbank[0:1, 0:512], rowbank[0:1, 512:1024],
                      rowbank[0:1, 1024:1536], rowbank[0:1, 1536:2048])

    def ln_coeffs(nchunk, sum_banks, sq_banks, eps_cell, dst_a, dst_b, first_wait):
        abc = 0
        prev_bt = 0
        for c in range(nchunk):
            bs, bq = sum_banks[c], sq_banks[c]
            w0 = list(first_wait) if c == 0 else [(tMM, prev_bt)]
            cp1 = b.op("scalar", lambda e, bs=bs: e.copy(sA, pbr(bs, 0, 1)),
                       waits=w0, inc=sEv)
            cp2 = b.op("scalar", lambda e, bq=bq: e.copy(sB, pbr(bq, 0, 1)), inc=sEv)
            b.op("vector", lambda e: e.tensor_scalar(sC, sA, 1.0 / 2048.0, None,
                                                     ALU.mult), waits=[(sEv, cp2)])
            b.op("vector", lambda e: e.tensor_scalar(sD, sB, 1.0 / 2048.0, None,
                                                     ALU.mult))
            b.op("vector", lambda e: e.tensor_mul(sA, sC, sC))
            var_v = b.op("vector", lambda e: e.tensor_sub(sB, sD, sA), inc=vEv)
            sq_v = b.op("scalar", lambda e, ec=eps_cell: e.activation(
                sD, sB, AF.Sqrt, bias=cells[0:1, ec:ec + 1], scale=cells[0:1, 7:8]),
                waits=[(vEv, var_v)], inc=sEv)
            b.op("vector", lambda e: e.reciprocal(sA, sD), waits=[(sEv, sq_v)])
            ab_v = b.op("vector", lambda e: e.scalar_tensor_tensor(
                sB, sC, -1.0, sA, ALU.mult, ALU.mult), inc=vEv)
            mm(bs, ones1_f32[:], sA, True, True,
               waits=[(vEv, ab_v), (sEv, cp2)])
            bt = mm(bq, ones1_f32[:], sB, True, True)
            prev_bt = bt
            a1 = b.op("scalar", lambda e, c=c, bs=bs: e.copy(
                dst_a[:, 512 * c:512 * (c + 1)], pb(bs)), waits=[(tMM, bt)], inc=sEv)
            abc = b.op("scalar", lambda e, c=c, bq=bq: e.copy(
                dst_b[:, 512 * c:512 * (c + 1)], pb(bq)), inc=sEv)
            bank_guard[bs] = [(sEv, abc)]
            bank_guard[bq] = [(sEv, abc)]
        return abc

    abc2 = ln_coeffs(4, [0, 1, 2, 3], [4, 5, 6, 7], 4, aB, bB,
                     [(tMM, stats_mm_done)])

    ln_v = [0] * NT
    for tau in range(NT):
        wv = [(sEv, abc2)] if tau == 0 else [(vEv, ln_v[tau - 1])]
        x_t = b.dma("sync", xsA[:, 0, :], XT_t[tau], dXp[0], waits=wv)
        b.op("vector", lambda e: e.tensor_mul(tmpF[:], xsA[:, 0, :], aB[:]),
             waits=[(dXp[0], x_t)])
        ln_v[tau] = b.op("vector", lambda e, tau=tau: e.tensor_add(
            ln1[:, tau, :], tmpF[:], bB[:]), inc=vEv)
    ln_done = ln_v[NT - 1]

    kw_done = [0] * NP
    vw_done = [0] * NP

    rope_state = {"v": 0}

    def rope_evac(bank, scale_c, tab_sl, dst, stg_guard):
        sc_v = b.op("scalar", lambda e, bk=bank: e.copy(s_rope[:], pb(bk)),
                    waits=list(stg_guard) + [(vEv, rope_state["v"])], inc=sEv)
        sh_t = 0
        for (d0, s0) in ((0, 32), (32, 0), (64, 96), (96, 64)):
            sh_t = b.dma("sync", sh_rope[d0:d0 + 32, :], s_rope[s0:s0 + 32, :], dSh,
                         waits=([(sEv, sc_v), (vEv, rope_state["v"])] if d0 == 0 else []))
        b.op("vector", lambda e, ts=tab_sl, c=scale_c: e.scalar_tensor_tensor(
            ropetmp[:], s_rope[:], c, cosR[:, ts[0]:ts[1]], ALU.mult, ALU.mult),
            waits=[(dSh, sh_t), (dTb, tables_done)])
        b.op("vector", lambda e, ts=tab_sl, c=scale_c: e.scalar_tensor_tensor(
            ropetmp2[:], sh_rope[:], c, sinA[:, ts[0]:ts[1]], ALU.mult, ALU.mult))
        ev = b.op("vector", lambda e: e.tensor_add(estg[:], ropetmp[:], ropetmp2[:]),
                  inc=vEv)
        rope_state["v"] = ev
        w_t = b.dma("sync", dst, estg[:], dKW, waits=[(vEv, ev)])
        return (sEv, sc_v), (dKW, w_t)

    prev_w_used = 0
    wslot_free = [(), ()]
    estg_guard = []
    for which, WT, dsts, ntc, rope_scale in (
        ("q", WQT, q_sc, 2, SC_ROPE),
        ("k", WKT, k_sc, 4, 1.0),
    ):
        WT_r = WT.rearrange("(t p) d -> p t d", p=128)
        for dc in range(24):
            hp, ct = dc // 3, dc % 3
            sl = dc % 2
            wv0 = list(wslot_free[sl])
            if which == "q" and dc < 2:
                wv0.append((vEv, ln_done))
            w_t = b.dma("gpsimd", wbufs[:, sl], WT_r[:, :, 128 * dc:128 * (dc + 1)],
                        dWp[sl], waits=wv0)
            banks = [2 * (dc % 4), 2 * (dc % 4) + 1] if ntc == 2 else \
                    ([0, 1, 2, 3] if dc % 2 == 0 else [4, 5, 6, 7])
            last_ldw_t = None
            for it in range(NT):
                lw = [(dWp[sl], w_t)] if it == 0 else []
                if it == 0:
                    lw.append((vEv, ln_done))
                ldw(wbufs[:, sl, it, :], waits=lw)
                for tc in range(ntc):
                    mm(banks[tc], wbufs[:, sl, it, :],
                       ln1[:, it, 512 * tc:512 * (tc + 1)],
                       start=(it == 0), stop=(it == NT - 1))
            mm_done = tMM.n
            wslot_free[sl] = [(tMM, mm_done)]
            for tc in range(ntc):
                bk = banks[tc]
                if ct < 2:
                    if which == "q":
                        ev = b.op("scalar", lambda e, bk=bk: e.mul(
                            estg[:], pb(bk), cells[:, 9:10]),
                            waits=[(tMM, mm_done)] + estg_guard, inc=sEv)
                        gsem = (sEv, ev)
                    else:
                        ev = b.op("scalar", lambda e, bk=bk: e.copy(estg[:], pb(bk)),
                                  waits=[(tMM, mm_done)] + estg_guard, inc=sEv)
                        gsem = (sEv, ev)
                    w2 = b.dma("sync", dsts[hp, ct, :, 512 * tc:512 * (tc + 1)],
                               estg[:], dKW, waits=[gsem])
                    bank_guard[bk] = [gsem]
                    estg_guard = [(dKW, w2)]
                else:
                    pg, wg = rope_evac(bk, rope_scale, (512 * tc, 512 * (tc + 1)),
                                       dsts[hp, 2, :, 512 * tc:512 * (tc + 1)],
                                       [(tMM, mm_done)] + estg_guard)
                    bank_guard[bk] = [pg]
                    estg_guard = [wg]
            if which == "k" and ct == 2:
                kw_done[hp] = dKW.n
            emit_dq_units(1)

    WVT_r = WVT.rearrange("(t p) d -> p t d", p=128)
    wv_free = []
    for dvc in range(4):
        w_t = b.dma("gpsimd", wvb[:], WVT_r[:, :, 512 * dvc:512 * (dvc + 1)], dWV,
                    waits=wv_free)
        for kc in range(NKT):
            bk = 4 + kc % 4
            for it in range(NT):
                lw = [(dWV, w_t), (vEv, ln_done)] if it == 0 else []
                ldw(ln1[:, it, 128 * kc:128 * (kc + 1)], waits=lw)
                mm(bk, ln1[:, it, 128 * kc:128 * (kc + 1)], wvb[:, it, :],
                   start=(it == 0), stop=(it == NT - 1))
            mm_done = tMM.n
            ev = b.op("scalar", lambda e, bk=bk: e.copy(estg[:], pb(bk)),
                      waits=[(tMM, mm_done)] + estg_guard, inc=sEv)
            bank_guard[bk] = [(sEv, ev)]
            w2 = 0
            for s in range(2):
                w2 = b.dma("sync", v_sc[2 * dvc + s, kc, :, :],
                           estg[:, 256 * s:256 * (s + 1)], dKW,
                           waits=([(sEv, ev)] if s == 0 else []))
            estg_guard = [(dKW, w2)]
            if kc % 4 == 3:
                emit_dq_units(1)
        wv_free = [(tMM, tMM.n)]
    for hp in range(NP):
        vw_done[hp] = dKW.n

    b.barrier()

    dkw_final = dKW.n
    pair_cons = [(), ()]
    p_free2 = [[], []]
    o_evac_v = 0
    sq_guard_outer = [0]
    eb_guard = [0]
    rb_guard = [0]
    bound_g = [0]
    for hp in range(NP):
        sl = hp % 2
        kr1 = b.dma("sync", kb[:, sl], k_sc[hp].rearrange("c p n -> p c n"), dKRp[sl],
                    waits=list(pair_cons[sl]) + [(dKW, dkw_final)])
        kr2 = b.dma("sync", qb[:, sl], q_sc[hp].rearrange("c p n -> p c n"), dKRp[sl])
        kr3 = b.dma("sync", vb[:, sl], v_sc[hp].rearrange("t p n -> p t n"), dKRp[sl])
        dKR = dKRp[sl]
        kqv_t = dKRp[sl].n
        sq_guard = sq_guard_outer[0]
        for hh in range(2):
            h = 2 * hp + hh
            o64 = ones64A if hh == 0 else ones64B
            for (which_k, kc_hh, dstcell) in ((0, hh, 20), (2, 2, 21)):
                ck = kc_hh if which_k == 0 else 2
                sq_s = b.op("scalar", lambda e, sl=sl, ck=ck: e.activation(
                    sqtile[:], kb[:, sl, ck, :], AF.Square),
                    waits=[(dKR, kqv_t), (tMM, sq_guard), (vEv, bound_g[0])],
                    inc=sEv)
                oo = ones_bf if which_k == 0 else o64
                for c in range(4):
                    m_t = mm(5, oo[:], sqtile[:, 512 * c:512 * (c + 1)], True, True,
                             out_rows=(0, 1), waits=[(sEv, sq_s)] if c == 0 else [])
                    c_t = b.op("scalar", lambda e, c=c: e.copy(
                        rowbank[0:1, 512 * c:512 * (c + 1)], pbr(5, 0, 1)),
                        waits=[(tMM, m_t), (vEv, bound_g[0]), (tMM, rb_guard[0])],
                        inc=sEv)
                    bank_guard[5] = [(sEv, c_t)]
                sq_guard = m_t
                bound_g[0] = b.op("vector", lambda e, dc=dstcell: e.reduce_max(
                    cells[0:1, dc:dc + 1], rowbank[0:1, 0:2048],
                    mybir.AxisListType.X), waits=[(sEv, c_t)], inc=vEv)
            kx_s = b.op("scalar", lambda e: e.activation(
                cells[0:1, 22:24], cells[0:1, 20:22], AF.Sqrt),
                waits=[(vEv, bound_g[0])], inc=sEv)
            sqn_s = b.op("scalar", lambda e, sl=sl, hh=hh: e.activation(
                sqtile[:, 0:1024], qb[:, sl, hh, :], AF.Square),
                waits=[(tMM, sq_guard)], inc=sEv)
            sqr_s = b.op("scalar", lambda e, sl=sl: e.activation(
                sqtile[:, 1024:2048], qb[:, sl, 2, :], AF.Square), inc=sEv)
            for c in range(2):
                mseg = rowbank[0:1, 512 * c:512 * (c + 1)]
                m1 = mm(5, ones_bf[:], sqtile[:, 512 * c:512 * (c + 1)], True, True,
                        out_rows=(0, 1), waits=[(sEv, sqn_s)] if c == 0 else [])
                c1 = b.op("scalar", lambda e: e.copy(sC, pbr(5, 0, 1)),
                          waits=[(tMM, m1), (vEv, bound_g[0])], inc=sEv)
                bank_guard[5] = [(sEv, c1)]
                m2 = mm(5, o64[:], sqtile[:, 1024 + 512 * c:1536 + 512 * c], True, True,
                        out_rows=(0, 1), waits=[(sEv, sqr_s)])
                c2 = b.op("scalar", lambda e: e.copy(sD, pbr(5, 0, 1)),
                          waits=[(tMM, m2), (tMM, rb_guard[0])], inc=sEv)
                bank_guard[5] = [(sEv, c2)]
                sq_guard = m2
                b.op("scalar", lambda e, ms=mseg: e.activation(ms, sC, AF.Sqrt))
                qs_s = b.op("scalar", lambda e: e.activation(sC, sD, AF.Sqrt),
                            inc=sEv)
                b.op("vector", lambda e, ms=mseg: e.tensor_scalar(
                    sD, ms, cells[0:1, 22:23], None, ALU.mult),
                    waits=[(sEv, qs_s), (sEv, kx_s)])
                bound_g[0] = b.op("vector", lambda e, ms=mseg: e.scalar_tensor_tensor(
                    ms, sC, cells[0:1, 23:24], sD, ALU.mult, ALU.add), inc=vEv)
            mx_v = b.op("vector", lambda e: e.reduce_max(
                cells[0:1, 10:11], rowbank[0:1, 0:1024], mybir.AxisListType.X),
                waits=[(tMM, eb_guard[0])], inc=vEv)
            bound_g[0] = mx_v
            m_t = mm(7, negones1_f32[:], cells[0:1, 10:11], True, True,
                     waits=[(vEv, mx_v)])
            eb_guard[0] = m_t
            eb_s = b.op("scalar", lambda e, h=h: e.copy(
                expbias[:, h:h + 1], pbr(7, 0, 128, 0, 1)),
                waits=[(tMM, m_t)], inc=sEv)
            bank_guard[7] = [(sEv, eb_s)]
            if debug is True:
                b.dma("sync", dbgM[h], cells[0:1, :], dKW,
                      waits=[(sEv, eb_s), (vEv, bound_g[0])])
                if hp == 0 and hh == 1:
                    b.dma("sync", dbgE[:, 16:32], expbias[:], dKW,
                          waits=[(sEv, eb_s)])
                ps7 = b.op("scalar", lambda e, h=h: e.copy(
                    dbgE_sb[:, h:h + 1], pbr(7, 0, 128, 0, 1)), inc=sEv)
                if hp == 3 and hh == 1:
                    b.dma("sync", dbgE[:, 32:48], expbias[:], dKW,
                          waits=[(sEv, ps7)])
            sq_guard_outer[0] = sq_guard
            emit_dq_units(1)
        for hh in range(2):
            h = 2 * hp + hh
            r0, r1_ = (0, 64) if hh == 0 else (64, 128)
            for qc in range(2):
                ppar = (2 * hh + qc) % 2
                sb_rot = [0, 1]
                exp_v = [0] * NKT
                for kt in range(NKT):
                    bk = sb_rot[kt % 2]
                    mm(bk, kb[:, sl, hh, 128 * kt:128 * (kt + 1)],
                       qb[:, sl, hh, 512 * qc:512 * (qc + 1)],
                       start=True, stop=False)
                    sc_t = mm(bk, kb[r0:r1_, sl, 2, 128 * kt:128 * (kt + 1)],
                              qb[r0:r1_, sl, 2, 512 * qc:512 * (qc + 1)],
                              start=False, stop=True)
                    e_t = b.op("scalar", lambda e, bk=bk, kt=kt, h=h, pp=ppar:
                               e.activation(Pbuf[:, pp, kt, :], pb(bk), AF.Exp,
                                            bias=expbias[:, h:h + 1]),
                        waits=[(tMM, sc_t), (sEv, eb_s)] +
                              (p_free2[ppar] if kt == 0 else []), inc=sEv)
                    exp_v[kt] = e_t
                    bank_guard[bk] = [(sEv, e_t)]
                for kt in range(NKT):
                    s_t = mm(2, ones_bf[:], Pbuf[:, ppar, kt, :], start=(kt == 0),
                             stop=(kt == NKT - 1), out_rows=(0, 1),
                             waits=[(sEv, exp_v[kt])])
                rc_v = b.op("vector", lambda e: e.reciprocal(sD, pbr(2, 0, 1)),
                            waits=[(tMM, s_t), (sEv, exp_v[NKT - 1]),
                                   (tMM, rb_guard[0])], inc=vEv)
                bank_guard[2] = [(vEv, rc_v)]
                m_t = mm(7, ones1_f32[:], sD, True, True, waits=[(vEv, rc_v)])
                rb_guard[0] = m_t
                rb_s = b.op("scalar", lambda e: e.copy(rBuf[:], pb(7)),
                            waits=[(tMM, m_t), (vEv, o_evac_v)], inc=sEv)
                bank_guard[7] = [(sEv, rb_s)]
                if debug:
                    b.dma("sync", dbgR[2 * h + qc], rBuf[:], dKW,
                          waits=[(sEv, rb_s)])
                ob = 3 + (qc % 2)
                for kt in range(NKT):
                    av_t = mm(ob, vb[:, sl, kt, 128 * hh:128 * (hh + 1)],
                              Pbuf[:, ppar, kt, :], start=(kt == 0),
                              stop=(kt == NKT - 1),
                              waits=[(sEv, exp_v[NKT - 1]), (dKR, kqv_t)]
                              if kt == 0 else [])
                o_evac_v = b.op("vector", lambda e, ob=ob, h=h, qc=qc: e.tensor_mul(
                    Obuf[:, h, 512 * qc:512 * (qc + 1)], pb(ob), rBuf[:]),
                    waits=[(tMM, av_t), (sEv, rb_s)], inc=vEv)
                bank_guard[ob] = [(vEv, o_evac_v)]
                p_free2[ppar] = [(tMM, av_t)]
                emit_dq_units(1)
        pair_cons[sl] = [(tMM, tMM.n)]
    while emit_dq_unit():
        pass
    emit_dq_flush()
    ccS = b.sem("ccS")
    for t_ in range(3):
        b.op("gpsimd", lambda e, t_=t_: e.collective_compute(
            "AllGather", ALU.bypass, replica_groups=[list(range(8))],
            ins=[scr_l[3 * t_:3 * t_ + 3]], outs=[scr_f[24 * t_:24 * t_ + 24]]),
            waits=[(dQWp[0], dQWp[0].n), (dQWp[1], dQWp[1].n)],
            inc=ccS)
    cc_done = ccS.n
    gu_w_final = [(ccS, cc_done)]
    if debug is True:
        for blk8 in range(4):
            b.dma("sync", dbgS[18 * blk8:18 * (blk8 + 1)],
                  scr_f[18 * blk8:18 * (blk8 + 1)], dKW,
                  waits=([(ccS, cc_done)] if blk8 == 0 else []))
    if debug is True:
        b.dma("sync", dbgO, Obuf[:], dKW, waits=[(vEv, o_evac_v)])
        b.dma("sync", dbgE[:, 0:16], expbias[:], dKW)
        b.dma("sync", dbgE[:, 48:64], dbgE_sb[:, 0:16], dKW)

    b.barrier()

    WOT_r = WOT.rearrange("(t p) d -> p t d", p=128)
    wslot_free = [(), ()]
    xslot_guard = [[(dqv, dqv.n)], [(dqv, dqv.n)]]
    st_t = 0
    x2w = [0] * NT
    for ht in range(NT):
        sl = ht % 2
        fb = fbufA if ht % 2 == 0 else fbufB
        w_t = b.dma("gpsimd", wbufs[:, sl], WOT_r[:, :, 128 * ht:128 * (ht + 1)],
                    dWp[sl], waits=wslot_free[sl])
        x_t = b.dma("sync", fb[:, 0, :], XT_t[ht][:, 0:TOK], dXp[sl],
                    waits=xslot_guard[sl])
        banks = [2 * (ht % 2), 2 * (ht % 2) + 1]
        for it in range(NT):
            lw = [(dWp[sl], w_t)] if it == 0 else []
            ldw(wbufs[:, sl, it, :], waits=lw)
            for tc in range(2):
                mm(banks[tc], wbufs[:, sl, it, :], Obuf[:, it, 512 * tc:512 * (tc + 1)],
                   start=(it == 0), stop=(it == NT - 1))
        mm_done = tMM.n
        wslot_free[sl] = [(tMM, mm_done)]
        for tc in range(2):
            wv = [(tMM, mm_done), (dXp[sl], x_t)]
            if tc == 0 and ht > 1:
                wv.append((dX2W, x2w[ht - 2]))
            b.op("vector", lambda e, fb=fb, bk=banks[tc], tc=tc: e.tensor_add(
                fb[:, 1, 512 * tc:512 * (tc + 1)], pb(bk),
                fb[:, 0, 512 * tc:512 * (tc + 1)]),
                waits=wv)
        x2_v = b.op("vector", lambda e, fb=fb: e.tensor_copy(xb2[:], fb[:, 1, :]),
                    inc=vEv)
        bank_guard[banks[0]] = [(vEv, x2_v)]
        bank_guard[banks[1]] = [(vEv, x2_v)]
        x2_s = b.op("scalar", lambda e, fb=fb: e.activation(xsq2[:], fb[:, 1, :],
                    AF.Square), waits=[(vEv, x2_v)], inc=sEv)
        for c in range(2):
            mm(4 + c, ones_bf[:], xb2[:, 512 * c:512 * (c + 1)],
               start=(ht == 0), stop=(ht == NT - 1), out_rows=(0, 1),
               waits=[(vEv, x2_v)] if c == 0 else [])
        for c in range(2):
            st_t = mm(6 + c, ones_bf[:], xsq2[:, 512 * c:512 * (c + 1)],
                      start=(ht == 0), stop=(ht == NT - 1), out_rows=(0, 1),
                      waits=[(sEv, x2_s)] if c == 0 else [])
        x2w[ht] = b.dma("sync", x2_sc[ht], fb[:, 1, :], dX2W, waits=[(vEv, x2_v)])
        xslot_guard[sl] = [(vEv, x2_v), (sEv, x2_s), (tMM, st_t)]

    ab2_s = ln_coeffs(2, [4, 5], [6, 7], 4, aB2, bB2, [(tMM, st_t)])
    ln2_v = 0
    for ht in range(NT):
        x_t = b.dma("sync", fbufA[:, 0, :], x2_sc[ht], dX2R,
                    waits=([(sEv, ab2_s), (dX2W, dX2W.n)] if ht == 0
                           else [(vEv, ln2_v)]))
        b.op("vector", lambda e: e.tensor_mul(fbufA[:, 1, :], fbufA[:, 0, :], aB2[:]),
             waits=[(dX2R, x_t)])
        ln2_v = b.op("vector", lambda e, ht=ht: e.tensor_add(
            ln2[:, ht, :], fbufA[:, 1, :], bB2[:]), inc=vEv)

    if debug is True:
        b.dma("sync", dbgL2, ln2[:], dKW, waits=[(vEv, ln2_v)])
    b.barrier()

    pc_free = [(), (), ()]
    htmp_guard = [0]
    hevac_g = [0]
    for cb in range(6):
        W = 512 if cb == 5 else 1024
        ncol = W // 8
        for qd in range(2):
            for half, base in (("g", 1024 * cb), ("u", NI + 1024 * cb)):
                piece_t = [0] * NT

                def d_piece(it, cb=cb, qd=qd, half=half, base=base, W=W):
                    psl = it % 3
                    wv = list(pc_free[psl])
                    if cb == 0 and qd == 0 and half == "g" and it < 2:
                        wv += gu_w_final
                    nch = W // 512
                    ci0 = 22 * (it // 8) + base // 512
                    srcap = scr_f[ci0:ci0 + nch, it % 8].rearrange("a p c -> p a c")
                    dstap = wpiece[:, psl, 0:W].rearrange("p (a c) -> p a c", c=512)
                    piece_t[it] = b.dma("scalar", dstap, srcap, dFRp[psl], waits=wv)
                d_piece(0)
                d_piece(1)
                d_piece(2)
                for it in range(NT):
                    psl = it % 3
                    for jj in range(4):
                        jq = 4 * qd + jj
                        lw = [(dFRp[psl], piece_t[it])] if jj == 0 else []
                        if cb == 0 and qd == 0 and half == "g" and it == 0 and jj == 0:
                            lw.append((vEv, ln2_v))
                        ldw(wpiece[:, psl, jq:W:8], waits=lw)
                        for tc in range(2):
                            mm(2 * jj + tc, wpiece[:, psl, jq:W:8],
                               ln2[:, it, 512 * tc:512 * (tc + 1)],
                               start=(it == 0), stop=(it == NT - 1),
                               out_rows=(0, ncol))
                    pc_free[psl] = [(tMM, tMM.n)]
                    if it + 3 < NT:
                        d_piece(it + 3)
                mm_done = tMM.n
                if half == "g":
                    sg_s = 0
                    for jj in range(4):
                        for tc in range(2):
                            sg_s = b.op("scalar", lambda e, bk=2 * jj + tc, jj=jj,
                                        tc=tc, nc_=ncol: e.activation(
                                            sgbuf[0:nc_, 2 * jj + tc, :],
                                            pbr(2 * jj + tc, 0, nc_), AF.Silu),
                                        waits=([(tMM, mm_done), (vEv, hevac_g[0])]
                                               if (jj == 0 and tc == 0) else []),
                                        inc=sEv)
                            bank_guard[2 * jj + tc] = [(sEv, sg_s)]
                else:
                    for jj in range(4):
                        jq = 4 * qd + jj
                        for tc in range(2):
                            bk = 2 * jj + tc
                            if cb < 5:
                                h_v = b.op("vector", lambda e, bk=bk, cb=cb, jq=jq,
                                           tc=tc: e.tensor_mul(
                                               Hbuf[:, 8 * cb + jq, 512 * tc:512 * (tc + 1)],
                                               pbr(bk, 0, 128), sgbuf[:, bk, :]),
                                           waits=[(tMM, mm_done), (sEv, sg_s)]
                                           if (jj == 0 and tc == 0) else [], inc=vEv)
                                bank_guard[bk] = [(vEv, h_v)]
                                hevac_g[0] = h_v
                            else:
                                itH = 40 + jq // 2
                                if jq % 2 == 0:
                                    h_v = b.op("vector", lambda e, bk=bk, itH=itH, tc=tc:
                                               e.tensor_mul(
                                                   Hbuf[0:64, itH, 512 * tc:512 * (tc + 1)],
                                                   pbr(bk, 0, 64), sgbuf[0:64, bk, :]),
                                               waits=[(tMM, mm_done), (sEv, sg_s)]
                                               if (jj == 0 and tc == 0) else [], inc=vEv)
                                    bank_guard[bk] = [(vEv, h_v)]
                                    hevac_g[0] = h_v
                                else:
                                    hw = ([(tMM, mm_done), (sEv, sg_s)]
                                          if (jj == 0 and tc == 0) else [])
                                    hw.append((dSh, htmp_guard[0]))
                                    h_v = b.op("vector", lambda e, bk=bk: e.tensor_mul(
                                        htmp[0:64, :], pbr(bk, 0, 64), sgbuf[0:64, bk, :]),
                                        waits=hw, inc=vEv)
                                    sh_t = b.dma("sync",
                                                 Hbuf[64:128, itH, 512 * tc:512 * (tc + 1)],
                                                 htmp[0:64, :], dSh, waits=[(vEv, h_v)])
                                    htmp_guard[0] = sh_t
                                    bank_guard[bk] = [(vEv, h_v)]
                                    hevac_g[0] = h_v
    dq_w_final = [(ccS, cc_done)]
    h_shift_done = dSh.n
    h_done_v = vEv.n
    if debug in (True, 3):
        b.dma("sync", dbgH, Hbuf[:], dKW, waits=[(vEv, h_done_v), (dSh, h_shift_done)])

    y_guard = []
    for tc in range(2):
        y_done_s = 0
        for hb in range(2):
            piece_t = [0] * NIT

            def e_piece(it, tc=tc, hb=hb):
                psl = it % 3
                wv = list(pc_free[psl])
                if tc == 0 and hb == 0 and it < 2:
                    wv += dq_w_final
                dstap = wpiece[:, psl, :].rearrange("p (a c) -> p a c", c=512)
                if it < 40:
                    ci0 = 44 + 4 * (it // 8) + 2 * hb
                    srcap = scr_f[ci0:ci0 + 2, it % 8].rearrange("a p c -> p a c")
                    piece_t[it] = b.dma("scalar", dstap, srcap, dFRp[psl], waits=wv)
                else:
                    m = it - 40
                    ci0 = 64 + 2 * hb
                    b.dma("scalar", wpiece[0:64, psl, :].rearrange(
                        "p (a c) -> p a c", c=512),
                        scr_f[ci0:ci0 + 2, 2 * m, 0:64, :].rearrange("a p c -> p a c"),
                        dFRp[psl], waits=wv)
                    piece_t[it] = b.dma("scalar", wpiece[64:128, psl, :].rearrange(
                        "p (a c) -> p a c", c=512),
                        scr_f[ci0:ci0 + 2, 2 * m + 1, 0:64, :].rearrange("a p c -> p a c"),
                        dFRp[psl])
            e_piece(0)
            e_piece(1)
            e_piece(2)
            for it in range(NIT):
                psl = it % 3
                for jh in range(8):
                    lw = [(dFRp[psl], piece_t[it])] if jh == 0 else []
                    if tc == 0 and hb == 0 and it == 0 and jh == 0:
                        lw += [(vEv, h_done_v), (dSh, h_shift_done)]
                    ldw(wpiece[:, psl, jh:1024:8], waits=lw)
                    mm(jh, wpiece[:, psl, jh:1024:8],
                       Hbuf[:, it, 512 * tc:512 * (tc + 1)],
                       start=(it == 0), stop=(it == NIT - 1))
                pc_free[psl] = [(tMM, tMM.n)]
                if it + 3 < NIT:
                    e_piece(it + 3)
            mm_done = tMM.n
            for jh in range(8):
                y_done_s = b.op("scalar", lambda e, jh=jh, hb=hb: e.copy(
                    ybuf[:, 8 * hb + jh, :], pb(jh)),
                    waits=([(tMM, mm_done)] + list(y_guard)) if jh == 0 else [],
                    inc=sEv)
                bank_guard[jh] = [(sEv, y_done_s)]
        if debug in (True, 3):
            b.dma("sync", dbgY[tc], ybuf[:], dKW, waits=[(sEv, y_done_s)])
        ys_t = 0
        for yt in range(NT):
            ys_s = b.op("scalar", lambda e, yt=yt: e.activation(
                ysqb[:], ybuf[:, yt, :], AF.Square),
                waits=[(sEv, y_done_s)] if yt == 0 else [(tMM, ys_t)], inc=sEv)
            ys_t = mm(0, ones_bf[:], ysqb[:], start=(yt == 0), stop=(yt == NT - 1),
                      out_rows=(0, 1), waits=[(sEv, ys_s)])
        rr_s = b.op("scalar", lambda e: e.activation(
            sA, pbr(0, 0, 1), AF.Sqrt,
            bias=cells[0:1, 5:6], scale=cells[0:1, 6:7]),
            waits=[(tMM, ys_t), (tMM, rb_guard[0])], inc=sEv)
        bank_guard[0] = [(sEv, rr_s)]
        rr_v = b.op("vector", lambda e: e.reciprocal(sB, sA),
                    waits=[(sEv, rr_s)], inc=vEv)
        m_t = mm(1, ones1_f32[:], sB, True, True, waits=[(vEv, rr_v)])
        rb_guard[0] = m_t
        rrb_s = b.op("scalar", lambda e: e.copy(rrB[:], pb(1)),
                     waits=[(tMM, m_t)], inc=sEv)
        bank_guard[1] = [(sEv, rrb_s)]
        for yt in range(NT):
            wfb = [(dOse, dOse.n)]
            x_t = b.dma("sync", fbufA[:, 0, 0:512],
                        x2_sc[yt, :, 512 * tc:512 * (tc + 1)], dX2R,
                        waits=wfb)
            b.op("vector", lambda e, yt=yt: e.tensor_mul(
                fbufA[:, 1, 0:512], ybuf[:, yt, :], rrB[:]),
                waits=[(sEv, rrb_s), (dX2R, x_t)])
            o_v = b.op("vector", lambda e: e.tensor_add(
                fbufA[:, 1, 512:1024], fbufA[:, 1, 0:512], fbufA[:, 0, 0:512]),
                inc=vEv)
            y_guard = [(vEv, o_v)]
            b.dma("sync", OUT[yt, :, 512 * tc:512 * (tc + 1)], fbufA[:, 1, 512:1024],
                  dOse, waits=[(vEv, o_v)])
    final_out = dOse.n
    b.plan["sync"].append(lambda e, h=dOse.h, v=final_out: e.wait_ge(h, v))

    with nc.Block() as block:
        for eng_name, deco in (("sync", block.sync), ("tensor", block.tensor),
                               ("vector", block.vector), ("scalar", block.scalar),
                               ("gpsimd", block.gpsimd)):
            def run(e, items=b.plan[eng_name]):
                for f in items:
                    f(e)
            deco(run)
    b.es.close()
    return nc


_NC_CACHE = {}


def _get_nc(debug=False):
    key = debug
    if key not in _NC_CACHE:
        _NC_CACHE[key] = build_program(debug=key)
    return _NC_CACHE[key]


def _stage(L0_attn_hidden_states, L0_attn_Wq, L0_attn_Wk, L0_attn_Wv, L0_attn_Wo,
           L0_ffn_gate_up_packed, L0_ffn_gate_up_scales, L0_ffn_down_packed,
           L0_ffn_down_scales):
    x = np.asarray(L0_attn_hidden_states)
    Wq = np.asarray(L0_attn_Wq)
    Wk = np.asarray(L0_attn_Wk)
    Wv = np.asarray(L0_attn_Wv)
    Wo = np.asarray(L0_attn_Wo)
    colord = []
    for hp in range(NP):
        colord += list(range(128 * 2 * hp, 128 * 2 * hp + 256))
        colord += list(range(2048 + 64 * 2 * hp, 2048 + 64 * 2 * hp + 128))
    colord = np.array(colord)
    WQT = np.ascontiguousarray(Wq[colord][:, PERM].T.astype(np.float32))
    WKT = np.ascontiguousarray(Wk[colord][:, PERM].T.astype(np.float32))
    WVT = np.ascontiguousarray(Wv[:, PERM].T.astype(np.float32))
    WOT = np.ascontiguousarray(Wo[PERM, :].T.astype(np.float32))
    PKGa = np.asarray(L0_ffn_gate_up_packed).T
    PKDa = np.asarray(L0_ffn_down_packed).T
    SCGa = np.repeat(np.asarray(L0_ffn_gate_up_scales), 16, axis=0)
    SCDa = np.repeat(np.asarray(L0_ffn_down_scales), 16, axis=0)
    chunks = [("g", cb, oc) for cb in range(2) for oc in range(22)] + \
             [("d", cb, oc) for cb in range(6) for oc in range(4)]
    PKS_all = np.zeros((72, 128, 512), dtype=np.int32)
    SCS_all = np.zeros((72, 128, 512), dtype=np.float16)
    for g, (kind, cb, oc) in enumerate(chunks):
        if kind == "g":
            PKS_all[g] = PKGa[128 * cb:128 * cb + 128, 512 * oc:512 * (oc + 1)]
            SCS_all[g] = SCGa[128 * cb:128 * cb + 128, 512 * oc:512 * (oc + 1)]
        else:
            rows = 64 if cb == 5 else 128
            PKS_all[g, 0:rows] = PKDa[128 * cb:128 * cb + rows, 512 * oc:512 * (oc + 1)]
            SCS_all[g, 0:rows] = SCDa[128 * cb:128 * cb + rows, 512 * oc:512 * (oc + 1)]

    inv = 1.0 / (10000.0 ** (np.arange(0, DR, 2, dtype=np.float64) / DR))
    t = np.arange(S, dtype=np.float64)
    fr = np.outer(t, inv)
    cosT = np.cos(fr).astype(np.float32)
    sinT = np.sin(fr).astype(np.float32)

    in_maps = []
    for k in range(8):
        bb_, hh_ = k // 2, k % 2
        t0 = TOK * hh_
        tokord = np.concatenate([np.arange(t0, t0 + TOK),
                                 np.arange((1 - hh_) * TOK, (1 - hh_) * TOK + TOK)])
        XTk = np.ascontiguousarray(x[bb_][tokord][:, PERM].T.astype(np.float32))
        ck = cosT[tokord].T
        sk = sinT[tokord].T
        COSRk = np.ascontiguousarray(np.tile(ck, (4, 1)).astype(np.float32))
        SINAk = np.ascontiguousarray(
            np.concatenate([-sk, sk, -sk, sk], axis=0).astype(np.float32))
        gidx = [24 * (l // 3) + 3 * k + (l % 3) for l in range(9)]
        in_maps.append({
            "XT": XTk, "WQT": WQT, "WKT": WKT, "WVT": WVT, "WOT": WOT,
            "PKS": np.ascontiguousarray(PKS_all[gidx]),
            "SCS": np.ascontiguousarray(SCS_all[gidx]),
            "COSR": COSRk, "SINA": SINAk,
        })
    return in_maps


def kernel(L0_attn_hidden_states, L0_attn_Wq, L0_attn_Wk, L0_attn_Wv, L0_attn_Wo,
           L0_ffn_gate_up_packed, L0_ffn_gate_up_scales, L0_ffn_down_packed,
           L0_ffn_down_scales):
    in_maps = _stage(L0_attn_hidden_states, L0_attn_Wq, L0_attn_Wk, L0_attn_Wv,
                     L0_attn_Wo, L0_ffn_gate_up_packed, L0_ffn_gate_up_scales,
                     L0_ffn_down_packed, L0_ffn_down_scales)
    nc = _get_nc(debug=_DEBUG)
    res = run_bass_kernel_spmd(nc, in_maps, core_ids=list(range(8)))
    out = np.empty((B, S, D), dtype=np.float32)
    for k in range(8):
        bb_, hh_ = k // 2, k % 2
        o = np.asarray(res.results[k]["OUT"]).reshape(D, TOK)
        arr = np.empty((TOK, D), dtype=np.float32)
        arr[:, PERM] = o.T
        out[bb_, TOK * hh_:TOK * (hh_ + 1), :] = arr
    if _DEBUG:
        kernel._last = res
    return out


# version 11
# speedup vs baseline: 1.4487x; 1.0001x over previous
import contextlib
import numpy as np
import ml_dtypes

import concourse.bass as bass
import concourse.mybir as mybir
from concourse.bass_utils import run_bass_kernel_spmd

F32 = mybir.dt.float32
BF16 = mybir.dt.bfloat16
I32 = mybir.dt.int32
F16 = mybir.dt.float16
ALU = mybir.AluOpType
AF = mybir.ActivationFunctionType
BF = ml_dtypes.bfloat16

B, S, D = 4, 2048, 2048
H, DN, DR, DV = 16, 128, 64, 128
NI = 5632
TOK = 1024
NT = 16
NKT = 16
NP = 8
GU = 2 * NI
NIT = 44
LN_EPS = 1e-5
RMS_EPS = 1e-6
SC_NOPE = float(1.0 / np.sqrt(128.0))
SC_ROPE = 0.125

_r = np.arange(D)
PERM = (1024 * (_r // 1024) + 8 * (_r % 128) + (_r // 128) % 8).astype(np.int64)

ENGINES = ("sync", "tensor", "vector", "scalar", "gpsimd")
_DEBUG = False


class Sem:
    __slots__ = ("h", "n", "name")

    def __init__(self, h, name):
        self.h, self.n, self.name = h, 0, name


class Bld:
    def __init__(self):
        self.nc = bass.Bass("TRN2", target_bir_lowering=False)
        self.plan = {e: [] for e in ENGINES}
        self.es = contextlib.ExitStack()
        self.sems = {}

    def sem(self, name):
        if name not in self.sems:
            self.sems[name] = Sem(self.es.enter_context(self.nc.semaphore(name)), name)
        return self.sems[name]

    def op(self, eng, make, waits=(), inc=None, incn=1):
        wl = [(s.h, int(v)) for (s, v) in waits if v > 0]
        ih = None
        if inc is not None:
            inc.n += incn
            ih = inc.h

        def fn(e, make=make, wl=wl, ih=ih, iv=incn):
            for (h, v) in wl:
                e.wait_ge(h, v)
            ins = make(e)
            if ih is not None:
                ins.then_inc(ih, iv)
        self.plan[eng].append(fn)
        return inc.n if inc is not None else None

    def dma(self, eng, out, in_, sem, waits=()):
        return self.op(eng, (lambda e, o=out, i=in_: e.dma_start(out=o, in_=i)),
                       waits=waits, inc=sem, incn=16)

    def sb(self, name, shape, dtype, offset):
        base = (self.nc.sbuf_base + 31) & ~31
        return self.nc.alloc_sbuf_tensor_at(name, list(shape), dtype,
                                            offset=base + offset)

    def barrier(self):
        ph = self.sem("ph")
        ph.n += 5
        t = ph.n
        for e in ENGINES:
            def fn(eng, h=ph.h, t=t):
                eng.nop().then_inc(h, 1)
                eng.wait_ge(h, t)
            self.plan[e].append(fn)


def build_program(debug=False):
    b = Bld()
    nc = b.nc
    okind = "ExternalOutput" if debug is True else "Internal"

    XT = nc.dram_tensor("XT", [D, S], F32, kind="ExternalInput").ap()
    WQT = nc.dram_tensor("WQT", [D, 3072], F32, kind="ExternalInput").ap()
    WKT = nc.dram_tensor("WKT", [D, 3072], F32, kind="ExternalInput").ap()
    WVT = nc.dram_tensor("WVT", [D, 2048], F32, kind="ExternalInput").ap()
    WOT = nc.dram_tensor("WOT", [D, 2048], F32, kind="ExternalInput").ap()
    PKS = nc.dram_tensor("PKS", [9, 128, 512], I32, kind="ExternalInput").ap()
    SCS = nc.dram_tensor("SCS", [9, 128, 512], F16, kind="ExternalInput").ap()
    COSR = nc.dram_tensor("COSR", [128, S], F32, kind="ExternalInput").ap()
    SINA = nc.dram_tensor("SINA", [128, S], F32, kind="ExternalInput").ap()
    OUT = nc.dram_tensor("OUT", [NT, 128, TOK], F32, kind="ExternalOutput").ap()

    dbgO = nc.dram_tensor("dbgO", [128, NT, 1024], BF16, kind=okind).ap()
    dbgE = nc.dram_tensor("dbgE", [128, 64], F32, kind=okind).ap()
    dbgM = nc.dram_tensor("dbgM", [H, 1, 32], F32, kind=okind).ap()
    dbgS = nc.dram_tensor("dbgS", [72, 8, 128, 512], BF16, kind=okind).ap()
    dbgL2 = nc.dram_tensor("dbgL2", [128, NT, 1024], BF16, kind=okind).ap()
    dbgH = nc.dram_tensor("dbgH", [128, NIT, 1024], BF16, kind=okind).ap()
    dbgY = nc.dram_tensor("dbgY", [2, 128, NT, 512], F32, kind=okind).ap()
    dbgR = nc.dram_tensor("dbgR", [2 * H, 128, 512], F32, kind=okind).ap()
    k_sc = nc.dram_tensor("k_sc", [NP, 3, 128, S], BF16, kind=okind).ap()
    q_sc = nc.dram_tensor("q_sc", [NP, 3, 128, TOK], BF16, kind=okind).ap()
    v_sc = nc.dram_tensor("v_sc", [NP, NKT, 128, 256], BF16, kind=okind).ap()
    scr_l = nc.dram_tensor("scr_l", [9, 8, 128, 512], BF16, kind="Internal").ap()
    scr_f = nc.dram_tensor("scr_f", [72, 8, 128, 512], BF16, kind="Internal").ap()
    x2_sc = nc.dram_tensor("x2_sc", [NT, 128, TOK], F32, kind=("ExternalOutput" if debug in (True, 2) else "Internal")).ap()

    cells = b.sb("cells", [128, 32], F32, 0)
    ones_bf = b.sb("ones_bf", [128, 1], BF16, 128)
    ones_f32 = b.sb("ones_f32", [128, 1], F32, 160)
    ones1_f32 = b.sb("ones1_f32", [1, 128], F32, 192)
    negones1_f32 = b.sb("negones1_f32", [1, 128], F32, 207872)
    dbgE_sb = b.sb("dbgE_sb", [128, 16], F32, 208384)
    ones64A = b.sb("ones64A", [128, 1], BF16, 704)
    ones64B = b.sb("ones64B", [128, 1], BF16, 736)
    expbias = b.sb("expbias", [128, 16], F32, 768)
    rowbank = b.sb("rowbank", [128, 2048], F32, 1024)
    fbufA = b.sb("fbufA", [128, 2, 1024], F32, 9216)
    dq_t05 = b.sb("dq_t05", [128, 2, 1024], BF16, 9216)
    dq_smul = b.sb("dq_smul", [128, 2, 1024], BF16, 13312)
    wbufs = b.sb("wbufs", [128, 2, 16, 128], BF16, 17408)
    wpiece = b.sb("wpiece", [128, 3, 1024], BF16, 17408)
    tmpF = b.sb("tmpF", [128, 2048], F32, 17408)

    RD = 25600
    fbufB = b.sb("fbufB", [128, 2, 1024], F32, RD)
    pkb = b.sb("pkb", [128, 2, 512], I32, RD)
    scb = b.sb("scb", [128, 2, 512], BF16, RD + 4096)
    m32 = b.sb("m32", [128, 512], I32, RD + 6144)
    s32 = b.sb("s32", [128, 512], I32, RD + 8192)
    dq_r1 = b.sb("dq_r1", [128, 2, 1024], BF16, RD + 10240)
    dq_r2 = b.sb("dq_r2", [128, 2, 1024], BF16, RD + 14336)
    dq_t2 = b.sb("dq_t2", [128, 1024], BF16, RD + 18432)
    dq_u = b.sb("dq_u", [128, 1024], BF16, RD + 20480)
    wout = b.sb("wout", [128, 2, 2, 512], BF16, RD + 22528)

    R1 = 52224
    ln1 = b.sb("ln1", [128, NT, 2048], BF16, R1)
    sqtile = b.sb("sqtile", [128, 2048], BF16, R1)
    Pbuf = b.sb("Pbuf", [128, 2, NKT, 512], BF16, R1 + 8192)
    rBuf = b.sb("rBuf", [128, 512], F32, R1 + 40960)
    xb2 = b.sb("xb2", [128, 1024], BF16, R1)
    xsq2 = b.sb("xsq2", [128, 1024], BF16, R1 + 2048)
    ln2 = b.sb("ln2", [128, NT, 1024], BF16, R1)
    sgbuf = b.sb("sgbuf", [128, 8, 512], BF16, R1 + 32768)
    htmp = b.sb("htmp", [128, 512], BF16, R1 + 40960)
    ybuf = b.sb("ybuf", [128, NT, 512], F32, R1)
    ysqb = b.sb("ysqb", [128, 512], BF16, R1 + 32768)
    rrB = b.sb("rrB", [128, 512], F32, R1 + 33792)

    R2 = 117760
    xsq = b.sb("xsq", [128, 2, 2048], BF16, R2)
    xb = b.sb("xb", [128, 2, 2048], BF16, R2 + 8192)
    xsA = b.sb("xsA", [128, 2, 2048], F32, R2 + 16384)
    cosR = b.sb("cosR", [128, 2048], F32, R2 + 32768)
    sinA = b.sb("sinA", [128, 2048], F32, R2 + 40960)
    aB = b.sb("aB", [128, 2048], F32, R2 + 49152)
    bB = b.sb("bB", [128, 2048], F32, R2 + 57344)
    estg = b.sb("estg", [128, 512], BF16, R2 + 65536)
    s_rope = b.sb("s_rope", [128, 512], BF16, R2 + 66560)
    sh_rope = b.sb("sh_rope", [128, 512], BF16, R2 + 67584)
    ropetmp = b.sb("ropetmp", [128, 512], F32, R2 + 68608)
    ropetmp2 = b.sb("ropetmp2", [128, 512], F32, R2 + 70656)
    wvb = b.sb("wvb", [128, 16, 512], BF16, R2 + 72704)
    kb = b.sb("kb", [128, 2, 3, 2048], BF16, R2)
    qb = b.sb("qb", [128, 2, 3, 1024], BF16, R2 + 24576)
    vb = b.sb("vb", [128, 2, NKT, 256], BF16, R2 + 36864)
    Obuf = b.sb("Obuf", [128, NT, 1024], BF16, R2 + 53248)
    aB2 = b.sb("aB2", [128, 1024], F32, R2)
    bB2 = b.sb("bB2", [128, 1024], F32, R2 + 4096)
    Hbuf = b.sb("Hbuf", [128, NIT, 1024], BF16, R2)

    psAll = b.es.enter_context(nc.psum_tensor("psAll", [128, 4096], F32))

    def pb(bank):
        return psAll[:, 512 * bank:512 * (bank + 1)]

    def pbr(bank, r0, r1, c0=0, c1=512):
        return psAll[r0:r1, 512 * bank + c0:512 * bank + c1]

    dXp = [b.sem("dX0"), b.sem("dX1")]
    dWp = [b.sem("dW0"), b.sem("dW1")]
    dWV = b.sem("dWV"); dTb = b.sem("dTb")
    dPKp = [b.sem("dPK0"), b.sem("dPK1")]
    dSCp = [b.sem("dSC0"), b.sem("dSC1")]
    dKW = b.sem("dKW")
    dKRp = [b.sem("dKR0"), b.sem("dKR1")]
    dQWp = [b.sem("dQW0"), b.sem("dQW1")]
    dFRp = [b.sem("dFR0"), b.sem("dFR1"), b.sem("dFR2")]
    dX2W = b.sem("dX2W"); dX2R = b.sem("dX2R")
    dOse = b.sem("dOse"); dSh = b.sem("dSh")
    dOsp = [b.sem("dOs0"), b.sem("dOs1")]; dX2Rp = [b.sem("dX2R0"), b.sem("dX2R1")]
    tMM = b.sem("tMM"); vEv = b.sem("vEv"); sEv = b.sem("sEv")
    dqa = b.sem("dqa"); dqv = b.sem("dqv"); dqx = b.sem("dqx")

    bank_guard = {i: [] for i in range(8)}

    def mm(bank, lhsT, rhs, start, stop, out_rows=None, waits=()):
        w = list(waits)
        if start:
            w += bank_guard[bank]
            bank_guard[bank] = []
        if out_rows is None:
            out = psAll[:, 512 * bank:512 * bank + rhs.shape[-1]]
        else:
            out = psAll[out_rows[0]:out_rows[1], 512 * bank:512 * bank + rhs.shape[-1]]
        return b.op("tensor",
                    lambda e, o=out, l=lhsT, r=rhs, st=start, sp=stop:
                    e.matmul(o, l, r, start=st, stop=sp),
                    waits=w, inc=tMM)

    def ldw(wap, waits=()):
        return b.op("tensor", lambda e, w=wap: e.ldweights(w), waits=waits)

    for ci, cv in {0: -2.0, 1: 0.5, 2: -6.0, 3: 1.0, 4: LN_EPS, 5: RMS_EPS,
                   6: 1.0 / 2048.0, 7: 1.0, 8: 2.0, 9: SC_NOPE}.items():
        b.op("vector", lambda e, ci=ci, cv=cv: e.memset(cells[:, ci:ci + 1], cv))
    b.op("vector", lambda e: e.memset(ones_bf[:], 1.0))
    b.op("vector", lambda e: e.memset(ones_f32[:], 1.0))
    b.op("vector", lambda e: e.memset(ones1_f32[:], 1.0))
    b.op("vector", lambda e: e.memset(negones1_f32[:], -1.0))
    b.op("vector", lambda e: e.memset(ones64A[0:64, :], 1.0))
    b.op("vector", lambda e: e.memset(ones64A[64:128, :], 0.0))
    b.op("vector", lambda e: e.memset(ones64B[0:64, :], 0.0))
    consts_done = b.op("vector", lambda e: e.memset(ones64B[64:128, :], 1.0), inc=vEv)

    b.dma("sync", cosR[:], COSR, dTb)
    tables_done = b.dma("sync", sinA[:], SINA, dTb)

    N_DQ_CH = 9
    N_DQ_UNITS = 4 * N_DQ_CH
    dqs = {"u": 0, "pk_t": None, "sc_t": None, "a_t": 0, "pending2": None,
           "qw_done": [None] * N_DQ_UNITS}

    def emit_dq_unit():
        u = dqs["u"]
        if u >= N_DQ_UNITS:
            return False
        ci, jp = u // 4, u % 4
        slot = ci % 2
        par = u % 2
        rows = 128
        if jp == 0:
            dqs["pk_t"] = (dPKp[slot], b.dma("scalar", pkb[:, slot, :], PKS[ci],
                           dPKp[slot], waits=([(dqx, 4 * (ci - 1))] if ci >= 2 else [])))
            dqs["sc_t"] = (dSCp[slot], b.dma("gpsimd", scb[:, slot, :], SCS[ci],
                           dSCp[slot], waits=([(dqv, 4 * (ci - 1))] if ci >= 2 else [])))
        xw = [dqs["pk_t"]] if jp == 0 else []
        if u >= 2:
            xw.append((dqa, u - 1))
        for jh in range(2):
            j = 2 * jp + jh
            b.op("vector", lambda e, sl=slot, j=j:
                 e.tensor_scalar(m32[:, :], pkb[:, sl, :], 4 * j, 7,
                                 ALU.logical_shift_right, ALU.bitwise_and),
                 waits=(xw if jh == 0 else []))
            b.op("vector", lambda e, sl=slot, j=j:
                 e.tensor_scalar(s32[:, :], pkb[:, sl, :], 4 * j + 3, 1,
                                 ALU.logical_shift_right, ALU.bitwise_and))
            b.op("vector", lambda e, p=par, jh=jh:
                 e.tensor_scalar(dq_t05[:, p, 512 * jh:512 * (jh + 1)],
                                 m32[:, :], 0.5, None, ALU.mult))
            b.op("vector", lambda e, p=par, jh=jh:
                 e.tensor_scalar(dq_smul[:, p, 512 * jh:512 * (jh + 1)],
                                 s32[:, :], -2.0, 1.0, ALU.mult, ALU.add),
                 inc=(dqx if jh == 1 else None))
        x_t = dqx.n
        sw = [(dqx, x_t)]
        if u == 0:
            sw.append((vEv, consts_done))
        if u >= 2:
            sw.append((dqv, u - 1))
        b.op("scalar", lambda e, p=par: e.activation(
            dq_r1[:, p, :], dq_t05[:, p, :], AF.Relu,
            bias=cells[:, 0:1], scale=cells[:, 3:4]),
            waits=sw)
        a_t = b.op("scalar", lambda e, p=par: e.activation(
            dq_r2[:, p, :], dq_t05[:, p, :], AF.Relu,
            bias=cells[:, 2:3], scale=cells[:, 8:9]),
            inc=dqa)
        dqs["a_t"] = a_t
        def stage2(u=u, slot=slot, par=par, a_t=a_t, sc_t=dqs["sc_t"]):
            ci_ = u // 4
            vw = [(dqa, a_t), sc_t]
            if u >= 2:
                vw.append(dqs["qw_done"][u - 2])
            b.op("vector", lambda e, p=par: e.tensor_add(
                dq_u[:, :], dq_t05[:, p, :], dq_r1[:, p, :]), waits=vw)
            b.op("vector", lambda e, p=par: e.tensor_add(
                dq_t2[:, :], dq_u[:, :], dq_r2[:, p, :]))
            b.op("vector", lambda e, p=par: e.tensor_mul(
                dq_u[:, :], dq_t2[:, :], dq_smul[:, p, :]))
            b.op("vector", lambda e, sl=slot, p=par: e.tensor_mul(
                wout[:, p, 0, :], dq_u[:, 0:512], scb[:, sl, :]))
            v_t = b.op("vector", lambda e, sl=slot, p=par: e.tensor_mul(
                wout[:, p, 1, :], dq_u[:, 512:1024], scb[:, sl, :]),
                inc=dqv)
            jp_ = u % 4
            for jh in range(2):
                j = 2 * jp_ + jh
                b.dma("sync", scr_l[ci_, j], wout[:, par, jh, :], dQWp[par],
                      waits=([(dqv, v_t)] if jh == 0 else []))
            dqs["qw_done"][u] = (dQWp[par], dQWp[par].n)
        if dqs["pending2"] is not None:
            pend = dqs["pending2"]
            dqs["pending2"] = stage2
            pend()
        else:
            dqs["pending2"] = stage2
        dqs["u"] = u + 1
        return True

    def emit_dq_flush():
        if dqs["pending2"] is not None:
            dqs["pending2"]()
            dqs["pending2"] = None

    def emit_dq_units(n):
        for _ in range(n):
            if not emit_dq_unit():
                return

    XT_t = XT.rearrange("(t p) n -> t p n", p=128)
    v_cnt = [0] * NT
    s_cnt = [0] * NT
    t_cnt = [0] * NT
    x_cnt = [0] * NT
    for tau in range(NT):
        sl = tau % 2
        wv = []
        if tau >= 2:
            wv = [(vEv, v_cnt[tau - 2]), (sEv, s_cnt[tau - 2])]
        x_cnt[tau] = b.dma("sync", xsA[:, sl, :], XT_t[tau], dXp[sl], waits=wv)
        wv2 = [(dXp[sl], x_cnt[tau])]
        if tau >= 2:
            wv2.append((tMM, t_cnt[tau - 2]))
        v_cnt[tau] = b.op("vector", lambda e, sl=sl: e.tensor_copy(xb[:, sl, :], xsA[:, sl, :]),
                          waits=wv2, inc=vEv)
        s_cnt[tau] = b.op("scalar", lambda e, sl=sl: e.activation(
            xsq[:, sl, :], xsA[:, sl, :], AF.Square), waits=wv2, inc=sEv)
        mw = [(vEv, v_cnt[tau]), (sEv, s_cnt[tau])]
        for c in range(4):
            mm(c, ones_bf[:], xb[:, sl, 512 * c:512 * (c + 1)],
               start=(tau == 0), stop=(tau == NT - 1), out_rows=(0, 1),
               waits=(mw if c == 0 else []))
        for c in range(4):
            t_cnt[tau] = mm(4 + c, ones_bf[:], xsq[:, sl, 512 * c:512 * (c + 1)],
                            start=(tau == 0), stop=(tau == NT - 1), out_rows=(0, 1))
    stats_mm_done = t_cnt[NT - 1]

    sA, sB, sC, sD = (rowbank[0:1, 0:512], rowbank[0:1, 512:1024],
                      rowbank[0:1, 1024:1536], rowbank[0:1, 1536:2048])

    def ln_coeffs(nchunk, sum_banks, sq_banks, eps_cell, dst_a, dst_b, first_wait):
        abc = 0
        prev_bt = 0
        for c in range(nchunk):
            bs, bq = sum_banks[c], sq_banks[c]
            w0 = list(first_wait) if c == 0 else [(tMM, prev_bt)]
            cp1 = b.op("scalar", lambda e, bs=bs: e.copy(sA, pbr(bs, 0, 1)),
                       waits=w0, inc=sEv)
            cp2 = b.op("scalar", lambda e, bq=bq: e.copy(sB, pbr(bq, 0, 1)), inc=sEv)
            b.op("vector", lambda e: e.tensor_scalar(sC, sA, 1.0 / 2048.0, None,
                                                     ALU.mult), waits=[(sEv, cp2)])
            b.op("vector", lambda e: e.tensor_scalar(sD, sB, 1.0 / 2048.0, None,
                                                     ALU.mult))
            b.op("vector", lambda e: e.tensor_mul(sA, sC, sC))
            var_v = b.op("vector", lambda e: e.tensor_sub(sB, sD, sA), inc=vEv)
            sq_v = b.op("scalar", lambda e, ec=eps_cell: e.activation(
                sD, sB, AF.Sqrt, bias=cells[0:1, ec:ec + 1], scale=cells[0:1, 7:8]),
                waits=[(vEv, var_v)], inc=sEv)
            b.op("vector", lambda e: e.reciprocal(sA, sD), waits=[(sEv, sq_v)])
            ab_v = b.op("vector", lambda e: e.scalar_tensor_tensor(
                sB, sC, -1.0, sA, ALU.mult, ALU.mult), inc=vEv)
            mm(bs, ones1_f32[:], sA, True, True,
               waits=[(vEv, ab_v), (sEv, cp2)])
            bt = mm(bq, ones1_f32[:], sB, True, True)
            prev_bt = bt
            a1 = b.op("scalar", lambda e, c=c, bs=bs: e.copy(
                dst_a[:, 512 * c:512 * (c + 1)], pb(bs)), waits=[(tMM, bt)], inc=sEv)
            abc = b.op("scalar", lambda e, c=c, bq=bq: e.copy(
                dst_b[:, 512 * c:512 * (c + 1)], pb(bq)), inc=sEv)
            bank_guard[bs] = [(sEv, abc)]
            bank_guard[bq] = [(sEv, abc)]
        return abc

    abc2 = ln_coeffs(4, [0, 1, 2, 3], [4, 5, 6, 7], 4, aB, bB,
                     [(tMM, stats_mm_done)])

    ln_v = [0] * NT
    for tau in range(NT):
        wv = [(sEv, abc2)] if tau == 0 else [(vEv, ln_v[tau - 1])]
        x_t = b.dma("sync", xsA[:, 0, :], XT_t[tau], dXp[0], waits=wv)
        b.op("vector", lambda e: e.tensor_mul(tmpF[:], xsA[:, 0, :], aB[:]),
             waits=[(dXp[0], x_t)])
        ln_v[tau] = b.op("vector", lambda e, tau=tau: e.tensor_add(
            ln1[:, tau, :], tmpF[:], bB[:]), inc=vEv)
    ln_done = ln_v[NT - 1]

    kw_done = [0] * NP
    vw_done = [0] * NP

    rope_state = {"v": 0}

    def rope_evac(bank, scale_c, tab_sl, dst, stg_guard):
        sc_v = b.op("scalar", lambda e, bk=bank: e.copy(s_rope[:], pb(bk)),
                    waits=list(stg_guard) + [(vEv, rope_state["v"])], inc=sEv)
        sh_t = 0
        for (d0, s0) in ((0, 32), (32, 0), (64, 96), (96, 64)):
            sh_t = b.dma("sync", sh_rope[d0:d0 + 32, :], s_rope[s0:s0 + 32, :], dSh,
                         waits=([(sEv, sc_v), (vEv, rope_state["v"])] if d0 == 0 else []))
        b.op("vector", lambda e, ts=tab_sl, c=scale_c: e.scalar_tensor_tensor(
            ropetmp[:], s_rope[:], c, cosR[:, ts[0]:ts[1]], ALU.mult, ALU.mult),
            waits=[(dSh, sh_t), (dTb, tables_done)])
        b.op("vector", lambda e, ts=tab_sl, c=scale_c: e.scalar_tensor_tensor(
            ropetmp2[:], sh_rope[:], c, sinA[:, ts[0]:ts[1]], ALU.mult, ALU.mult))
        ev = b.op("vector", lambda e: e.tensor_add(estg[:], ropetmp[:], ropetmp2[:]),
                  inc=vEv)
        rope_state["v"] = ev
        w_t = b.dma("sync", dst, estg[:], dKW, waits=[(vEv, ev)])
        return (sEv, sc_v), (dKW, w_t)

    prev_w_used = 0
    wslot_free = [(), ()]
    estg_guard = []
    for which, WT, dsts, ntc, rope_scale in (
        ("q", WQT, q_sc, 2, SC_ROPE),
        ("k", WKT, k_sc, 4, 1.0),
    ):
        WT_r = WT.rearrange("(t p) d -> p t d", p=128)
        for dc in range(24):
            hp, ct = dc // 3, dc % 3
            sl = dc % 2
            wv0 = list(wslot_free[sl])
            if which == "q" and dc < 2:
                wv0.append((vEv, ln_done))
            w_t = b.dma("gpsimd", wbufs[:, sl], WT_r[:, :, 128 * dc:128 * (dc + 1)],
                        dWp[sl], waits=wv0)
            banks = [2 * (dc % 4), 2 * (dc % 4) + 1] if ntc == 2 else \
                    ([0, 1, 2, 3] if dc % 2 == 0 else [4, 5, 6, 7])
            last_ldw_t = None
            for it in range(NT):
                lw = [(dWp[sl], w_t)] if it == 0 else []
                if it == 0:
                    lw.append((vEv, ln_done))
                ldw(wbufs[:, sl, it, :], waits=lw)
                for tc in range(ntc):
                    mm(banks[tc], wbufs[:, sl, it, :],
                       ln1[:, it, 512 * tc:512 * (tc + 1)],
                       start=(it == 0), stop=(it == NT - 1))
            mm_done = tMM.n
            wslot_free[sl] = [(tMM, mm_done)]
            for tc in range(ntc):
                bk = banks[tc]
                if ct < 2:
                    if which == "q":
                        ev = b.op("scalar", lambda e, bk=bk: e.mul(
                            estg[:], pb(bk), cells[:, 9:10]),
                            waits=[(tMM, mm_done)] + estg_guard, inc=sEv)
                        gsem = (sEv, ev)
                    else:
                        ev = b.op("scalar", lambda e, bk=bk: e.copy(estg[:], pb(bk)),
                                  waits=[(tMM, mm_done)] + estg_guard, inc=sEv)
                        gsem = (sEv, ev)
                    w2 = b.dma("sync", dsts[hp, ct, :, 512 * tc:512 * (tc + 1)],
                               estg[:], dKW, waits=[gsem])
                    bank_guard[bk] = [gsem]
                    estg_guard = [(dKW, w2)]
                else:
                    pg, wg = rope_evac(bk, rope_scale, (512 * tc, 512 * (tc + 1)),
                                       dsts[hp, 2, :, 512 * tc:512 * (tc + 1)],
                                       [(tMM, mm_done)] + estg_guard)
                    bank_guard[bk] = [pg]
                    estg_guard = [wg]
            if which == "k" and ct == 2:
                kw_done[hp] = dKW.n
            emit_dq_units(1)

    WVT_r = WVT.rearrange("(t p) d -> p t d", p=128)
    wv_free = []
    for dvc in range(4):
        w_t = b.dma("gpsimd", wvb[:], WVT_r[:, :, 512 * dvc:512 * (dvc + 1)], dWV,
                    waits=wv_free)
        for kc in range(NKT):
            bk = 4 + kc % 4
            for it in range(NT):
                lw = [(dWV, w_t), (vEv, ln_done)] if it == 0 else []
                ldw(ln1[:, it, 128 * kc:128 * (kc + 1)], waits=lw)
                mm(bk, ln1[:, it, 128 * kc:128 * (kc + 1)], wvb[:, it, :],
                   start=(it == 0), stop=(it == NT - 1))
            mm_done = tMM.n
            ev = b.op("scalar", lambda e, bk=bk: e.copy(estg[:], pb(bk)),
                      waits=[(tMM, mm_done)] + estg_guard, inc=sEv)
            bank_guard[bk] = [(sEv, ev)]
            w2 = 0
            for s in range(2):
                w2 = b.dma("sync", v_sc[2 * dvc + s, kc, :, :],
                           estg[:, 256 * s:256 * (s + 1)], dKW,
                           waits=([(sEv, ev)] if s == 0 else []))
            estg_guard = [(dKW, w2)]
            if kc % 4 == 3:
                emit_dq_units(1)
        wv_free = [(tMM, tMM.n)]
    for hp in range(NP):
        vw_done[hp] = dKW.n

    b.barrier()

    dkw_final = dKW.n
    pair_cons = [(), ()]
    p_free2 = [[], []]
    o_evac_v = 0
    sq_guard_outer = [0]
    eb_guard = [0]
    rb_guard = [0]
    bound_g = [0]
    for hp in range(NP):
        sl = hp % 2
        kr1 = b.dma("sync", kb[:, sl], k_sc[hp].rearrange("c p n -> p c n"), dKRp[sl],
                    waits=list(pair_cons[sl]) + [(dKW, dkw_final)])
        kr2 = b.dma("sync", qb[:, sl], q_sc[hp].rearrange("c p n -> p c n"), dKRp[sl])
        kr3 = b.dma("sync", vb[:, sl], v_sc[hp].rearrange("t p n -> p t n"), dKRp[sl])
        dKR = dKRp[sl]
        kqv_t = dKRp[sl].n
        sq_guard = sq_guard_outer[0]
        for hh in range(2):
            h = 2 * hp + hh
            o64 = ones64A if hh == 0 else ones64B
            for (which_k, kc_hh, dstcell) in ((0, hh, 20), (2, 2, 21)):
                ck = kc_hh if which_k == 0 else 2
                sq_s = b.op("scalar", lambda e, sl=sl, ck=ck: e.activation(
                    sqtile[:], kb[:, sl, ck, :], AF.Square),
                    waits=[(dKR, kqv_t), (tMM, sq_guard), (vEv, bound_g[0])],
                    inc=sEv)
                oo = ones_bf if which_k == 0 else o64
                for c in range(4):
                    m_t = mm(5, oo[:], sqtile[:, 512 * c:512 * (c + 1)], True, True,
                             out_rows=(0, 1), waits=[(sEv, sq_s)] if c == 0 else [])
                    c_t = b.op("scalar", lambda e, c=c: e.copy(
                        rowbank[0:1, 512 * c:512 * (c + 1)], pbr(5, 0, 1)),
                        waits=[(tMM, m_t), (vEv, bound_g[0]), (tMM, rb_guard[0])],
                        inc=sEv)
                    bank_guard[5] = [(sEv, c_t)]
                sq_guard = m_t
                bound_g[0] = b.op("vector", lambda e, dc=dstcell: e.reduce_max(
                    cells[0:1, dc:dc + 1], rowbank[0:1, 0:2048],
                    mybir.AxisListType.X), waits=[(sEv, c_t)], inc=vEv)
            kx_s = b.op("scalar", lambda e: e.activation(
                cells[0:1, 22:24], cells[0:1, 20:22], AF.Sqrt),
                waits=[(vEv, bound_g[0])], inc=sEv)
            sqn_s = b.op("scalar", lambda e, sl=sl, hh=hh: e.activation(
                sqtile[:, 0:1024], qb[:, sl, hh, :], AF.Square),
                waits=[(tMM, sq_guard)], inc=sEv)
            sqr_s = b.op("scalar", lambda e, sl=sl: e.activation(
                sqtile[:, 1024:2048], qb[:, sl, 2, :], AF.Square), inc=sEv)
            for c in range(2):
                mseg = rowbank[0:1, 512 * c:512 * (c + 1)]
                m1 = mm(5, ones_bf[:], sqtile[:, 512 * c:512 * (c + 1)], True, True,
                        out_rows=(0, 1), waits=[(sEv, sqn_s)] if c == 0 else [])
                c1 = b.op("scalar", lambda e: e.copy(sC, pbr(5, 0, 1)),
                          waits=[(tMM, m1), (vEv, bound_g[0])], inc=sEv)
                bank_guard[5] = [(sEv, c1)]
                m2 = mm(5, o64[:], sqtile[:, 1024 + 512 * c:1536 + 512 * c], True, True,
                        out_rows=(0, 1), waits=[(sEv, sqr_s)])
                c2 = b.op("scalar", lambda e: e.copy(sD, pbr(5, 0, 1)),
                          waits=[(tMM, m2), (tMM, rb_guard[0])], inc=sEv)
                bank_guard[5] = [(sEv, c2)]
                sq_guard = m2
                b.op("scalar", lambda e, ms=mseg: e.activation(ms, sC, AF.Sqrt))
                qs_s = b.op("scalar", lambda e: e.activation(sC, sD, AF.Sqrt),
                            inc=sEv)
                b.op("vector", lambda e, ms=mseg: e.tensor_scalar(
                    sD, ms, cells[0:1, 22:23], None, ALU.mult),
                    waits=[(sEv, qs_s), (sEv, kx_s)])
                bound_g[0] = b.op("vector", lambda e, ms=mseg: e.scalar_tensor_tensor(
                    ms, sC, cells[0:1, 23:24], sD, ALU.mult, ALU.add), inc=vEv)
            mx_v = b.op("vector", lambda e: e.reduce_max(
                cells[0:1, 10:11], rowbank[0:1, 0:1024], mybir.AxisListType.X),
                waits=[(tMM, eb_guard[0])], inc=vEv)
            bound_g[0] = mx_v
            m_t = mm(7, negones1_f32[:], cells[0:1, 10:11], True, True,
                     waits=[(vEv, mx_v)])
            eb_guard[0] = m_t
            eb_s = b.op("scalar", lambda e, h=h: e.copy(
                expbias[:, h:h + 1], pbr(7, 0, 128, 0, 1)),
                waits=[(tMM, m_t)], inc=sEv)
            bank_guard[7] = [(sEv, eb_s)]
            if debug is True:
                b.dma("sync", dbgM[h], cells[0:1, :], dKW,
                      waits=[(sEv, eb_s), (vEv, bound_g[0])])
                if hp == 0 and hh == 1:
                    b.dma("sync", dbgE[:, 16:32], expbias[:], dKW,
                          waits=[(sEv, eb_s)])
                ps7 = b.op("scalar", lambda e, h=h: e.copy(
                    dbgE_sb[:, h:h + 1], pbr(7, 0, 128, 0, 1)), inc=sEv)
                if hp == 3 and hh == 1:
                    b.dma("sync", dbgE[:, 32:48], expbias[:], dKW,
                          waits=[(sEv, ps7)])
            sq_guard_outer[0] = sq_guard
            emit_dq_units(1)
        for hh in range(2):
            h = 2 * hp + hh
            r0, r1_ = (0, 64) if hh == 0 else (64, 128)
            for qc in range(2):
                ppar = (2 * hh + qc) % 2
                sb_rot = [0, 1]
                exp_v = [0] * NKT
                for kt in range(NKT):
                    bk = sb_rot[kt % 2]
                    mm(bk, kb[:, sl, hh, 128 * kt:128 * (kt + 1)],
                       qb[:, sl, hh, 512 * qc:512 * (qc + 1)],
                       start=True, stop=False)
                    sc_t = mm(bk, kb[r0:r1_, sl, 2, 128 * kt:128 * (kt + 1)],
                              qb[r0:r1_, sl, 2, 512 * qc:512 * (qc + 1)],
                              start=False, stop=True)
                    e_t = b.op("scalar", lambda e, bk=bk, kt=kt, h=h, pp=ppar:
                               e.activation(Pbuf[:, pp, kt, :], pb(bk), AF.Exp,
                                            bias=expbias[:, h:h + 1]),
                        waits=[(tMM, sc_t), (sEv, eb_s)] +
                              (p_free2[ppar] if kt == 0 else []), inc=sEv)
                    exp_v[kt] = e_t
                    bank_guard[bk] = [(sEv, e_t)]
                for kt in range(NKT):
                    s_t = mm(2, ones_bf[:], Pbuf[:, ppar, kt, :], start=(kt == 0),
                             stop=(kt == NKT - 1), out_rows=(0, 1),
                             waits=[(sEv, exp_v[kt])])
                rc_v = b.op("vector", lambda e: e.reciprocal(sD, pbr(2, 0, 1)),
                            waits=[(tMM, s_t), (sEv, exp_v[NKT - 1]),
                                   (tMM, rb_guard[0])], inc=vEv)
                bank_guard[2] = [(vEv, rc_v)]
                m_t = mm(7, ones1_f32[:], sD, True, True, waits=[(vEv, rc_v)])
                rb_guard[0] = m_t
                rb_s = b.op("scalar", lambda e: e.copy(rBuf[:], pb(7)),
                            waits=[(tMM, m_t), (vEv, o_evac_v)], inc=sEv)
                bank_guard[7] = [(sEv, rb_s)]
                if debug:
                    b.dma("sync", dbgR[2 * h + qc], rBuf[:], dKW,
                          waits=[(sEv, rb_s)])
                ob = 3 + (qc % 2)
                for kt in range(NKT):
                    av_t = mm(ob, vb[:, sl, kt, 128 * hh:128 * (hh + 1)],
                              Pbuf[:, ppar, kt, :], start=(kt == 0),
                              stop=(kt == NKT - 1),
                              waits=[(sEv, exp_v[NKT - 1]), (dKR, kqv_t)]
                              if kt == 0 else [])
                o_evac_v = b.op("vector", lambda e, ob=ob, h=h, qc=qc: e.tensor_mul(
                    Obuf[:, h, 512 * qc:512 * (qc + 1)], pb(ob), rBuf[:]),
                    waits=[(tMM, av_t), (sEv, rb_s)], inc=vEv)
                bank_guard[ob] = [(vEv, o_evac_v)]
                p_free2[ppar] = [(tMM, av_t)]
                emit_dq_units(1)
        pair_cons[sl] = [(tMM, tMM.n)]
    while emit_dq_unit():
        pass
    emit_dq_flush()
    ccS = b.sem("ccS")
    for t_ in range(3):
        b.op("gpsimd", lambda e, t_=t_: e.collective_compute(
            "AllGather", ALU.bypass, replica_groups=[list(range(8))],
            ins=[scr_l[3 * t_:3 * t_ + 3]], outs=[scr_f[24 * t_:24 * t_ + 24]]),
            waits=[(dQWp[0], dQWp[0].n), (dQWp[1], dQWp[1].n)],
            inc=ccS)
    cc_done = ccS.n
    gu_w_final = [(ccS, cc_done)]
    if debug is True:
        for blk8 in range(4):
            b.dma("sync", dbgS[18 * blk8:18 * (blk8 + 1)],
                  scr_f[18 * blk8:18 * (blk8 + 1)], dKW,
                  waits=([(ccS, cc_done)] if blk8 == 0 else []))
    if debug is True:
        b.dma("sync", dbgO, Obuf[:], dKW, waits=[(vEv, o_evac_v)])
        b.dma("sync", dbgE[:, 0:16], expbias[:], dKW)
        b.dma("sync", dbgE[:, 48:64], dbgE_sb[:, 0:16], dKW)

    b.barrier()

    WOT_r = WOT.rearrange("(t p) d -> p t d", p=128)
    wslot_free = [(), ()]
    xslot_guard = [[(dqv, dqv.n)], [(dqv, dqv.n)]]
    st_t = 0
    x2w = [0] * NT
    for ht in range(NT):
        sl = ht % 2
        fb = fbufA if ht % 2 == 0 else fbufB
        w_t = b.dma("gpsimd", wbufs[:, sl], WOT_r[:, :, 128 * ht:128 * (ht + 1)],
                    dWp[sl], waits=wslot_free[sl])
        x_t = b.dma("sync", fb[:, 0, :], XT_t[ht][:, 0:TOK], dXp[sl],
                    waits=xslot_guard[sl])
        banks = [2 * (ht % 2), 2 * (ht % 2) + 1]
        for it in range(NT):
            lw = [(dWp[sl], w_t)] if it == 0 else []
            ldw(wbufs[:, sl, it, :], waits=lw)
            for tc in range(2):
                mm(banks[tc], wbufs[:, sl, it, :], Obuf[:, it, 512 * tc:512 * (tc + 1)],
                   start=(it == 0), stop=(it == NT - 1))
        mm_done = tMM.n
        wslot_free[sl] = [(tMM, mm_done)]
        for tc in range(2):
            wv = [(tMM, mm_done), (dXp[sl], x_t)]
            if tc == 0 and ht > 1:
                wv.append((dX2W, x2w[ht - 2]))
            b.op("vector", lambda e, fb=fb, bk=banks[tc], tc=tc: e.tensor_add(
                fb[:, 1, 512 * tc:512 * (tc + 1)], pb(bk),
                fb[:, 0, 512 * tc:512 * (tc + 1)]),
                waits=wv)
        x2_v = b.op("vector", lambda e, fb=fb: e.tensor_copy(xb2[:], fb[:, 1, :]),
                    inc=vEv)
        bank_guard[banks[0]] = [(vEv, x2_v)]
        bank_guard[banks[1]] = [(vEv, x2_v)]
        x2_s = b.op("scalar", lambda e, fb=fb: e.activation(xsq2[:], fb[:, 1, :],
                    AF.Square), waits=[(vEv, x2_v)], inc=sEv)
        for c in range(2):
            mm(4 + c, ones_bf[:], xb2[:, 512 * c:512 * (c + 1)],
               start=(ht == 0), stop=(ht == NT - 1), out_rows=(0, 1),
               waits=[(vEv, x2_v)] if c == 0 else [])
        for c in range(2):
            st_t = mm(6 + c, ones_bf[:], xsq2[:, 512 * c:512 * (c + 1)],
                      start=(ht == 0), stop=(ht == NT - 1), out_rows=(0, 1),
                      waits=[(sEv, x2_s)] if c == 0 else [])
        x2w[ht] = b.dma("sync", x2_sc[ht], fb[:, 1, :], dX2W, waits=[(vEv, x2_v)])
        xslot_guard[sl] = [(vEv, x2_v), (sEv, x2_s), (tMM, st_t)]

    ab2_s = ln_coeffs(2, [4, 5], [6, 7], 4, aB2, bB2, [(tMM, st_t)])
    ln2_v = 0
    for ht in range(NT):
        x_t = b.dma("sync", fbufA[:, 0, :], x2_sc[ht], dX2R,
                    waits=([(sEv, ab2_s), (dX2W, dX2W.n)] if ht == 0
                           else [(vEv, ln2_v)]))
        b.op("vector", lambda e: e.tensor_mul(fbufA[:, 1, :], fbufA[:, 0, :], aB2[:]),
             waits=[(dX2R, x_t)])
        ln2_v = b.op("vector", lambda e, ht=ht: e.tensor_add(
            ln2[:, ht, :], fbufA[:, 1, :], bB2[:]), inc=vEv)

    if debug is True:
        b.dma("sync", dbgL2, ln2[:], dKW, waits=[(vEv, ln2_v)])
    b.barrier()

    pc_free = [(), (), ()]
    htmp_guard = [0]
    hevac_g = [0]
    for cb in range(6):
        W = 512 if cb == 5 else 1024
        ncol = W // 8
        for qd in range(2):
            for half, base in (("g", 1024 * cb), ("u", NI + 1024 * cb)):
                piece_t = [0] * NT

                def d_piece(it, cb=cb, qd=qd, half=half, base=base, W=W):
                    psl = it % 3
                    wv = list(pc_free[psl])
                    if cb == 0 and qd == 0 and half == "g" and it < 2:
                        wv += gu_w_final
                    nch = W // 512
                    ci0 = 22 * (it // 8) + base // 512
                    srcap = scr_f[ci0:ci0 + nch, it % 8].rearrange("a p c -> p a c")
                    dstap = wpiece[:, psl, 0:W].rearrange("p (a c) -> p a c", c=512)
                    piece_t[it] = b.dma("scalar", dstap, srcap, dFRp[psl], waits=wv)
                d_piece(0)
                d_piece(1)
                d_piece(2)
                for it in range(NT):
                    psl = it % 3
                    for jj in range(4):
                        jq = 4 * qd + jj
                        lw = [(dFRp[psl], piece_t[it])] if jj == 0 else []
                        if cb == 0 and qd == 0 and half == "g" and it == 0 and jj == 0:
                            lw.append((vEv, ln2_v))
                        ldw(wpiece[:, psl, jq:W:8], waits=lw)
                        for tc in range(2):
                            mm(2 * jj + tc, wpiece[:, psl, jq:W:8],
                               ln2[:, it, 512 * tc:512 * (tc + 1)],
                               start=(it == 0), stop=(it == NT - 1),
                               out_rows=(0, ncol))
                    pc_free[psl] = [(tMM, tMM.n)]
                    if it + 3 < NT:
                        d_piece(it + 3)
                mm_done = tMM.n
                if half == "g":
                    sg_s = 0
                    for jj in range(4):
                        for tc in range(2):
                            sg_s = b.op("scalar", lambda e, bk=2 * jj + tc, jj=jj,
                                        tc=tc, nc_=ncol: e.activation(
                                            sgbuf[0:nc_, 2 * jj + tc, :],
                                            pbr(2 * jj + tc, 0, nc_), AF.Silu),
                                        waits=([(tMM, mm_done), (vEv, hevac_g[0])]
                                               if (jj == 0 and tc == 0) else []),
                                        inc=sEv)
                            bank_guard[2 * jj + tc] = [(sEv, sg_s)]
                else:
                    for jj in range(4):
                        jq = 4 * qd + jj
                        for tc in range(2):
                            bk = 2 * jj + tc
                            if cb < 5:
                                h_v = b.op("vector", lambda e, bk=bk, cb=cb, jq=jq,
                                           tc=tc: e.tensor_mul(
                                               Hbuf[:, 8 * cb + jq, 512 * tc:512 * (tc + 1)],
                                               pbr(bk, 0, 128), sgbuf[:, bk, :]),
                                           waits=[(tMM, mm_done), (sEv, sg_s)]
                                           if (jj == 0 and tc == 0) else [], inc=vEv)
                                bank_guard[bk] = [(vEv, h_v)]
                                hevac_g[0] = h_v
                            else:
                                itH = 40 + jq // 2
                                if jq % 2 == 0:
                                    h_v = b.op("vector", lambda e, bk=bk, itH=itH, tc=tc:
                                               e.tensor_mul(
                                                   Hbuf[0:64, itH, 512 * tc:512 * (tc + 1)],
                                                   pbr(bk, 0, 64), sgbuf[0:64, bk, :]),
                                               waits=[(tMM, mm_done), (sEv, sg_s)]
                                               if (jj == 0 and tc == 0) else [], inc=vEv)
                                    bank_guard[bk] = [(vEv, h_v)]
                                    hevac_g[0] = h_v
                                else:
                                    hw = ([(tMM, mm_done), (sEv, sg_s)]
                                          if (jj == 0 and tc == 0) else [])
                                    hw.append((dSh, htmp_guard[0]))
                                    h_v = b.op("vector", lambda e, bk=bk: e.tensor_mul(
                                        htmp[0:64, :], pbr(bk, 0, 64), sgbuf[0:64, bk, :]),
                                        waits=hw, inc=vEv)
                                    sh_t = b.dma("sync",
                                                 Hbuf[64:128, itH, 512 * tc:512 * (tc + 1)],
                                                 htmp[0:64, :], dSh, waits=[(vEv, h_v)])
                                    htmp_guard[0] = sh_t
                                    bank_guard[bk] = [(vEv, h_v)]
                                    hevac_g[0] = h_v
    dq_w_final = [(ccS, cc_done)]
    h_shift_done = dSh.n
    h_done_v = vEv.n
    if debug in (True, 3):
        b.dma("sync", dbgH, Hbuf[:], dKW, waits=[(vEv, h_done_v), (dSh, h_shift_done)])

    y_guard = []
    for tc in range(2):
        y_done_s = 0
        for hb in range(2):
            piece_t = [0] * NIT

            def e_piece(it, tc=tc, hb=hb):
                psl = it % 3
                wv = list(pc_free[psl])
                if tc == 0 and hb == 0 and it < 2:
                    wv += dq_w_final
                dstap = wpiece[:, psl, :].rearrange("p (a c) -> p a c", c=512)
                if it < 40:
                    ci0 = 44 + 4 * (it // 8) + 2 * hb
                    srcap = scr_f[ci0:ci0 + 2, it % 8].rearrange("a p c -> p a c")
                    piece_t[it] = b.dma("scalar", dstap, srcap, dFRp[psl], waits=wv)
                else:
                    m = it - 40
                    ci0 = 64 + 2 * hb
                    b.dma("scalar", wpiece[0:64, psl, :].rearrange(
                        "p (a c) -> p a c", c=512),
                        scr_f[ci0:ci0 + 2, 2 * m, 0:64, :].rearrange("a p c -> p a c"),
                        dFRp[psl], waits=wv)
                    piece_t[it] = b.dma("scalar", wpiece[64:128, psl, :].rearrange(
                        "p (a c) -> p a c", c=512),
                        scr_f[ci0:ci0 + 2, 2 * m + 1, 0:64, :].rearrange("a p c -> p a c"),
                        dFRp[psl])
            e_piece(0)
            e_piece(1)
            e_piece(2)
            for it in range(NIT):
                psl = it % 3
                for jh in range(8):
                    lw = [(dFRp[psl], piece_t[it])] if jh == 0 else []
                    if tc == 0 and hb == 0 and it == 0 and jh == 0:
                        lw += [(vEv, h_done_v), (dSh, h_shift_done)]
                    ldw(wpiece[:, psl, jh:1024:8], waits=lw)
                    mm(jh, wpiece[:, psl, jh:1024:8],
                       Hbuf[:, it, 512 * tc:512 * (tc + 1)],
                       start=(it == 0), stop=(it == NIT - 1))
                pc_free[psl] = [(tMM, tMM.n)]
                if it + 3 < NIT:
                    e_piece(it + 3)
            mm_done = tMM.n
            for jh in range(8):
                y_done_s = b.op("scalar", lambda e, jh=jh, hb=hb: e.copy(
                    ybuf[:, 8 * hb + jh, :], pb(jh)),
                    waits=([(tMM, mm_done)] + list(y_guard)) if jh == 0 else [],
                    inc=sEv)
                bank_guard[jh] = [(sEv, y_done_s)]
        if debug in (True, 3):
            b.dma("sync", dbgY[tc], ybuf[:], dKW, waits=[(sEv, y_done_s)])
        ys_t = 0
        for yt in range(NT):
            ys_s = b.op("scalar", lambda e, yt=yt: e.activation(
                ysqb[:], ybuf[:, yt, :], AF.Square),
                waits=[(sEv, y_done_s)] if yt == 0 else [(tMM, ys_t)], inc=sEv)
            ys_t = mm(0, ones_bf[:], ysqb[:], start=(yt == 0), stop=(yt == NT - 1),
                      out_rows=(0, 1), waits=[(sEv, ys_s)])
        rr_s = b.op("scalar", lambda e: e.activation(
            sA, pbr(0, 0, 1), AF.Sqrt,
            bias=cells[0:1, 5:6], scale=cells[0:1, 6:7]),
            waits=[(tMM, ys_t), (tMM, rb_guard[0])], inc=sEv)
        bank_guard[0] = [(sEv, rr_s)]
        rr_v = b.op("vector", lambda e: e.reciprocal(sB, sA),
                    waits=[(sEv, rr_s)], inc=vEv)
        m_t = mm(1, ones1_f32[:], sB, True, True, waits=[(vEv, rr_v)])
        rb_guard[0] = m_t
        rrb_s = b.op("scalar", lambda e: e.copy(rrB[:], pb(1)),
                     waits=[(tMM, m_t)], inc=sEv)
        bank_guard[1] = [(sEv, rrb_s)]
        for yt in range(NT):
            par = yt % 2
            fb = fbufA if par == 0 else fbufB
            x_t = b.dma("sync", fb[:, 0, 0:512],
                        x2_sc[yt, :, 512 * tc:512 * (tc + 1)], dX2Rp[par],
                        waits=[(dOsp[par], dOsp[par].n)])
            b.op("vector", lambda e, fb=fb, yt=yt: e.tensor_mul(
                fb[:, 1, 0:512], ybuf[:, yt, :], rrB[:]),
                waits=[(sEv, rrb_s), (dX2Rp[par], x_t)])
            o_v = b.op("vector", lambda e, fb=fb: e.tensor_add(
                fb[:, 1, 512:1024], fb[:, 1, 0:512], fb[:, 0, 0:512]),
                inc=vEv)
            y_guard = [(vEv, o_v)]
            b.dma("sync", OUT[yt, :, 512 * tc:512 * (tc + 1)], fb[:, 1, 512:1024],
                  dOsp[par], waits=[(vEv, o_v)])
    f0, f1 = dOsp[0].n, dOsp[1].n
    b.plan["sync"].append(lambda e, h0=dOsp[0].h, v0=f0, h1=dOsp[1].h, v1=f1:
                          (e.wait_ge(h0, v0), e.wait_ge(h1, v1)))

    with nc.Block() as block:
        for eng_name, deco in (("sync", block.sync), ("tensor", block.tensor),
                               ("vector", block.vector), ("scalar", block.scalar),
                               ("gpsimd", block.gpsimd)):
            def run(e, items=b.plan[eng_name]):
                for f in items:
                    f(e)
            deco(run)
    b.es.close()
    return nc


_NC_CACHE = {}


def _get_nc(debug=False):
    key = debug
    if key not in _NC_CACHE:
        _NC_CACHE[key] = build_program(debug=key)
    return _NC_CACHE[key]


def _stage(L0_attn_hidden_states, L0_attn_Wq, L0_attn_Wk, L0_attn_Wv, L0_attn_Wo,
           L0_ffn_gate_up_packed, L0_ffn_gate_up_scales, L0_ffn_down_packed,
           L0_ffn_down_scales):
    x = np.asarray(L0_attn_hidden_states)
    Wq = np.asarray(L0_attn_Wq)
    Wk = np.asarray(L0_attn_Wk)
    Wv = np.asarray(L0_attn_Wv)
    Wo = np.asarray(L0_attn_Wo)
    colord = []
    for hp in range(NP):
        colord += list(range(128 * 2 * hp, 128 * 2 * hp + 256))
        colord += list(range(2048 + 64 * 2 * hp, 2048 + 64 * 2 * hp + 128))
    colord = np.array(colord)
    WQT = np.ascontiguousarray(Wq[colord][:, PERM].T.astype(np.float32))
    WKT = np.ascontiguousarray(Wk[colord][:, PERM].T.astype(np.float32))
    WVT = np.ascontiguousarray(Wv[:, PERM].T.astype(np.float32))
    WOT = np.ascontiguousarray(Wo[PERM, :].T.astype(np.float32))
    PKGa = np.asarray(L0_ffn_gate_up_packed).T
    PKDa = np.asarray(L0_ffn_down_packed).T
    SCGa = np.repeat(np.asarray(L0_ffn_gate_up_scales), 16, axis=0)
    SCDa = np.repeat(np.asarray(L0_ffn_down_scales), 16, axis=0)
    chunks = [("g", cb, oc) for cb in range(2) for oc in range(22)] + \
             [("d", cb, oc) for cb in range(6) for oc in range(4)]
    PKS_all = np.zeros((72, 128, 512), dtype=np.int32)
    SCS_all = np.zeros((72, 128, 512), dtype=np.float16)
    for g, (kind, cb, oc) in enumerate(chunks):
        if kind == "g":
            PKS_all[g] = PKGa[128 * cb:128 * cb + 128, 512 * oc:512 * (oc + 1)]
            SCS_all[g] = SCGa[128 * cb:128 * cb + 128, 512 * oc:512 * (oc + 1)]
        else:
            rows = 64 if cb == 5 else 128
            PKS_all[g, 0:rows] = PKDa[128 * cb:128 * cb + rows, 512 * oc:512 * (oc + 1)]
            SCS_all[g, 0:rows] = SCDa[128 * cb:128 * cb + rows, 512 * oc:512 * (oc + 1)]

    inv = 1.0 / (10000.0 ** (np.arange(0, DR, 2, dtype=np.float64) / DR))
    t = np.arange(S, dtype=np.float64)
    fr = np.outer(t, inv)
    cosT = np.cos(fr).astype(np.float32)
    sinT = np.sin(fr).astype(np.float32)

    in_maps = []
    for k in range(8):
        bb_, hh_ = k // 2, k % 2
        t0 = TOK * hh_
        tokord = np.concatenate([np.arange(t0, t0 + TOK),
                                 np.arange((1 - hh_) * TOK, (1 - hh_) * TOK + TOK)])
        XTk = np.ascontiguousarray(x[bb_][tokord][:, PERM].T.astype(np.float32))
        ck = cosT[tokord].T
        sk = sinT[tokord].T
        COSRk = np.ascontiguousarray(np.tile(ck, (4, 1)).astype(np.float32))
        SINAk = np.ascontiguousarray(
            np.concatenate([-sk, sk, -sk, sk], axis=0).astype(np.float32))
        gidx = [24 * (l // 3) + 3 * k + (l % 3) for l in range(9)]
        in_maps.append({
            "XT": XTk, "WQT": WQT, "WKT": WKT, "WVT": WVT, "WOT": WOT,
            "PKS": np.ascontiguousarray(PKS_all[gidx]),
            "SCS": np.ascontiguousarray(SCS_all[gidx]),
            "COSR": COSRk, "SINA": SINAk,
        })
    return in_maps


def kernel(L0_attn_hidden_states, L0_attn_Wq, L0_attn_Wk, L0_attn_Wv, L0_attn_Wo,
           L0_ffn_gate_up_packed, L0_ffn_gate_up_scales, L0_ffn_down_packed,
           L0_ffn_down_scales):
    in_maps = _stage(L0_attn_hidden_states, L0_attn_Wq, L0_attn_Wk, L0_attn_Wv,
                     L0_attn_Wo, L0_ffn_gate_up_packed, L0_ffn_gate_up_scales,
                     L0_ffn_down_packed, L0_ffn_down_scales)
    nc = _get_nc(debug=_DEBUG)
    res = run_bass_kernel_spmd(nc, in_maps, core_ids=list(range(8)))
    out = np.empty((B, S, D), dtype=np.float32)
    for k in range(8):
        bb_, hh_ = k // 2, k % 2
        o = np.asarray(res.results[k]["OUT"]).reshape(D, TOK)
        arr = np.empty((TOK, D), dtype=np.float32)
        arr[:, PERM] = o.T
        out[bb_, TOK * hh_:TOK * (hh_ + 1), :] = arr
    if _DEBUG:
        kernel._last = res
    return out


# version 12 (final)
# speedup vs baseline: 1.4793x; 1.0211x over previous
import contextlib
import numpy as np
import ml_dtypes

import concourse.bass as bass
import concourse.mybir as mybir
from concourse.bass_utils import run_bass_kernel_spmd

F32 = mybir.dt.float32
BF16 = mybir.dt.bfloat16
I32 = mybir.dt.int32
F16 = mybir.dt.float16
ALU = mybir.AluOpType
AF = mybir.ActivationFunctionType
BF = ml_dtypes.bfloat16

B, S, D = 4, 2048, 2048
H, DN, DR, DV = 16, 128, 64, 128
NI = 5632
TOK = 1024
NT = 16
NKT = 16
NP = 8
GU = 2 * NI
NIT = 44
LN_EPS = 1e-5
RMS_EPS = 1e-6
SC_NOPE = float(1.0 / np.sqrt(128.0))
SC_ROPE = 0.125

_r = np.arange(D)
PERM = (1024 * (_r // 1024) + 8 * (_r % 128) + (_r // 128) % 8).astype(np.int64)

ENGINES = ("sync", "tensor", "vector", "scalar", "gpsimd")
_DEBUG = False


class Sem:
    __slots__ = ("h", "n", "name")

    def __init__(self, h, name):
        self.h, self.n, self.name = h, 0, name


class Bld:
    def __init__(self):
        self.nc = bass.Bass("TRN2", target_bir_lowering=False)
        self.plan = {e: [] for e in ENGINES}
        self.es = contextlib.ExitStack()
        self.sems = {}

    def sem(self, name):
        if name not in self.sems:
            self.sems[name] = Sem(self.es.enter_context(self.nc.semaphore(name)), name)
        return self.sems[name]

    def op(self, eng, make, waits=(), inc=None, incn=1):
        wl = [(s.h, int(v)) for (s, v) in waits if v > 0]
        ih = None
        if inc is not None:
            inc.n += incn
            ih = inc.h

        def fn(e, make=make, wl=wl, ih=ih, iv=incn):
            for (h, v) in wl:
                e.wait_ge(h, v)
            ins = make(e)
            if ih is not None:
                ins.then_inc(ih, iv)
        self.plan[eng].append(fn)
        return inc.n if inc is not None else None

    def dma(self, eng, out, in_, sem, waits=()):
        return self.op(eng, (lambda e, o=out, i=in_: e.dma_start(out=o, in_=i)),
                       waits=waits, inc=sem, incn=16)

    def sb(self, name, shape, dtype, offset):
        base = (self.nc.sbuf_base + 31) & ~31
        return self.nc.alloc_sbuf_tensor_at(name, list(shape), dtype,
                                            offset=base + offset)

    def barrier(self):
        ph = self.sem("ph")
        ph.n += 5
        t = ph.n
        for e in ENGINES:
            def fn(eng, h=ph.h, t=t):
                eng.nop().then_inc(h, 1)
                eng.wait_ge(h, t)
            self.plan[e].append(fn)


def build_program(debug=False):
    b = Bld()
    nc = b.nc
    okind = "ExternalOutput" if debug is True else "Internal"

    XT = nc.dram_tensor("XT", [D, S], F32, kind="ExternalInput").ap()
    WQT = nc.dram_tensor("WQT", [D, 3072], F32, kind="ExternalInput").ap()
    WKT = nc.dram_tensor("WKT", [D, 3072], F32, kind="ExternalInput").ap()
    WVT = nc.dram_tensor("WVT", [D, 2048], F32, kind="ExternalInput").ap()
    WOT = nc.dram_tensor("WOT", [D, 2048], F32, kind="ExternalInput").ap()
    PKS = nc.dram_tensor("PKS", [9, 128, 512], I32, kind="ExternalInput").ap()
    SCS = nc.dram_tensor("SCS", [9, 128, 512], F16, kind="ExternalInput").ap()
    COSR = nc.dram_tensor("COSR", [128, S], F32, kind="ExternalInput").ap()
    SINA = nc.dram_tensor("SINA", [128, S], F32, kind="ExternalInput").ap()
    OUT = nc.dram_tensor("OUT", [NT, 128, TOK], F32, kind="ExternalOutput").ap()

    dbgO = nc.dram_tensor("dbgO", [128, NT, 1024], BF16, kind=okind).ap()
    dbgE = nc.dram_tensor("dbgE", [128, 64], F32, kind=okind).ap()
    dbgM = nc.dram_tensor("dbgM", [H, 1, 32], F32, kind=okind).ap()
    dbgS = nc.dram_tensor("dbgS", [72, 8, 128, 512], BF16, kind=okind).ap()
    dbgL2 = nc.dram_tensor("dbgL2", [128, NT, 1024], BF16, kind=okind).ap()
    dbgH = nc.dram_tensor("dbgH", [128, NIT, 1024], BF16, kind=okind).ap()
    dbgY = nc.dram_tensor("dbgY", [2, 128, NT, 512], F32, kind=okind).ap()
    dbgR = nc.dram_tensor("dbgR", [2 * H, 128, 512], F32, kind=okind).ap()
    k_sc = nc.dram_tensor("k_sc", [NP, 3, 128, S], BF16, kind=okind).ap()
    q_sc = nc.dram_tensor("q_sc", [NP, 3, 128, TOK], BF16, kind=okind).ap()
    v_sc = nc.dram_tensor("v_sc", [NP, NKT, 128, 256], BF16, kind=okind).ap()
    scr_l = nc.dram_tensor("scr_l", [9, 8, 128, 512], BF16, kind="Internal").ap()
    scr_f = nc.dram_tensor("scr_f", [72, 8, 128, 512], BF16, kind="Internal").ap()
    x2_sc = nc.dram_tensor("x2_sc", [NT, 128, TOK], F32, kind=("ExternalOutput" if debug in (True, 2) else "Internal")).ap()

    cells = b.sb("cells", [128, 32], F32, 0)
    ones_bf = b.sb("ones_bf", [128, 1], BF16, 128)
    ones_f32 = b.sb("ones_f32", [128, 1], F32, 160)
    ones1_f32 = b.sb("ones1_f32", [1, 128], F32, 192)
    negones1_f32 = b.sb("negones1_f32", [1, 128], F32, 207872)
    dbgE_sb = b.sb("dbgE_sb", [128, 16], F32, 208384)
    ones64A = b.sb("ones64A", [128, 1], BF16, 704)
    ones64B = b.sb("ones64B", [128, 1], BF16, 736)
    expbias = b.sb("expbias", [128, 16], F32, 768)
    rowbank = b.sb("rowbank", [128, 2048], F32, 1024)
    fbufA = b.sb("fbufA", [128, 2, 1024], F32, 9216)
    dq_t05 = b.sb("dq_t05", [128, 2, 1024], BF16, 9216)
    dq_smul = b.sb("dq_smul", [128, 2, 1024], BF16, 13312)
    wbufs = b.sb("wbufs", [128, 2, 16, 128], BF16, 17408)
    wpiece = b.sb("wpiece", [128, 3, 1024], BF16, 17408)
    tmpF = b.sb("tmpF", [128, 2048], F32, 17408)

    RD = 25600
    fbufB = b.sb("fbufB", [128, 2, 1024], F32, RD)
    pkb = b.sb("pkb", [128, 2, 512], I32, RD)
    scb = b.sb("scb", [128, 2, 512], BF16, RD + 4096)
    m32 = b.sb("m32", [128, 512], I32, RD + 6144)
    s32 = b.sb("s32", [128, 512], I32, RD + 8192)
    dq_r1 = b.sb("dq_r1", [128, 2, 1024], BF16, RD + 10240)
    dq_r2 = b.sb("dq_r2", [128, 2, 1024], BF16, RD + 14336)
    dq_t2 = b.sb("dq_t2", [128, 1024], BF16, RD + 18432)
    dq_u = b.sb("dq_u", [128, 1024], BF16, RD + 20480)
    wout = b.sb("wout", [128, 2, 2, 512], BF16, RD + 22528)

    R1 = 52224
    ln1 = b.sb("ln1", [128, NT, 2048], BF16, R1)
    sqtile = b.sb("sqtile", [128, 2048], BF16, R1)
    Pbuf = b.sb("Pbuf", [128, 2, NKT, 512], BF16, R1 + 8192)
    rBuf = b.sb("rBuf", [128, 512], F32, R1 + 40960)
    xb2 = b.sb("xb2", [128, 1024], BF16, R1)
    xsq2 = b.sb("xsq2", [128, 1024], BF16, R1 + 2048)
    ln2 = b.sb("ln2", [128, NT, 1024], BF16, R1)
    sgbuf = b.sb("sgbuf", [128, 8, 512], BF16, R1 + 32768)
    htmp = b.sb("htmp", [128, 512], BF16, R1 + 40960)
    ybuf = b.sb("ybuf", [128, NT, 512], F32, R1)
    ysqb = b.sb("ysqb", [128, 512], BF16, R1 + 32768)
    rrB = b.sb("rrB", [128, 512], F32, R1 + 33792)

    R2 = 117760
    xsq = b.sb("xsq", [128, 2, 2048], BF16, R2)
    xb = b.sb("xb", [128, 2, 2048], BF16, R2 + 8192)
    xsA = b.sb("xsA", [128, 2, 2048], F32, R2 + 16384)
    cosR = b.sb("cosR", [128, 2048], F32, R2 + 32768)
    sinA = b.sb("sinA", [128, 2048], F32, R2 + 40960)
    aB = b.sb("aB", [128, 2048], F32, R2 + 49152)
    bB = b.sb("bB", [128, 2048], F32, R2 + 57344)
    estg = b.sb("estg", [128, 512], BF16, R2 + 65536)
    s_rope = b.sb("s_rope", [128, 512], BF16, R2 + 66560)
    sh_rope = b.sb("sh_rope", [128, 512], BF16, R2 + 67584)
    ropetmp = b.sb("ropetmp", [128, 512], F32, R2 + 68608)
    ropetmp2 = b.sb("ropetmp2", [128, 512], F32, R2 + 70656)
    wvb = b.sb("wvb", [128, 16, 512], BF16, R2 + 72704)
    kb = b.sb("kb", [128, 2, 3, 2048], BF16, R2)
    qb = b.sb("qb", [128, 2, 3, 1024], BF16, R2 + 24576)
    vb = b.sb("vb", [128, 2, NKT, 256], BF16, R2 + 36864)
    Obuf = b.sb("Obuf", [128, NT, 1024], BF16, R2 + 53248)
    aB2 = b.sb("aB2", [128, 1024], F32, R2)
    bB2 = b.sb("bB2", [128, 1024], F32, R2 + 4096)
    Hbuf = b.sb("Hbuf", [128, NIT, 1024], BF16, R2)

    psAll = b.es.enter_context(nc.psum_tensor("psAll", [128, 4096], F32))

    def pb(bank):
        return psAll[:, 512 * bank:512 * (bank + 1)]

    def pbr(bank, r0, r1, c0=0, c1=512):
        return psAll[r0:r1, 512 * bank + c0:512 * bank + c1]

    dXp = [b.sem("dX0"), b.sem("dX1")]
    dWp = [b.sem("dW0"), b.sem("dW1")]
    dWV = b.sem("dWV"); dTb = b.sem("dTb")
    dPKp = [b.sem("dPK0"), b.sem("dPK1")]
    dSCp = [b.sem("dSC0"), b.sem("dSC1")]
    dKW = b.sem("dKW")
    dKRp = [b.sem("dKR0"), b.sem("dKR1")]
    dQWp = [b.sem("dQW0"), b.sem("dQW1")]
    dFRp = [b.sem("dFR0"), b.sem("dFR1"), b.sem("dFR2")]
    dX2W = b.sem("dX2W"); dX2R = b.sem("dX2R")
    dOse = b.sem("dOse"); dSh = b.sem("dSh")
    dOsp = [b.sem("dOs0"), b.sem("dOs1")]; dX2Rp = [b.sem("dX2R0"), b.sem("dX2R1")]
    tMM = b.sem("tMM"); vEv = b.sem("vEv"); sEv = b.sem("sEv")
    dqa = b.sem("dqa"); dqv = b.sem("dqv"); dqx = b.sem("dqx")

    bank_guard = {i: [] for i in range(8)}

    def mm(bank, lhsT, rhs, start, stop, out_rows=None, waits=()):
        w = list(waits)
        if start:
            w += bank_guard[bank]
            bank_guard[bank] = []
        if out_rows is None:
            out = psAll[:, 512 * bank:512 * bank + rhs.shape[-1]]
        else:
            out = psAll[out_rows[0]:out_rows[1], 512 * bank:512 * bank + rhs.shape[-1]]
        return b.op("tensor",
                    lambda e, o=out, l=lhsT, r=rhs, st=start, sp=stop:
                    e.matmul(o, l, r, start=st, stop=sp),
                    waits=w, inc=tMM)

    def ldw(wap, waits=()):
        return b.op("tensor", lambda e, w=wap: e.ldweights(w), waits=waits)

    for ci, cv in {0: -2.0, 1: 0.5, 2: -6.0, 3: 1.0, 4: LN_EPS, 5: RMS_EPS,
                   6: 1.0 / 2048.0, 7: 1.0, 8: 2.0, 9: SC_NOPE}.items():
        b.op("vector", lambda e, ci=ci, cv=cv: e.memset(cells[:, ci:ci + 1], cv))
    b.op("vector", lambda e: e.memset(ones_bf[:], 1.0))
    b.op("vector", lambda e: e.memset(ones_f32[:], 1.0))
    b.op("vector", lambda e: e.memset(ones1_f32[:], 1.0))
    b.op("vector", lambda e: e.memset(negones1_f32[:], -1.0))
    b.op("vector", lambda e: e.memset(ones64A[0:64, :], 1.0))
    b.op("vector", lambda e: e.memset(ones64A[64:128, :], 0.0))
    b.op("vector", lambda e: e.memset(ones64B[0:64, :], 0.0))
    consts_done = b.op("vector", lambda e: e.memset(ones64B[64:128, :], 1.0), inc=vEv)

    b.dma("sync", cosR[:], COSR, dTb)
    tables_done = b.dma("sync", sinA[:], SINA, dTb)

    N_DQ_CH = 9
    N_DQ_UNITS = 4 * N_DQ_CH
    dqs = {"u": 0, "pk_t": None, "sc_t": None, "a_t": 0, "pending2": None,
           "qw_done": [None] * N_DQ_UNITS}

    def emit_dq_unit():
        u = dqs["u"]
        if u >= N_DQ_UNITS:
            return False
        ci, jp = u // 4, u % 4
        slot = ci % 2
        par = u % 2
        rows = 128
        if jp == 0:
            dqs["pk_t"] = (dPKp[slot], b.dma("scalar", pkb[:, slot, :], PKS[ci],
                           dPKp[slot], waits=([(dqx, 4 * (ci - 1))] if ci >= 2 else [])))
            dqs["sc_t"] = (dSCp[slot], b.dma("gpsimd", scb[:, slot, :], SCS[ci],
                           dSCp[slot], waits=([(dqv, 4 * (ci - 1))] if ci >= 2 else [])))
        xw = [dqs["pk_t"]] if jp == 0 else []
        if u >= 2:
            xw.append((dqa, u - 1))
        for jh in range(2):
            j = 2 * jp + jh
            b.op("vector", lambda e, sl=slot, j=j:
                 e.tensor_scalar(m32[:, :], pkb[:, sl, :], 4 * j, 7,
                                 ALU.logical_shift_right, ALU.bitwise_and),
                 waits=(xw if jh == 0 else []))
            b.op("vector", lambda e, sl=slot, j=j:
                 e.tensor_scalar(s32[:, :], pkb[:, sl, :], 4 * j + 3, 1,
                                 ALU.logical_shift_right, ALU.bitwise_and))
            b.op("vector", lambda e, p=par, jh=jh:
                 e.tensor_scalar(dq_t05[:, p, 512 * jh:512 * (jh + 1)],
                                 m32[:, :], 0.5, None, ALU.mult))
            b.op("vector", lambda e, p=par, jh=jh:
                 e.tensor_scalar(dq_smul[:, p, 512 * jh:512 * (jh + 1)],
                                 s32[:, :], -2.0, 1.0, ALU.mult, ALU.add),
                 inc=(dqx if jh == 1 else None))
        x_t = dqx.n
        sw = [(dqx, x_t)]
        if u == 0:
            sw.append((vEv, consts_done))
        if u >= 2:
            sw.append((dqv, u - 1))
        b.op("scalar", lambda e, p=par: e.activation(
            dq_r1[:, p, :], dq_t05[:, p, :], AF.Relu,
            bias=cells[:, 0:1], scale=cells[:, 3:4]),
            waits=sw)
        a_t = b.op("scalar", lambda e, p=par: e.activation(
            dq_r2[:, p, :], dq_t05[:, p, :], AF.Relu,
            bias=cells[:, 2:3], scale=cells[:, 8:9]),
            inc=dqa)
        dqs["a_t"] = a_t
        def stage2(u=u, slot=slot, par=par, a_t=a_t, sc_t=dqs["sc_t"]):
            ci_ = u // 4
            vw = [(dqa, a_t), sc_t]
            if u >= 2:
                vw.append(dqs["qw_done"][u - 2])
            b.op("vector", lambda e, p=par: e.tensor_add(
                dq_u[:, :], dq_t05[:, p, :], dq_r1[:, p, :]), waits=vw)
            b.op("vector", lambda e, p=par: e.tensor_add(
                dq_t2[:, :], dq_u[:, :], dq_r2[:, p, :]))
            b.op("vector", lambda e, p=par: e.tensor_mul(
                dq_u[:, :], dq_t2[:, :], dq_smul[:, p, :]))
            b.op("vector", lambda e, sl=slot, p=par: e.tensor_mul(
                wout[:, p, 0, :], dq_u[:, 0:512], scb[:, sl, :]))
            v_t = b.op("vector", lambda e, sl=slot, p=par: e.tensor_mul(
                wout[:, p, 1, :], dq_u[:, 512:1024], scb[:, sl, :]),
                inc=dqv)
            jp_ = u % 4
            for jh in range(2):
                j = 2 * jp_ + jh
                b.dma("sync", scr_l[ci_, j], wout[:, par, jh, :], dQWp[par],
                      waits=([(dqv, v_t)] if jh == 0 else []))
            dqs["qw_done"][u] = (dQWp[par], dQWp[par].n)
        if dqs["pending2"] is not None:
            pend = dqs["pending2"]
            dqs["pending2"] = stage2
            pend()
        else:
            dqs["pending2"] = stage2
        dqs["u"] = u + 1
        return True

    def emit_dq_flush():
        if dqs["pending2"] is not None:
            dqs["pending2"]()
            dqs["pending2"] = None

    def emit_dq_units(n):
        for _ in range(n):
            if not emit_dq_unit():
                return

    XT_t = XT.rearrange("(t p) n -> t p n", p=128)
    v_cnt = [0] * NT
    s_cnt = [0] * NT
    t_cnt = [0] * NT
    x_cnt = [0] * NT
    for tau in range(NT):
        sl = tau % 2
        wv = []
        if tau >= 2:
            wv = [(vEv, v_cnt[tau - 2]), (sEv, s_cnt[tau - 2])]
        x_cnt[tau] = b.dma("sync", xsA[:, sl, :], XT_t[tau], dXp[sl], waits=wv)
        wv2 = [(dXp[sl], x_cnt[tau])]
        if tau >= 2:
            wv2.append((tMM, t_cnt[tau - 2]))
        v_cnt[tau] = b.op("vector", lambda e, sl=sl: e.tensor_copy(xb[:, sl, :], xsA[:, sl, :]),
                          waits=wv2, inc=vEv)
        s_cnt[tau] = b.op("scalar", lambda e, sl=sl: e.activation(
            xsq[:, sl, :], xsA[:, sl, :], AF.Square), waits=wv2, inc=sEv)
        mw = [(vEv, v_cnt[tau]), (sEv, s_cnt[tau])]
        for c in range(4):
            mm(c, ones_bf[:], xb[:, sl, 512 * c:512 * (c + 1)],
               start=(tau == 0), stop=(tau == NT - 1), out_rows=(0, 1),
               waits=(mw if c == 0 else []))
        for c in range(4):
            t_cnt[tau] = mm(4 + c, ones_bf[:], xsq[:, sl, 512 * c:512 * (c + 1)],
                            start=(tau == 0), stop=(tau == NT - 1), out_rows=(0, 1))
    stats_mm_done = t_cnt[NT - 1]

    sA, sB, sC, sD = (rowbank[0:1, 0:512], rowbank[0:1, 512:1024],
                      rowbank[0:1, 1024:1536], rowbank[0:1, 1536:2048])

    def ln_coeffs(nchunk, sum_banks, sq_banks, eps_cell, dst_a, dst_b, first_wait):
        abc = 0
        prev_bt = 0
        for c in range(nchunk):
            bs, bq = sum_banks[c], sq_banks[c]
            w0 = list(first_wait) if c == 0 else [(tMM, prev_bt)]
            cp1 = b.op("scalar", lambda e, bs=bs: e.copy(sA, pbr(bs, 0, 1)),
                       waits=w0, inc=sEv)
            cp2 = b.op("scalar", lambda e, bq=bq: e.copy(sB, pbr(bq, 0, 1)), inc=sEv)
            b.op("vector", lambda e: e.tensor_scalar(sC, sA, 1.0 / 2048.0, None,
                                                     ALU.mult), waits=[(sEv, cp2)])
            b.op("vector", lambda e: e.tensor_scalar(sD, sB, 1.0 / 2048.0, None,
                                                     ALU.mult))
            b.op("vector", lambda e: e.tensor_mul(sA, sC, sC))
            var_v = b.op("vector", lambda e: e.tensor_sub(sB, sD, sA), inc=vEv)
            sq_v = b.op("scalar", lambda e, ec=eps_cell: e.activation(
                sD, sB, AF.Sqrt, bias=cells[0:1, ec:ec + 1], scale=cells[0:1, 7:8]),
                waits=[(vEv, var_v)], inc=sEv)
            b.op("vector", lambda e: e.reciprocal(sA, sD), waits=[(sEv, sq_v)])
            ab_v = b.op("vector", lambda e: e.scalar_tensor_tensor(
                sB, sC, -1.0, sA, ALU.mult, ALU.mult), inc=vEv)
            mm(bs, ones1_f32[:], sA, True, True,
               waits=[(vEv, ab_v), (sEv, cp2)])
            bt = mm(bq, ones1_f32[:], sB, True, True)
            prev_bt = bt
            a1 = b.op("scalar", lambda e, c=c, bs=bs: e.copy(
                dst_a[:, 512 * c:512 * (c + 1)], pb(bs)), waits=[(tMM, bt)], inc=sEv)
            abc = b.op("scalar", lambda e, c=c, bq=bq: e.copy(
                dst_b[:, 512 * c:512 * (c + 1)], pb(bq)), inc=sEv)
            bank_guard[bs] = [(sEv, abc)]
            bank_guard[bq] = [(sEv, abc)]
        return abc

    abc2 = ln_coeffs(4, [0, 1, 2, 3], [4, 5, 6, 7], 4, aB, bB,
                     [(tMM, stats_mm_done)])

    ln_v = [0] * NT
    for tau in range(NT):
        wv = [(sEv, abc2)] if tau == 0 else [(vEv, ln_v[tau - 1])]
        x_t = b.dma("sync", xsA[:, 0, :], XT_t[tau], dXp[0], waits=wv)
        b.op("vector", lambda e: e.tensor_mul(tmpF[:], xsA[:, 0, :], aB[:]),
             waits=[(dXp[0], x_t)])
        ln_v[tau] = b.op("vector", lambda e, tau=tau: e.tensor_add(
            ln1[:, tau, :], tmpF[:], bB[:]), inc=vEv)
    ln_done = ln_v[NT - 1]

    kw_done = [0] * NP
    vw_done = [0] * NP

    rope_state = {"v": 0}

    def rope_evac(bank, scale_c, tab_sl, dst, stg_guard):
        sc_v = b.op("scalar", lambda e, bk=bank: e.copy(s_rope[:], pb(bk)),
                    waits=list(stg_guard) + [(vEv, rope_state["v"])], inc=sEv)
        sh_t = 0
        for (d0, s0) in ((0, 32), (32, 0), (64, 96), (96, 64)):
            sh_t = b.dma("sync", sh_rope[d0:d0 + 32, :], s_rope[s0:s0 + 32, :], dSh,
                         waits=([(sEv, sc_v), (vEv, rope_state["v"])] if d0 == 0 else []))
        b.op("vector", lambda e, ts=tab_sl, c=scale_c: e.scalar_tensor_tensor(
            ropetmp[:], s_rope[:], c, cosR[:, ts[0]:ts[1]], ALU.mult, ALU.mult),
            waits=[(dSh, sh_t), (dTb, tables_done)])
        b.op("vector", lambda e, ts=tab_sl, c=scale_c: e.scalar_tensor_tensor(
            ropetmp2[:], sh_rope[:], c, sinA[:, ts[0]:ts[1]], ALU.mult, ALU.mult))
        ev = b.op("vector", lambda e: e.tensor_add(estg[:], ropetmp[:], ropetmp2[:]),
                  inc=vEv)
        rope_state["v"] = ev
        w_t = b.dma("sync", dst, estg[:], dKW, waits=[(vEv, ev)])
        return (sEv, sc_v), (dKW, w_t)

    prev_w_used = 0
    wslot_free = [(), ()]
    estg_guard = []
    for which, WT, dsts, ntc, rope_scale in (
        ("q", WQT, q_sc, 2, SC_ROPE),
        ("k", WKT, k_sc, 4, 1.0),
    ):
        WT_r = WT.rearrange("(t p) d -> p t d", p=128)
        for dc in range(24):
            hp, ct = dc // 3, dc % 3
            sl = dc % 2
            wv0 = list(wslot_free[sl])
            if which == "q" and dc < 2:
                wv0.append((vEv, ln_done))
            w_t = b.dma("gpsimd", wbufs[:, sl], WT_r[:, :, 128 * dc:128 * (dc + 1)],
                        dWp[sl], waits=wv0)
            banks = [2 * (dc % 4), 2 * (dc % 4) + 1] if ntc == 2 else \
                    ([0, 1, 2, 3] if dc % 2 == 0 else [4, 5, 6, 7])
            last_ldw_t = None
            for it in range(NT):
                lw = [(dWp[sl], w_t)] if it == 0 else []
                if it == 0:
                    lw.append((vEv, ln_done))
                ldw(wbufs[:, sl, it, :], waits=lw)
                for tc in range(ntc):
                    mm(banks[tc], wbufs[:, sl, it, :],
                       ln1[:, it, 512 * tc:512 * (tc + 1)],
                       start=(it == 0), stop=(it == NT - 1))
            mm_done = tMM.n
            wslot_free[sl] = [(tMM, mm_done)]
            for tc in range(ntc):
                bk = banks[tc]
                if ct < 2:
                    if which == "q":
                        ev = b.op("scalar", lambda e, bk=bk: e.mul(
                            estg[:], pb(bk), cells[:, 9:10]),
                            waits=[(tMM, mm_done)] + estg_guard, inc=sEv)
                        gsem = (sEv, ev)
                    else:
                        ev = b.op("scalar", lambda e, bk=bk: e.copy(estg[:], pb(bk)),
                                  waits=[(tMM, mm_done)] + estg_guard, inc=sEv)
                        gsem = (sEv, ev)
                    w2 = b.dma("sync", dsts[hp, ct, :, 512 * tc:512 * (tc + 1)],
                               estg[:], dKW, waits=[gsem])
                    bank_guard[bk] = [gsem]
                    estg_guard = [(dKW, w2)]
                else:
                    pg, wg = rope_evac(bk, rope_scale, (512 * tc, 512 * (tc + 1)),
                                       dsts[hp, 2, :, 512 * tc:512 * (tc + 1)],
                                       [(tMM, mm_done)] + estg_guard)
                    bank_guard[bk] = [pg]
                    estg_guard = [wg]
            if which == "k" and ct == 2:
                kw_done[hp] = dKW.n
            emit_dq_units(1)

    WVT_r = WVT.rearrange("(t p) d -> p t d", p=128)
    wv_free = []
    for dvc in range(4):
        w_t = b.dma("gpsimd", wvb[:], WVT_r[:, :, 512 * dvc:512 * (dvc + 1)], dWV,
                    waits=wv_free)
        for kc in range(NKT):
            bk = 4 + kc % 4
            for it in range(NT):
                lw = [(dWV, w_t), (vEv, ln_done)] if it == 0 else []
                ldw(ln1[:, it, 128 * kc:128 * (kc + 1)], waits=lw)
                mm(bk, ln1[:, it, 128 * kc:128 * (kc + 1)], wvb[:, it, :],
                   start=(it == 0), stop=(it == NT - 1))
            mm_done = tMM.n
            ev = b.op("scalar", lambda e, bk=bk: e.copy(estg[:], pb(bk)),
                      waits=[(tMM, mm_done)] + estg_guard, inc=sEv)
            bank_guard[bk] = [(sEv, ev)]
            w2 = 0
            for s in range(2):
                w2 = b.dma("sync", v_sc[2 * dvc + s, kc, :, :],
                           estg[:, 256 * s:256 * (s + 1)], dKW,
                           waits=([(sEv, ev)] if s == 0 else []))
            estg_guard = [(dKW, w2)]
            if kc % 4 == 3:
                emit_dq_units(1)
        wv_free = [(tMM, tMM.n)]
    for hp in range(NP):
        vw_done[hp] = dKW.n

    b.barrier()

    dkw_final = dKW.n
    pair_cons = [(), ()]
    p_free2 = [[], []]
    o_evac_v = 0
    sq_guard_outer = [0]
    eb_guard = [0]
    rb_guard = [0]
    bound_g = [0]
    for hp in range(NP):
        sl = hp % 2
        kr1 = b.dma("sync", kb[:, sl], k_sc[hp].rearrange("c p n -> p c n"), dKRp[sl],
                    waits=list(pair_cons[sl]) + [(dKW, dkw_final)])
        kr2 = b.dma("sync", qb[:, sl], q_sc[hp].rearrange("c p n -> p c n"), dKRp[sl])
        kr3 = b.dma("sync", vb[:, sl], v_sc[hp].rearrange("t p n -> p t n"), dKRp[sl])
        dKR = dKRp[sl]
        kqv_t = dKRp[sl].n
        sq_guard = sq_guard_outer[0]
        for hh in range(2):
            h = 2 * hp + hh
            o64 = ones64A if hh == 0 else ones64B
            for (which_k, kc_hh, dstcell) in ((0, hh, 20), (2, 2, 21)):
                ck = kc_hh if which_k == 0 else 2
                sq_s = b.op("scalar", lambda e, sl=sl, ck=ck: e.activation(
                    sqtile[:], kb[:, sl, ck, :], AF.Square),
                    waits=[(dKR, kqv_t), (tMM, sq_guard), (vEv, bound_g[0])],
                    inc=sEv)
                oo = ones_bf if which_k == 0 else o64
                for c in range(4):
                    m_t = mm(5, oo[:], sqtile[:, 512 * c:512 * (c + 1)], True, True,
                             out_rows=(0, 1), waits=[(sEv, sq_s)] if c == 0 else [])
                    c_t = b.op("scalar", lambda e, c=c: e.copy(
                        rowbank[0:1, 512 * c:512 * (c + 1)], pbr(5, 0, 1)),
                        waits=[(tMM, m_t), (vEv, bound_g[0]), (tMM, rb_guard[0])],
                        inc=sEv)
                    bank_guard[5] = [(sEv, c_t)]
                sq_guard = m_t
                bound_g[0] = b.op("vector", lambda e, dc=dstcell: e.reduce_max(
                    cells[0:1, dc:dc + 1], rowbank[0:1, 0:2048],
                    mybir.AxisListType.X), waits=[(sEv, c_t)], inc=vEv)
            kx_s = b.op("scalar", lambda e: e.activation(
                cells[0:1, 22:24], cells[0:1, 20:22], AF.Sqrt),
                waits=[(vEv, bound_g[0])], inc=sEv)
            sqn_s = b.op("scalar", lambda e, sl=sl, hh=hh: e.activation(
                sqtile[:, 0:1024], qb[:, sl, hh, :], AF.Square),
                waits=[(tMM, sq_guard)], inc=sEv)
            sqr_s = b.op("scalar", lambda e, sl=sl: e.activation(
                sqtile[:, 1024:2048], qb[:, sl, 2, :], AF.Square), inc=sEv)
            for c in range(2):
                mseg = rowbank[0:1, 512 * c:512 * (c + 1)]
                m1 = mm(5, ones_bf[:], sqtile[:, 512 * c:512 * (c + 1)], True, True,
                        out_rows=(0, 1), waits=[(sEv, sqn_s)] if c == 0 else [])
                c1 = b.op("scalar", lambda e: e.copy(sC, pbr(5, 0, 1)),
                          waits=[(tMM, m1), (vEv, bound_g[0])], inc=sEv)
                bank_guard[5] = [(sEv, c1)]
                m2 = mm(5, o64[:], sqtile[:, 1024 + 512 * c:1536 + 512 * c], True, True,
                        out_rows=(0, 1), waits=[(sEv, sqr_s)])
                c2 = b.op("scalar", lambda e: e.copy(sD, pbr(5, 0, 1)),
                          waits=[(tMM, m2), (tMM, rb_guard[0])], inc=sEv)
                bank_guard[5] = [(sEv, c2)]
                sq_guard = m2
                b.op("scalar", lambda e, ms=mseg: e.activation(ms, sC, AF.Sqrt))
                qs_s = b.op("scalar", lambda e: e.activation(sC, sD, AF.Sqrt),
                            inc=sEv)
                b.op("vector", lambda e, ms=mseg: e.tensor_scalar(
                    sD, ms, cells[0:1, 22:23], None, ALU.mult),
                    waits=[(sEv, qs_s), (sEv, kx_s)])
                bound_g[0] = b.op("vector", lambda e, ms=mseg: e.scalar_tensor_tensor(
                    ms, sC, cells[0:1, 23:24], sD, ALU.mult, ALU.add), inc=vEv)
            mx_v = b.op("vector", lambda e: e.reduce_max(
                cells[0:1, 10:11], rowbank[0:1, 0:1024], mybir.AxisListType.X),
                waits=[(tMM, eb_guard[0])], inc=vEv)
            bound_g[0] = mx_v
            m_t = mm(7, negones1_f32[:], cells[0:1, 10:11], True, True,
                     waits=[(vEv, mx_v)])
            eb_guard[0] = m_t
            eb_s = b.op("scalar", lambda e, h=h: e.copy(
                expbias[:, h:h + 1], pbr(7, 0, 128, 0, 1)),
                waits=[(tMM, m_t)], inc=sEv)
            bank_guard[7] = [(sEv, eb_s)]
            if debug is True:
                b.dma("sync", dbgM[h], cells[0:1, :], dKW,
                      waits=[(sEv, eb_s), (vEv, bound_g[0])])
                if hp == 0 and hh == 1:
                    b.dma("sync", dbgE[:, 16:32], expbias[:], dKW,
                          waits=[(sEv, eb_s)])
                ps7 = b.op("scalar", lambda e, h=h: e.copy(
                    dbgE_sb[:, h:h + 1], pbr(7, 0, 128, 0, 1)), inc=sEv)
                if hp == 3 and hh == 1:
                    b.dma("sync", dbgE[:, 32:48], expbias[:], dKW,
                          waits=[(sEv, ps7)])
            sq_guard_outer[0] = sq_guard
            emit_dq_units(1)
        for hh in range(2):
            h = 2 * hp + hh
            r0, r1_ = (0, 64) if hh == 0 else (64, 128)
            for qc in range(2):
                ppar = (2 * hh + qc) % 2
                sb_rot = [0, 1, 5]
                exp_v = [0] * NKT
                for kt in range(NKT):
                    bk = sb_rot[kt % 3]
                    mm(bk, kb[:, sl, hh, 128 * kt:128 * (kt + 1)],
                       qb[:, sl, hh, 512 * qc:512 * (qc + 1)],
                       start=True, stop=False)
                    sc_t = mm(bk, kb[r0:r1_, sl, 2, 128 * kt:128 * (kt + 1)],
                              qb[r0:r1_, sl, 2, 512 * qc:512 * (qc + 1)],
                              start=False, stop=True)
                    e_t = b.op("scalar", lambda e, bk=bk, kt=kt, h=h, pp=ppar:
                               e.activation(Pbuf[:, pp, kt, :], pb(bk), AF.Exp,
                                            bias=expbias[:, h:h + 1]),
                        waits=[(tMM, sc_t), (sEv, eb_s)] +
                              (p_free2[ppar] if kt == 0 else []), inc=sEv)
                    exp_v[kt] = e_t
                    bank_guard[bk] = [(sEv, e_t)]
                for kt in range(NKT):
                    s_t = mm(2, ones_bf[:], Pbuf[:, ppar, kt, :], start=(kt == 0),
                             stop=(kt == NKT - 1), out_rows=(0, 1),
                             waits=[(sEv, exp_v[kt])])
                rc_v = b.op("vector", lambda e: e.reciprocal(sD, pbr(2, 0, 1)),
                            waits=[(tMM, s_t), (sEv, exp_v[NKT - 1]),
                                   (tMM, rb_guard[0])], inc=vEv)
                bank_guard[2] = [(vEv, rc_v)]
                m_t = mm(7, ones1_f32[:], sD, True, True, waits=[(vEv, rc_v)])
                rb_guard[0] = m_t
                rb_s = b.op("scalar", lambda e: e.copy(rBuf[:], pb(7)),
                            waits=[(tMM, m_t), (vEv, o_evac_v)], inc=sEv)
                bank_guard[7] = [(sEv, rb_s)]
                if debug:
                    b.dma("sync", dbgR[2 * h + qc], rBuf[:], dKW,
                          waits=[(sEv, rb_s)])
                ob = 3 + (qc % 2)
                for kt in range(NKT):
                    av_t = mm(ob, vb[:, sl, kt, 128 * hh:128 * (hh + 1)],
                              Pbuf[:, ppar, kt, :], start=(kt == 0),
                              stop=(kt == NKT - 1),
                              waits=[(sEv, exp_v[NKT - 1]), (dKR, kqv_t)]
                              if kt == 0 else [])
                o_evac_v = b.op("vector", lambda e, ob=ob, h=h, qc=qc: e.tensor_mul(
                    Obuf[:, h, 512 * qc:512 * (qc + 1)], pb(ob), rBuf[:]),
                    waits=[(tMM, av_t), (sEv, rb_s)], inc=vEv)
                bank_guard[ob] = [(vEv, o_evac_v)]
                p_free2[ppar] = [(tMM, av_t)]
                emit_dq_units(1)
        pair_cons[sl] = [(tMM, tMM.n)]
    while emit_dq_unit():
        pass
    emit_dq_flush()
    ccS = b.sem("ccS")
    for t_ in range(3):
        b.op("gpsimd", lambda e, t_=t_: e.collective_compute(
            "AllGather", ALU.bypass, replica_groups=[list(range(8))],
            ins=[scr_l[3 * t_:3 * t_ + 3]], outs=[scr_f[24 * t_:24 * t_ + 24]]),
            waits=[(dQWp[0], dQWp[0].n), (dQWp[1], dQWp[1].n)],
            inc=ccS)
    cc_done = ccS.n
    gu_w_final = [(ccS, cc_done)]
    if debug is True:
        for blk8 in range(4):
            b.dma("sync", dbgS[18 * blk8:18 * (blk8 + 1)],
                  scr_f[18 * blk8:18 * (blk8 + 1)], dKW,
                  waits=([(ccS, cc_done)] if blk8 == 0 else []))
    if debug is True:
        b.dma("sync", dbgO, Obuf[:], dKW, waits=[(vEv, o_evac_v)])
        b.dma("sync", dbgE[:, 0:16], expbias[:], dKW)
        b.dma("sync", dbgE[:, 48:64], dbgE_sb[:, 0:16], dKW)

    b.barrier()

    WOT_r = WOT.rearrange("(t p) d -> p t d", p=128)
    wslot_free = [(), ()]
    xslot_guard = [[(dqv, dqv.n)], [(dqv, dqv.n)]]
    st_t = 0
    x2w = [0] * NT
    for ht in range(NT):
        sl = ht % 2
        fb = fbufA if ht % 2 == 0 else fbufB
        w_t = b.dma("gpsimd", wbufs[:, sl], WOT_r[:, :, 128 * ht:128 * (ht + 1)],
                    dWp[sl], waits=wslot_free[sl])
        x_t = b.dma("sync", fb[:, 0, :], XT_t[ht][:, 0:TOK], dXp[sl],
                    waits=xslot_guard[sl])
        banks = [2 * (ht % 2), 2 * (ht % 2) + 1]
        for it in range(NT):
            lw = [(dWp[sl], w_t)] if it == 0 else []
            ldw(wbufs[:, sl, it, :], waits=lw)
            for tc in range(2):
                mm(banks[tc], wbufs[:, sl, it, :], Obuf[:, it, 512 * tc:512 * (tc + 1)],
                   start=(it == 0), stop=(it == NT - 1))
        mm_done = tMM.n
        wslot_free[sl] = [(tMM, mm_done)]
        for tc in range(2):
            wv = [(tMM, mm_done), (dXp[sl], x_t)]
            if tc == 0 and ht > 1:
                wv.append((dX2W, x2w[ht - 2]))
            b.op("vector", lambda e, fb=fb, bk=banks[tc], tc=tc: e.tensor_add(
                fb[:, 1, 512 * tc:512 * (tc + 1)], pb(bk),
                fb[:, 0, 512 * tc:512 * (tc + 1)]),
                waits=wv)
        x2_v = b.op("vector", lambda e, fb=fb: e.tensor_copy(xb2[:], fb[:, 1, :]),
                    inc=vEv)
        bank_guard[banks[0]] = [(vEv, x2_v)]
        bank_guard[banks[1]] = [(vEv, x2_v)]
        x2_s = b.op("scalar", lambda e, fb=fb: e.activation(xsq2[:], fb[:, 1, :],
                    AF.Square), waits=[(vEv, x2_v)], inc=sEv)
        for c in range(2):
            mm(4 + c, ones_bf[:], xb2[:, 512 * c:512 * (c + 1)],
               start=(ht == 0), stop=(ht == NT - 1), out_rows=(0, 1),
               waits=[(vEv, x2_v)] if c == 0 else [])
        for c in range(2):
            st_t = mm(6 + c, ones_bf[:], xsq2[:, 512 * c:512 * (c + 1)],
                      start=(ht == 0), stop=(ht == NT - 1), out_rows=(0, 1),
                      waits=[(sEv, x2_s)] if c == 0 else [])
        x2w[ht] = b.dma("sync", x2_sc[ht], fb[:, 1, :], dX2W, waits=[(vEv, x2_v)])
        xslot_guard[sl] = [(vEv, x2_v), (sEv, x2_s), (tMM, st_t)]

    ab2_s = ln_coeffs(2, [4, 5], [6, 7], 4, aB2, bB2, [(tMM, st_t)])
    ln2_v = 0
    for ht in range(NT):
        x_t = b.dma("sync", fbufA[:, 0, :], x2_sc[ht], dX2R,
                    waits=([(sEv, ab2_s), (dX2W, dX2W.n)] if ht == 0
                           else [(vEv, ln2_v)]))
        b.op("vector", lambda e: e.tensor_mul(fbufA[:, 1, :], fbufA[:, 0, :], aB2[:]),
             waits=[(dX2R, x_t)])
        ln2_v = b.op("vector", lambda e, ht=ht: e.tensor_add(
            ln2[:, ht, :], fbufA[:, 1, :], bB2[:]), inc=vEv)

    if debug is True:
        b.dma("sync", dbgL2, ln2[:], dKW, waits=[(vEv, ln2_v)])
    b.barrier()

    pc_free = [(), (), ()]
    htmp_guard = [0]
    hevac_g = [0]
    for cb in range(6):
        W = 512 if cb == 5 else 1024
        ncol = W // 8
        for qd in range(2):
            for half, base in (("g", 1024 * cb), ("u", NI + 1024 * cb)):
                piece_t = [0] * NT

                def d_piece(it, cb=cb, qd=qd, half=half, base=base, W=W):
                    psl = it % 3
                    wv = list(pc_free[psl])
                    if cb == 0 and qd == 0 and half == "g" and it < 2:
                        wv += gu_w_final
                    nch = W // 512
                    ci0 = 22 * (it // 8) + base // 512
                    srcap = scr_f[ci0:ci0 + nch, it % 8].rearrange("a p c -> p a c")
                    dstap = wpiece[:, psl, 0:W].rearrange("p (a c) -> p a c", c=512)
                    piece_t[it] = b.dma("scalar", dstap, srcap, dFRp[psl], waits=wv)
                d_piece(0)
                d_piece(1)
                d_piece(2)
                for it in range(NT):
                    psl = it % 3
                    for jj in range(4):
                        jq = 4 * qd + jj
                        lw = [(dFRp[psl], piece_t[it])] if jj == 0 else []
                        if cb == 0 and qd == 0 and half == "g" and it == 0 and jj == 0:
                            lw.append((vEv, ln2_v))
                        ldw(wpiece[:, psl, jq:W:8], waits=lw)
                        for tc in range(2):
                            mm(2 * jj + tc, wpiece[:, psl, jq:W:8],
                               ln2[:, it, 512 * tc:512 * (tc + 1)],
                               start=(it == 0), stop=(it == NT - 1),
                               out_rows=(0, ncol))
                    pc_free[psl] = [(tMM, tMM.n)]
                    if it + 3 < NT:
                        d_piece(it + 3)
                mm_done = tMM.n
                if half == "g":
                    sg_s = 0
                    for jj in range(4):
                        for tc in range(2):
                            sg_s = b.op("scalar", lambda e, bk=2 * jj + tc, jj=jj,
                                        tc=tc, nc_=ncol: e.activation(
                                            sgbuf[0:nc_, 2 * jj + tc, :],
                                            pbr(2 * jj + tc, 0, nc_), AF.Silu),
                                        waits=([(tMM, mm_done), (vEv, hevac_g[0])]
                                               if (jj == 0 and tc == 0) else []),
                                        inc=sEv)
                            bank_guard[2 * jj + tc] = [(sEv, sg_s)]
                else:
                    for jj in range(4):
                        jq = 4 * qd + jj
                        for tc in range(2):
                            bk = 2 * jj + tc
                            if cb < 5:
                                h_v = b.op("vector", lambda e, bk=bk, cb=cb, jq=jq,
                                           tc=tc: e.tensor_mul(
                                               Hbuf[:, 8 * cb + jq, 512 * tc:512 * (tc + 1)],
                                               pbr(bk, 0, 128), sgbuf[:, bk, :]),
                                           waits=[(tMM, mm_done), (sEv, sg_s)]
                                           if (jj == 0 and tc == 0) else [], inc=vEv)
                                bank_guard[bk] = [(vEv, h_v)]
                                hevac_g[0] = h_v
                            else:
                                itH = 40 + jq // 2
                                if jq % 2 == 0:
                                    h_v = b.op("vector", lambda e, bk=bk, itH=itH, tc=tc:
                                               e.tensor_mul(
                                                   Hbuf[0:64, itH, 512 * tc:512 * (tc + 1)],
                                                   pbr(bk, 0, 64), sgbuf[0:64, bk, :]),
                                               waits=[(tMM, mm_done), (sEv, sg_s)]
                                               if (jj == 0 and tc == 0) else [], inc=vEv)
                                    bank_guard[bk] = [(vEv, h_v)]
                                    hevac_g[0] = h_v
                                else:
                                    hw = ([(tMM, mm_done), (sEv, sg_s)]
                                          if (jj == 0 and tc == 0) else [])
                                    hw.append((dSh, htmp_guard[0]))
                                    h_v = b.op("vector", lambda e, bk=bk: e.tensor_mul(
                                        htmp[0:64, :], pbr(bk, 0, 64), sgbuf[0:64, bk, :]),
                                        waits=hw, inc=vEv)
                                    sh_t = b.dma("sync",
                                                 Hbuf[64:128, itH, 512 * tc:512 * (tc + 1)],
                                                 htmp[0:64, :], dSh, waits=[(vEv, h_v)])
                                    htmp_guard[0] = sh_t
                                    bank_guard[bk] = [(vEv, h_v)]
                                    hevac_g[0] = h_v
    dq_w_final = [(ccS, cc_done)]
    h_shift_done = dSh.n
    h_done_v = vEv.n
    if debug in (True, 3):
        b.dma("sync", dbgH, Hbuf[:], dKW, waits=[(vEv, h_done_v), (dSh, h_shift_done)])

    y_guard = []
    for tc in range(2):
        y_done_s = 0
        for hb in range(2):
            piece_t = [0] * NIT

            def e_piece(it, tc=tc, hb=hb):
                psl = it % 3
                wv = list(pc_free[psl])
                if tc == 0 and hb == 0 and it < 2:
                    wv += dq_w_final
                dstap = wpiece[:, psl, :].rearrange("p (a c) -> p a c", c=512)
                if it < 40:
                    ci0 = 44 + 4 * (it // 8) + 2 * hb
                    srcap = scr_f[ci0:ci0 + 2, it % 8].rearrange("a p c -> p a c")
                    piece_t[it] = b.dma("scalar", dstap, srcap, dFRp[psl], waits=wv)
                else:
                    m = it - 40
                    ci0 = 64 + 2 * hb
                    b.dma("scalar", wpiece[0:64, psl, :].rearrange(
                        "p (a c) -> p a c", c=512),
                        scr_f[ci0:ci0 + 2, 2 * m, 0:64, :].rearrange("a p c -> p a c"),
                        dFRp[psl], waits=wv)
                    piece_t[it] = b.dma("scalar", wpiece[64:128, psl, :].rearrange(
                        "p (a c) -> p a c", c=512),
                        scr_f[ci0:ci0 + 2, 2 * m + 1, 0:64, :].rearrange("a p c -> p a c"),
                        dFRp[psl])
            e_piece(0)
            e_piece(1)
            e_piece(2)
            for it in range(NIT):
                psl = it % 3
                for jh in range(8):
                    lw = [(dFRp[psl], piece_t[it])] if jh == 0 else []
                    if tc == 0 and hb == 0 and it == 0 and jh == 0:
                        lw += [(vEv, h_done_v), (dSh, h_shift_done)]
                    ldw(wpiece[:, psl, jh:1024:8], waits=lw)
                    mm(jh, wpiece[:, psl, jh:1024:8],
                       Hbuf[:, it, 512 * tc:512 * (tc + 1)],
                       start=(it == 0), stop=(it == NIT - 1))
                pc_free[psl] = [(tMM, tMM.n)]
                if it + 3 < NIT:
                    e_piece(it + 3)
            mm_done = tMM.n
            for jh in range(8):
                y_done_s = b.op("scalar", lambda e, jh=jh, hb=hb: e.copy(
                    ybuf[:, 8 * hb + jh, :], pb(jh)),
                    waits=([(tMM, mm_done)] + list(y_guard)) if jh == 0 else [],
                    inc=sEv)
                bank_guard[jh] = [(sEv, y_done_s)]
        if debug in (True, 3):
            b.dma("sync", dbgY[tc], ybuf[:], dKW, waits=[(sEv, y_done_s)])
        ys_t = 0
        for yt in range(NT):
            ys_s = b.op("scalar", lambda e, yt=yt: e.activation(
                ysqb[:], ybuf[:, yt, :], AF.Square),
                waits=[(sEv, y_done_s)] if yt == 0 else [(tMM, ys_t)], inc=sEv)
            ys_t = mm(0, ones_bf[:], ysqb[:], start=(yt == 0), stop=(yt == NT - 1),
                      out_rows=(0, 1), waits=[(sEv, ys_s)])
        rr_s = b.op("scalar", lambda e: e.activation(
            sA, pbr(0, 0, 1), AF.Sqrt,
            bias=cells[0:1, 5:6], scale=cells[0:1, 6:7]),
            waits=[(tMM, ys_t), (tMM, rb_guard[0])], inc=sEv)
        bank_guard[0] = [(sEv, rr_s)]
        rr_v = b.op("vector", lambda e: e.reciprocal(sB, sA),
                    waits=[(sEv, rr_s)], inc=vEv)
        m_t = mm(1, ones1_f32[:], sB, True, True, waits=[(vEv, rr_v)])
        rb_guard[0] = m_t
        rrb_s = b.op("scalar", lambda e: e.copy(rrB[:], pb(1)),
                     waits=[(tMM, m_t)], inc=sEv)
        bank_guard[1] = [(sEv, rrb_s)]
        for yt in range(NT):
            par = yt % 2
            fb = fbufA if par == 0 else fbufB
            x_t = b.dma("sync", fb[:, 0, 0:512],
                        x2_sc[yt, :, 512 * tc:512 * (tc + 1)], dX2Rp[par],
                        waits=[(dOsp[par], dOsp[par].n)])
            b.op("vector", lambda e, fb=fb, yt=yt: e.tensor_mul(
                fb[:, 1, 0:512], ybuf[:, yt, :], rrB[:]),
                waits=[(sEv, rrb_s), (dX2Rp[par], x_t)])
            o_v = b.op("vector", lambda e, fb=fb: e.tensor_add(
                fb[:, 1, 512:1024], fb[:, 1, 0:512], fb[:, 0, 0:512]),
                inc=vEv)
            y_guard = [(vEv, o_v)]
            b.dma("sync", OUT[yt, :, 512 * tc:512 * (tc + 1)], fb[:, 1, 512:1024],
                  dOsp[par], waits=[(vEv, o_v)])
    f0, f1 = dOsp[0].n, dOsp[1].n
    b.plan["sync"].append(lambda e, h0=dOsp[0].h, v0=f0, h1=dOsp[1].h, v1=f1:
                          (e.wait_ge(h0, v0), e.wait_ge(h1, v1)))

    with nc.Block() as block:
        for eng_name, deco in (("sync", block.sync), ("tensor", block.tensor),
                               ("vector", block.vector), ("scalar", block.scalar),
                               ("gpsimd", block.gpsimd)):
            def run(e, items=b.plan[eng_name]):
                for f in items:
                    f(e)
            deco(run)
    b.es.close()
    return nc


_NC_CACHE = {}


def _get_nc(debug=False):
    key = debug
    if key not in _NC_CACHE:
        _NC_CACHE[key] = build_program(debug=key)
    return _NC_CACHE[key]


def _stage(L0_attn_hidden_states, L0_attn_Wq, L0_attn_Wk, L0_attn_Wv, L0_attn_Wo,
           L0_ffn_gate_up_packed, L0_ffn_gate_up_scales, L0_ffn_down_packed,
           L0_ffn_down_scales):
    x = np.asarray(L0_attn_hidden_states)
    Wq = np.asarray(L0_attn_Wq)
    Wk = np.asarray(L0_attn_Wk)
    Wv = np.asarray(L0_attn_Wv)
    Wo = np.asarray(L0_attn_Wo)
    colord = []
    for hp in range(NP):
        colord += list(range(128 * 2 * hp, 128 * 2 * hp + 256))
        colord += list(range(2048 + 64 * 2 * hp, 2048 + 64 * 2 * hp + 128))
    colord = np.array(colord)
    WQT = np.ascontiguousarray(Wq[colord][:, PERM].T.astype(np.float32))
    WKT = np.ascontiguousarray(Wk[colord][:, PERM].T.astype(np.float32))
    WVT = np.ascontiguousarray(Wv[:, PERM].T.astype(np.float32))
    WOT = np.ascontiguousarray(Wo[PERM, :].T.astype(np.float32))
    PKGa = np.asarray(L0_ffn_gate_up_packed).T
    PKDa = np.asarray(L0_ffn_down_packed).T
    SCGa = np.repeat(np.asarray(L0_ffn_gate_up_scales), 16, axis=0)
    SCDa = np.repeat(np.asarray(L0_ffn_down_scales), 16, axis=0)
    chunks = [("g", cb, oc) for cb in range(2) for oc in range(22)] + \
             [("d", cb, oc) for cb in range(6) for oc in range(4)]
    PKS_all = np.zeros((72, 128, 512), dtype=np.int32)
    SCS_all = np.zeros((72, 128, 512), dtype=np.float16)
    for g, (kind, cb, oc) in enumerate(chunks):
        if kind == "g":
            PKS_all[g] = PKGa[128 * cb:128 * cb + 128, 512 * oc:512 * (oc + 1)]
            SCS_all[g] = SCGa[128 * cb:128 * cb + 128, 512 * oc:512 * (oc + 1)]
        else:
            rows = 64 if cb == 5 else 128
            PKS_all[g, 0:rows] = PKDa[128 * cb:128 * cb + rows, 512 * oc:512 * (oc + 1)]
            SCS_all[g, 0:rows] = SCDa[128 * cb:128 * cb + rows, 512 * oc:512 * (oc + 1)]

    inv = 1.0 / (10000.0 ** (np.arange(0, DR, 2, dtype=np.float64) / DR))
    t = np.arange(S, dtype=np.float64)
    fr = np.outer(t, inv)
    cosT = np.cos(fr).astype(np.float32)
    sinT = np.sin(fr).astype(np.float32)

    in_maps = []
    for k in range(8):
        bb_, hh_ = k // 2, k % 2
        t0 = TOK * hh_
        tokord = np.concatenate([np.arange(t0, t0 + TOK),
                                 np.arange((1 - hh_) * TOK, (1 - hh_) * TOK + TOK)])
        XTk = np.ascontiguousarray(x[bb_][tokord][:, PERM].T.astype(np.float32))
        ck = cosT[tokord].T
        sk = sinT[tokord].T
        COSRk = np.ascontiguousarray(np.tile(ck, (4, 1)).astype(np.float32))
        SINAk = np.ascontiguousarray(
            np.concatenate([-sk, sk, -sk, sk], axis=0).astype(np.float32))
        gidx = [24 * (l // 3) + 3 * k + (l % 3) for l in range(9)]
        in_maps.append({
            "XT": XTk, "WQT": WQT, "WKT": WKT, "WVT": WVT, "WOT": WOT,
            "PKS": np.ascontiguousarray(PKS_all[gidx]),
            "SCS": np.ascontiguousarray(SCS_all[gidx]),
            "COSR": COSRk, "SINA": SINAk,
        })
    return in_maps


def kernel(L0_attn_hidden_states, L0_attn_Wq, L0_attn_Wk, L0_attn_Wv, L0_attn_Wo,
           L0_ffn_gate_up_packed, L0_ffn_gate_up_scales, L0_ffn_down_packed,
           L0_ffn_down_scales):
    in_maps = _stage(L0_attn_hidden_states, L0_attn_Wq, L0_attn_Wk, L0_attn_Wv,
                     L0_attn_Wo, L0_ffn_gate_up_packed, L0_ffn_gate_up_scales,
                     L0_ffn_down_packed, L0_ffn_down_scales)
    nc = _get_nc(debug=_DEBUG)
    res = run_bass_kernel_spmd(nc, in_maps, core_ids=list(range(8)))
    out = np.empty((B, S, D), dtype=np.float32)
    for k in range(8):
        bb_, hh_ = k // 2, k % 2
        o = np.asarray(res.results[k]["OUT"]).reshape(D, TOK)
        arr = np.empty((TOK, D), dtype=np.float32)
        arr[:, PERM] = o.T
        out[bb_, TOK * hh_:TOK * (hh_ + 1), :] = arr
    if _DEBUG:
        kernel._last = res
    return out
